# version 01
# baseline (speedup 1.0000x reference)
import math
import numpy as np
import concourse.bass as bass
import concourse.mybir as mybir
from concourse.bass_utils import run_bass_kernel_spmd

F32 = mybir.dt.float32
U32 = mybir.dt.uint32
AF = mybir.ActivationFunctionType
Alu = mybir.AluOpType
AX = mybir.AxisListType.X

NC8 = 8
T = 2048
D = 1024
HD = 64
NB = 2
NTOK = NB * T
CEXP = 4096
EPS = 1e-5

CW1 = 6.25
CW2 = 0.033203125
CW3 = float(np.float32(2 * math.pi - CW1 - CW2))
INV2PI = float(np.float32(1.0 / (2 * math.pi)))
PIO2 = float(np.float32(math.pi / 2))

_STAGE = "full"
_RSTOP = 9
_TSTOP = 9
_MSTOP = 9

FBJ_BITS = [0x0, 0x3ca9130c, 0x3d31cc5a, 0x3d8c577a, 0x3dc517c7, 0x3e01d9a8,
            0x3e246269, 0x3e4a7b7b, 0x3e748306, 0x3e917073, 0x3eab03d5,
            0x3ec73ae1, 0x3ee65b4f, 0x3f045905, 0x3f1749f5, 0x3f2c2f4b,
            0x3f433ca5, 0x3f5caafa, 0x3f78b92b, 0x3f8bd641, 0x3f9ce8bc,
            0x3fafbe41, 0x3fc48550, 0x3fdb7148, 0x3ff4bac7, 0x40085028,
            0x4017b369, 0x4028ad2a, 0x403b6764, 0x40501060, 0x4066db22,
            0x40800002]


class KB:
    ENGS = ("sync", "scalar", "vector", "tensor", "gpsimd")

    def __init__(self, nc):
        self.nc = nc
        self.q = {e: [] for e in self.ENGS}
        self.sems = {}
        self.cnt = {}
        for s in ("LD", "SD", "TT", "VV", "SS", "GG", "CC"):
            self.sems[s] = nc.semaphore(s).__enter__()
            self.cnt[s] = 0

    def _rec(self, eng, fn, inc, n):
        if inc is not None:
            self.cnt[inc] += n
        self.q[eng].append((fn, inc, n))

    def wait(self, eng, sem, val=None):
        v = self.cnt[sem] if val is None else val
        h = self.sems[sem]
        self.q[eng].append((lambda e, h=h, v=v: e.wait_ge(h, v), None, 0))

    def wait_all(self, eng):
        for s in ("LD", "SD", "TT", "VV", "SS", "GG", "CC"):
            if self.cnt[s] > 0:
                self.wait(eng, s)

    def barrier(self):
        for e in self.ENGS:
            self.wait_all(e)

    def load(self, out, in_, slow=False):
        if slow:
            self._rec("sync", lambda e, o=out, i=in_: e.dma_start(out=o, in_=i, allow_slow_non_contiguous=True), "LD", 16)
        else:
            self._rec("sync", lambda e, o=out, i=in_: e.dma_start(out=o, in_=i), "LD", 16)

    def store(self, out, in_):
        self._rec("gpsimd", lambda e, o=out, i=in_: e.dma_start(out=o, in_=i), "SD", 16)

    def ind_gather(self, out, dram_ap, idx_ap):
        self._rec("gpsimd", lambda e, o=out, d=dram_ap, ix=idx_ap: e.indirect_dma_start(
            out=o, out_offset=None, in_=d,
            in_offset=bass.IndirectOffsetOnAxis(ap=ix, axis=0)), "SD", 16)

    def ind_scatter(self, dram_ap, in_, idx_ap):
        self._rec("gpsimd", lambda e, d=dram_ap, i=in_, ix=idx_ap: e.indirect_dma_start(
            out=d, out_offset=bass.IndirectOffsetOnAxis(ap=ix, axis=0),
            in_=i, in_offset=None), "SD", 16)

    def cc(self, kind, in_ap, out_ap):
        self.wait("gpsimd", "SD")
        self._rec("gpsimd", lambda e, k=kind, i=in_ap, o=out_ap: e.collective_compute(
            k, Alu.bypass, replica_groups=[list(range(NC8))],
            ins=[i.opt()], outs=[o.opt()]), "CC", 1)

    def mm(self, out, lhsT, rhs, start=True, stop=True, tp=None, mark=False):
        if tp is None:
            self._rec("tensor", lambda e, o=out, l=lhsT, r=rhs, s1=start, s2=stop:
                      e.matmul(o, l, r, start=s1, stop=s2), "TT" if mark else None, 1)
        else:
            self._rec("tensor", lambda e, o=out, l=lhsT, r=rhs, s1=start, s2=stop, t=tp:
                      e.matmul(o, l, r, start=s1, stop=s2, tile_position=t), "TT" if mark else None, 1)

    def transp(self, out, in_, ident, mark=False):
        self._rec("tensor", lambda e, o=out, i=in_, d=ident: e.transpose(o, i, d),
                  "TT" if mark else None, 1)

    def tmark(self):
        for j in range(len(self.q["tensor"]) - 1, -1, -1):
            fn, inc, n = self.q["tensor"][j]
            if n == 0:
                continue
            assert inc is None
            self.cnt["TT"] += 1
            self.q["tensor"][j] = (fn, "TT", 1)
            return

    def act(self, out, in_, func, bias=0.0, scale=1.0, accum_out=None):
        self._rec("scalar", lambda e, o=out, i=in_, f=func, b=bias, s=scale, a=accum_out:
                  e.activation(out=o, in_=i, func=f, bias=b, scale=s, accum_out=a), "SS", 1)

    def scopy(self, out, in_):
        self._rec("scalar", lambda e, o=out, i=in_: e.copy(out=o, in_=i), "SS", 1)

    def _v(self, fn):
        self._rec("vector", fn, "VV", 1)

    def _md(self, out):
        try:
            n = 1
            for d in out.shape[1:]:
                n *= d
        except Exception:
            n = 999
        if n <= 64:
            self._v(lambda e: e.drain())

    def tt(self, out, a, b, op):
        self._v(lambda e, o=out, x=a, y=b, p=op: e.tensor_tensor(o, x, y, p))
        self._md(out)

    def ts(self, out, in0, s1, s2, op0, op1=None, accum_out=None, safe=False):
        if not safe and (not isinstance(s1, (int, float)) or
                         (s2 is not None and not isinstance(s2, (int, float)))):
            self._v(lambda e: e.drain())
        if op1 is None:
            self._v(lambda e, o=out, i=in0, a=s1, p0=op0: e.tensor_scalar(o, i, a, None, op0=p0))
        else:
            self._v(lambda e, o=out, i=in0, a=s1, b=s2, p0=op0, p1=op1, ac=accum_out:
                    e.tensor_scalar(o, i, a, b, op0=p0, op1=p1, accum_out=ac))
        if accum_out is not None:
            self._v(lambda e: e.drain())
        else:
            self._md(out)

    def stt(self, out, in0, scalar, in1, op0, op1, accum_out=None, safe=False):
        if not safe and not isinstance(scalar, (int, float)):
            self._v(lambda e: e.drain())
        self._v(lambda e, o=out, i=in0, s=scalar, j=in1, p0=op0, p1=op1, ac=accum_out:
                e.scalar_tensor_tensor(o, i, s, j, op0=p0, op1=p1, accum_out=ac))
        if accum_out is not None:
            self._v(lambda e: e.drain())
        else:
            self._md(out)

    def red(self, out, in_, op):
        self._v(lambda e, o=out, i=in_, p=op: e.tensor_reduce(o, i, axis=AX, op=p))
        self._v(lambda e: e.drain())

    def vcopy(self, out, in_):
        self._v(lambda e, o=out, i=in_: e.tensor_copy(out=o, in_=i))
        self._md(out)

    def memset(self, ap, c):
        self._v(lambda e, a=ap, v=c: e.memset(a, v))
        self._md(ap)

    def recip(self, out, in_):
        self._v(lambda e, o=out, i=in_: e.reciprocal(out=o, in_=i))
        self._v(lambda e: e.drain())

    def scan(self, out, d0, d1, init, op0, op1):
        self._v(lambda e, o=out, a=d0, b=d1, i=init, p0=op0, p1=op1:
                e.tensor_tensor_scan(o, a, b, i, op0=p0, op1=p1))
        self._v(lambda e: e.drain())

    def shuf(self, out, in_, mask):
        self._v(lambda e, o=out, i=in_, m=mask: e.stream_shuffle(o, i, m))

    def _g(self, fn):
        self._rec("gpsimd", fn, "GG", 1)

    def iota(self, out, pattern, base=0, chm=0):
        self._g(lambda e, o=out, p=pattern, b=base, c=chm: e.iota(
            o, pattern=p, base=b, channel_multiplier=c, allow_small_or_imprecise_dtypes=True))

    def affine(self, out, in_, pattern, cmp, fill, base, chm):
        self._g(lambda e, o=out, i=in_, p=pattern, c=cmp, f=fill, b=base, m=chm:
                e.affine_select(o, i, pattern=p, compare_op=c, fill=f, base=b, channel_multiplier=m))

    def gtt(self, out, a, b, op):
        self._g(lambda e, o=out, x=a, y=b, p=op: e.tensor_tensor(o, x, y, p))

    def emit(self):
        blk = self.nc.Block().__enter__()
        for eng in self.ENGS:
            items = self.q[eng]
            sems = self.sems

            def body(e, items=items, sems=sems):
                for fn, inc, n in items:
                    ins = fn(e)
                    if inc is not None:
                        ins.then_inc(sems[inc], n)

            getattr(blk, eng)(body)
        blk.__exit__(None, None, None)


def stage_idx(s):
    return {"x1": 1, "x2": 2, "x3": 3, "route": 4, "m1": 5, "full": 5}[s]


def build(stage):
    SI = stage_idx(stage)
    nc = bass.Bass("TRN2", target_bir_lowering=False)
    di = {}

    def IN(name, shape, dt=F32):
        di[name] = nc.dram_tensor(name, list(shape), dt, kind="ExternalInput").ap()
        return di[name]

    IN("x0", [D, T]); IN("x1", [D, T])
    IN("wqkv", [D, 3 * D]); IN("wo0", [D, D])
    IN("cnw1", [D, D]); IN("cnw2", [D, 512]); IN("cnw3", [512, 256]); IN("cnw4", [256, 1])
    IN("cnb1", [128, 8]); IN("cnb2", [128, 4]); IN("cnb3", [128, 2]); IN("cnb4", [1, 1])
    IN("gcw1", [D, 512]); IN("gcw2", [512, 256]); IN("gcw3", [256, 1])
    IN("gcb1", [128, 4]); IN("gcb2", [128, 2]); IN("gcb3", [1, 1]); IN("ar", [1, 1])
    IN("lnw1", [128, 8]); IN("lnb1", [128, 8])
    IN("wq1", [D, D]); IN("wkv1", [D, 2 * D]); IN("wo1", [D, D]); IN("theta", [1, 1])
    IN("centT", [D, 8]); IN("lng", [128, 8]); IN("lnb2", [128, 8])
    IN("mw1", [D, 4 * D]); IN("mb1", [128, 32]); IN("mw2", [4 * D, D]); IN("mb2", [128, 8])
    IN("esel", [8, 1])
    IN("ones", [128, 128]); IN("lmask", [128, 128]); IN("ident", [128, 128])
    IN("fbhi", [128, 1]); IN("fblo", [128, 1]); IN("signrow", [128, 1]); IN("irow32", [1, 32])
    IN("c4096col", [8, 1]); IN("lb", [128, 128]); IN("bs", [128, 8])
    IN("exp16", [8, 128]); IN("esel128", [128, 16])

    out = nc.dram_tensor("out", [NTOK, D], F32, kind="ExternalOutput").ap()
    dbg = None
    if SI < 5:
        dbg = nc.dram_tensor("dbg", [NB * 128, 8 * T], F32, kind="ExternalOutput").ap()
    dbg3 = None
    if stage == "x2":
        dbg3 = nc.dram_tensor("dbg3", [2 * 128, 1024], F32, kind="ExternalOutput").ap()
    dbg4 = None
    dbg5 = None
    if stage == "x3":
        dbg4 = nc.dram_tensor("dbg4", [128, 4096], F32, kind="ExternalOutput").ap()
        dbg5 = nc.dram_tensor("dbg5", [128, 2048], F32, kind="ExternalOutput").ap()
    dbg6 = None
    dbg7 = None
    if stage == "m1":
        dbg6 = nc.dram_tensor("dbg6", [512, 1024], F32, kind="ExternalOutput").ap()
        dbg7 = nc.dram_tensor("dbg7", [128, 96], U32, kind="ExternalOutput").ap()
    dbg2 = None
    if stage == "route":
        dbg2 = nc.dram_tensor("dbg2", [6 * 128, 256], F32, kind="ExternalOutput").ap()

    wq1p = nc.dram_tensor("wq1p", [D, D], F32).ap()
    wkv1p = nc.dram_tensor("wkv1p", [D, 2 * D], F32).ap()
    park1 = nc.dram_tensor("park1", [128, 8, T], F32).ap()
    feat_tm = nc.dram_tensor("feat_tm", [NTOK, D], F32).ap()
    lnf_tm = nc.dram_tensor("lnf_tm", [NTOK, D], F32).ap()
    agin = nc.dram_tensor("agin", [CEXP, D], F32)
    agout = nc.dram_tensor("agout", [NC8 * CEXP, D], F32)
    scr_blk = nc.dram_tensor("scr_blk", [8, NTOK], F32)
    scr_g = nc.dram_tensor("scr_g", [NC8, 8, NTOK], F32)
    h_loc = nc.dram_tensor("h_loc", [CEXP, D], F32)
    h_g = nc.dram_tensor("h_g", [NC8 * CEXP, D], F32)

    xbuf = nc.sbuf_tensor("xbuf", [128, 16384], F32).__enter__()
    accb = nc.sbuf_tensor("accb", [128, 16384], F32).__enter__()
    qkb = nc.sbuf_tensor("qkb", [128, 4096], F32).__enter__()
    vtb = nc.sbuf_tensor("vtb", [128, 2080], F32).__enter__()
    fscr = nc.sbuf_tensor("fscr", [128, 4096], F32).__enter__()
    gscr = nc.sbuf_tensor("gscr", [128, 4096], F32).__enter__()
    wbuf = nc.sbuf_tensor("wbuf", [128, 4096], F32).__enter__()
    small = nc.sbuf_tensor("small", [128, 1024], F32).__enter__()
    u32b = nc.sbuf_tensor("u32b", [128, 96], U32).__enter__()
    cones = nc.sbuf_tensor("cones", [128, 128], F32).__enter__()
    clmask = nc.sbuf_tensor("clmask", [128, 128], F32).__enter__()
    cident = nc.sbuf_tensor("cident", [128, 128], F32).__enter__()
    cmisc = nc.sbuf_tensor("cmisc", [128, 48], F32).__enter__()
    clb = nc.sbuf_tensor("clb", [128, 128], F32).__enter__()
    cexp16 = nc.sbuf_tensor("cexp16", [8, 128], F32).__enter__()
    cesel128 = nc.sbuf_tensor("cesel128", [128, 16], F32).__enter__()
    ccent = nc.sbuf_tensor("ccent", [128, 64], F32).__enter__()
    cesel = nc.sbuf_tensor("cesel", [8, 1], F32).__enter__()

    ps = [nc.psum_tensor("psb%d" % i, [128, 512], F32).__enter__() for i in range(8)]

    kb = KB(nc)

    S_CNB1, S_CNB2, S_CNB3, S_GCB1, S_GCB2 = 0, 8, 12, 14, 18
    S_CELL = 20
    S_LNW1, S_LNB1, S_LNG, S_LNB2 = 40, 48, 56, 64
    S_BCQ, S_BCK, S_BCV = 72, 80, 88
    S_BVROW = 96
    S_MB1, S_MB2 = 224, 256
    S_P1 = 264
    S_ROWS = 296

    def cell(i):
        return small[0:1, S_CELL + i:S_CELL + i + 1]

    def p1(i):
        return small[:, S_P1 + i:S_P1 + i + 1]

    def srow(i, w=512):
        return small[0:1, S_ROWS + i * 512:S_ROWS + i * 512 + w]

    def xs(kc, lo, hi):
        return xbuf[:, kc * T + lo: kc * T + hi]

    def accs(kc, lo, hi):
        return accb[:, kc * T + lo: kc * T + hi]

    kb.load(cones[:, :], di["ones"])
    kb.load(clmask[:, :], di["lmask"])
    kb.load(cident[:, :], di["ident"])
    kb.load(cmisc[:, 44:45], di["fbhi"])
    kb.load(cmisc[:, 45:46], di["fblo"])
    kb.load(cmisc[:, 0:1], di["signrow"])
    kb.load(clb[:, :], di["lb"])
    kb.load(cmisc[0:8, 46:47], di["c4096col"])
    kb.load(cmisc[:, 1:9], di["bs"])
    kb.load(cexp16[:, :], di["exp16"])
    kb.load(cesel128[:, :], di["esel128"])
    kb.load(cmisc[0:1, 9:41], di["irow32"])
    kb.load(small[:, S_CNB1:S_CNB1 + 8], di["cnb1"])
    kb.load(small[:, S_CNB2:S_CNB2 + 4], di["cnb2"])
    kb.load(small[:, S_CNB3:S_CNB3 + 2], di["cnb3"])
    kb.load(small[:, S_GCB1:S_GCB1 + 4], di["gcb1"])
    kb.load(small[:, S_GCB2:S_GCB2 + 2], di["gcb2"])
    kb.load(cell(0), di["cnb4"])
    kb.load(cell(1), di["gcb3"])
    kb.load(cell(2), di["ar"])
    kb.load(cell(3), di["theta"])
    kb.load(small[:, S_LNW1:S_LNW1 + 8], di["lnw1"])
    kb.load(small[:, S_LNB1:S_LNB1 + 8], di["lnb1"])
    kb.load(small[:, S_LNG:S_LNG + 8], di["lng"])
    kb.load(small[:, S_LNB2:S_LNB2 + 8], di["lnb2"])
    kb.load(small[:, S_MB1:S_MB1 + 32], di["mb1"])
    kb.load(small[:, S_MB2:S_MB2 + 8], di["mb2"])
    kb.load(cesel[:, :], di["esel"])

    kb.load(ccent[:, 0:64].rearrange("p (k n) -> p k n", k=8), di["centT"].rearrange("(k p) n -> p k n", p=128))
    kb.barrier()

    first = [True]
    oc_list = [("q", oc) for oc in range(8)] + [("kv", oc) for oc in range(16)]
    for idx, (which, oc) in enumerate(oc_list):
        wsrc = di["wq1"] if which == "q" else di["wkv1"]
        wdst = wq1p if which == "q" else wkv1p
        half = idx % 2
        wb = wbuf[:, half * 2048: half * 2048 + 1024]
        src = wsrc.rearrange("(k p) n -> p k n", p=128)[:, :, oc * 128:(oc + 1) * 128]
        kb.load(wb.rearrange("p (k n) -> p k n", k=8), src)
        kb.wait("tensor", "LD")
        col = oc if which == "q" else 8 + oc
        for kc in range(8):
            kb.mm(ps[0][:, col:col + 1], wb[:, kc * 128:(kc + 1) * 128],
                  small[:, S_LNB1 + kc:S_LNB1 + kc + 1],
                  start=first[0], stop=(idx == len(oc_list) - 1 and kc == 7))
            first[0] = False
        kb.tmark()
        kb.wait("vector", "TT")
        for kc in range(8):
            kb.ts(wb[:, kc * 128:(kc + 1) * 128], wb[:, kc * 128:(kc + 1) * 128],
                  small[:, S_LNW1 + kc:S_LNW1 + kc + 1], None, Alu.mult)
        kb.wait("gpsimd", "VV")
        kb.store(wdst.rearrange("(k p) n -> p k n", p=128)[:, :, oc * 128:(oc + 1) * 128],
                 wb.rearrange("p (k n) -> p k n", k=8))
        kb.wait("sync", "SD")
    kb.wait("scalar", "TT")
    kb.scopy(small[:, S_BCQ:S_BCQ + 24], ps[0][:, 0:24])
    kb.barrier()

    kb.memset(cmisc[:, 42:43], 1.0)
    kb.memset(cmisc[:, 43:44], PIO2)
    RA = small[0:1, 808:816]
    RB = small[0:1, 816:824]
    kb.memset(RA[0:1, 0:1], 220.0)
    kb.recip(RB, RA)
    kb.tt(cell(8), cell(3), RB[0:1, 0:1], Alu.mult)
    kb.tt(p1(15), cmisc[:, 44:45], cmisc[:, 45:46], Alu.add)
    kb.wait("tensor", "VV")
    kb.mm(ps[3][:, 0:1], cones[0:1, :], cell(8), mark=True)
    kb.wait("vector", "TT")
    kb.tt(p1(1), p1(15), ps[3][:, 0:1], Alu.mult)
    kb.iota(fscr[:, 0:2048], [[1, 2048]], 0, 0)
    kb.wait("vector", "GG")
    kb.ts(qkb[:, 2048:4096], fscr[:, 0:2048], p1(1), None, Alu.mult)
    kb.ts(fscr[:, 2048:4096], qkb[:, 2048:4096], INV2PI, 0.5, Alu.mult, Alu.add)
    kb.vcopy(qkb[:, 0:2048], fscr[:, 2048:4096])
    for sh in range(16, -1, -1):
        c = float(2 ** sh)
        kb.ts(fscr[:, 0:2048], fscr[:, 2048:4096], c, None, Alu.is_ge)
        kb.stt(fscr[:, 2048:4096], fscr[:, 0:2048], -c, fscr[:, 2048:4096], Alu.mult, Alu.add)
    kb.tt(fscr[:, 0:2048], qkb[:, 0:2048], fscr[:, 2048:4096], Alu.subtract)
    kb.stt(fscr[:, 2048:4096], fscr[:, 0:2048], -CW1, qkb[:, 2048:4096], Alu.mult, Alu.add)
    kb.stt(fscr[:, 2048:4096], fscr[:, 0:2048], -CW2, fscr[:, 2048:4096], Alu.mult, Alu.add)
    kb.stt(fscr[:, 2048:4096], fscr[:, 0:2048], -CW3, fscr[:, 2048:4096], Alu.mult, Alu.add)
    kb.wait("scalar", "VV")
    kb.act(gscr[:, 2048:4096], fscr[:, 2048:4096], AF.Sin)
    kb.act(fscr[:, 0:2048], fscr[:, 2048:4096], AF.Abs)
    kb.act(gscr[:, 0:2048], fscr[:, 0:2048], AF.Sin, bias=cmisc[:, 43:44], scale=-1.0)
    kb.wait("vector", "SS")
    kb.ts(gscr[:, 2048:4096], gscr[:, 2048:4096], cmisc[:, 0:1], None, Alu.mult)
    kb.barrier()
    if dbg4 is not None:
        kb.store(dbg4, gscr[:, :])
        kb.wait("sync", "SD")
    kb.barrier()

    def wrow(i, w=512):
        return wbuf[0:1, i * 512:i * 512 + w]

    def fm_ln(src, dst, gamma=None, beta=None):
        for tc in range(4):
            lo, hi = tc * 512, tc * 512 + 512
            kb.wait("tensor", "VV")
            for kc in range(8):
                sq = fscr[:, (kc % 2) * 512:(kc % 2) * 512 + 512]
                if kc >= 2:
                    kb.wait("scalar", "TT")
                kb.act(sq, src(kc, lo, hi), AF.Square)
                kb.wait("tensor", "SS")
                kb.mm(ps[4][0:1, :], cones[:, 0:1], src(kc, lo, hi),
                      start=(kc == 0), stop=(kc == 7))
                kb.mm(ps[5][0:1, :], cones[:, 0:1], sq,
                      start=(kc == 0), stop=(kc == 7), mark=True)
            kb.wait("vector", "TT")
            kb.ts(wrow(0), ps[4][0:1, :], 1.0 / 1024, None, Alu.mult)
            kb.ts(wrow(1), ps[5][0:1, :], 1.0 / 1024, None, Alu.mult)
            kb.tt(wrow(2), wrow(0), wrow(0), Alu.mult)
            kb.tt(wrow(1), wrow(1), wrow(2), Alu.subtract)
            kb.ts(wrow(1), wrow(1), EPS, None, Alu.add)
            kb.wait("scalar", "VV")
            kb.act(wrow(2), wrow(1), AF.Ln)
            kb.act(wrow(2), wrow(2), AF.Exp, scale=-0.5)
            kb.wait("vector", "SS")
            kb.tt(wrow(3), wrow(2), wrow(2), Alu.mult)
            kb.tt(wrow(3), wrow(3), wrow(1), Alu.mult)
            kb.ts(wrow(3), wrow(3), -0.5, 1.5, Alu.mult, Alu.add)
            kb.tt(wrow(2), wrow(2), wrow(3), Alu.mult)
            kb.tt(wrow(0), wrow(0), wrow(2), Alu.mult)
            kb.wait("tensor", "VV")
            kb.mm(ps[6][:, :], cones[0:1, :], wrow(2), mark=True)
            kb.mm(ps[7][:, :], cones[0:1, :], wrow(0), mark=True)
            kb.wait("vector", "TT")
            for kc in range(8):
                tmp = fscr[:, 1024:1536]
                kb.tt(tmp, src(kc, lo, hi), ps[6][:, :], Alu.mult)
                kb.tt(dst(kc, lo, hi), tmp, ps[7][:, :], Alu.subtract)
                if gamma is not None:
                    kb.ts(dst(kc, lo, hi), dst(kc, lo, hi),
                          small[:, gamma + kc:gamma + kc + 1],
                          small[:, beta + kc:beta + kc + 1], Alu.mult, Alu.add)

    def attn_block(b, layer):
        attn_block.b = b
        wom = [0, 0]
        for hp in range(8):
            kb.wait("sync", "TT")
            kb.wait("sync", "VV")
            if layer == 0:
                src4 = di["wqkv"].rearrange("(k p) n -> p k n", p=128)
                for j, base in enumerate((2 * hp * 192, 2 * hp * 192 + 64, 2 * hp * 192 + 128)):
                    dst4 = wbuf[:, j * 1024:(j + 1) * 1024].rearrange("p (k h c) -> p k h c", k=8, h=2)
                    for hh in range(2):
                        kb.load(dst4[:, :, hh, :], src4[:, :, base + hh * 192:base + hh * 192 + 64])
                kb.load(wbuf[:, 3072:4096], di["wo0"][hp * 128:(hp + 1) * 128, :])
            else:
                kb.load(wbuf[:, 0:1024].rearrange("p (k n) -> p k n", k=8),
                        wq1p.rearrange("(k p) n -> p k n", p=128)[:, :, hp * 128:(hp + 1) * 128])
                kb.load(wbuf[:, 1024:2048].rearrange("p (k n) -> p k n", k=8),
                        wkv1p.rearrange("(k p) n -> p k n", p=128)[:, :, hp * 128:(hp + 1) * 128])
                kb.load(wbuf[:, 2048:3072].rearrange("p (k n) -> p k n", k=8),
                        wkv1p.rearrange("(k p) n -> p k n", p=128)[:, :, 1024 + hp * 128:1024 + hp * 128 + 128])
                kb.load(wbuf[:, 3072:4096], di["wo1"][hp * 128:(hp + 1) * 128, :])
            kb.wait("tensor", "LD")
            if layer == 1:
                kb.store(small[0:1, S_BVROW:S_BVROW + 128], small[:, S_BCV + hp:S_BCV + hp + 1])
                kb.wait("tensor", "SD")
            for tc in range(4):
                lo, hi = tc * 512, tc * 512 + 512
                if tc > 0:
                    kb.wait("tensor", "SS")
                for kc in range(8):
                    kb.mm(ps[0][:, :], wbuf[:, kc * 128:kc * 128 + 128], xs(kc, lo, hi),
                          start=(kc == 0), stop=(kc == 7))
                for kc in range(8):
                    kb.mm(ps[1][:, :], wbuf[:, 1024 + kc * 128:1024 + kc * 128 + 128], xs(kc, lo, hi),
                          start=(kc == 0), stop=(kc == 7), mark=True)
                kb.wait("scalar", "TT")
                if layer == 0:
                    kb.scopy(qkb[:, lo:hi], ps[0][:, :])
                    kb.scopy(qkb[:, 2048 + lo:2048 + hi], ps[1][:, :])
                else:
                    kb.act(qkb[:, lo:hi], ps[0][:, :], AF.Identity,
                           bias=small[:, S_BCQ + hp:S_BCQ + hp + 1])
                    kb.act(qkb[:, 2048 + lo:2048 + hi], ps[1][:, :], AF.Identity,
                           bias=small[:, S_BCK + hp:S_BCK + hp + 1])
            kb.memset(vtb[:, :], 1.0)
            kb.wait("tensor", "VV")
            kb.wait("scalar", "VV")
            vt3 = vtb.rearrange("p (t c) -> p t c", c=130)
            for tkc in range(16):
                bank = ps[2 + (tkc % 2)]
                if tkc >= 2:
                    kb.wait("tensor", "SS")
                for kc in range(8):
                    kb.mm(bank[:, 0:128], xbuf[:, kc * T + tkc * 128:kc * T + tkc * 128 + 128],
                          wbuf[:, 2048 + kc * 128:2048 + kc * 128 + 128],
                          start=(kc == 0), stop=(kc == 7 and layer == 0))
                if layer == 1:
                    kb.mm(bank[:, 0:128], cones[0:1, 0:128],
                          small[0:1, S_BVROW:S_BVROW + 128], start=False, stop=True)
                kb.tmark()
                kb.wait("scalar", "TT")
                kb.scopy(vt3[:, tkc, 0:64], bank[:, 0:64])
                kb.scopy(vt3[:, tkc, 65:129], bank[:, 64:128])
            if layer == 1:
                kb.wait("vector", "SS")
                for base in (0, 2048):
                    q = qkb[:, base:base + 2048]
                    kb.shuf(fscr[:, 0:2048], q, [i ^ 1 for i in range(32)])
                    kb.tt(q, q, gscr[:, 0:2048], Alu.mult)
                    kb.tt(fscr[:, 0:2048], fscr[:, 0:2048], gscr[:, 2048:4096], Alu.mult)
                    kb.tt(q, q, fscr[:, 0:2048], Alu.add)
                kb.wait("tensor", "VV")
            if layer == 1:
                kb.wait("tensor", "SS")
                for h, bank, dst in ((0, ps[6], 10), (1, ps[7], 11)):
                    for tkc in range(16):
                        kb.mm(bank[0:65, 0:1], vt3[:, tkc, h * 65:h * 65 + 65], cones[:, 0:1],
                              start=(tkc == 0), stop=(tkc == 15))
                    kb.tmark()
                kb.wait("scalar", "TT")
                kb.scopy(p1(10)[0:65, :], ps[6][0:65, 0:1])
                kb.scopy(p1(11)[0:65, :], ps[7][0:65, 0:1])
            scale = (1.0 / 32.0) if layer == 0 else 0.125
            for qt in range(4):
                kmax = 4 * qt + 4 if layer == 0 else 16
                kb.wait("tensor", "VV")
                sevac = []
                for kc in range(kmax):
                    sb0 = ps[0] if kc % 2 == 0 else ps[2]
                    sb1 = ps[1] if kc % 2 == 0 else ps[3]
                    if kc >= 2:
                        kb.wait("tensor", "SS", sevac[kc - 2])
                    kb.mm(sb0[:, :], qkb[0:64, 2048 + kc * 128:2048 + kc * 128 + 128],
                          qkb[0:64, qt * 512:qt * 512 + 512], tp=(0, 0))
                    kb.tmark()
                    kb.mm(sb1[:, :], qkb[64:128, 2048 + kc * 128:2048 + kc * 128 + 128],
                          qkb[64:128, qt * 512:qt * 512 + 512], tp=(64, 0))
                    kb.tmark()
                    kb.wait("scalar", "TT")
                    p0 = fscr[:, (kc % 2) * 512:(kc % 2) * 512 + 512]
                    p1_ = fscr[:, 1024 + (kc % 2) * 512:1024 + (kc % 2) * 512 + 512]
                    if layer == 0:
                        kb.act(p0, sb0[:, :], AF.Exp, scale=scale)
                        kb.act(p1_, sb1[:, :], AF.Exp, scale=scale)
                        if kc >= 4 * qt:
                            kb.wait("gpsimd", "SS")
                            kb.affine(p0, p0, [[1, 512]], Alu.is_ge, 0.0,
                                      qt * 512 - kc * 128, -1)
                            kb.affine(p1_, p1_, [[1, 512]], Alu.is_ge, 0.0,
                                      qt * 512 - kc * 128, -1)
                            kb.wait("tensor", "GG")
                    else:
                        kb.act(p0, sb0[:, :], AF.Square, scale=scale, bias=cmisc[:, 42:43])
                        kb.act(p1_, sb1[:, :], AF.Square, scale=scale, bias=cmisc[:, 42:43])
                    kb.wait("tensor", "SS")
                    kb.mm(ps[4][0:65, :], vt3[:, kc, 0:65], p0,
                          start=(kc == 0), stop=(kc == kmax - 1))
                    kb.mm(ps[5][0:65, :], vt3[:, kc, 65:130], p1_,
                          start=(kc == 0), stop=(kc == kmax - 1))
                    kb.tmark()
                    sevac.append(kb.cnt["SS"])
                kb.wait("vector", "TT")
                for h, ob in ((0, ps[4]), (1, ps[5])):
                    rr = wrow(h)
                    if layer == 0:
                        kb.recip(rr, ob[64:65, :])
                    else:
                        kb.ts(rr, ob[64:65, :], 2048.0, None, Alu.add)
                        kb.recip(rr, rr)
                    kb.wait("tensor", "VV")
                    rb = ps[6] if h == 0 else ps[7]
                    kb.mm(rb[0:64, :], cones[0:1, 0:64], rr, mark=True)
                    kb.wait("scalar", "TT")
                    kb.scopy(fscr[0:64, 0:512], rb[0:64, :])
                    kb.wait("vector", "SS")
                    dst = fscr[h * 64:h * 64 + 64, 2048 + qt * 512:2048 + qt * 512 + 512]
                    if layer == 0:
                        kb.tt(dst, ob[0:64, :], fscr[0:64, 0:512], Alu.mult)
                    else:
                        kb.stt(dst, ob[0:64, :], p1(10 + h)[0:64, :],
                               fscr[0:64, 0:512], Alu.add, Alu.mult)
            kb.wait("tensor", "VV")
            if layer == 1 and hp == 0 and attn_block.b == 0 and dbg5 is not None:
                kb.wait("gpsimd", "VV")
                kb.store(dbg5, fscr[:, 2048:4096])
                kb.wait("tensor", "SD")
            for tc in range(4):
                for oc in range(8):
                    bank = ps[2 + (oc % 2)]
                    if tc * 8 + oc >= 2:
                        kb.wait("tensor", "VV", wom[-2])
                    kb.mm(bank[:, :], wbuf[:, 3072 + oc * 128:3072 + oc * 128 + 128],
                          fscr[:, 2048 + tc * 512:2048 + tc * 512 + 512], mark=True)
                    kb.wait("vector", "TT")
                    src1 = xs(oc, tc * 512, tc * 512 + 512) if (layer == 0 and hp == 0) \
                        else accs(oc, tc * 512, tc * 512 + 512)
                    kb.tt(accs(oc, tc * 512, tc * 512 + 512), bank[:, :], src1, Alu.add)
                    wom.append(kb.cnt["VV"])

    def ffn_block(b):
        ffn_block.b = b
        x3 = xbuf.rearrange("p (k t) -> p k t", k=8)
        kb.wait("gpsimd", "VV")
        kb.wait("gpsimd", "SS")
        kb.store(park1, x3)
        kb.vcopy(accb[:, :], xbuf[:, :])
        for kc in range(8):
            kb.red(p1(24 + kc), xs(kc, 0, T), Alu.add)
        kb.ts(small[:, S_P1 + 24:S_P1 + 32], small[:, S_P1 + 24:S_P1 + 32],
              1.0 / T, None, Alu.mult)
        if dbg3 is not None and ffn_block.b == 0:
            kb.wait("gpsimd", "VV")
            kb.store(dbg3[0:128, :], small[:, 0:1024])
        kb.wait("sync", "TT")
        kb.load(wbuf[:, 0:4096].rearrange("p (k n) -> p k n", k=8),
                di["gcw1"].rearrange("(k p) n -> p k n", p=128))
        kb.wait("tensor", "LD")
        kb.wait("tensor", "VV")
        first = True
        for oc in range(4):
            for kc in range(8):
                kb.mm(ps[0][:, oc:oc + 1], wbuf[:, kc * 512 + oc * 128:kc * 512 + oc * 128 + 128],
                      p1(24 + kc), start=first, stop=(oc == 3 and kc == 7))
                first = False
        kb.tmark()
        kb.wait("scalar", "TT")
        for oc in range(4):
            kb.act(p1(16 + oc), ps[0][:, oc:oc + 1], AF.Gelu,
                   bias=small[:, S_GCB1 + oc:S_GCB1 + oc + 1])
        kb.wait("sync", "TT")
        kb.load(wbuf[:, 0:1024].rearrange("p (k n) -> p k n", k=4),
                di["gcw2"].rearrange("(k p) n -> p k n", p=128))
        kb.wait("tensor", "LD")
        kb.wait("tensor", "SS")
        first = True
        for oc in range(2):
            for kc in range(4):
                kb.mm(ps[1][:, oc:oc + 1], wbuf[:, kc * 256 + oc * 128:kc * 256 + oc * 128 + 128],
                      p1(16 + kc), start=first, stop=(oc == 1 and kc == 3))
                first = False
        kb.tmark()
        kb.wait("scalar", "TT")
        for oc in range(2):
            kb.act(p1(20 + oc), ps[1][:, oc:oc + 1], AF.Gelu,
                   bias=small[:, S_GCB2 + oc:S_GCB2 + oc + 1])
        kb.wait("sync", "TT")
        kb.load(wbuf[:, 0:2].rearrange("p (k n) -> p k n", k=2),
                di["gcw3"].rearrange("(k p) n -> p k n", p=128), slow=True)
        kb.wait("tensor", "LD")
        kb.wait("tensor", "SS")
        for kc in range(2):
            kb.mm(ps[2][0:1, 0:1], wbuf[:, kc:kc + 1], p1(20 + kc),
                  start=(kc == 0), stop=(kc == 1))
        kb.tmark()
        kb.wait("scalar", "TT")
        kb.act(cell(4), ps[2][0:1, 0:1], AF.Sigmoid, bias=cell(1))
        kb.wait("vector", "SS")
        kb.ts(cell(5), cell(4), 0.7, None, Alu.mult)
        kb.wait("sync", "SD")
        kb.wait("sync", "VV")
        kb.load(xbuf[:, 8192:16384].rearrange("p (k n) -> p k n", k=8),
                di["cnw1"].rearrange("(k p) n -> p k n", p=128))
        kb.load(qkb[:, 0:4096].rearrange("p (k n) -> p k n", k=8),
                di["cnw2"].rearrange("(k p) n -> p k n", p=128))
        kb.load(vtb[:, 0:1024].rearrange("p (k n) -> p k n", k=4),
                di["cnw3"].rearrange("(k p) n -> p k n", p=128))
        kb.load(vtb[:, 1024:1026].rearrange("p (k n) -> p k n", k=2),
                di["cnw4"].rearrange("(k p) n -> p k n", p=128), slow=True)
        kb.wait("tensor", "LD")
        for cyc in range(3):
            for tc in range(4):
                lo, hi = tc * 512, tc * 512 + 512
                kb.wait("tensor", "VV")
                kb.wait("tensor", "SS")
                for oc in range(8):
                    bank = ps[oc % 2]
                    if oc >= 2:
                        kb.wait("tensor", "SS")
                    for kc in range(8):
                        kb.mm(bank[:, :], xbuf[:, 8192 + kc * 1024 + oc * 128:8192 + kc * 1024 + oc * 128 + 128],
                              accs(kc, lo, hi), start=(kc == 0), stop=(kc == 7))
                    kb.tmark()
                    kb.wait("scalar", "TT")
                    kb.act(xbuf[:, oc * 512:oc * 512 + 512], bank[:, :], AF.Gelu,
                           bias=small[:, S_CNB1 + oc:S_CNB1 + oc + 1])
                for oc in range(4):
                    bank = ps[2 + oc % 2]
                    if oc >= 2:
                        kb.wait("tensor", "SS")
                    for kc in range(8):
                        kb.mm(bank[:, :], qkb[:, kc * 512 + oc * 128:kc * 512 + oc * 128 + 128],
                              xbuf[:, kc * 512:kc * 512 + 512], start=(kc == 0), stop=(kc == 7))
                    kb.tmark()
                    kb.wait("scalar", "TT")
                    kb.act(xbuf[:, 4096 + oc * 512:4096 + oc * 512 + 512], bank[:, :], AF.Gelu,
                           bias=small[:, S_CNB2 + oc:S_CNB2 + oc + 1])
                for oc in range(2):
                    bank = ps[4 + oc % 2]
                    for kc in range(4):
                        kb.mm(bank[:, :], vtb[:, kc * 256 + oc * 128:kc * 256 + oc * 128 + 128],
                              xbuf[:, 4096 + kc * 512:4096 + kc * 512 + 512],
                              start=(kc == 0), stop=(kc == 3))
                    kb.tmark()
                    kb.wait("scalar", "TT")
                    kb.act(xbuf[:, 6144 + oc * 512:6144 + oc * 512 + 512], bank[:, :], AF.Gelu,
                           bias=small[:, S_CNB3 + oc:S_CNB3 + oc + 1])
                for kc in range(2):
                    kb.mm(ps[6][0:1, :], vtb[:, 1024 + kc:1024 + kc + 1],
                          xbuf[:, 6144 + kc * 512:6144 + kc * 512 + 512],
                          start=(kc == 0), stop=(kc == 1))
                kb.tmark()
                kb.wait("scalar", "TT")
                kb.act(wrow(0), ps[6][0:1, :], AF.Sigmoid, bias=cell(0))
                kb.wait("vector", "SS")
                kb.ts(wrow(1), wrow(0), 0.3, cell(5), Alu.mult, Alu.add)
                kb.ts(wrow(1), wrow(1), 0.5, cell(2), Alu.subtract, Alu.mult)
                kb.ts(wrow(1), wrow(1), 2.0, 0.1, Alu.mult, Alu.add)
                kb.wait("tensor", "VV")
                kb.mm(ps[7][:, :], cones[0:1, :], wrow(1), mark=True)
                kb.wait("vector", "TT")
                for j in range(4):
                    sgn = 1.0 if j < 2 else -1.0
                    a = accs(j, lo, hi)
                    bb = accs(j + 4, lo, hi)
                    kb.tt(fscr[:, 0:512], bb, ps[7][:, :], Alu.mult)
                    kb.tt(fscr[:, 512:1024], a, ps[7][:, :], Alu.mult)
                    kb.stt(a, fscr[:, 0:512], sgn, a, Alu.mult, Alu.add)
                    kb.stt(bb, fscr[:, 512:1024], -sgn, bb, Alu.mult, Alu.add)
        for tc in range(4):
            lo, hi = tc * 512, tc * 512 + 512
            for kc in range(8):
                tile = fscr[:, 1024 + (kc % 2) * 512:1024 + (kc % 2) * 512 + 512]
                if tc * 8 + kc >= 2:
                    kb.wait("sync", "VV")
                kb.load(tile, park1[:, kc, lo:hi])
                kb.wait("vector", "LD")
                kb.tt(xs(kc, lo, hi), tile, accs(kc, lo, hi), Alu.add)
        kb.barrier()
        fm_ln(xs, xs)

    def l1_tail(b):
        fm_ln(xs, accs, gamma=S_LNG, beta=S_LNB2)
        kb.barrier()
        if _TSTOP <= 1:
            return
        for tc in range(4):
            lo, hi = tc * 512, tc * 512 + 512
            for kc in range(8):
                kb.mm(ps[0][0:8, :], ccent[:, kc * 8:kc * 8 + 8], accs(kc, lo, hi),
                      start=(kc == 0), stop=(kc == 7))
            kb.tmark()
            kb.wait("scalar", "TT")
            kb.scopy(fscr[0:8, 0:512], ps[0][0:8, :])
            kb.wait("gpsimd", "SS")
            j0 = b * T + tc * 512
            kb.store(scr_blk.ap()[:, j0:j0 + 512], fscr[0:8, 0:512])
            kb.wait("tensor", "SD")
        if _TSTOP <= 2:
            return
        NTG = 1 if _TSTOP == 4 else 16
        DOST = _TSTOP != 5
        for tg in range(NTG):
            for (srcb, dst_dram, off) in ((xbuf, feat_tm, 1024), (accb, lnf_tm, 2048)):
                for kc in range(8):
                    bank = ps[1 + kc % 2]
                    kb.wait("tensor", "SS")
                    kb.transp(bank[:, 0:128], srcb[:, kc * T + tg * 128:kc * T + tg * 128 + 128],
                              cident[:, :], mark=True)
                    kb.wait("scalar", "TT")
                    kb.scopy(fscr[:, off + kc * 128:off + kc * 128 + 128], bank[:, 0:128])
                if DOST:
                    kb.wait("gpsimd", "SS")
                    r0 = (b * 16 + tg) * 128
                    kb.store(dst_dram[r0:r0 + 128, :], fscr[:, off:off + 1024])
            if DOST:
                kb.wait("scalar", "SD")

    def cross_reduce(dst, src, op, bA, bB):
        kb.wait("gpsimd", "VV")
        kb.store(small[0:1, 680:808], src)
        kb.wait("vector", "SD")
        kb.red(cell(14), small[0:1, 680:808], op)
        kb.wait("tensor", "VV")
        kb.mm(bB[:, 0:1], cones[0:1, :], cell(14), mark=True)
        kb.wait("vector", "TT")
        kb.vcopy(dst, bB[:, 0:1])

    def routing():
        G = lambda c0, w: gscr[:, c0:c0 + w]
        s_all = lambda e: gscr[:, e * 256:e * 256 + 256]
        mem = lambda e: fscr[:, e * 256:e * 256 + 256]
        s_cur = G(2048, 256)
        dump = G(2304, 256)
        eq_t = G(2560, 256)
        gt_t = G(2816, 256)
        u_t = G(3072, 256)
        taken = G(3328, 256)
        scn = G(3584, 256)
        zrow = qkb[0:1, 2048:2304]

        if _RSTOP <= -1:
            return
        kb.cc("AllGather", scr_blk.ap(), scr_g.ap())
        kb.wait("sync", "CC")
        if _RSTOP <= -2:
            return
        for c in range(8):
            kb.load(gscr[c * 16:(c + 1) * 16, 0:2048].rearrange("p (e f) -> p e f", e=8),
                    scr_g.ap().rearrange("c e (pr f) -> c pr e f", pr=16)[c, :, :, :])
        kb.memset(zrow, 0.0)
        kb.memset(taken, 0.0)
        kb.wait("vector", "LD")
        if _RSTOP <= 0:
            return
        LO, HI, TH, PRED, CNT, M_, VREP, NOTP, D_ = (p1(0), p1(1), p1(2), p1(3),
                                                     p1(4), p1(5), p1(6), p1(7), p1(8))
        for e in range(8):
            kb.wait("tensor", "VV")
            if e == 0:
                kb.vcopy(s_cur, s_all(0))
            elif e < 7:
                kb.stt(s_cur, taken, -2e30, s_all(e), Alu.mult, Alu.add)
            if e == 7:
                kb.ts(mem(7), taken, -1.0, 1.0, Alu.mult, Alu.add)
            else:
                kb.stt(u_t, taken, 4e30, s_cur, Alu.mult, Alu.add)
                kb.red(D_, u_t, Alu.min)
                cross_reduce(LO, D_, Alu.min, ps[0], ps[1])
                kb.red(D_, s_cur, Alu.max)
                cross_reduce(HI, D_, Alu.max, ps[0], ps[1])
                for _ in range(30):
                    kb.ts(TH, LO, HI, 0.5, Alu.add, Alu.mult)
                    kb.ts(dump, s_cur, TH, 0.0, Alu.is_ge, Alu.add, accum_out=CNT)
                    kb.wait("tensor", "VV")
                    kb.mm(ps[0][:, 0:1], cones[:, :], CNT, mark=True)
                    kb.wait("vector", "TT")
                    kb.ts(PRED, ps[0][:, 0:1], float(CEXP), None, Alu.is_ge)
                    kb.tt(D_, TH, LO, Alu.subtract)
                    kb.tt(D_, PRED, D_, Alu.mult)
                    kb.tt(LO, LO, D_, Alu.add)
                    kb.ts(NOTP, PRED, -1.0, 1.0, Alu.mult, Alu.add)
                    kb.tt(D_, TH, HI, Alu.subtract)
                    kb.tt(D_, NOTP, D_, Alu.mult)
                    kb.tt(HI, HI, D_, Alu.add)
                kb.ts(dump, s_cur, LO, None, Alu.is_lt)
                kb.stt(u_t, dump, 4e30, s_cur, Alu.mult, Alu.add)
                kb.red(D_, u_t, Alu.min)
                cross_reduce(VREP, D_, Alu.min, ps[0], ps[1])
                kb.ts(dump, s_cur, VREP, 0.0, Alu.is_gt, Alu.add, accum_out=CNT)
                kb.wait("tensor", "VV")
                kb.mm(ps[0][:, 0:1], cones[:, :], CNT, mark=True)
                kb.wait("vector", "TT")
                kb.ts(M_, ps[0][:, 0:1], -1.0, float(CEXP), Alu.mult, Alu.add)
                kb.ts(eq_t, s_cur, VREP, None, Alu.is_equal)
                kb.scan(scn, eq_t, eq_t, 0.0, Alu.add, Alu.bypass)
                kb.vcopy(p1(9), scn[:, 255:256])
                kb.tt(scn, scn, eq_t, Alu.subtract)
                kb.wait("tensor", "VV")
                kb.mm(ps[2][:, 0:1], clmask[:, :], p1(9), mark=True)
                kb.wait("vector", "TT")
                kb.ts(scn, scn, ps[2][:, 0:1], None, Alu.add)
                kb.ts(dump, scn, M_, None, Alu.is_lt)
                kb.ts(gt_t, s_cur, VREP, None, Alu.is_gt)
                kb.tt(eq_t, eq_t, dump, Alu.mult)
                kb.tt(mem(e), gt_t, eq_t, Alu.add)
            if e < 7:
                kb.tt(taken, taken, mem(e), Alu.add)
        if _RSTOP <= 1:
            return
        hrow_full = G(2048, 256)
        lp_full = G(2304, 256)
        kb.wait("vector", "TT")
        kb.memset(hrow_full, 0.0)
        kb.memset(lp_full, 0.0)
        bp8m = qkb[0:8, 2304:2312]
        st8m = qkb[0:8, 2312:2320]
        cnt8 = qkb[0:8, 2320:2328]
        run8 = qkb[0:8, 2328:2336]
        c1 = p1(10)
        c3 = p1(11)
        kb.memset(run8[:, 0:1], 0.0)
        for e in range(8):
            kb.wait("tensor", "VV")
            kb.scan(scn, mem(e), mem(e), 0.0, Alu.add, Alu.bypass)
            kb.vcopy(p1(9), scn[:, 255:256])
            kb.tt(scn, scn, mem(e), Alu.subtract)
            kb.wait("tensor", "VV")
            kb.mm(ps[2][:, 0:1], clmask[:, :], p1(9), start=True, stop=True)
            kb.mm(ps[3][:, 0:1], clb[:, :], p1(9), start=True, stop=True)
            kb.mm(ps[4][0:8, 0:1], cmisc[:, 1:9], p1(9), mark=True)
            kb.wait("vector", "TT")
            kb.vcopy(p1(12)[0:8, :], ps[4][0:8, 0:1])
            kb.vcopy(qkb[0:8, 2320 + e:2321 + e], p1(12)[0:8, :])
            kb.vcopy(st8m[:, e:e + 1], run8[:, 0:1])
            kb.tt(run8[:, 0:1], run8[:, 0:1], p1(12)[0:8, :], Alu.add)
            kb.wait("tensor", "VV")
            kb.mm(ps[5][0:8, 0:1], clmask[0:8, 0:8], p1(12)[0:8, :], mark=True)
            kb.wait("vector", "TT")
            kb.vcopy(bp8m[:, e:e + 1], ps[5][0:8, 0:1])
            kb.ts(u_t, scn, ps[2][:, 0:1], float(CEXP * e), Alu.add, Alu.add)
            kb.tt(taken, mem(e), u_t, Alu.mult)
            kb.tt(hrow_full, hrow_full, taken, Alu.add)
            kb.vcopy(c1, ps[2][:, 0:1])
            kb.tt(c1, c1, ps[3][:, 0:1], Alu.subtract)
            kb.wait("tensor", "VV")
            kb.mm(ps[6][:, 0:1], cexp16[:, :], st8m[:, e:e + 1], mark=True)
            kb.wait("vector", "TT")
            kb.tt(c3, c1, ps[6][:, 0:1], Alu.add)
            kb.ts(u_t, scn, c3, None, Alu.add)
            kb.tt(taken, mem(e), u_t, Alu.mult)
            kb.tt(lp_full, lp_full, taken, Alu.add)
        if _RSTOP <= 2:
            return
        kb.wait("tensor", "VV")
        kb.mm(ps[0][0:16, 0:256], cesel128[:, :], lp_full, mark=True)
        kb.mm(ps[1][0:16, 0:256], cesel128[:, :], hrow_full, mark=True)
        kb.wait("scalar", "TT")
        kb.scopy(gscr[0:16, 2048:2304], ps[0][0:16, 0:256])
        kb.scopy(gscr[0:16, 2304:2560], ps[1][0:16, 0:256])
        kb.wait("gpsimd", "SS")
        for k in range(32):
            pr, half = k // 2, (k % 2) * 128
            kb.store(fscr[:, 2048 + k:2048 + k + 1], gscr[pr:pr + 1, 2048 + half:2048 + half + 128])
            kb.store(fscr[:, 2080 + k:2080 + k + 1], gscr[pr:pr + 1, 2304 + half:2304 + half + 128])
        kb.wait("vector", "SD")
        kb.ts(u32b[:, 0:32], fscr[:, 2048:2080], 0.0, None, Alu.add)
        kb.ts(u32b[:, 32:64], fscr[:, 2080:2112], 0.0, None, Alu.add)
        if _RSTOP <= 3:
            return
        esr = qkb[0:1, 2336:2344]
        kb.wait("gpsimd", "VV")
        kb.store(esr, cesel[:, :])
        kb.wait("tensor", "SD")
        kb.mm(ps[7][0:8, 0:8], cones[0:1, 0:8], esr, mark=True)
        kb.wait("vector", "TT")
        esrep8 = qkb[0:8, 2344:2352]
        kb.vcopy(esrep8, ps[7][0:8, 0:8])
        cumown = p1(13)
        stown = p1(14)
        kb.memset(cumown[0:8, :], 0.0)
        kb.memset(stown[0:8, :], 0.0)
        for e in range(8):
            kb.stt(cumown[0:8, :], bp8m[:, e:e + 1], esrep8[:, e:e + 1], cumown[0:8, :],
                   Alu.mult, Alu.add)
            kb.stt(stown[0:8, :], st8m[:, e:e + 1], esrep8[:, e:e + 1], stown[0:8, :],
                   Alu.mult, Alu.add)
        lam = p1(15)
        kb.tt(lam[0:8, :], stown[0:8, :], cumown[0:8, :], Alu.subtract)
        kb.tt(lam[0:8, :], lam[0:8, :], cmisc[0:8, 46:47], Alu.add)
        kb.wait("gpsimd", "VV")
        kb.store(qkb[0:1, 2352:2360], lam[0:8, :])
        kb.store(qkb[0:1, 2360:2368], cumown[0:8, :])
        kb.wait("tensor", "SD")
        kb.mm(ps[6][:, 0:8], cones[0:1, :], qkb[0:1, 2352:2360])
        kb.mm(ps[7][:, 0:8], cones[0:1, :], qkb[0:1, 2360:2368], mark=True)
        kb.wait("vector", "TT")
        lamrep = G(3584, 8)
        cumrep = G(3592, 8)
        kb.vcopy(lamrep, ps[6][:, 0:8])
        kb.vcopy(cumrep, ps[7][:, 0:8])
        sgrid = G(3600, 32)
        kb.iota(sgrid, [[128, 32]], 0, 1)
        kb.wait("vector", "GG")
        acc32 = G(3632, 32)
        ge_t = lambda c: gscr[:, 3664 + c * 32:3664 + c * 32 + 32]
        kb.memset(acc32, 0.0)
        for c in range(8):
            kb.ts(ge_t(c), sgrid, cumrep[:, c:c + 1], None, Alu.is_ge)
        for c in range(8):
            if c < 7:
                kb.tt(s_cur[:, 0:32], ge_t(c), ge_t(c + 1), Alu.subtract)
                ind = s_cur[:, 0:32]
            else:
                ind = ge_t(7)
            kb.stt(acc32, ind, lamrep[:, c:c + 1], acc32, Alu.mult, Alu.add)
        kb.tt(acc32, sgrid, acc32, Alu.add)
        kb.ts(u32b[:, 64:96], acc32, 0.0, None, Alu.add)
        kb.barrier()
        if stage == "route" and _RSTOP >= 5:
            kb.store(dbg2[0 * 128:1 * 128, :], mem(0))
            kb.memset(gt_t, 0.0)
            kb.vcopy(gt_t[:, 0:64], fscr[:, 2048:2112])
            kb.wait("gpsimd", "VV")
            kb.store(dbg2[1 * 128:2 * 128, :], gt_t)
            kb.store(dbg2[2 * 128:3 * 128, :], mem(7))
            kb.store(dbg2[3 * 128:4 * 128, :], hrow_full)
            kb.store(dbg2[4 * 128:5 * 128, :], lp_full)
            kb.memset(dump, 0.0)
            kb.vcopy(dump[:, 0:32], acc32)
            kb.wait("gpsimd", "VV")
            kb.store(dbg2[5 * 128:6 * 128, :], dump)
            kb.wait("gpsimd", "SD")
        if _RSTOP <= 4:
            return
        for k in range(32):
            tile = fscr[:, (k % 2) * 1024:(k % 2) * 1024 + 1024]
            kb.wait("sync", "SD")
            kb.load(tile, lnf_tm[k * 128:k * 128 + 128, :])
            kb.wait("gpsimd", "LD")
            kb.ind_scatter(agin.ap(), tile, u32b[:, k:k + 1])
        if dbg7 is not None:
            kb.wait("gpsimd", "VV")
            kb.store(dbg7, u32b[:, :])
        kb.cc("AllGather", agin.ap(), agout.ap())
        if dbg6 is not None and _MSTOP == 1:
            kb.wait("gpsimd", "CC")
            kb.store(dbg6[0:256, :], agout.ap()[0:256, :])
            kb.store(dbg6[256:512, :], agout.ap()[CEXP * 3:CEXP * 3 + 256, :])

    def moe():
        kb.wait("gpsimd", "CC")
        kb.wait("tensor", "CC")
        mw1r = di["mw1"].rearrange("(k p) n -> p k n", p=128)
        for st in range(8):
            for g in range(4):
                tile = fscr[:, (g % 2) * 1024:(g % 2) * 1024 + 1024]
                kb.wait("gpsimd", "SS")
                kb.wait("gpsimd", "SD")
                kb.ind_gather(tile, agout.ap(), u32b[:, 64 + st * 4 + g:64 + st * 4 + g + 1])
                if dbg6 is not None and _MSTOP == 5 and st == 4 and g == 0:
                    kb.wait("gpsimd", "SD")
                    kb.store(dbg6[0:128, :], tile)
                    kb.wait("tensor", "SD")
                kb.wait("tensor", "SD")
                for kc in range(8):
                    bank = ps[kc % 2]
                    kb.wait("tensor", "SS")
                    kb.transp(bank[:, 0:128], tile[:, kc * 128:kc * 128 + 128],
                              cident[:, :], mark=True)
                    kb.wait("scalar", "TT")
                    kb.scopy(qkb[:, kc * 512 + g * 128:kc * 512 + g * 128 + 128],
                             bank[:, 0:128])
            if dbg6 is not None and _MSTOP == 3 and st == 4:
                kb.wait("gpsimd", "SS")
                kb.store(dbg6[0:128, :], qkb[:, 0:1024])
                kb.store(dbg6[128:256, :], qkb[:, 1024:2048])
                kb.store(dbg6[256:384, :], qkb[:, 2048:3072])
                kb.store(dbg6[384:512, :], qkb[:, 3072:4096])
                kb.wait("sync", "SD")
            for hc in range(32):
                wsl = accb[:, (hc % 2) * 1024:(hc % 2) * 1024 + 1024]
                kb.wait("sync", "TT")
                kb.load(wsl.rearrange("p (k n) -> p k n", k=8), mw1r[:, :, hc * 128:hc * 128 + 128])
                kb.wait("tensor", "LD")
                if hc < 2:
                    kb.wait("tensor", "SS")
                bank = ps[2 + hc % 2]
                if hc >= 2:
                    kb.wait("tensor", "SS")
                for kc in range(8):
                    kb.mm(bank[:, :], wsl[:, kc * 128:kc * 128 + 128],
                          qkb[:, kc * 512:kc * 512 + 512], start=(kc == 0), stop=(kc == 7))
                kb.tmark()
                kb.wait("scalar", "TT")
                kb.act(xbuf[:, hc * 512:hc * 512 + 512], bank[:, :], AF.Gelu_apprx_tanh,
                       bias=small[:, S_MB1 + hc:S_MB1 + hc + 1])
            if dbg6 is not None and _MSTOP == 4 and st == 4:
                kb.wait("gpsimd", "SS")
                for q_ in range(4):
                    kb.store(dbg6[q_ * 128:(q_ + 1) * 128, :], xbuf[:, q_ * 1024:(q_ + 1) * 1024])
                kb.wait("sync", "SD")
            kb.wait("tensor", "SS")
            for hc in range(32):
                w2sl = accb[:, 2048 + (hc % 2) * 1024:2048 + (hc % 2) * 1024 + 1024]
                kb.wait("sync", "TT")
                kb.load(w2sl, di["mw2"][hc * 128:hc * 128 + 128, :])
                kb.wait("tensor", "LD")
                for oc in range(8):
                    kb.mm(ps[oc][:, :], w2sl[:, oc * 128:oc * 128 + 128],
                          xbuf[:, hc * 512:hc * 512 + 512],
                          start=(hc == 0), stop=(hc == 31))
                kb.tmark()
            kb.wait("scalar", "TT")
            for oc in range(8):
                kb.act(qkb[:, oc * 512:oc * 512 + 512], ps[oc][:, :], AF.Identity,
                       bias=small[:, S_MB2 + oc:S_MB2 + oc + 1])
            for g in range(4):
                tile = fscr[:, (g % 2) * 1024:(g % 2) * 1024 + 1024]
                if g >= 2:
                    kb.wait("scalar", "SD")
                for kc in range(8):
                    bank = ps[kc % 2]
                    kb.wait("tensor", "SS")
                    kb.transp(bank[:, 0:128], qkb[:, kc * 512 + g * 128:kc * 512 + g * 128 + 128],
                              cident[:, :], mark=True)
                    kb.wait("scalar", "TT")
                    kb.scopy(tile[:, kc * 128:kc * 128 + 128], bank[:, 0:128])
                kb.wait("gpsimd", "SS")
                kb.store(h_loc.ap()[st * 512 + g * 128:st * 512 + g * 128 + 128, :], tile)
        if dbg6 is not None and _MSTOP == 2:
            kb.wait("gpsimd", "SD")
            kb.store(dbg6[0:256, :], h_loc.ap()[0:256, :])
            kb.store(dbg6[256:512, :], h_loc.ap()[2048:2048 + 256, :])
        kb.cc("AllGather", h_loc.ap(), h_g.ap())

    def final():
        kb.wait("sync", "CC")
        kb.wait("gpsimd", "CC")
        for k in range(32):
            xt = fscr[:, (k % 2) * 1024:(k % 2) * 1024 + 1024]
            ht = fscr[:, 2048 + (k % 2) * 1024:2048 + (k % 2) * 1024 + 1024]
            if k >= 2:
                kb.wait("sync", "VV")
                kb.wait("gpsimd", "VV")
            kb.load(xt, feat_tm[k * 128:k * 128 + 128, :])
            kb.ind_gather(ht, h_g.ap(), u32b[:, 32 + k:32 + k + 1])
            kb.wait("vector", "LD")
            kb.wait("vector", "SD")
            u = gscr[:, (k % 2) * 1024:(k % 2) * 1024 + 1024]
            du = gscr[:, 2048 + (k % 2) * 1024:2048 + (k % 2) * 1024 + 1024]
            kb.stt(u, xt, 2.0, ht, Alu.mult, Alu.add)
            kb.ts(du, u, 0.0, 0.0, Alu.add, Alu.add, accum_out=p1(0))
            kb.stt(du, u, 1.0, u, Alu.mult, Alu.mult, accum_out=p1(1))
            kb.ts(p1(0), p1(0), 1.0 / 1024, None, Alu.mult)
            kb.ts(p1(1), p1(1), 1.0 / 1024, None, Alu.mult)
            kb.tt(p1(2), p1(0), p1(0), Alu.mult)
            kb.tt(p1(1), p1(1), p1(2), Alu.subtract)
            kb.ts(p1(1), p1(1), EPS, None, Alu.add)
            kb.wait("scalar", "VV")
            kb.act(p1(2), p1(1), AF.Ln)
            kb.act(p1(2), p1(2), AF.Exp, scale=-0.5)
            kb.wait("vector", "SS")
            kb.tt(p1(3), p1(2), p1(2), Alu.mult)
            kb.tt(p1(3), p1(3), p1(1), Alu.mult)
            kb.ts(p1(3), p1(3), -0.5, 1.5, Alu.mult, Alu.add)
            kb.tt(p1(2), p1(2), p1(3), Alu.mult)
            kb.ts(u, u, p1(0), p1(2), Alu.subtract, Alu.mult)
            kb.wait("gpsimd", "VV")
            kb.store(out[k * 128:k * 128 + 128, :], u)

    x3v = xbuf.rearrange("p (k t) -> p k t", k=8)
    for b in range(NB):
        kb.wait("sync", "VV")
        kb.wait("sync", "TT")
        kb.wait("sync", "SS")
        kb.wait("sync", "SD")
        kb.load(x3v, di["x0" if b == 0 else "x1"].rearrange("(k p) t -> p k t", p=128))
        kb.barrier()
        attn_block(b, 0)
        kb.barrier()
        fm_ln(accs, xs)
        kb.barrier()
        if SI == 1:
            kb.store(dbg[b * 128:(b + 1) * 128, :], xbuf[:, :])
            continue
        ffn_block(b)
        kb.barrier()
        if SI == 2:
            kb.store(dbg[b * 128:(b + 1) * 128, :], xbuf[:, :])
            if b == 0:
                kb.store(dbg3[128:256, :], small[:, 0:1024])
            continue
        kb.vcopy(accb[:, :], xbuf[:, :])
        kb.barrier()
        fm_ln(xs, xs)
        kb.barrier()
        attn_block(b, 1)
        kb.barrier()
        fm_ln(accs, xs)
        kb.barrier()
        if SI == 3:
            kb.store(dbg[b * 128:(b + 1) * 128, :], xbuf[:, :])
            continue
        l1_tail(b)
        kb.barrier()
    if SI >= 4:
        routing()
        kb.barrier()
    if SI >= 5:
        moe()
        kb.barrier()
        final()
    kb.barrier()
    kb.emit()
    return nc


def _consts():
    f32 = np.float32
    c = {}
    c["ones"] = np.ones((128, 128), f32)
    l = np.zeros((128, 128), f32)
    for k in range(128):
        l[k, k + 1:] = 1.0
    c["lmask"] = l
    c["ident"] = np.eye(128, dtype=f32)
    top = 2595.0 * np.log10(21.0)
    mel = np.power(10.0, np.linspace(0.0, top, 32) / 2595.0) - 1.0
    fb64 = 200.0 * mel / 1000.0
    fbhi32 = np.array(FBJ_BITS, dtype=np.uint32).view(np.float32)
    lo = fb64 - fbhi32.astype(np.float64)
    lim = 0.4999 * np.spacing(np.abs(fbhi32)).astype(np.float64)
    lo = np.clip(lo, -lim, lim)
    idx = (np.arange(128) % 64) // 2
    c["fbhi"] = fbhi32[idx].astype(np.float32).reshape(128, 1)
    c["fblo"] = lo[idx].astype(np.float32).reshape(128, 1)
    c["signrow"] = np.array([[-1.0 if d % 2 == 0 else 1.0] for d in range(128)], f32)
    c["irow32"] = np.arange(32, dtype=f32).reshape(1, 32)
    c["c4096col"] = (np.arange(8, dtype=f32) * 4096.0).reshape(8, 1)
    lb = np.zeros((128, 128), f32)
    for m in range(128):
        lb[: 16 * (m // 16), m] = 1.0
    c["lb"] = lb
    bs = np.zeros((128, 8), f32)
    for p in range(128):
        bs[p, p // 16] = 1.0
    c["bs"] = bs
    e16 = np.zeros((8, 128), f32)
    for p in range(128):
        e16[p // 16, p] = 1.0
    c["exp16"] = e16
    return c


def _colmajor(v, k):
    return np.ascontiguousarray(np.asarray(v, np.float32).reshape(k, 128).T)


def _prep(inputs):
    f32 = np.float32
    A = lambda n: np.asarray(inputs[n], f32)
    x = A("L0_attn_x")
    consts = _consts()
    shared = dict(consts)
    shared.update(
        wqkv=A("L0_attn_w_qkv"), wo0=A("L0_attn_w_o"),
        cnw1=A("L0_ffn_cn_w1"), cnw2=A("L0_ffn_cn_w2"), cnw3=A("L0_ffn_cn_w3"),
        cnw4=A("L0_ffn_cn_w4"),
        cnb1=_colmajor(inputs["L0_ffn_cn_b1"], 8), cnb2=_colmajor(inputs["L0_ffn_cn_b2"], 4),
        cnb3=_colmajor(inputs["L0_ffn_cn_b3"], 2),
        cnb4=A("L0_ffn_cn_b4").reshape(1, 1),
        gcw1=A("L0_ffn_gc_w1"), gcw2=A("L0_ffn_gc_w2"), gcw3=A("L0_ffn_gc_w3"),
        gcb1=_colmajor(inputs["L0_ffn_gc_b1"], 4), gcb2=_colmajor(inputs["L0_ffn_gc_b2"], 2),
        gcb3=A("L0_ffn_gc_b3").reshape(1, 1),
        ar=A("L0_ffn_adaptive_range").reshape(1, 1),
        lnw1=_colmajor(inputs["L1_attn_ln_w"], 8), lnb1=_colmajor(inputs["L1_attn_ln_b"], 8),
        wq1=A("L1_attn_w_q"), wkv1=A("L1_attn_w_kv"), wo1=A("L1_attn_w_o"),
        theta=A("L1_attn_theta").reshape(1, 1),
        centT=np.ascontiguousarray(A("L1_ffn_centroids").T),
        lng=_colmajor(inputs["L1_ffn_ln_g"], 8), lnb2=_colmajor(inputs["L1_ffn_ln_b"], 8),
    )
    w1 = A("L1_ffn_w1")
    b1 = np.asarray(inputs["L1_ffn_b1"], f32)
    w2 = A("L1_ffn_w2")
    b2 = np.asarray(inputs["L1_ffn_b2"], f32)
    in_maps = []
    for c in range(NC8):
        m = dict(shared)
        m["x0"] = np.ascontiguousarray(x[2 * c].T)
        m["x1"] = np.ascontiguousarray(x[2 * c + 1].T)
        m["mw1"] = np.ascontiguousarray(w1[c])
        m["mb1"] = _colmajor(b1[c], 32)
        m["mw2"] = np.ascontiguousarray(w2[c])
        m["mb2"] = _colmajor(b2[c], 8)
        es = np.zeros((8, 1), f32)
        es[c, 0] = 1.0
        m["esel"] = es
        e128 = np.zeros((128, 16), f32)
        for pr in range(16):
            e128[16 * c + pr, pr] = 1.0
        m["esel128"] = e128
        in_maps.append(m)
    return in_maps


LAST_RES = [None]


def kernel(**inputs):
    in_maps = _prep(inputs)
    nc = build(_STAGE)
    res = run_bass_kernel_spmd(nc, in_maps, core_ids=list(range(NC8)))
    LAST_RES[0] = res
    outs = [np.asarray(res.results[c]["out"]) for c in range(NC8)]
    return np.concatenate(outs, axis=0).reshape(16, T, D)


# version 02
# speedup vs baseline: 1.0128x; 1.0128x over previous
import math
import numpy as np
import concourse.bass as bass
import concourse.mybir as mybir
from concourse.bass_utils import run_bass_kernel_spmd

F32 = mybir.dt.float32
U32 = mybir.dt.uint32
AF = mybir.ActivationFunctionType
Alu = mybir.AluOpType
AX = mybir.AxisListType.X

NC8 = 8
T = 2048
D = 1024
HD = 64
NB = 2
NTOK = NB * T
CEXP = 4096
EPS = 1e-5

CW1 = 6.25
CW2 = 0.033203125
CW3 = float(np.float32(2 * math.pi - CW1 - CW2))
INV2PI = float(np.float32(1.0 / (2 * math.pi)))
PIO2 = float(np.float32(math.pi / 2))

_STAGE = "full"
_RSTOP = 9
_TSTOP = 9
_MSTOP = 9

FBJ_BITS = [0x0, 0x3ca9130c, 0x3d31cc5a, 0x3d8c577a, 0x3dc517c7, 0x3e01d9a8,
            0x3e246269, 0x3e4a7b7b, 0x3e748306, 0x3e917073, 0x3eab03d5,
            0x3ec73ae1, 0x3ee65b4f, 0x3f045905, 0x3f1749f5, 0x3f2c2f4b,
            0x3f433ca5, 0x3f5caafa, 0x3f78b92b, 0x3f8bd641, 0x3f9ce8bc,
            0x3fafbe41, 0x3fc48550, 0x3fdb7148, 0x3ff4bac7, 0x40085028,
            0x4017b369, 0x4028ad2a, 0x403b6764, 0x40501060, 0x4066db22,
            0x40800002]


class KB:
    ENGS = ("sync", "scalar", "vector", "tensor", "gpsimd")

    def __init__(self, nc):
        self.nc = nc
        self.q = {e: [] for e in self.ENGS}
        self.sems = {}
        self.cnt = {}
        for s in ("LD", "SD", "TT", "TA", "VV", "SS", "GG", "CC"):
            self.sems[s] = nc.semaphore(s).__enter__()
            self.cnt[s] = 0

    def _rec(self, eng, fn, inc, n):
        if inc is not None:
            self.cnt[inc] += n
        self.q[eng].append((fn, inc, n))

    def wait(self, eng, sem, val=None):
        v = self.cnt[sem] if val is None else val
        h = self.sems[sem]
        self.q[eng].append((lambda e, h=h, v=v: e.wait_ge(h, v), None, 0))

    def wait_all(self, eng):
        for s in ("LD", "SD", "TT", "TA", "VV", "SS", "GG", "CC"):
            if self.cnt[s] > 0:
                self.wait(eng, s)

    def barrier(self):
        for e in self.ENGS:
            self.wait_all(e)

    def load(self, out, in_, slow=False):
        if slow:
            self._rec("sync", lambda e, o=out, i=in_: e.dma_start(out=o, in_=i, allow_slow_non_contiguous=True), "LD", 16)
        else:
            self._rec("sync", lambda e, o=out, i=in_: e.dma_start(out=o, in_=i), "LD", 16)

    def store(self, out, in_):
        self._rec("gpsimd", lambda e, o=out, i=in_: e.dma_start(out=o, in_=i), "SD", 16)

    def ind_gather(self, out, dram_ap, idx_ap):
        self._rec("gpsimd", lambda e, o=out, d=dram_ap, ix=idx_ap: e.indirect_dma_start(
            out=o, out_offset=None, in_=d,
            in_offset=bass.IndirectOffsetOnAxis(ap=ix, axis=0)), "SD", 16)

    def ind_scatter(self, dram_ap, in_, idx_ap):
        self._rec("gpsimd", lambda e, d=dram_ap, i=in_, ix=idx_ap: e.indirect_dma_start(
            out=d, out_offset=bass.IndirectOffsetOnAxis(ap=ix, axis=0),
            in_=i, in_offset=None), "SD", 16)

    def cc(self, kind, in_ap, out_ap):
        self.wait("gpsimd", "SD")
        self._rec("gpsimd", lambda e, k=kind, i=in_ap, o=out_ap: e.collective_compute(
            k, Alu.bypass, replica_groups=[list(range(NC8))],
            ins=[i.opt()], outs=[o.opt()]), "CC", 1)

    def mm(self, out, lhsT, rhs, start=True, stop=True, tp=None, mark=False, sem=None):
        s_ = sem if sem is not None else ("TT" if mark else None)
        if tp is None:
            self._rec("tensor", lambda e, o=out, l=lhsT, r=rhs, s1=start, s2=stop:
                      e.matmul(o, l, r, start=s1, stop=s2), s_, 1)
        else:
            self._rec("tensor", lambda e, o=out, l=lhsT, r=rhs, s1=start, s2=stop, t=tp:
                      e.matmul(o, l, r, start=s1, stop=s2, tile_position=t), s_, 1)

    def transp(self, out, in_, ident, mark=False):
        self._rec("tensor", lambda e, o=out, i=in_, d=ident: e.transpose(o, i, d),
                  "TT" if mark else None, 1)

    def tmark(self):
        for j in range(len(self.q["tensor"]) - 1, -1, -1):
            fn, inc, n = self.q["tensor"][j]
            if n == 0:
                continue
            assert inc is None
            self.cnt["TT"] += 1
            self.q["tensor"][j] = (fn, "TT", 1)
            return

    def act(self, out, in_, func, bias=0.0, scale=1.0, accum_out=None):
        self._rec("scalar", lambda e, o=out, i=in_, f=func, b=bias, s=scale, a=accum_out:
                  e.activation(out=o, in_=i, func=f, bias=b, scale=s, accum_out=a), "SS", 1)

    def scopy(self, out, in_):
        self._rec("scalar", lambda e, o=out, i=in_: e.copy(out=o, in_=i), "SS", 1)

    def _v(self, fn):
        self._rec("vector", fn, "VV", 1)

    def _md(self, out):
        try:
            n = 1
            for d in out.shape[1:]:
                n *= d
        except Exception:
            n = 999
        if n <= 64:
            self._v(lambda e: e.drain())

    def tt(self, out, a, b, op):
        self._v(lambda e, o=out, x=a, y=b, p=op: e.tensor_tensor(o, x, y, p))
        self._md(out)

    def ts(self, out, in0, s1, s2, op0, op1=None, accum_out=None, safe=False):
        if not safe and (not isinstance(s1, (int, float)) or
                         (s2 is not None and not isinstance(s2, (int, float)))):
            self._v(lambda e: e.drain())
        if op1 is None:
            self._v(lambda e, o=out, i=in0, a=s1, p0=op0: e.tensor_scalar(o, i, a, None, op0=p0))
        else:
            self._v(lambda e, o=out, i=in0, a=s1, b=s2, p0=op0, p1=op1, ac=accum_out:
                    e.tensor_scalar(o, i, a, b, op0=p0, op1=p1, accum_out=ac))
        if accum_out is not None:
            self._v(lambda e: e.drain())
        else:
            self._md(out)

    def stt(self, out, in0, scalar, in1, op0, op1, accum_out=None, safe=False):
        if not safe and not isinstance(scalar, (int, float)):
            self._v(lambda e: e.drain())
        self._v(lambda e, o=out, i=in0, s=scalar, j=in1, p0=op0, p1=op1, ac=accum_out:
                e.scalar_tensor_tensor(o, i, s, j, op0=p0, op1=p1, accum_out=ac))
        if accum_out is not None:
            self._v(lambda e: e.drain())
        else:
            self._md(out)

    def red(self, out, in_, op):
        self._v(lambda e, o=out, i=in_, p=op: e.tensor_reduce(o, i, axis=AX, op=p))
        self._v(lambda e: e.drain())

    def vcopy(self, out, in_):
        self._v(lambda e, o=out, i=in_: e.tensor_copy(out=o, in_=i))
        self._md(out)

    def memset(self, ap, c):
        self._v(lambda e, a=ap, v=c: e.memset(a, v))
        self._md(ap)

    def recip(self, out, in_):
        self._v(lambda e, o=out, i=in_: e.reciprocal(out=o, in_=i))
        self._v(lambda e: e.drain())

    def scan(self, out, d0, d1, init, op0, op1):
        self._v(lambda e, o=out, a=d0, b=d1, i=init, p0=op0, p1=op1:
                e.tensor_tensor_scan(o, a, b, i, op0=p0, op1=p1))
        self._v(lambda e: e.drain())

    def shuf(self, out, in_, mask):
        self._v(lambda e, o=out, i=in_, m=mask: e.stream_shuffle(o, i, m))

    def _g(self, fn):
        self._rec("gpsimd", fn, "GG", 1)

    def iota(self, out, pattern, base=0, chm=0):
        self._g(lambda e, o=out, p=pattern, b=base, c=chm: e.iota(
            o, pattern=p, base=b, channel_multiplier=c, allow_small_or_imprecise_dtypes=True))

    def affine(self, out, in_, pattern, cmp, fill, base, chm):
        self._g(lambda e, o=out, i=in_, p=pattern, c=cmp, f=fill, b=base, m=chm:
                e.affine_select(o, i, pattern=p, compare_op=c, fill=f, base=b, channel_multiplier=m))

    def gtt(self, out, a, b, op):
        self._g(lambda e, o=out, x=a, y=b, p=op: e.tensor_tensor(o, x, y, p))

    def emit(self):
        blk = self.nc.Block().__enter__()
        for eng in self.ENGS:
            items = self.q[eng]
            sems = self.sems

            def body(e, items=items, sems=sems):
                for fn, inc, n in items:
                    ins = fn(e)
                    if inc is not None:
                        ins.then_inc(sems[inc], n)

            getattr(blk, eng)(body)
        blk.__exit__(None, None, None)


def stage_idx(s):
    return {"x1": 1, "x2": 2, "x3": 3, "route": 4, "m1": 5, "full": 5}[s]


def build(stage):
    SI = stage_idx(stage)
    nc = bass.Bass("TRN2", target_bir_lowering=False)
    di = {}

    def IN(name, shape, dt=F32):
        di[name] = nc.dram_tensor(name, list(shape), dt, kind="ExternalInput").ap()
        return di[name]

    IN("x0", [D, T]); IN("x1", [D, T])
    IN("wqkv", [D, 3 * D]); IN("wo0", [D, D])
    IN("cnw1", [D, D]); IN("cnw2", [D, 512]); IN("cnw3", [512, 256]); IN("cnw4", [256, 1])
    IN("cnb1", [128, 8]); IN("cnb2", [128, 4]); IN("cnb3", [128, 2]); IN("cnb4", [1, 1])
    IN("gcw1", [D, 512]); IN("gcw2", [512, 256]); IN("gcw3", [256, 1])
    IN("gcb1", [128, 4]); IN("gcb2", [128, 2]); IN("gcb3", [1, 1]); IN("ar", [1, 1])
    IN("lnw1", [128, 8]); IN("lnb1", [128, 8])
    IN("wq1", [D, D]); IN("wkv1", [D, 2 * D]); IN("wo1", [D, D]); IN("theta", [1, 1])
    IN("centT", [D, 8]); IN("lng", [128, 8]); IN("lnb2", [128, 8])
    IN("mw1", [D, 4 * D]); IN("mb1", [128, 32]); IN("mw2", [4 * D, D]); IN("mb2", [128, 8])
    IN("esel", [8, 1])
    IN("ones", [128, 128]); IN("lmask", [128, 128]); IN("ident", [128, 128])
    IN("fbhi", [128, 1]); IN("fblo", [128, 1]); IN("signrow", [128, 1]); IN("irow32", [1, 32])
    IN("c4096col", [8, 1]); IN("lb", [128, 128]); IN("bs", [128, 8])
    IN("exp16", [8, 128]); IN("esel128", [128, 16])

    out = nc.dram_tensor("out", [NTOK, D], F32, kind="ExternalOutput").ap()
    dbg = None
    if SI < 5:
        dbg = nc.dram_tensor("dbg", [NB * 128, 8 * T], F32, kind="ExternalOutput").ap()
    dbg3 = None
    if stage == "x2":
        dbg3 = nc.dram_tensor("dbg3", [2 * 128, 1024], F32, kind="ExternalOutput").ap()
    dbg4 = None
    dbg5 = None
    if stage == "x3":
        dbg4 = nc.dram_tensor("dbg4", [128, 4096], F32, kind="ExternalOutput").ap()
        dbg5 = nc.dram_tensor("dbg5", [128, 2048], F32, kind="ExternalOutput").ap()
    dbg6 = None
    dbg7 = None
    if stage == "m1":
        dbg6 = nc.dram_tensor("dbg6", [512, 1024], F32, kind="ExternalOutput").ap()
        dbg7 = nc.dram_tensor("dbg7", [128, 96], U32, kind="ExternalOutput").ap()
    dbg2 = None
    if stage == "route":
        dbg2 = nc.dram_tensor("dbg2", [6 * 128, 256], F32, kind="ExternalOutput").ap()

    wq1p = nc.dram_tensor("wq1p", [D, D], F32).ap()
    wkv1p = nc.dram_tensor("wkv1p", [D, 2 * D], F32).ap()
    park1 = nc.dram_tensor("park1", [128, 8, T], F32).ap()
    feat_tm = nc.dram_tensor("feat_tm", [NTOK, D], F32).ap()
    lnf_tm = nc.dram_tensor("lnf_tm", [NTOK, D], F32).ap()
    agin = nc.dram_tensor("agin", [CEXP, D], F32)
    agout = nc.dram_tensor("agout", [NC8 * CEXP, D], F32)
    scr_blk = nc.dram_tensor("scr_blk", [8, NTOK], F32)
    scr_g = nc.dram_tensor("scr_g", [NC8, 8, NTOK], F32)
    h_loc = nc.dram_tensor("h_loc", [CEXP, D], F32)
    h_g = nc.dram_tensor("h_g", [NC8 * CEXP, D], F32)

    xbuf = nc.sbuf_tensor("xbuf", [128, 16384], F32).__enter__()
    accb = nc.sbuf_tensor("accb", [128, 16384], F32).__enter__()
    qkb = nc.sbuf_tensor("qkb", [128, 4096], F32).__enter__()
    vtb = nc.sbuf_tensor("vtb", [128, 2080], F32).__enter__()
    fscr = nc.sbuf_tensor("fscr", [128, 4096], F32).__enter__()
    gscr = nc.sbuf_tensor("gscr", [128, 4096], F32).__enter__()
    wbuf = nc.sbuf_tensor("wbuf", [128, 4096], F32).__enter__()
    small = nc.sbuf_tensor("small", [128, 1024], F32).__enter__()
    u32b = nc.sbuf_tensor("u32b", [128, 96], U32).__enter__()
    cones = nc.sbuf_tensor("cones", [128, 128], F32).__enter__()
    clmask = nc.sbuf_tensor("clmask", [128, 128], F32).__enter__()
    cident = nc.sbuf_tensor("cident", [128, 128], F32).__enter__()
    cmisc = nc.sbuf_tensor("cmisc", [128, 48], F32).__enter__()
    clb = nc.sbuf_tensor("clb", [128, 128], F32).__enter__()
    cexp16 = nc.sbuf_tensor("cexp16", [8, 128], F32).__enter__()
    cesel128 = nc.sbuf_tensor("cesel128", [128, 16], F32).__enter__()
    ccent = nc.sbuf_tensor("ccent", [128, 64], F32).__enter__()
    cesel = nc.sbuf_tensor("cesel", [8, 1], F32).__enter__()

    ps = [nc.psum_tensor("psb%d" % i, [128, 512], F32).__enter__() for i in range(8)]

    kb = KB(nc)

    S_CNB1, S_CNB2, S_CNB3, S_GCB1, S_GCB2 = 0, 8, 12, 14, 18
    S_CELL = 20
    S_LNW1, S_LNB1, S_LNG, S_LNB2 = 40, 48, 56, 64
    S_BCQ, S_BCK, S_BCV = 72, 80, 88
    S_BVROW = 96
    S_MB1, S_MB2 = 224, 256
    S_P1 = 264
    S_ROWS = 296

    def cell(i):
        return small[0:1, S_CELL + i:S_CELL + i + 1]

    def p1(i):
        return small[:, S_P1 + i:S_P1 + i + 1]

    def srow(i, w=512):
        return small[0:1, S_ROWS + i * 512:S_ROWS + i * 512 + w]

    def xs(kc, lo, hi):
        return xbuf[:, kc * T + lo: kc * T + hi]

    def accs(kc, lo, hi):
        return accb[:, kc * T + lo: kc * T + hi]

    kb.load(cones[:, :], di["ones"])
    kb.load(clmask[:, :], di["lmask"])
    kb.load(cident[:, :], di["ident"])
    kb.load(cmisc[:, 44:45], di["fbhi"])
    kb.load(cmisc[:, 45:46], di["fblo"])
    kb.load(cmisc[:, 0:1], di["signrow"])
    kb.load(clb[:, :], di["lb"])
    kb.load(cmisc[0:8, 46:47], di["c4096col"])
    kb.load(cmisc[:, 1:9], di["bs"])
    kb.load(cexp16[:, :], di["exp16"])
    kb.load(cesel128[:, :], di["esel128"])
    kb.load(cmisc[0:1, 9:41], di["irow32"])
    kb.load(small[:, S_CNB1:S_CNB1 + 8], di["cnb1"])
    kb.load(small[:, S_CNB2:S_CNB2 + 4], di["cnb2"])
    kb.load(small[:, S_CNB3:S_CNB3 + 2], di["cnb3"])
    kb.load(small[:, S_GCB1:S_GCB1 + 4], di["gcb1"])
    kb.load(small[:, S_GCB2:S_GCB2 + 2], di["gcb2"])
    kb.load(cell(0), di["cnb4"])
    kb.load(cell(1), di["gcb3"])
    kb.load(cell(2), di["ar"])
    kb.load(cell(3), di["theta"])
    kb.load(small[:, S_LNW1:S_LNW1 + 8], di["lnw1"])
    kb.load(small[:, S_LNB1:S_LNB1 + 8], di["lnb1"])
    kb.load(small[:, S_LNG:S_LNG + 8], di["lng"])
    kb.load(small[:, S_LNB2:S_LNB2 + 8], di["lnb2"])
    kb.load(small[:, S_MB1:S_MB1 + 32], di["mb1"])
    kb.load(small[:, S_MB2:S_MB2 + 8], di["mb2"])
    kb.load(cesel[:, :], di["esel"])

    kb.load(ccent[:, 0:64].rearrange("p (k n) -> p k n", k=8), di["centT"].rearrange("(k p) n -> p k n", p=128))
    kb.barrier()

    first = [True]
    oc_list = [("q", oc) for oc in range(8)] + [("kv", oc) for oc in range(16)]
    for idx, (which, oc) in enumerate(oc_list):
        wsrc = di["wq1"] if which == "q" else di["wkv1"]
        wdst = wq1p if which == "q" else wkv1p
        half = idx % 2
        wb = wbuf[:, half * 2048: half * 2048 + 1024]
        src = wsrc.rearrange("(k p) n -> p k n", p=128)[:, :, oc * 128:(oc + 1) * 128]
        kb.load(wb.rearrange("p (k n) -> p k n", k=8), src)
        kb.wait("tensor", "LD")
        col = oc if which == "q" else 8 + oc
        for kc in range(8):
            kb.mm(ps[0][:, col:col + 1], wb[:, kc * 128:(kc + 1) * 128],
                  small[:, S_LNB1 + kc:S_LNB1 + kc + 1],
                  start=first[0], stop=(idx == len(oc_list) - 1 and kc == 7))
            first[0] = False
        kb.tmark()
        kb.wait("vector", "TT")
        for kc in range(8):
            kb.ts(wb[:, kc * 128:(kc + 1) * 128], wb[:, kc * 128:(kc + 1) * 128],
                  small[:, S_LNW1 + kc:S_LNW1 + kc + 1], None, Alu.mult)
        kb.wait("gpsimd", "VV")
        kb.store(wdst.rearrange("(k p) n -> p k n", p=128)[:, :, oc * 128:(oc + 1) * 128],
                 wb.rearrange("p (k n) -> p k n", k=8))
        kb.wait("sync", "SD")
    kb.wait("scalar", "TT")
    kb.scopy(small[:, S_BCQ:S_BCQ + 24], ps[0][:, 0:24])
    kb.barrier()

    kb.memset(cmisc[:, 42:43], 1.0)
    kb.memset(cmisc[:, 43:44], PIO2)
    RA = small[0:1, 808:816]
    RB = small[0:1, 816:824]
    kb.memset(RA[0:1, 0:1], 220.0)
    kb.recip(RB, RA)
    kb.tt(cell(8), cell(3), RB[0:1, 0:1], Alu.mult)
    kb.tt(p1(15), cmisc[:, 44:45], cmisc[:, 45:46], Alu.add)
    kb.wait("tensor", "VV")
    kb.mm(ps[3][:, 0:1], cones[0:1, :], cell(8), mark=True)
    kb.wait("vector", "TT")
    kb.tt(p1(1), p1(15), ps[3][:, 0:1], Alu.mult)
    kb.iota(fscr[:, 0:2048], [[1, 2048]], 0, 0)
    kb.wait("vector", "GG")
    kb.ts(qkb[:, 2048:4096], fscr[:, 0:2048], p1(1), None, Alu.mult)
    kb.ts(fscr[:, 2048:4096], qkb[:, 2048:4096], INV2PI, 0.5, Alu.mult, Alu.add)
    kb.vcopy(qkb[:, 0:2048], fscr[:, 2048:4096])
    for sh in range(16, -1, -1):
        c = float(2 ** sh)
        kb.ts(fscr[:, 0:2048], fscr[:, 2048:4096], c, None, Alu.is_ge)
        kb.stt(fscr[:, 2048:4096], fscr[:, 0:2048], -c, fscr[:, 2048:4096], Alu.mult, Alu.add)
    kb.tt(fscr[:, 0:2048], qkb[:, 0:2048], fscr[:, 2048:4096], Alu.subtract)
    kb.stt(fscr[:, 2048:4096], fscr[:, 0:2048], -CW1, qkb[:, 2048:4096], Alu.mult, Alu.add)
    kb.stt(fscr[:, 2048:4096], fscr[:, 0:2048], -CW2, fscr[:, 2048:4096], Alu.mult, Alu.add)
    kb.stt(fscr[:, 2048:4096], fscr[:, 0:2048], -CW3, fscr[:, 2048:4096], Alu.mult, Alu.add)
    kb.wait("scalar", "VV")
    kb.act(gscr[:, 2048:4096], fscr[:, 2048:4096], AF.Sin)
    kb.act(fscr[:, 0:2048], fscr[:, 2048:4096], AF.Abs)
    kb.act(gscr[:, 0:2048], fscr[:, 0:2048], AF.Sin, bias=cmisc[:, 43:44], scale=-1.0)
    kb.wait("vector", "SS")
    kb.ts(gscr[:, 2048:4096], gscr[:, 2048:4096], cmisc[:, 0:1], None, Alu.mult)
    kb.barrier()
    if dbg4 is not None:
        kb.store(dbg4, gscr[:, :])
        kb.wait("sync", "SD")
    kb.barrier()

    def wrow(i, w=512):
        return wbuf[0:1, i * 512:i * 512 + w]

    def fm_ln(src, dst, gamma=None, beta=None):
        for tc in range(4):
            lo, hi = tc * 512, tc * 512 + 512
            kb.wait("tensor", "VV")
            for kc in range(8):
                sq = fscr[:, (kc % 2) * 512:(kc % 2) * 512 + 512]
                if kc >= 2:
                    kb.wait("scalar", "TT")
                kb.act(sq, src(kc, lo, hi), AF.Square)
                kb.wait("tensor", "SS")
                kb.mm(ps[4][0:1, :], cones[:, 0:1], src(kc, lo, hi),
                      start=(kc == 0), stop=(kc == 7))
                kb.mm(ps[5][0:1, :], cones[:, 0:1], sq,
                      start=(kc == 0), stop=(kc == 7), mark=True)
            kb.wait("vector", "TT")
            kb.ts(wrow(0), ps[4][0:1, :], 1.0 / 1024, None, Alu.mult)
            kb.ts(wrow(1), ps[5][0:1, :], 1.0 / 1024, None, Alu.mult)
            kb.tt(wrow(2), wrow(0), wrow(0), Alu.mult)
            kb.tt(wrow(1), wrow(1), wrow(2), Alu.subtract)
            kb.ts(wrow(1), wrow(1), EPS, None, Alu.add)
            kb.wait("scalar", "VV")
            kb.act(wrow(2), wrow(1), AF.Ln)
            kb.act(wrow(2), wrow(2), AF.Exp, scale=-0.5)
            kb.wait("vector", "SS")
            kb.tt(wrow(3), wrow(2), wrow(2), Alu.mult)
            kb.tt(wrow(3), wrow(3), wrow(1), Alu.mult)
            kb.ts(wrow(3), wrow(3), -0.5, 1.5, Alu.mult, Alu.add)
            kb.tt(wrow(2), wrow(2), wrow(3), Alu.mult)
            kb.tt(wrow(0), wrow(0), wrow(2), Alu.mult)
            kb.wait("tensor", "VV")
            kb.mm(ps[6][:, :], cones[0:1, :], wrow(2), mark=True)
            kb.mm(ps[7][:, :], cones[0:1, :], wrow(0), mark=True)
            kb.wait("vector", "TT")
            for kc in range(8):
                tmp = fscr[:, 1024:1536]
                kb.tt(tmp, src(kc, lo, hi), ps[6][:, :], Alu.mult)
                kb.tt(dst(kc, lo, hi), tmp, ps[7][:, :], Alu.subtract)
                if gamma is not None:
                    kb.ts(dst(kc, lo, hi), dst(kc, lo, hi),
                          small[:, gamma + kc:gamma + kc + 1],
                          small[:, beta + kc:beta + kc + 1], Alu.mult, Alu.add)

    def attn_block(b, layer):
        attn_block.b = b
        wom = [0, 0]
        for hp in range(8):
            kb.wait("sync", "TT")
            kb.wait("sync", "VV")
            if layer == 0:
                src4 = di["wqkv"].rearrange("(k p) n -> p k n", p=128)
                for j, base in enumerate((2 * hp * 192, 2 * hp * 192 + 64, 2 * hp * 192 + 128)):
                    dst4 = wbuf[:, j * 1024:(j + 1) * 1024].rearrange("p (k h c) -> p k h c", k=8, h=2)
                    for hh in range(2):
                        kb.load(dst4[:, :, hh, :], src4[:, :, base + hh * 192:base + hh * 192 + 64])
                kb.load(wbuf[:, 3072:4096], di["wo0"][hp * 128:(hp + 1) * 128, :])
            else:
                kb.load(wbuf[:, 0:1024].rearrange("p (k n) -> p k n", k=8),
                        wq1p.rearrange("(k p) n -> p k n", p=128)[:, :, hp * 128:(hp + 1) * 128])
                kb.load(wbuf[:, 1024:2048].rearrange("p (k n) -> p k n", k=8),
                        wkv1p.rearrange("(k p) n -> p k n", p=128)[:, :, hp * 128:(hp + 1) * 128])
                kb.load(wbuf[:, 2048:3072].rearrange("p (k n) -> p k n", k=8),
                        wkv1p.rearrange("(k p) n -> p k n", p=128)[:, :, 1024 + hp * 128:1024 + hp * 128 + 128])
                kb.load(wbuf[:, 3072:4096], di["wo1"][hp * 128:(hp + 1) * 128, :])
            kb.wait("tensor", "LD")
            if layer == 1:
                kb.store(small[0:1, S_BVROW:S_BVROW + 128], small[:, S_BCV + hp:S_BCV + hp + 1])
                kb.wait("tensor", "SD")
            for tc in range(4):
                lo, hi = tc * 512, tc * 512 + 512
                if tc > 0:
                    kb.wait("tensor", "SS")
                for kc in range(8):
                    kb.mm(ps[0][:, :], wbuf[:, kc * 128:kc * 128 + 128], xs(kc, lo, hi),
                          start=(kc == 0), stop=(kc == 7))
                for kc in range(8):
                    kb.mm(ps[1][:, :], wbuf[:, 1024 + kc * 128:1024 + kc * 128 + 128], xs(kc, lo, hi),
                          start=(kc == 0), stop=(kc == 7), mark=True)
                kb.wait("scalar", "TT")
                if layer == 0:
                    kb.scopy(qkb[:, lo:hi], ps[0][:, :])
                    kb.scopy(qkb[:, 2048 + lo:2048 + hi], ps[1][:, :])
                else:
                    kb.act(qkb[:, lo:hi], ps[0][:, :], AF.Identity,
                           bias=small[:, S_BCQ + hp:S_BCQ + hp + 1])
                    kb.act(qkb[:, 2048 + lo:2048 + hi], ps[1][:, :], AF.Identity,
                           bias=small[:, S_BCK + hp:S_BCK + hp + 1])
            kb.memset(vtb[:, :], 1.0)
            kb.wait("tensor", "VV")
            kb.wait("scalar", "VV")
            vt3 = vtb.rearrange("p (t c) -> p t c", c=130)
            for tkc in range(16):
                bank = ps[2 + (tkc % 2)]
                if tkc >= 2:
                    kb.wait("tensor", "SS")
                for kc in range(8):
                    kb.mm(bank[:, 0:128], xbuf[:, kc * T + tkc * 128:kc * T + tkc * 128 + 128],
                          wbuf[:, 2048 + kc * 128:2048 + kc * 128 + 128],
                          start=(kc == 0), stop=(kc == 7 and layer == 0))
                if layer == 1:
                    kb.mm(bank[:, 0:128], cones[0:1, 0:128],
                          small[0:1, S_BVROW:S_BVROW + 128], start=False, stop=True)
                kb.tmark()
                kb.wait("scalar", "TT")
                kb.scopy(vt3[:, tkc, 0:64], bank[:, 0:64])
                kb.scopy(vt3[:, tkc, 65:129], bank[:, 64:128])
            if layer == 1:
                kb.wait("vector", "SS")
                for base in (0, 2048):
                    q = qkb[:, base:base + 2048]
                    kb.shuf(fscr[:, 0:2048], q, [i ^ 1 for i in range(32)])
                    kb.tt(q, q, gscr[:, 0:2048], Alu.mult)
                    kb.tt(fscr[:, 0:2048], fscr[:, 0:2048], gscr[:, 2048:4096], Alu.mult)
                    kb.tt(q, q, fscr[:, 0:2048], Alu.add)
                kb.wait("tensor", "VV")
            if layer == 1:
                kb.wait("tensor", "SS")
                for h, bank, dst in ((0, ps[6], 10), (1, ps[7], 11)):
                    for tkc in range(16):
                        kb.mm(bank[0:65, 0:1], vt3[:, tkc, h * 65:h * 65 + 65], cones[:, 0:1],
                              start=(tkc == 0), stop=(tkc == 15))
                    kb.tmark()
                kb.wait("scalar", "TT")
                kb.scopy(p1(10)[0:65, :], ps[6][0:65, 0:1])
                kb.scopy(p1(11)[0:65, :], ps[7][0:65, 0:1])
            scale = (1.0 / 32.0) if layer == 0 else 0.125
            for qt in range(4):
                kmax = 4 * qt + 4 if layer == 0 else 16
                kb.wait("tensor", "VV")
                kb.wait("tensor", "TA")
                sevac = []
                avm = []
                for kc in range(kmax):
                    sb0 = ps[0] if kc % 2 == 0 else ps[2]
                    sb1 = ps[1] if kc % 2 == 0 else ps[3]
                    if kc >= 2:
                        kb.wait("tensor", "SS", sevac[kc - 2])
                    kb.mm(sb0[:, :], qkb[0:64, 2048 + kc * 128:2048 + kc * 128 + 128],
                          qkb[0:64, qt * 512:qt * 512 + 512], tp=(0, 0))
                    kb.tmark()
                    kb.mm(sb1[:, :], qkb[64:128, 2048 + kc * 128:2048 + kc * 128 + 128],
                          qkb[64:128, qt * 512:qt * 512 + 512], tp=(64, 0))
                    kb.tmark()
                    if kc >= 2:
                        kb.wait("scalar", "TA", avm[kc - 2])
                    kb.wait("scalar", "TT")
                    p0 = fscr[:, (kc % 2) * 512:(kc % 2) * 512 + 512]
                    p1_ = fscr[:, 1024 + (kc % 2) * 512:1024 + (kc % 2) * 512 + 512]
                    if layer == 0:
                        kb.act(p0, sb0[:, :], AF.Exp, scale=scale)
                        kb.act(p1_, sb1[:, :], AF.Exp, scale=scale)
                        if kc >= 4 * qt:
                            kb.wait("gpsimd", "SS")
                            kb.affine(p0, p0, [[1, 512]], Alu.is_ge, 0.0,
                                      qt * 512 - kc * 128, -1)
                            kb.affine(p1_, p1_, [[1, 512]], Alu.is_ge, 0.0,
                                      qt * 512 - kc * 128, -1)
                            kb.wait("tensor", "GG")
                    else:
                        kb.act(p0, sb0[:, :], AF.Square, scale=scale, bias=cmisc[:, 42:43])
                        kb.act(p1_, sb1[:, :], AF.Square, scale=scale, bias=cmisc[:, 42:43])
                    kb.wait("tensor", "SS")
                    kb.mm(ps[4][0:65, :], vt3[:, kc, 0:65], p0,
                          start=(kc == 0), stop=(kc == kmax - 1))
                    kb.mm(ps[5][0:65, :], vt3[:, kc, 65:130], p1_,
                          start=(kc == 0), stop=(kc == kmax - 1), sem="TA")
                    avm.append(kb.cnt["TA"])
                    sevac.append(kb.cnt["SS"])
                kb.wait("vector", "TT")
                kb.wait("vector", "TA")
                for h, ob in ((0, ps[4]), (1, ps[5])):
                    rr = wrow(h)
                    if layer == 0:
                        kb.recip(rr, ob[64:65, :])
                    else:
                        kb.ts(rr, ob[64:65, :], 2048.0, None, Alu.add)
                        kb.recip(rr, rr)
                    kb.wait("tensor", "VV")
                    rb = ps[6] if h == 0 else ps[7]
                    kb.mm(rb[0:64, :], cones[0:1, 0:64], rr, mark=True)
                    kb.wait("scalar", "TT")
                    kb.scopy(fscr[0:64, 0:512], rb[0:64, :])
                    kb.wait("vector", "SS")
                    dst = fscr[h * 64:h * 64 + 64, 2048 + qt * 512:2048 + qt * 512 + 512]
                    if layer == 0:
                        kb.tt(dst, ob[0:64, :], fscr[0:64, 0:512], Alu.mult)
                    else:
                        kb.stt(dst, ob[0:64, :], p1(10 + h)[0:64, :],
                               fscr[0:64, 0:512], Alu.add, Alu.mult)
            kb.wait("tensor", "VV")
            if layer == 1 and hp == 0 and attn_block.b == 0 and dbg5 is not None:
                kb.wait("gpsimd", "VV")
                kb.store(dbg5, fscr[:, 2048:4096])
                kb.wait("tensor", "SD")
            for tc in range(4):
                for oc in range(8):
                    bank = ps[2 + (oc % 2)]
                    if tc * 8 + oc >= 2:
                        kb.wait("tensor", "VV", wom[-2])
                    kb.mm(bank[:, :], wbuf[:, 3072 + oc * 128:3072 + oc * 128 + 128],
                          fscr[:, 2048 + tc * 512:2048 + tc * 512 + 512], mark=True)
                    kb.wait("vector", "TT")
                    src1 = xs(oc, tc * 512, tc * 512 + 512) if (layer == 0 and hp == 0) \
                        else accs(oc, tc * 512, tc * 512 + 512)
                    kb.tt(accs(oc, tc * 512, tc * 512 + 512), bank[:, :], src1, Alu.add)
                    wom.append(kb.cnt["VV"])

    def ffn_block(b):
        ffn_block.b = b
        x3 = xbuf.rearrange("p (k t) -> p k t", k=8)
        kb.wait("gpsimd", "VV")
        kb.wait("gpsimd", "SS")
        kb.store(park1, x3)
        kb.vcopy(accb[:, :], xbuf[:, :])
        for kc in range(8):
            kb.red(p1(24 + kc), xs(kc, 0, T), Alu.add)
        kb.ts(small[:, S_P1 + 24:S_P1 + 32], small[:, S_P1 + 24:S_P1 + 32],
              1.0 / T, None, Alu.mult)
        if dbg3 is not None and ffn_block.b == 0:
            kb.wait("gpsimd", "VV")
            kb.store(dbg3[0:128, :], small[:, 0:1024])
        kb.wait("sync", "TT")
        kb.load(wbuf[:, 0:4096].rearrange("p (k n) -> p k n", k=8),
                di["gcw1"].rearrange("(k p) n -> p k n", p=128))
        kb.wait("tensor", "LD")
        kb.wait("tensor", "VV")
        first = True
        for oc in range(4):
            for kc in range(8):
                kb.mm(ps[0][:, oc:oc + 1], wbuf[:, kc * 512 + oc * 128:kc * 512 + oc * 128 + 128],
                      p1(24 + kc), start=first, stop=(oc == 3 and kc == 7))
                first = False
        kb.tmark()
        kb.wait("scalar", "TT")
        for oc in range(4):
            kb.act(p1(16 + oc), ps[0][:, oc:oc + 1], AF.Gelu,
                   bias=small[:, S_GCB1 + oc:S_GCB1 + oc + 1])
        kb.wait("sync", "TT")
        kb.load(wbuf[:, 0:1024].rearrange("p (k n) -> p k n", k=4),
                di["gcw2"].rearrange("(k p) n -> p k n", p=128))
        kb.wait("tensor", "LD")
        kb.wait("tensor", "SS")
        first = True
        for oc in range(2):
            for kc in range(4):
                kb.mm(ps[1][:, oc:oc + 1], wbuf[:, kc * 256 + oc * 128:kc * 256 + oc * 128 + 128],
                      p1(16 + kc), start=first, stop=(oc == 1 and kc == 3))
                first = False
        kb.tmark()
        kb.wait("scalar", "TT")
        for oc in range(2):
            kb.act(p1(20 + oc), ps[1][:, oc:oc + 1], AF.Gelu,
                   bias=small[:, S_GCB2 + oc:S_GCB2 + oc + 1])
        kb.wait("sync", "TT")
        kb.load(wbuf[:, 0:2].rearrange("p (k n) -> p k n", k=2),
                di["gcw3"].rearrange("(k p) n -> p k n", p=128), slow=True)
        kb.wait("tensor", "LD")
        kb.wait("tensor", "SS")
        for kc in range(2):
            kb.mm(ps[2][0:1, 0:1], wbuf[:, kc:kc + 1], p1(20 + kc),
                  start=(kc == 0), stop=(kc == 1))
        kb.tmark()
        kb.wait("scalar", "TT")
        kb.act(cell(4), ps[2][0:1, 0:1], AF.Sigmoid, bias=cell(1))
        kb.wait("vector", "SS")
        kb.ts(cell(5), cell(4), 0.7, None, Alu.mult)
        kb.wait("sync", "SD")
        kb.wait("sync", "VV")
        kb.load(xbuf[:, 8192:16384].rearrange("p (k n) -> p k n", k=8),
                di["cnw1"].rearrange("(k p) n -> p k n", p=128))
        kb.load(qkb[:, 0:4096].rearrange("p (k n) -> p k n", k=8),
                di["cnw2"].rearrange("(k p) n -> p k n", p=128))
        kb.load(vtb[:, 0:1024].rearrange("p (k n) -> p k n", k=4),
                di["cnw3"].rearrange("(k p) n -> p k n", p=128))
        kb.load(vtb[:, 1024:1026].rearrange("p (k n) -> p k n", k=2),
                di["cnw4"].rearrange("(k p) n -> p k n", p=128), slow=True)
        kb.wait("tensor", "LD")
        for cyc in range(3):
            for tc in range(4):
                lo, hi = tc * 512, tc * 512 + 512
                kb.wait("tensor", "VV")
                kb.wait("tensor", "SS")
                h1e = []
                for oc in range(8):
                    bank = ps[oc % 2]
                    if oc >= 2:
                        kb.wait("tensor", "SS", h1e[oc - 2])
                    for kc in range(8):
                        kb.mm(bank[:, :], xbuf[:, 8192 + kc * 1024 + oc * 128:8192 + kc * 1024 + oc * 128 + 128],
                              accs(kc, lo, hi), start=(kc == 0), stop=(kc == 7))
                    kb.tmark()
                    kb.wait("scalar", "TT")
                    kb.act(xbuf[:, oc * 512:oc * 512 + 512], bank[:, :], AF.Gelu,
                           bias=small[:, S_CNB1 + oc:S_CNB1 + oc + 1])
                    h1e.append(kb.cnt["SS"])
                for oc in range(4):
                    bank = ps[2 + oc % 2]
                    if oc >= 2:
                        kb.wait("tensor", "SS")
                    for kc in range(8):
                        kb.mm(bank[:, :], qkb[:, kc * 512 + oc * 128:kc * 512 + oc * 128 + 128],
                              xbuf[:, kc * 512:kc * 512 + 512], start=(kc == 0), stop=(kc == 7))
                    kb.tmark()
                    kb.wait("scalar", "TT")
                    kb.act(xbuf[:, 4096 + oc * 512:4096 + oc * 512 + 512], bank[:, :], AF.Gelu,
                           bias=small[:, S_CNB2 + oc:S_CNB2 + oc + 1])
                for oc in range(2):
                    bank = ps[4 + oc % 2]
                    for kc in range(4):
                        kb.mm(bank[:, :], vtb[:, kc * 256 + oc * 128:kc * 256 + oc * 128 + 128],
                              xbuf[:, 4096 + kc * 512:4096 + kc * 512 + 512],
                              start=(kc == 0), stop=(kc == 3))
                    kb.tmark()
                    kb.wait("scalar", "TT")
                    kb.act(xbuf[:, 6144 + oc * 512:6144 + oc * 512 + 512], bank[:, :], AF.Gelu,
                           bias=small[:, S_CNB3 + oc:S_CNB3 + oc + 1])
                for kc in range(2):
                    kb.mm(ps[6][0:1, :], vtb[:, 1024 + kc:1024 + kc + 1],
                          xbuf[:, 6144 + kc * 512:6144 + kc * 512 + 512],
                          start=(kc == 0), stop=(kc == 1))
                kb.tmark()
                kb.wait("scalar", "TT")
                kb.act(wrow(0), ps[6][0:1, :], AF.Sigmoid, bias=cell(0))
                kb.wait("vector", "SS")
                kb.ts(wrow(1), wrow(0), 0.3, cell(5), Alu.mult, Alu.add)
                kb.ts(wrow(1), wrow(1), 0.5, cell(2), Alu.subtract, Alu.mult)
                kb.ts(wrow(1), wrow(1), 2.0, 0.1, Alu.mult, Alu.add)
                kb.wait("tensor", "VV")
                kb.mm(ps[7][:, :], cones[0:1, :], wrow(1), mark=True)
                kb.wait("vector", "TT")
                for j in range(4):
                    sgn = 1.0 if j < 2 else -1.0
                    a = accs(j, lo, hi)
                    bb = accs(j + 4, lo, hi)
                    kb.tt(fscr[:, 0:512], bb, ps[7][:, :], Alu.mult)
                    kb.tt(fscr[:, 512:1024], a, ps[7][:, :], Alu.mult)
                    kb.stt(a, fscr[:, 0:512], sgn, a, Alu.mult, Alu.add)
                    kb.stt(bb, fscr[:, 512:1024], -sgn, bb, Alu.mult, Alu.add)
        for tc in range(4):
            lo, hi = tc * 512, tc * 512 + 512
            for kc in range(8):
                tile = fscr[:, 1024 + (kc % 2) * 512:1024 + (kc % 2) * 512 + 512]
                if tc * 8 + kc >= 2:
                    kb.wait("sync", "VV")
                kb.load(tile, park1[:, kc, lo:hi])
                kb.wait("vector", "LD")
                kb.tt(xs(kc, lo, hi), tile, accs(kc, lo, hi), Alu.add)
        kb.barrier()
        fm_ln(xs, xs)

    def l1_tail(b):
        fm_ln(xs, accs, gamma=S_LNG, beta=S_LNB2)
        kb.barrier()
        if _TSTOP <= 1:
            return
        for tc in range(4):
            lo, hi = tc * 512, tc * 512 + 512
            for kc in range(8):
                kb.mm(ps[0][0:8, :], ccent[:, kc * 8:kc * 8 + 8], accs(kc, lo, hi),
                      start=(kc == 0), stop=(kc == 7))
            kb.tmark()
            kb.wait("scalar", "TT")
            kb.scopy(fscr[0:8, 0:512], ps[0][0:8, :])
            kb.wait("gpsimd", "SS")
            j0 = b * T + tc * 512
            kb.store(scr_blk.ap()[:, j0:j0 + 512], fscr[0:8, 0:512])
            kb.wait("tensor", "SD")
        if _TSTOP <= 2:
            return
        NTG = 1 if _TSTOP == 4 else 16
        DOST = _TSTOP != 5
        for tg in range(NTG):
            for (srcb, dst_dram, off) in ((xbuf, feat_tm, 1024), (accb, lnf_tm, 2048)):
                for kc in range(8):
                    bank = ps[1 + kc % 2]
                    kb.wait("tensor", "SS")
                    kb.transp(bank[:, 0:128], srcb[:, kc * T + tg * 128:kc * T + tg * 128 + 128],
                              cident[:, :], mark=True)
                    kb.wait("scalar", "TT")
                    kb.scopy(fscr[:, off + kc * 128:off + kc * 128 + 128], bank[:, 0:128])
                if DOST:
                    kb.wait("gpsimd", "SS")
                    r0 = (b * 16 + tg) * 128
                    kb.store(dst_dram[r0:r0 + 128, :], fscr[:, off:off + 1024])
            if DOST:
                kb.wait("scalar", "SD")

    def cross_reduce(dst, src, op, bA, bB):
        kb.wait("gpsimd", "VV")
        kb.store(small[0:1, 680:808], src)
        kb.wait("vector", "SD")
        kb.red(cell(14), small[0:1, 680:808], op)
        kb.wait("tensor", "VV")
        kb.mm(bB[:, 0:1], cones[0:1, :], cell(14), mark=True)
        kb.wait("vector", "TT")
        kb.vcopy(dst, bB[:, 0:1])

    def routing():
        G = lambda c0, w: gscr[:, c0:c0 + w]
        s_all = lambda e: gscr[:, e * 256:e * 256 + 256]
        mem = lambda e: fscr[:, e * 256:e * 256 + 256]
        s_cur = G(2048, 256)
        dump = G(2304, 256)
        eq_t = G(2560, 256)
        gt_t = G(2816, 256)
        u_t = G(3072, 256)
        taken = G(3328, 256)
        scn = G(3584, 256)
        zrow = qkb[0:1, 2048:2304]

        if _RSTOP <= -1:
            return
        kb.cc("AllGather", scr_blk.ap(), scr_g.ap())
        kb.wait("sync", "CC")
        if _RSTOP <= -2:
            return
        for c in range(8):
            kb.load(gscr[c * 16:(c + 1) * 16, 0:2048].rearrange("p (e f) -> p e f", e=8),
                    scr_g.ap().rearrange("c e (pr f) -> c pr e f", pr=16)[c, :, :, :])
        kb.memset(zrow, 0.0)
        kb.memset(taken, 0.0)
        kb.wait("vector", "LD")
        if _RSTOP <= 0:
            return
        LO, HI, TH, PRED, CNT, M_, VREP, NOTP, D_ = (p1(0), p1(1), p1(2), p1(3),
                                                     p1(4), p1(5), p1(6), p1(7), p1(8))
        for e in range(8):
            kb.wait("tensor", "VV")
            if e == 0:
                kb.vcopy(s_cur, s_all(0))
            elif e < 7:
                kb.stt(s_cur, taken, -2e30, s_all(e), Alu.mult, Alu.add)
            if e == 7:
                kb.ts(mem(7), taken, -1.0, 1.0, Alu.mult, Alu.add)
            else:
                kb.stt(u_t, taken, 4e30, s_cur, Alu.mult, Alu.add)
                kb.red(D_, u_t, Alu.min)
                cross_reduce(LO, D_, Alu.min, ps[0], ps[1])
                kb.red(D_, s_cur, Alu.max)
                cross_reduce(HI, D_, Alu.max, ps[0], ps[1])
                for _ in range(30):
                    kb.ts(TH, LO, HI, 0.5, Alu.add, Alu.mult)
                    kb.ts(dump, s_cur, TH, 0.0, Alu.is_ge, Alu.add, accum_out=CNT)
                    kb.wait("tensor", "VV")
                    kb.mm(ps[0][:, 0:1], cones[:, :], CNT, mark=True)
                    kb.wait("vector", "TT")
                    kb.ts(PRED, ps[0][:, 0:1], float(CEXP), None, Alu.is_ge)
                    kb.tt(D_, TH, LO, Alu.subtract)
                    kb.tt(D_, PRED, D_, Alu.mult)
                    kb.tt(LO, LO, D_, Alu.add)
                    kb.ts(NOTP, PRED, -1.0, 1.0, Alu.mult, Alu.add)
                    kb.tt(D_, TH, HI, Alu.subtract)
                    kb.tt(D_, NOTP, D_, Alu.mult)
                    kb.tt(HI, HI, D_, Alu.add)
                kb.ts(dump, s_cur, LO, None, Alu.is_lt)
                kb.stt(u_t, dump, 4e30, s_cur, Alu.mult, Alu.add)
                kb.red(D_, u_t, Alu.min)
                cross_reduce(VREP, D_, Alu.min, ps[0], ps[1])
                kb.ts(dump, s_cur, VREP, 0.0, Alu.is_gt, Alu.add, accum_out=CNT)
                kb.wait("tensor", "VV")
                kb.mm(ps[0][:, 0:1], cones[:, :], CNT, mark=True)
                kb.wait("vector", "TT")
                kb.ts(M_, ps[0][:, 0:1], -1.0, float(CEXP), Alu.mult, Alu.add)
                kb.ts(eq_t, s_cur, VREP, None, Alu.is_equal)
                kb.scan(scn, eq_t, eq_t, 0.0, Alu.add, Alu.bypass)
                kb.vcopy(p1(9), scn[:, 255:256])
                kb.tt(scn, scn, eq_t, Alu.subtract)
                kb.wait("tensor", "VV")
                kb.mm(ps[2][:, 0:1], clmask[:, :], p1(9), mark=True)
                kb.wait("vector", "TT")
                kb.ts(scn, scn, ps[2][:, 0:1], None, Alu.add)
                kb.ts(dump, scn, M_, None, Alu.is_lt)
                kb.ts(gt_t, s_cur, VREP, None, Alu.is_gt)
                kb.tt(eq_t, eq_t, dump, Alu.mult)
                kb.tt(mem(e), gt_t, eq_t, Alu.add)
            if e < 7:
                kb.tt(taken, taken, mem(e), Alu.add)
        if _RSTOP <= 1:
            return
        hrow_full = G(2048, 256)
        lp_full = G(2304, 256)
        kb.wait("vector", "TT")
        kb.memset(hrow_full, 0.0)
        kb.memset(lp_full, 0.0)
        bp8m = qkb[0:8, 2304:2312]
        st8m = qkb[0:8, 2312:2320]
        cnt8 = qkb[0:8, 2320:2328]
        run8 = qkb[0:8, 2328:2336]
        c1 = p1(10)
        c3 = p1(11)
        kb.memset(run8[:, 0:1], 0.0)
        for e in range(8):
            kb.wait("tensor", "VV")
            kb.scan(scn, mem(e), mem(e), 0.0, Alu.add, Alu.bypass)
            kb.vcopy(p1(9), scn[:, 255:256])
            kb.tt(scn, scn, mem(e), Alu.subtract)
            kb.wait("tensor", "VV")
            kb.mm(ps[2][:, 0:1], clmask[:, :], p1(9), start=True, stop=True)
            kb.mm(ps[3][:, 0:1], clb[:, :], p1(9), start=True, stop=True)
            kb.mm(ps[4][0:8, 0:1], cmisc[:, 1:9], p1(9), mark=True)
            kb.wait("vector", "TT")
            kb.vcopy(p1(12)[0:8, :], ps[4][0:8, 0:1])
            kb.vcopy(qkb[0:8, 2320 + e:2321 + e], p1(12)[0:8, :])
            kb.vcopy(st8m[:, e:e + 1], run8[:, 0:1])
            kb.tt(run8[:, 0:1], run8[:, 0:1], p1(12)[0:8, :], Alu.add)
            kb.wait("tensor", "VV")
            kb.mm(ps[5][0:8, 0:1], clmask[0:8, 0:8], p1(12)[0:8, :], mark=True)
            kb.wait("vector", "TT")
            kb.vcopy(bp8m[:, e:e + 1], ps[5][0:8, 0:1])
            kb.ts(u_t, scn, ps[2][:, 0:1], float(CEXP * e), Alu.add, Alu.add)
            kb.tt(taken, mem(e), u_t, Alu.mult)
            kb.tt(hrow_full, hrow_full, taken, Alu.add)
            kb.vcopy(c1, ps[2][:, 0:1])
            kb.tt(c1, c1, ps[3][:, 0:1], Alu.subtract)
            kb.wait("tensor", "VV")
            kb.mm(ps[6][:, 0:1], cexp16[:, :], st8m[:, e:e + 1], mark=True)
            kb.wait("vector", "TT")
            kb.tt(c3, c1, ps[6][:, 0:1], Alu.add)
            kb.ts(u_t, scn, c3, None, Alu.add)
            kb.tt(taken, mem(e), u_t, Alu.mult)
            kb.tt(lp_full, lp_full, taken, Alu.add)
        if _RSTOP <= 2:
            return
        kb.wait("tensor", "VV")
        kb.mm(ps[0][0:16, 0:256], cesel128[:, :], lp_full, mark=True)
        kb.mm(ps[1][0:16, 0:256], cesel128[:, :], hrow_full, mark=True)
        kb.wait("scalar", "TT")
        kb.scopy(gscr[0:16, 2048:2304], ps[0][0:16, 0:256])
        kb.scopy(gscr[0:16, 2304:2560], ps[1][0:16, 0:256])
        kb.wait("gpsimd", "SS")
        for k in range(32):
            pr, half = k // 2, (k % 2) * 128
            kb.store(fscr[:, 2048 + k:2048 + k + 1], gscr[pr:pr + 1, 2048 + half:2048 + half + 128])
            kb.store(fscr[:, 2080 + k:2080 + k + 1], gscr[pr:pr + 1, 2304 + half:2304 + half + 128])
        kb.wait("vector", "SD")
        kb.ts(u32b[:, 0:32], fscr[:, 2048:2080], 0.0, None, Alu.add)
        kb.ts(u32b[:, 32:64], fscr[:, 2080:2112], 0.0, None, Alu.add)
        if _RSTOP <= 3:
            return
        esr = qkb[0:1, 2336:2344]
        kb.wait("gpsimd", "VV")
        kb.store(esr, cesel[:, :])
        kb.wait("tensor", "SD")
        kb.mm(ps[7][0:8, 0:8], cones[0:1, 0:8], esr, mark=True)
        kb.wait("vector", "TT")
        esrep8 = qkb[0:8, 2344:2352]
        kb.vcopy(esrep8, ps[7][0:8, 0:8])
        cumown = p1(13)
        stown = p1(14)
        kb.memset(cumown[0:8, :], 0.0)
        kb.memset(stown[0:8, :], 0.0)
        for e in range(8):
            kb.stt(cumown[0:8, :], bp8m[:, e:e + 1], esrep8[:, e:e + 1], cumown[0:8, :],
                   Alu.mult, Alu.add)
            kb.stt(stown[0:8, :], st8m[:, e:e + 1], esrep8[:, e:e + 1], stown[0:8, :],
                   Alu.mult, Alu.add)
        lam = p1(15)
        kb.tt(lam[0:8, :], stown[0:8, :], cumown[0:8, :], Alu.subtract)
        kb.tt(lam[0:8, :], lam[0:8, :], cmisc[0:8, 46:47], Alu.add)
        kb.wait("gpsimd", "VV")
        kb.store(qkb[0:1, 2352:2360], lam[0:8, :])
        kb.store(qkb[0:1, 2360:2368], cumown[0:8, :])
        kb.wait("tensor", "SD")
        kb.mm(ps[6][:, 0:8], cones[0:1, :], qkb[0:1, 2352:2360])
        kb.mm(ps[7][:, 0:8], cones[0:1, :], qkb[0:1, 2360:2368], mark=True)
        kb.wait("vector", "TT")
        lamrep = G(3584, 8)
        cumrep = G(3592, 8)
        kb.vcopy(lamrep, ps[6][:, 0:8])
        kb.vcopy(cumrep, ps[7][:, 0:8])
        sgrid = G(3600, 32)
        kb.iota(sgrid, [[128, 32]], 0, 1)
        kb.wait("vector", "GG")
        acc32 = G(3632, 32)
        ge_t = lambda c: gscr[:, 3664 + c * 32:3664 + c * 32 + 32]
        kb.memset(acc32, 0.0)
        for c in range(8):
            kb.ts(ge_t(c), sgrid, cumrep[:, c:c + 1], None, Alu.is_ge)
        for c in range(8):
            if c < 7:
                kb.tt(s_cur[:, 0:32], ge_t(c), ge_t(c + 1), Alu.subtract)
                ind = s_cur[:, 0:32]
            else:
                ind = ge_t(7)
            kb.stt(acc32, ind, lamrep[:, c:c + 1], acc32, Alu.mult, Alu.add)
        kb.tt(acc32, sgrid, acc32, Alu.add)
        kb.ts(u32b[:, 64:96], acc32, 0.0, None, Alu.add)
        kb.barrier()
        if stage == "route" and _RSTOP >= 5:
            kb.store(dbg2[0 * 128:1 * 128, :], mem(0))
            kb.memset(gt_t, 0.0)
            kb.vcopy(gt_t[:, 0:64], fscr[:, 2048:2112])
            kb.wait("gpsimd", "VV")
            kb.store(dbg2[1 * 128:2 * 128, :], gt_t)
            kb.store(dbg2[2 * 128:3 * 128, :], mem(7))
            kb.store(dbg2[3 * 128:4 * 128, :], hrow_full)
            kb.store(dbg2[4 * 128:5 * 128, :], lp_full)
            kb.memset(dump, 0.0)
            kb.vcopy(dump[:, 0:32], acc32)
            kb.wait("gpsimd", "VV")
            kb.store(dbg2[5 * 128:6 * 128, :], dump)
            kb.wait("gpsimd", "SD")
        if _RSTOP <= 4:
            return
        for k in range(32):
            tile = fscr[:, (k % 2) * 1024:(k % 2) * 1024 + 1024]
            kb.wait("sync", "SD")
            kb.load(tile, lnf_tm[k * 128:k * 128 + 128, :])
            kb.wait("gpsimd", "LD")
            kb.ind_scatter(agin.ap(), tile, u32b[:, k:k + 1])
        if dbg7 is not None:
            kb.wait("gpsimd", "VV")
            kb.store(dbg7, u32b[:, :])
        kb.cc("AllGather", agin.ap(), agout.ap())
        if dbg6 is not None and _MSTOP == 1:
            kb.wait("gpsimd", "CC")
            kb.store(dbg6[0:256, :], agout.ap()[0:256, :])
            kb.store(dbg6[256:512, :], agout.ap()[CEXP * 3:CEXP * 3 + 256, :])

    def moe():
        kb.wait("gpsimd", "CC")
        kb.wait("tensor", "CC")
        mw1r = di["mw1"].rearrange("(k p) n -> p k n", p=128)
        for st in range(8):
            for g in range(4):
                tile = fscr[:, (g % 2) * 1024:(g % 2) * 1024 + 1024]
                kb.wait("gpsimd", "SS")
                kb.wait("gpsimd", "SD")
                kb.ind_gather(tile, agout.ap(), u32b[:, 64 + st * 4 + g:64 + st * 4 + g + 1])
                if dbg6 is not None and _MSTOP == 5 and st == 4 and g == 0:
                    kb.wait("gpsimd", "SD")
                    kb.store(dbg6[0:128, :], tile)
                    kb.wait("tensor", "SD")
                kb.wait("tensor", "SD")
                for kc in range(8):
                    bank = ps[kc % 2]
                    kb.wait("tensor", "SS")
                    kb.transp(bank[:, 0:128], tile[:, kc * 128:kc * 128 + 128],
                              cident[:, :], mark=True)
                    kb.wait("scalar", "TT")
                    kb.scopy(qkb[:, kc * 512 + g * 128:kc * 512 + g * 128 + 128],
                             bank[:, 0:128])
            if dbg6 is not None and _MSTOP == 3 and st == 4:
                kb.wait("gpsimd", "SS")
                kb.store(dbg6[0:128, :], qkb[:, 0:1024])
                kb.store(dbg6[128:256, :], qkb[:, 1024:2048])
                kb.store(dbg6[256:384, :], qkb[:, 2048:3072])
                kb.store(dbg6[384:512, :], qkb[:, 3072:4096])
                kb.wait("sync", "SD")
            w1evac = []
            for hc in range(32):
                wsl = accb[:, (hc % 2) * 1024:(hc % 2) * 1024 + 1024]
                kb.wait("sync", "TT")
                kb.load(wsl.rearrange("p (k n) -> p k n", k=8), mw1r[:, :, hc * 128:hc * 128 + 128])
                kb.wait("tensor", "LD")
                bank = ps[2 + hc % 2]
                if hc < 2:
                    kb.wait("tensor", "SS")
                else:
                    kb.wait("tensor", "SS", w1evac[hc - 2])
                for kc in range(8):
                    kb.mm(bank[:, :], wsl[:, kc * 128:kc * 128 + 128],
                          qkb[:, kc * 512:kc * 512 + 512], start=(kc == 0), stop=(kc == 7))
                kb.tmark()
                kb.wait("scalar", "TT")
                kb.act(xbuf[:, hc * 512:hc * 512 + 512], bank[:, :], AF.Gelu_apprx_tanh,
                       bias=small[:, S_MB1 + hc:S_MB1 + hc + 1])
                w1evac.append(kb.cnt["SS"])
            if dbg6 is not None and _MSTOP == 4 and st == 4:
                kb.wait("gpsimd", "SS")
                for q_ in range(4):
                    kb.store(dbg6[q_ * 128:(q_ + 1) * 128, :], xbuf[:, q_ * 1024:(q_ + 1) * 1024])
                kb.wait("sync", "SD")
            kb.wait("tensor", "SS")
            for hc in range(32):
                w2sl = accb[:, 2048 + (hc % 2) * 1024:2048 + (hc % 2) * 1024 + 1024]
                kb.wait("sync", "TT")
                kb.load(w2sl, di["mw2"][hc * 128:hc * 128 + 128, :])
                kb.wait("tensor", "LD")
                for oc in range(8):
                    kb.mm(ps[oc][:, :], w2sl[:, oc * 128:oc * 128 + 128],
                          xbuf[:, hc * 512:hc * 512 + 512],
                          start=(hc == 0), stop=(hc == 31))
                kb.tmark()
            kb.wait("scalar", "TT")
            for oc in range(8):
                kb.act(qkb[:, oc * 512:oc * 512 + 512], ps[oc][:, :], AF.Identity,
                       bias=small[:, S_MB2 + oc:S_MB2 + oc + 1])
            for g in range(4):
                tile = fscr[:, (g % 2) * 1024:(g % 2) * 1024 + 1024]
                if g >= 2:
                    kb.wait("scalar", "SD")
                for kc in range(8):
                    bank = ps[kc % 2]
                    kb.wait("tensor", "SS")
                    kb.transp(bank[:, 0:128], qkb[:, kc * 512 + g * 128:kc * 512 + g * 128 + 128],
                              cident[:, :], mark=True)
                    kb.wait("scalar", "TT")
                    kb.scopy(tile[:, kc * 128:kc * 128 + 128], bank[:, 0:128])
                kb.wait("gpsimd", "SS")
                kb.store(h_loc.ap()[st * 512 + g * 128:st * 512 + g * 128 + 128, :], tile)
        if dbg6 is not None and _MSTOP == 2:
            kb.wait("gpsimd", "SD")
            kb.store(dbg6[0:256, :], h_loc.ap()[0:256, :])
            kb.store(dbg6[256:512, :], h_loc.ap()[2048:2048 + 256, :])
        kb.cc("AllGather", h_loc.ap(), h_g.ap())

    def final():
        kb.wait("sync", "CC")
        kb.wait("gpsimd", "CC")
        for k in range(32):
            xt = fscr[:, (k % 2) * 1024:(k % 2) * 1024 + 1024]
            ht = fscr[:, 2048 + (k % 2) * 1024:2048 + (k % 2) * 1024 + 1024]
            if k >= 2:
                kb.wait("sync", "VV")
                kb.wait("gpsimd", "VV")
            kb.load(xt, feat_tm[k * 128:k * 128 + 128, :])
            kb.ind_gather(ht, h_g.ap(), u32b[:, 32 + k:32 + k + 1])
            kb.wait("vector", "LD")
            kb.wait("vector", "SD")
            u = gscr[:, (k % 2) * 1024:(k % 2) * 1024 + 1024]
            du = gscr[:, 2048 + (k % 2) * 1024:2048 + (k % 2) * 1024 + 1024]
            kb.stt(u, xt, 2.0, ht, Alu.mult, Alu.add)
            kb.ts(du, u, 0.0, 0.0, Alu.add, Alu.add, accum_out=p1(0))
            kb.stt(du, u, 1.0, u, Alu.mult, Alu.mult, accum_out=p1(1))
            kb.ts(p1(0), p1(0), 1.0 / 1024, None, Alu.mult)
            kb.ts(p1(1), p1(1), 1.0 / 1024, None, Alu.mult)
            kb.tt(p1(2), p1(0), p1(0), Alu.mult)
            kb.tt(p1(1), p1(1), p1(2), Alu.subtract)
            kb.ts(p1(1), p1(1), EPS, None, Alu.add)
            kb.wait("scalar", "VV")
            kb.act(p1(2), p1(1), AF.Ln)
            kb.act(p1(2), p1(2), AF.Exp, scale=-0.5)
            kb.wait("vector", "SS")
            kb.tt(p1(3), p1(2), p1(2), Alu.mult)
            kb.tt(p1(3), p1(3), p1(1), Alu.mult)
            kb.ts(p1(3), p1(3), -0.5, 1.5, Alu.mult, Alu.add)
            kb.tt(p1(2), p1(2), p1(3), Alu.mult)
            kb.ts(u, u, p1(0), p1(2), Alu.subtract, Alu.mult)
            kb.wait("gpsimd", "VV")
            kb.store(out[k * 128:k * 128 + 128, :], u)

    x3v = xbuf.rearrange("p (k t) -> p k t", k=8)
    for b in range(NB):
        kb.wait("sync", "VV")
        kb.wait("sync", "TT")
        kb.wait("sync", "SS")
        kb.wait("sync", "SD")
        kb.load(x3v, di["x0" if b == 0 else "x1"].rearrange("(k p) t -> p k t", p=128))
        kb.barrier()
        attn_block(b, 0)
        kb.barrier()
        fm_ln(accs, xs)
        kb.barrier()
        if SI == 1:
            kb.store(dbg[b * 128:(b + 1) * 128, :], xbuf[:, :])
            continue
        ffn_block(b)
        kb.barrier()
        if SI == 2:
            kb.store(dbg[b * 128:(b + 1) * 128, :], xbuf[:, :])
            if b == 0:
                kb.store(dbg3[128:256, :], small[:, 0:1024])
            continue
        kb.vcopy(accb[:, :], xbuf[:, :])
        kb.barrier()
        fm_ln(xs, xs)
        kb.barrier()
        attn_block(b, 1)
        kb.barrier()
        fm_ln(accs, xs)
        kb.barrier()
        if SI == 3:
            kb.store(dbg[b * 128:(b + 1) * 128, :], xbuf[:, :])
            continue
        l1_tail(b)
        kb.barrier()
    if SI >= 4:
        routing()
        kb.barrier()
    if SI >= 5:
        moe()
        kb.barrier()
        final()
    kb.barrier()
    kb.emit()
    return nc


def _consts():
    f32 = np.float32
    c = {}
    c["ones"] = np.ones((128, 128), f32)
    l = np.zeros((128, 128), f32)
    for k in range(128):
        l[k, k + 1:] = 1.0
    c["lmask"] = l
    c["ident"] = np.eye(128, dtype=f32)
    top = 2595.0 * np.log10(21.0)
    mel = np.power(10.0, np.linspace(0.0, top, 32) / 2595.0) - 1.0
    fb64 = 200.0 * mel / 1000.0
    fbhi32 = np.array(FBJ_BITS, dtype=np.uint32).view(np.float32)
    lo = fb64 - fbhi32.astype(np.float64)
    lim = 0.4999 * np.spacing(np.abs(fbhi32)).astype(np.float64)
    lo = np.clip(lo, -lim, lim)
    idx = (np.arange(128) % 64) // 2
    c["fbhi"] = fbhi32[idx].astype(np.float32).reshape(128, 1)
    c["fblo"] = lo[idx].astype(np.float32).reshape(128, 1)
    c["signrow"] = np.array([[-1.0 if d % 2 == 0 else 1.0] for d in range(128)], f32)
    c["irow32"] = np.arange(32, dtype=f32).reshape(1, 32)
    c["c4096col"] = (np.arange(8, dtype=f32) * 4096.0).reshape(8, 1)
    lb = np.zeros((128, 128), f32)
    for m in range(128):
        lb[: 16 * (m // 16), m] = 1.0
    c["lb"] = lb
    bs = np.zeros((128, 8), f32)
    for p in range(128):
        bs[p, p // 16] = 1.0
    c["bs"] = bs
    e16 = np.zeros((8, 128), f32)
    for p in range(128):
        e16[p // 16, p] = 1.0
    c["exp16"] = e16
    return c


def _colmajor(v, k):
    return np.ascontiguousarray(np.asarray(v, np.float32).reshape(k, 128).T)


def _prep(inputs):
    f32 = np.float32
    A = lambda n: np.asarray(inputs[n], f32)
    x = A("L0_attn_x")
    consts = _consts()
    shared = dict(consts)
    shared.update(
        wqkv=A("L0_attn_w_qkv"), wo0=A("L0_attn_w_o"),
        cnw1=A("L0_ffn_cn_w1"), cnw2=A("L0_ffn_cn_w2"), cnw3=A("L0_ffn_cn_w3"),
        cnw4=A("L0_ffn_cn_w4"),
        cnb1=_colmajor(inputs["L0_ffn_cn_b1"], 8), cnb2=_colmajor(inputs["L0_ffn_cn_b2"], 4),
        cnb3=_colmajor(inputs["L0_ffn_cn_b3"], 2),
        cnb4=A("L0_ffn_cn_b4").reshape(1, 1),
        gcw1=A("L0_ffn_gc_w1"), gcw2=A("L0_ffn_gc_w2"), gcw3=A("L0_ffn_gc_w3"),
        gcb1=_colmajor(inputs["L0_ffn_gc_b1"], 4), gcb2=_colmajor(inputs["L0_ffn_gc_b2"], 2),
        gcb3=A("L0_ffn_gc_b3").reshape(1, 1),
        ar=A("L0_ffn_adaptive_range").reshape(1, 1),
        lnw1=_colmajor(inputs["L1_attn_ln_w"], 8), lnb1=_colmajor(inputs["L1_attn_ln_b"], 8),
        wq1=A("L1_attn_w_q"), wkv1=A("L1_attn_w_kv"), wo1=A("L1_attn_w_o"),
        theta=A("L1_attn_theta").reshape(1, 1),
        centT=np.ascontiguousarray(A("L1_ffn_centroids").T),
        lng=_colmajor(inputs["L1_ffn_ln_g"], 8), lnb2=_colmajor(inputs["L1_ffn_ln_b"], 8),
    )
    w1 = A("L1_ffn_w1")
    b1 = np.asarray(inputs["L1_ffn_b1"], f32)
    w2 = A("L1_ffn_w2")
    b2 = np.asarray(inputs["L1_ffn_b2"], f32)
    in_maps = []
    for c in range(NC8):
        m = dict(shared)
        m["x0"] = np.ascontiguousarray(x[2 * c].T)
        m["x1"] = np.ascontiguousarray(x[2 * c + 1].T)
        m["mw1"] = np.ascontiguousarray(w1[c])
        m["mb1"] = _colmajor(b1[c], 32)
        m["mw2"] = np.ascontiguousarray(w2[c])
        m["mb2"] = _colmajor(b2[c], 8)
        es = np.zeros((8, 1), f32)
        es[c, 0] = 1.0
        m["esel"] = es
        e128 = np.zeros((128, 16), f32)
        for pr in range(16):
            e128[16 * c + pr, pr] = 1.0
        m["esel128"] = e128
        in_maps.append(m)
    return in_maps


LAST_RES = [None]


def kernel(**inputs):
    in_maps = _prep(inputs)
    nc = build(_STAGE)
    res = run_bass_kernel_spmd(nc, in_maps, core_ids=list(range(NC8)))
    LAST_RES[0] = res
    outs = [np.asarray(res.results[c]["out"]) for c in range(NC8)]
    return np.concatenate(outs, axis=0).reshape(16, T, D)


# version 03
# speedup vs baseline: 1.0935x; 1.0796x over previous
import math
import numpy as np
import concourse.bass as bass
import concourse.mybir as mybir
from concourse.bass_utils import run_bass_kernel_spmd

F32 = mybir.dt.float32
U32 = mybir.dt.uint32
AF = mybir.ActivationFunctionType
Alu = mybir.AluOpType
AX = mybir.AxisListType.X

NC8 = 8
T = 2048
D = 1024
HD = 64
NB = 2
NTOK = NB * T
CEXP = 4096
EPS = 1e-5

CW1 = 6.25
CW2 = 0.033203125
CW3 = float(np.float32(2 * math.pi - CW1 - CW2))
INV2PI = float(np.float32(1.0 / (2 * math.pi)))
PIO2 = float(np.float32(math.pi / 2))

_STAGE = "full"
_RSTOP = 9
_TSTOP = 9
_MSTOP = 9

FBJ_BITS = [0x0, 0x3ca9130c, 0x3d31cc5a, 0x3d8c577a, 0x3dc517c7, 0x3e01d9a8,
            0x3e246269, 0x3e4a7b7b, 0x3e748306, 0x3e917073, 0x3eab03d5,
            0x3ec73ae1, 0x3ee65b4f, 0x3f045905, 0x3f1749f5, 0x3f2c2f4b,
            0x3f433ca5, 0x3f5caafa, 0x3f78b92b, 0x3f8bd641, 0x3f9ce8bc,
            0x3fafbe41, 0x3fc48550, 0x3fdb7148, 0x3ff4bac7, 0x40085028,
            0x4017b369, 0x4028ad2a, 0x403b6764, 0x40501060, 0x4066db22,
            0x40800002]


class KB:
    ENGS = ("sync", "scalar", "vector", "tensor", "gpsimd")

    def __init__(self, nc):
        self.nc = nc
        self.q = {e: [] for e in self.ENGS}
        self.sems = {}
        self.cnt = {}
        for s in ("LD", "SD", "TT", "TA", "VV", "SS", "GG", "CC"):
            self.sems[s] = nc.semaphore(s).__enter__()
            self.cnt[s] = 0

    def _rec(self, eng, fn, inc, n):
        if inc is not None:
            self.cnt[inc] += n
        self.q[eng].append((fn, inc, n))

    def wait(self, eng, sem, val=None):
        v = self.cnt[sem] if val is None else val
        h = self.sems[sem]
        self.q[eng].append((lambda e, h=h, v=v: e.wait_ge(h, v), None, 0))

    def wait_all(self, eng):
        for s in ("LD", "SD", "TT", "TA", "VV", "SS", "GG", "CC"):
            if self.cnt[s] > 0:
                self.wait(eng, s)

    def barrier(self):
        for e in self.ENGS:
            self.wait_all(e)

    def load(self, out, in_, slow=False):
        if slow:
            self._rec("sync", lambda e, o=out, i=in_: e.dma_start(out=o, in_=i, allow_slow_non_contiguous=True), "LD", 16)
        else:
            self._rec("sync", lambda e, o=out, i=in_: e.dma_start(out=o, in_=i), "LD", 16)

    def store(self, out, in_):
        self._rec("gpsimd", lambda e, o=out, i=in_: e.dma_start(out=o, in_=i), "SD", 16)

    def ind_gather(self, out, dram_ap, idx_ap):
        self._rec("gpsimd", lambda e, o=out, d=dram_ap, ix=idx_ap: e.indirect_dma_start(
            out=o, out_offset=None, in_=d,
            in_offset=bass.IndirectOffsetOnAxis(ap=ix, axis=0)), "SD", 16)

    def ind_scatter(self, dram_ap, in_, idx_ap):
        self._rec("gpsimd", lambda e, d=dram_ap, i=in_, ix=idx_ap: e.indirect_dma_start(
            out=d, out_offset=bass.IndirectOffsetOnAxis(ap=ix, axis=0),
            in_=i, in_offset=None), "SD", 16)

    def cc(self, kind, in_ap, out_ap):
        self.wait("gpsimd", "SD")
        self._rec("gpsimd", lambda e, k=kind, i=in_ap, o=out_ap: e.collective_compute(
            k, Alu.bypass, replica_groups=[list(range(NC8))],
            ins=[i.opt()], outs=[o.opt()]), "CC", 1)

    def mm(self, out, lhsT, rhs, start=True, stop=True, tp=None, mark=False, sem=None):
        s_ = sem if sem is not None else ("TT" if mark else None)
        if tp is None:
            self._rec("tensor", lambda e, o=out, l=lhsT, r=rhs, s1=start, s2=stop:
                      e.matmul(o, l, r, start=s1, stop=s2), s_, 1)
        else:
            self._rec("tensor", lambda e, o=out, l=lhsT, r=rhs, s1=start, s2=stop, t=tp:
                      e.matmul(o, l, r, start=s1, stop=s2, tile_position=t), s_, 1)

    def transp(self, out, in_, ident, mark=False):
        self._rec("tensor", lambda e, o=out, i=in_, d=ident: e.transpose(o, i, d),
                  "TT" if mark else None, 1)

    def tmark(self):
        for j in range(len(self.q["tensor"]) - 1, -1, -1):
            fn, inc, n = self.q["tensor"][j]
            if n == 0:
                continue
            assert inc is None
            self.cnt["TT"] += 1
            self.q["tensor"][j] = (fn, "TT", 1)
            return

    def act(self, out, in_, func, bias=0.0, scale=1.0, accum_out=None):
        self._rec("scalar", lambda e, o=out, i=in_, f=func, b=bias, s=scale, a=accum_out:
                  e.activation(out=o, in_=i, func=f, bias=b, scale=s, accum_out=a), "SS", 1)

    def scopy(self, out, in_):
        self._rec("scalar", lambda e, o=out, i=in_: e.copy(out=o, in_=i), "SS", 1)

    def _v(self, fn):
        self._rec("vector", fn, "VV", 1)

    def _md(self, out):
        try:
            n = 1
            for d in out.shape[1:]:
                n *= d
        except Exception:
            n = 999
        if n <= 64:
            self._v(lambda e: e.drain())

    def tt(self, out, a, b, op):
        self._v(lambda e, o=out, x=a, y=b, p=op: e.tensor_tensor(o, x, y, p))
        self._md(out)

    def ts(self, out, in0, s1, s2, op0, op1=None, accum_out=None, safe=False):
        if not safe and (not isinstance(s1, (int, float)) or
                         (s2 is not None and not isinstance(s2, (int, float)))):
            self._v(lambda e: e.drain())
        if op1 is None:
            self._v(lambda e, o=out, i=in0, a=s1, p0=op0: e.tensor_scalar(o, i, a, None, op0=p0))
        else:
            self._v(lambda e, o=out, i=in0, a=s1, b=s2, p0=op0, p1=op1, ac=accum_out:
                    e.tensor_scalar(o, i, a, b, op0=p0, op1=p1, accum_out=ac))
        if accum_out is not None:
            self._v(lambda e: e.drain())
        else:
            self._md(out)

    def stt(self, out, in0, scalar, in1, op0, op1, accum_out=None, safe=False):
        if not safe and not isinstance(scalar, (int, float)):
            self._v(lambda e: e.drain())
        self._v(lambda e, o=out, i=in0, s=scalar, j=in1, p0=op0, p1=op1, ac=accum_out:
                e.scalar_tensor_tensor(o, i, s, j, op0=p0, op1=p1, accum_out=ac))
        if accum_out is not None:
            self._v(lambda e: e.drain())
        else:
            self._md(out)

    def red(self, out, in_, op):
        self._v(lambda e, o=out, i=in_, p=op: e.tensor_reduce(o, i, axis=AX, op=p))
        self._v(lambda e: e.drain())

    def vcopy(self, out, in_):
        self._v(lambda e, o=out, i=in_: e.tensor_copy(out=o, in_=i))
        self._md(out)

    def memset(self, ap, c):
        self._v(lambda e, a=ap, v=c: e.memset(a, v))
        self._md(ap)

    def recip(self, out, in_):
        self._v(lambda e, o=out, i=in_: e.reciprocal(out=o, in_=i))
        self._v(lambda e: e.drain())

    def scan(self, out, d0, d1, init, op0, op1):
        self._v(lambda e, o=out, a=d0, b=d1, i=init, p0=op0, p1=op1:
                e.tensor_tensor_scan(o, a, b, i, op0=p0, op1=p1))
        self._v(lambda e: e.drain())

    def shuf(self, out, in_, mask):
        self._v(lambda e, o=out, i=in_, m=mask: e.stream_shuffle(o, i, m))

    def _g(self, fn):
        self._rec("gpsimd", fn, "GG", 1)

    def iota(self, out, pattern, base=0, chm=0):
        self._g(lambda e, o=out, p=pattern, b=base, c=chm: e.iota(
            o, pattern=p, base=b, channel_multiplier=c, allow_small_or_imprecise_dtypes=True))

    def affine(self, out, in_, pattern, cmp, fill, base, chm):
        self._g(lambda e, o=out, i=in_, p=pattern, c=cmp, f=fill, b=base, m=chm:
                e.affine_select(o, i, pattern=p, compare_op=c, fill=f, base=b, channel_multiplier=m))

    def gtt(self, out, a, b, op):
        self._g(lambda e, o=out, x=a, y=b, p=op: e.tensor_tensor(o, x, y, p))

    def emit(self):
        blk = self.nc.Block().__enter__()
        for eng in self.ENGS:
            items = self.q[eng]
            sems = self.sems

            def body(e, items=items, sems=sems):
                for fn, inc, n in items:
                    ins = fn(e)
                    if inc is not None:
                        ins.then_inc(sems[inc], n)

            getattr(blk, eng)(body)
        blk.__exit__(None, None, None)


def stage_idx(s):
    return {"x1": 1, "x2": 2, "x3": 3, "route": 4, "m1": 5, "full": 5}[s]


def build(stage):
    SI = stage_idx(stage)
    nc = bass.Bass("TRN2", target_bir_lowering=False)
    di = {}

    def IN(name, shape, dt=F32):
        di[name] = nc.dram_tensor(name, list(shape), dt, kind="ExternalInput").ap()
        return di[name]

    IN("x0", [D, T]); IN("x1", [D, T])
    IN("wqkv", [D, 3 * D]); IN("wo0", [D, D])
    IN("cnw1", [D, D]); IN("cnw2", [D, 512]); IN("cnw3", [512, 256]); IN("cnw4", [256, 1])
    IN("cnb1", [128, 8]); IN("cnb2", [128, 4]); IN("cnb3", [128, 2]); IN("cnb4", [1, 1])
    IN("gcw1", [D, 512]); IN("gcw2", [512, 256]); IN("gcw3", [256, 1])
    IN("gcb1", [128, 4]); IN("gcb2", [128, 2]); IN("gcb3", [1, 1]); IN("ar", [1, 1])
    IN("lnw1", [128, 8]); IN("lnb1", [128, 8])
    IN("wq1", [D, D]); IN("wkv1", [D, 2 * D]); IN("wo1", [D, D]); IN("theta", [1, 1])
    IN("centT", [D, 8]); IN("lng", [128, 8]); IN("lnb2", [128, 8])
    IN("mw1", [D, 4 * D]); IN("mb1", [128, 32]); IN("mw2", [4 * D, D]); IN("mb2", [128, 8])
    IN("esel", [8, 1])
    IN("ones", [128, 128]); IN("lmask", [128, 128]); IN("ident", [128, 128])
    IN("fbhi", [128, 1]); IN("fblo", [128, 1]); IN("signrow", [128, 1]); IN("irow32", [1, 32])
    IN("c4096col", [8, 1]); IN("lb", [128, 128]); IN("bs", [128, 8])
    IN("exp16", [8, 128]); IN("esel128", [128, 16])

    out = nc.dram_tensor("out", [NTOK, D], F32, kind="ExternalOutput").ap()
    dbg = None
    if SI < 5:
        dbg = nc.dram_tensor("dbg", [NB * 128, 8 * T], F32, kind="ExternalOutput").ap()
    dbg3 = None
    if stage == "x2":
        dbg3 = nc.dram_tensor("dbg3", [2 * 128, 1024], F32, kind="ExternalOutput").ap()
    dbg4 = None
    dbg5 = None
    if stage == "x3":
        dbg4 = nc.dram_tensor("dbg4", [128, 4096], F32, kind="ExternalOutput").ap()
        dbg5 = nc.dram_tensor("dbg5", [128, 2048], F32, kind="ExternalOutput").ap()
    dbg6 = None
    dbg7 = None
    if stage == "m1":
        dbg6 = nc.dram_tensor("dbg6", [512, 1024], F32, kind="ExternalOutput").ap()
        dbg7 = nc.dram_tensor("dbg7", [128, 96], U32, kind="ExternalOutput").ap()
    dbg2 = None
    if stage == "route":
        dbg2 = nc.dram_tensor("dbg2", [6 * 128, 256], F32, kind="ExternalOutput").ap()

    wq1p = nc.dram_tensor("wq1p", [D, D], F32).ap()
    wkv1p = nc.dram_tensor("wkv1p", [D, 2 * D], F32).ap()
    park1 = nc.dram_tensor("park1", [128, 8, T], F32).ap()
    feat_tm = nc.dram_tensor("feat_tm", [NTOK, D], F32).ap()
    lnf_tm = nc.dram_tensor("lnf_tm", [NTOK, D], F32).ap()
    agin = nc.dram_tensor("agin", [CEXP, D], F32)
    agout = nc.dram_tensor("agout", [NC8 * CEXP, D], F32)
    scr_blk = nc.dram_tensor("scr_blk", [8, NTOK], F32)
    scr_g = nc.dram_tensor("scr_g", [NC8, 8, NTOK], F32)
    h_loc = nc.dram_tensor("h_loc", [CEXP, D], F32)
    h_g = nc.dram_tensor("h_g", [NC8 * CEXP, D], F32)

    xbuf = nc.sbuf_tensor("xbuf", [128, 16384], F32).__enter__()
    accb = nc.sbuf_tensor("accb", [128, 16384], F32).__enter__()
    qkb = nc.sbuf_tensor("qkb", [128, 4096], F32).__enter__()
    vtb = nc.sbuf_tensor("vtb", [128, 2080], F32).__enter__()
    fscr = nc.sbuf_tensor("fscr", [128, 4096], F32).__enter__()
    gscr = nc.sbuf_tensor("gscr", [128, 4096], F32).__enter__()
    wbuf = nc.sbuf_tensor("wbuf", [128, 4096], F32).__enter__()
    small = nc.sbuf_tensor("small", [128, 1024], F32).__enter__()
    u32b = nc.sbuf_tensor("u32b", [128, 96], U32).__enter__()
    cones = nc.sbuf_tensor("cones", [128, 128], F32).__enter__()
    clmask = nc.sbuf_tensor("clmask", [128, 128], F32).__enter__()
    cident = nc.sbuf_tensor("cident", [128, 128], F32).__enter__()
    cmisc = nc.sbuf_tensor("cmisc", [128, 48], F32).__enter__()
    clb = nc.sbuf_tensor("clb", [128, 128], F32).__enter__()
    cexp16 = nc.sbuf_tensor("cexp16", [8, 128], F32).__enter__()
    cesel128 = nc.sbuf_tensor("cesel128", [128, 16], F32).__enter__()
    ccent = nc.sbuf_tensor("ccent", [128, 64], F32).__enter__()
    cesel = nc.sbuf_tensor("cesel", [8, 1], F32).__enter__()

    ps = [nc.psum_tensor("psb%d" % i, [128, 512], F32).__enter__() for i in range(8)]

    kb = KB(nc)

    S_CNB1, S_CNB2, S_CNB3, S_GCB1, S_GCB2 = 0, 8, 12, 14, 18
    S_CELL = 20
    S_LNW1, S_LNB1, S_LNG, S_LNB2 = 40, 48, 56, 64
    S_BCQ, S_BCK, S_BCV = 72, 80, 88
    S_BVROW = 96
    S_MB1, S_MB2 = 224, 256
    S_P1 = 264
    S_ROWS = 296

    def cell(i):
        return small[0:1, S_CELL + i:S_CELL + i + 1]

    def p1(i):
        return small[:, S_P1 + i:S_P1 + i + 1]

    def srow(i, w=512):
        return small[0:1, S_ROWS + i * 512:S_ROWS + i * 512 + w]

    def xs(kc, lo, hi):
        return xbuf[:, kc * T + lo: kc * T + hi]

    def accs(kc, lo, hi):
        return accb[:, kc * T + lo: kc * T + hi]

    kb.load(cones[:, :], di["ones"])
    kb.load(clmask[:, :], di["lmask"])
    kb.load(cident[:, :], di["ident"])
    kb.load(cmisc[:, 44:45], di["fbhi"])
    kb.load(cmisc[:, 45:46], di["fblo"])
    kb.load(cmisc[:, 0:1], di["signrow"])
    kb.load(clb[:, :], di["lb"])
    kb.load(cmisc[0:8, 46:47], di["c4096col"])
    kb.load(cmisc[:, 1:9], di["bs"])
    kb.load(cexp16[:, :], di["exp16"])
    kb.load(cesel128[:, :], di["esel128"])
    kb.load(cmisc[0:1, 9:41], di["irow32"])
    kb.load(small[:, S_CNB1:S_CNB1 + 8], di["cnb1"])
    kb.load(small[:, S_CNB2:S_CNB2 + 4], di["cnb2"])
    kb.load(small[:, S_CNB3:S_CNB3 + 2], di["cnb3"])
    kb.load(small[:, S_GCB1:S_GCB1 + 4], di["gcb1"])
    kb.load(small[:, S_GCB2:S_GCB2 + 2], di["gcb2"])
    kb.load(cell(0), di["cnb4"])
    kb.load(cell(1), di["gcb3"])
    kb.load(cell(2), di["ar"])
    kb.load(cell(3), di["theta"])
    kb.load(small[:, S_LNW1:S_LNW1 + 8], di["lnw1"])
    kb.load(small[:, S_LNB1:S_LNB1 + 8], di["lnb1"])
    kb.load(small[:, S_LNG:S_LNG + 8], di["lng"])
    kb.load(small[:, S_LNB2:S_LNB2 + 8], di["lnb2"])
    kb.load(small[:, S_MB1:S_MB1 + 32], di["mb1"])
    kb.load(small[:, S_MB2:S_MB2 + 8], di["mb2"])
    kb.load(cesel[:, :], di["esel"])

    kb.load(ccent[:, 0:64].rearrange("p (k n) -> p k n", k=8), di["centT"].rearrange("(k p) n -> p k n", p=128))
    kb.barrier()

    first = [True]
    oc_list = [("q", oc) for oc in range(8)] + [("kv", oc) for oc in range(16)]
    for idx, (which, oc) in enumerate(oc_list):
        wsrc = di["wq1"] if which == "q" else di["wkv1"]
        wdst = wq1p if which == "q" else wkv1p
        half = idx % 2
        wb = wbuf[:, half * 2048: half * 2048 + 1024]
        src = wsrc.rearrange("(k p) n -> p k n", p=128)[:, :, oc * 128:(oc + 1) * 128]
        kb.load(wb.rearrange("p (k n) -> p k n", k=8), src)
        kb.wait("tensor", "LD")
        col = oc if which == "q" else 8 + oc
        for kc in range(8):
            kb.mm(ps[0][:, col:col + 1], wb[:, kc * 128:(kc + 1) * 128],
                  small[:, S_LNB1 + kc:S_LNB1 + kc + 1],
                  start=first[0], stop=(idx == len(oc_list) - 1 and kc == 7))
            first[0] = False
        kb.tmark()
        kb.wait("vector", "TT")
        for kc in range(8):
            kb.ts(wb[:, kc * 128:(kc + 1) * 128], wb[:, kc * 128:(kc + 1) * 128],
                  small[:, S_LNW1 + kc:S_LNW1 + kc + 1], None, Alu.mult)
        kb.wait("gpsimd", "VV")
        kb.store(wdst.rearrange("(k p) n -> p k n", p=128)[:, :, oc * 128:(oc + 1) * 128],
                 wb.rearrange("p (k n) -> p k n", k=8))
        kb.wait("sync", "SD")
    kb.wait("scalar", "TT")
    kb.scopy(small[:, S_BCQ:S_BCQ + 24], ps[0][:, 0:24])
    kb.barrier()

    kb.memset(cmisc[:, 42:43], 1.0)
    kb.memset(cmisc[:, 43:44], PIO2)
    RA = small[0:1, 808:816]
    RB = small[0:1, 816:824]
    kb.memset(RA[0:1, 0:1], 220.0)
    kb.recip(RB, RA)
    kb.tt(cell(8), cell(3), RB[0:1, 0:1], Alu.mult)
    kb.tt(p1(15), cmisc[:, 44:45], cmisc[:, 45:46], Alu.add)
    kb.wait("tensor", "VV")
    kb.mm(ps[3][:, 0:1], cones[0:1, :], cell(8), mark=True)
    kb.wait("vector", "TT")
    kb.tt(p1(1), p1(15), ps[3][:, 0:1], Alu.mult)
    kb.iota(fscr[:, 0:2048], [[1, 2048]], 0, 0)
    kb.wait("vector", "GG")
    kb.ts(qkb[:, 2048:4096], fscr[:, 0:2048], p1(1), None, Alu.mult)
    kb.ts(fscr[:, 2048:4096], qkb[:, 2048:4096], INV2PI, 0.5, Alu.mult, Alu.add)
    kb.vcopy(qkb[:, 0:2048], fscr[:, 2048:4096])
    for sh in range(16, -1, -1):
        c = float(2 ** sh)
        kb.ts(fscr[:, 0:2048], fscr[:, 2048:4096], c, None, Alu.is_ge)
        kb.stt(fscr[:, 2048:4096], fscr[:, 0:2048], -c, fscr[:, 2048:4096], Alu.mult, Alu.add)
    kb.tt(fscr[:, 0:2048], qkb[:, 0:2048], fscr[:, 2048:4096], Alu.subtract)
    kb.stt(fscr[:, 2048:4096], fscr[:, 0:2048], -CW1, qkb[:, 2048:4096], Alu.mult, Alu.add)
    kb.stt(fscr[:, 2048:4096], fscr[:, 0:2048], -CW2, fscr[:, 2048:4096], Alu.mult, Alu.add)
    kb.stt(fscr[:, 2048:4096], fscr[:, 0:2048], -CW3, fscr[:, 2048:4096], Alu.mult, Alu.add)
    kb.wait("scalar", "VV")
    kb.act(gscr[:, 2048:4096], fscr[:, 2048:4096], AF.Sin)
    kb.act(fscr[:, 0:2048], fscr[:, 2048:4096], AF.Abs)
    kb.act(gscr[:, 0:2048], fscr[:, 0:2048], AF.Sin, bias=cmisc[:, 43:44], scale=-1.0)
    kb.wait("vector", "SS")
    kb.ts(gscr[:, 2048:4096], gscr[:, 2048:4096], cmisc[:, 0:1], None, Alu.mult)
    kb.barrier()
    if dbg4 is not None:
        kb.store(dbg4, gscr[:, :])
        kb.wait("sync", "SD")
    kb.barrier()

    def wrow(i, w=512):
        return wbuf[0:1, i * 512:i * 512 + w]

    def fm_ln(src, dst, gamma=None, beta=None):
        for tc in range(4):
            lo, hi = tc * 512, tc * 512 + 512
            kb.wait("tensor", "VV")
            for kc in range(8):
                sq = fscr[:, (kc % 2) * 512:(kc % 2) * 512 + 512]
                if kc >= 2:
                    kb.wait("scalar", "TT")
                kb.act(sq, src(kc, lo, hi), AF.Square)
                kb.wait("tensor", "SS")
                kb.mm(ps[4][0:1, :], cones[:, 0:1], src(kc, lo, hi),
                      start=(kc == 0), stop=(kc == 7))
                kb.mm(ps[5][0:1, :], cones[:, 0:1], sq,
                      start=(kc == 0), stop=(kc == 7), mark=True)
            kb.wait("vector", "TT")
            kb.ts(wrow(0), ps[4][0:1, :], 1.0 / 1024, None, Alu.mult)
            kb.ts(wrow(1), ps[5][0:1, :], 1.0 / 1024, None, Alu.mult)
            kb.tt(wrow(2), wrow(0), wrow(0), Alu.mult)
            kb.tt(wrow(1), wrow(1), wrow(2), Alu.subtract)
            kb.ts(wrow(1), wrow(1), EPS, None, Alu.add)
            kb.wait("scalar", "VV")
            kb.act(wrow(2), wrow(1), AF.Ln)
            kb.act(wrow(2), wrow(2), AF.Exp, scale=-0.5)
            kb.wait("vector", "SS")
            kb.tt(wrow(3), wrow(2), wrow(2), Alu.mult)
            kb.tt(wrow(3), wrow(3), wrow(1), Alu.mult)
            kb.ts(wrow(3), wrow(3), -0.5, 1.5, Alu.mult, Alu.add)
            kb.tt(wrow(2), wrow(2), wrow(3), Alu.mult)
            kb.tt(wrow(0), wrow(0), wrow(2), Alu.mult)
            kb.wait("tensor", "VV")
            kb.mm(ps[6][:, :], cones[0:1, :], wrow(2), mark=True)
            kb.mm(ps[7][:, :], cones[0:1, :], wrow(0), mark=True)
            kb.wait("vector", "TT")
            for kc in range(8):
                tmp = fscr[:, 1024:1536]
                kb.tt(tmp, src(kc, lo, hi), ps[6][:, :], Alu.mult)
                kb.tt(dst(kc, lo, hi), tmp, ps[7][:, :], Alu.subtract)
                if gamma is not None:
                    kb.ts(dst(kc, lo, hi), dst(kc, lo, hi),
                          small[:, gamma + kc:gamma + kc + 1],
                          small[:, beta + kc:beta + kc + 1], Alu.mult, Alu.add)

    def attn_block(b, layer):
        attn_block.b = b
        wom = [0, 0]
        for hp in range(8):
            kb.wait("sync", "TT")
            kb.wait("sync", "VV")
            if layer == 0:
                src4 = di["wqkv"].rearrange("(k p) n -> p k n", p=128)
                for j, base in enumerate((2 * hp * 192, 2 * hp * 192 + 64, 2 * hp * 192 + 128)):
                    dst4 = wbuf[:, j * 1024:(j + 1) * 1024].rearrange("p (k h c) -> p k h c", k=8, h=2)
                    for hh in range(2):
                        kb.load(dst4[:, :, hh, :], src4[:, :, base + hh * 192:base + hh * 192 + 64])
                kb.load(wbuf[:, 3072:4096], di["wo0"][hp * 128:(hp + 1) * 128, :])
            else:
                kb.load(wbuf[:, 0:1024].rearrange("p (k n) -> p k n", k=8),
                        wq1p.rearrange("(k p) n -> p k n", p=128)[:, :, hp * 128:(hp + 1) * 128])
                kb.load(wbuf[:, 1024:2048].rearrange("p (k n) -> p k n", k=8),
                        wkv1p.rearrange("(k p) n -> p k n", p=128)[:, :, hp * 128:(hp + 1) * 128])
                kb.load(wbuf[:, 2048:3072].rearrange("p (k n) -> p k n", k=8),
                        wkv1p.rearrange("(k p) n -> p k n", p=128)[:, :, 1024 + hp * 128:1024 + hp * 128 + 128])
                kb.load(wbuf[:, 3072:4096], di["wo1"][hp * 128:(hp + 1) * 128, :])
            kb.wait("tensor", "LD")
            if layer == 1:
                kb.store(small[0:1, S_BVROW:S_BVROW + 128], small[:, S_BCV + hp:S_BCV + hp + 1])
                kb.wait("tensor", "SD")
            for tc in range(4):
                lo, hi = tc * 512, tc * 512 + 512
                if tc > 0:
                    kb.wait("tensor", "SS")
                for kc in range(8):
                    kb.mm(ps[0][:, :], wbuf[:, kc * 128:kc * 128 + 128], xs(kc, lo, hi),
                          start=(kc == 0), stop=(kc == 7))
                for kc in range(8):
                    kb.mm(ps[1][:, :], wbuf[:, 1024 + kc * 128:1024 + kc * 128 + 128], xs(kc, lo, hi),
                          start=(kc == 0), stop=(kc == 7), mark=True)
                kb.wait("scalar", "TT")
                if layer == 0:
                    kb.scopy(qkb[:, lo:hi], ps[0][:, :])
                    kb.scopy(qkb[:, 2048 + lo:2048 + hi], ps[1][:, :])
                else:
                    kb.act(qkb[:, lo:hi], ps[0][:, :], AF.Identity,
                           bias=small[:, S_BCQ + hp:S_BCQ + hp + 1])
                    kb.act(qkb[:, 2048 + lo:2048 + hi], ps[1][:, :], AF.Identity,
                           bias=small[:, S_BCK + hp:S_BCK + hp + 1])
            kb.memset(vtb[:, :], 1.0)
            kb.wait("tensor", "VV")
            kb.wait("scalar", "VV")
            vt3 = vtb.rearrange("p (t c) -> p t c", c=130)
            for tkc in range(16):
                bank = ps[2 + (tkc % 2)]
                if tkc >= 2:
                    kb.wait("tensor", "SS")
                for kc in range(8):
                    kb.mm(bank[:, 0:128], xbuf[:, kc * T + tkc * 128:kc * T + tkc * 128 + 128],
                          wbuf[:, 2048 + kc * 128:2048 + kc * 128 + 128],
                          start=(kc == 0), stop=(kc == 7 and layer == 0))
                if layer == 1:
                    kb.mm(bank[:, 0:128], cones[0:1, 0:128],
                          small[0:1, S_BVROW:S_BVROW + 128], start=False, stop=True)
                kb.tmark()
                kb.wait("scalar", "TT")
                kb.scopy(vt3[:, tkc, 0:64], bank[:, 0:64])
                kb.scopy(vt3[:, tkc, 65:129], bank[:, 64:128])
            if layer == 1:
                kb.wait("vector", "SS")
                for base in (0, 2048):
                    q = qkb[:, base:base + 2048]
                    kb.shuf(fscr[:, 0:2048], q, [i ^ 1 for i in range(32)])
                    kb.tt(q, q, gscr[:, 0:2048], Alu.mult)
                    kb.tt(fscr[:, 0:2048], fscr[:, 0:2048], gscr[:, 2048:4096], Alu.mult)
                    kb.tt(q, q, fscr[:, 0:2048], Alu.add)
                kb.wait("tensor", "VV")
            if layer == 1:
                kb.wait("tensor", "SS")
                for h, bank, dst in ((0, ps[6], 10), (1, ps[7], 11)):
                    for tkc in range(16):
                        kb.mm(bank[0:65, 0:1], vt3[:, tkc, h * 65:h * 65 + 65], cones[:, 0:1],
                              start=(tkc == 0), stop=(tkc == 15))
                    kb.tmark()
                kb.wait("scalar", "TT")
                kb.scopy(p1(10)[0:65, :], ps[6][0:65, 0:1])
                kb.scopy(p1(11)[0:65, :], ps[7][0:65, 0:1])
            scale = (1.0 / 32.0) if layer == 0 else 0.125
            for qt in range(4):
                kmax = 4 * qt + 4 if layer == 0 else 16
                kb.wait("tensor", "VV")
                kb.wait("tensor", "TA")
                sevac = []
                avm = []
                for kc in range(kmax):
                    sb0 = ps[0] if kc % 2 == 0 else ps[2]
                    sb1 = ps[1] if kc % 2 == 0 else ps[3]
                    if kc >= 2:
                        kb.wait("tensor", "SS", sevac[kc - 2])
                    kb.mm(sb0[:, :], qkb[0:64, 2048 + kc * 128:2048 + kc * 128 + 128],
                          qkb[0:64, qt * 512:qt * 512 + 512], tp=(0, 0))
                    kb.tmark()
                    kb.mm(sb1[:, :], qkb[64:128, 2048 + kc * 128:2048 + kc * 128 + 128],
                          qkb[64:128, qt * 512:qt * 512 + 512], tp=(64, 0))
                    kb.tmark()
                    if kc >= 2:
                        kb.wait("scalar", "TA", avm[kc - 2])
                    kb.wait("scalar", "TT")
                    p0 = fscr[:, (kc % 2) * 512:(kc % 2) * 512 + 512]
                    p1_ = fscr[:, 1024 + (kc % 2) * 512:1024 + (kc % 2) * 512 + 512]
                    if layer == 0:
                        kb.act(p0, sb0[:, :], AF.Exp, scale=scale)
                        kb.act(p1_, sb1[:, :], AF.Exp, scale=scale)
                        if kc >= 4 * qt:
                            kb.wait("gpsimd", "SS")
                            kb.affine(p0, p0, [[1, 512]], Alu.is_ge, 0.0,
                                      qt * 512 - kc * 128, -1)
                            kb.affine(p1_, p1_, [[1, 512]], Alu.is_ge, 0.0,
                                      qt * 512 - kc * 128, -1)
                            kb.wait("tensor", "GG")
                    else:
                        kb.act(p0, sb0[:, :], AF.Square, scale=scale, bias=cmisc[:, 42:43])
                        kb.act(p1_, sb1[:, :], AF.Square, scale=scale, bias=cmisc[:, 42:43])
                    kb.wait("tensor", "SS")
                    kb.mm(ps[4][0:65, :], vt3[:, kc, 0:65], p0,
                          start=(kc == 0), stop=(kc == kmax - 1))
                    kb.mm(ps[5][0:65, :], vt3[:, kc, 65:130], p1_,
                          start=(kc == 0), stop=(kc == kmax - 1), sem="TA")
                    avm.append(kb.cnt["TA"])
                    sevac.append(kb.cnt["SS"])
                kb.wait("vector", "TT")
                kb.wait("vector", "TA")
                for h, ob in ((0, ps[4]), (1, ps[5])):
                    rr = wrow(h)
                    if layer == 0:
                        kb.recip(rr, ob[64:65, :])
                    else:
                        kb.ts(rr, ob[64:65, :], 2048.0, None, Alu.add)
                        kb.recip(rr, rr)
                    kb.wait("tensor", "VV")
                    rb = ps[6] if h == 0 else ps[7]
                    kb.mm(rb[0:64, :], cones[0:1, 0:64], rr, mark=True)
                    kb.wait("scalar", "TT")
                    kb.scopy(fscr[0:64, 0:512], rb[0:64, :])
                    kb.wait("vector", "SS")
                    dst = fscr[h * 64:h * 64 + 64, 2048 + qt * 512:2048 + qt * 512 + 512]
                    if layer == 0:
                        kb.tt(dst, ob[0:64, :], fscr[0:64, 0:512], Alu.mult)
                    else:
                        kb.stt(dst, ob[0:64, :], p1(10 + h)[0:64, :],
                               fscr[0:64, 0:512], Alu.add, Alu.mult)
            kb.wait("tensor", "VV")
            if layer == 1 and hp == 0 and attn_block.b == 0 and dbg5 is not None:
                kb.wait("gpsimd", "VV")
                kb.store(dbg5, fscr[:, 2048:4096])
                kb.wait("tensor", "SD")
            for tc in range(4):
                for oc in range(8):
                    bank = ps[2 + (oc % 2)]
                    if tc * 8 + oc >= 2:
                        kb.wait("tensor", "VV", wom[-2])
                    kb.mm(bank[:, :], wbuf[:, 3072 + oc * 128:3072 + oc * 128 + 128],
                          fscr[:, 2048 + tc * 512:2048 + tc * 512 + 512], mark=True)
                    kb.wait("vector", "TT")
                    src1 = xs(oc, tc * 512, tc * 512 + 512) if (layer == 0 and hp == 0) \
                        else accs(oc, tc * 512, tc * 512 + 512)
                    kb.tt(accs(oc, tc * 512, tc * 512 + 512), bank[:, :], src1, Alu.add)
                    wom.append(kb.cnt["VV"])

    def ffn_block(b):
        ffn_block.b = b
        x3 = xbuf.rearrange("p (k t) -> p k t", k=8)
        kb.wait("gpsimd", "VV")
        kb.wait("gpsimd", "SS")
        kb.store(park1, x3)
        kb.vcopy(accb[:, :], xbuf[:, :])
        for kc in range(8):
            kb.red(p1(24 + kc), xs(kc, 0, T), Alu.add)
        kb.ts(small[:, S_P1 + 24:S_P1 + 32], small[:, S_P1 + 24:S_P1 + 32],
              1.0 / T, None, Alu.mult)
        if dbg3 is not None and ffn_block.b == 0:
            kb.wait("gpsimd", "VV")
            kb.store(dbg3[0:128, :], small[:, 0:1024])
        kb.wait("sync", "TT")
        kb.load(wbuf[:, 0:4096].rearrange("p (k n) -> p k n", k=8),
                di["gcw1"].rearrange("(k p) n -> p k n", p=128))
        kb.wait("tensor", "LD")
        kb.wait("tensor", "VV")
        first = True
        for oc in range(4):
            for kc in range(8):
                kb.mm(ps[0][:, oc:oc + 1], wbuf[:, kc * 512 + oc * 128:kc * 512 + oc * 128 + 128],
                      p1(24 + kc), start=first, stop=(oc == 3 and kc == 7))
                first = False
        kb.tmark()
        kb.wait("scalar", "TT")
        for oc in range(4):
            kb.act(p1(16 + oc), ps[0][:, oc:oc + 1], AF.Gelu,
                   bias=small[:, S_GCB1 + oc:S_GCB1 + oc + 1])
        kb.wait("sync", "TT")
        kb.load(wbuf[:, 0:1024].rearrange("p (k n) -> p k n", k=4),
                di["gcw2"].rearrange("(k p) n -> p k n", p=128))
        kb.wait("tensor", "LD")
        kb.wait("tensor", "SS")
        first = True
        for oc in range(2):
            for kc in range(4):
                kb.mm(ps[1][:, oc:oc + 1], wbuf[:, kc * 256 + oc * 128:kc * 256 + oc * 128 + 128],
                      p1(16 + kc), start=first, stop=(oc == 1 and kc == 3))
                first = False
        kb.tmark()
        kb.wait("scalar", "TT")
        for oc in range(2):
            kb.act(p1(20 + oc), ps[1][:, oc:oc + 1], AF.Gelu,
                   bias=small[:, S_GCB2 + oc:S_GCB2 + oc + 1])
        kb.wait("sync", "TT")
        kb.load(wbuf[:, 0:2].rearrange("p (k n) -> p k n", k=2),
                di["gcw3"].rearrange("(k p) n -> p k n", p=128), slow=True)
        kb.wait("tensor", "LD")
        kb.wait("tensor", "SS")
        for kc in range(2):
            kb.mm(ps[2][0:1, 0:1], wbuf[:, kc:kc + 1], p1(20 + kc),
                  start=(kc == 0), stop=(kc == 1))
        kb.tmark()
        kb.wait("scalar", "TT")
        kb.act(cell(4), ps[2][0:1, 0:1], AF.Sigmoid, bias=cell(1))
        kb.wait("vector", "SS")
        kb.ts(cell(5), cell(4), 0.7, None, Alu.mult)
        kb.wait("sync", "SD")
        kb.wait("sync", "VV")
        kb.load(xbuf[:, 8192:16384].rearrange("p (k n) -> p k n", k=8),
                di["cnw1"].rearrange("(k p) n -> p k n", p=128))
        kb.load(qkb[:, 0:4096].rearrange("p (k n) -> p k n", k=8),
                di["cnw2"].rearrange("(k p) n -> p k n", p=128))
        kb.load(vtb[:, 0:1024].rearrange("p (k n) -> p k n", k=4),
                di["cnw3"].rearrange("(k p) n -> p k n", p=128))
        kb.load(vtb[:, 1024:1026].rearrange("p (k n) -> p k n", k=2),
                di["cnw4"].rearrange("(k p) n -> p k n", p=128), slow=True)
        kb.wait("tensor", "LD")
        for cyc in range(3):
            for tc in range(4):
                lo, hi = tc * 512, tc * 512 + 512
                kb.wait("tensor", "VV")
                kb.wait("tensor", "SS")
                h1e = []
                for oc in range(8):
                    bank = ps[oc % 2]
                    if oc >= 2:
                        kb.wait("tensor", "SS", h1e[oc - 2])
                    for kc in range(8):
                        kb.mm(bank[:, :], xbuf[:, 8192 + kc * 1024 + oc * 128:8192 + kc * 1024 + oc * 128 + 128],
                              accs(kc, lo, hi), start=(kc == 0), stop=(kc == 7))
                    kb.tmark()
                    kb.wait("scalar", "TT")
                    kb.act(xbuf[:, oc * 512:oc * 512 + 512], bank[:, :], AF.Gelu,
                           bias=small[:, S_CNB1 + oc:S_CNB1 + oc + 1])
                    h1e.append(kb.cnt["SS"])
                for oc in range(4):
                    bank = ps[2 + oc % 2]
                    if oc >= 2:
                        kb.wait("tensor", "SS")
                    for kc in range(8):
                        kb.mm(bank[:, :], qkb[:, kc * 512 + oc * 128:kc * 512 + oc * 128 + 128],
                              xbuf[:, kc * 512:kc * 512 + 512], start=(kc == 0), stop=(kc == 7))
                    kb.tmark()
                    kb.wait("scalar", "TT")
                    kb.act(xbuf[:, 4096 + oc * 512:4096 + oc * 512 + 512], bank[:, :], AF.Gelu,
                           bias=small[:, S_CNB2 + oc:S_CNB2 + oc + 1])
                for oc in range(2):
                    bank = ps[4 + oc % 2]
                    for kc in range(4):
                        kb.mm(bank[:, :], vtb[:, kc * 256 + oc * 128:kc * 256 + oc * 128 + 128],
                              xbuf[:, 4096 + kc * 512:4096 + kc * 512 + 512],
                              start=(kc == 0), stop=(kc == 3))
                    kb.tmark()
                    kb.wait("scalar", "TT")
                    kb.act(xbuf[:, 6144 + oc * 512:6144 + oc * 512 + 512], bank[:, :], AF.Gelu,
                           bias=small[:, S_CNB3 + oc:S_CNB3 + oc + 1])
                for kc in range(2):
                    kb.mm(ps[6][0:1, :], vtb[:, 1024 + kc:1024 + kc + 1],
                          xbuf[:, 6144 + kc * 512:6144 + kc * 512 + 512],
                          start=(kc == 0), stop=(kc == 1))
                kb.tmark()
                kb.wait("scalar", "TT")
                kb.act(wrow(0), ps[6][0:1, :], AF.Sigmoid, bias=cell(0))
                kb.wait("vector", "SS")
                kb.ts(wrow(1), wrow(0), 0.3, cell(5), Alu.mult, Alu.add)
                kb.ts(wrow(1), wrow(1), 0.5, cell(2), Alu.subtract, Alu.mult)
                kb.ts(wrow(1), wrow(1), 2.0, 0.1, Alu.mult, Alu.add)
                kb.wait("tensor", "VV")
                kb.mm(ps[7][:, :], cones[0:1, :], wrow(1), mark=True)
                kb.wait("vector", "TT")
                for j in range(4):
                    sgn = 1.0 if j < 2 else -1.0
                    a = accs(j, lo, hi)
                    bb = accs(j + 4, lo, hi)
                    kb.tt(fscr[:, 0:512], bb, ps[7][:, :], Alu.mult)
                    kb.tt(fscr[:, 512:1024], a, ps[7][:, :], Alu.mult)
                    kb.stt(a, fscr[:, 0:512], sgn, a, Alu.mult, Alu.add)
                    kb.stt(bb, fscr[:, 512:1024], -sgn, bb, Alu.mult, Alu.add)
        for tc in range(4):
            lo, hi = tc * 512, tc * 512 + 512
            for kc in range(8):
                tile = fscr[:, 1024 + (kc % 2) * 512:1024 + (kc % 2) * 512 + 512]
                if tc * 8 + kc >= 2:
                    kb.wait("sync", "VV")
                kb.load(tile, park1[:, kc, lo:hi])
                kb.wait("vector", "LD")
                kb.tt(xs(kc, lo, hi), tile, accs(kc, lo, hi), Alu.add)
        kb.barrier()
        fm_ln(xs, xs)

    def l1_tail(b):
        fm_ln(xs, accs, gamma=S_LNG, beta=S_LNB2)
        kb.barrier()
        if _TSTOP <= 1:
            return
        for tc in range(4):
            lo, hi = tc * 512, tc * 512 + 512
            for kc in range(8):
                kb.mm(ps[0][0:8, :], ccent[:, kc * 8:kc * 8 + 8], accs(kc, lo, hi),
                      start=(kc == 0), stop=(kc == 7))
            kb.tmark()
            kb.wait("scalar", "TT")
            kb.scopy(fscr[0:8, 0:512], ps[0][0:8, :])
            kb.wait("gpsimd", "SS")
            j0 = b * T + tc * 512
            kb.store(scr_blk.ap()[:, j0:j0 + 512], fscr[0:8, 0:512])
            kb.wait("tensor", "SD")
        if _TSTOP <= 2:
            return
        NTG = 1 if _TSTOP == 4 else 16
        DOST = _TSTOP != 5
        for tg in range(NTG):
            for (srcb, dst_dram, off) in ((xbuf, feat_tm, 1024), (accb, lnf_tm, 2048)):
                for kc in range(8):
                    bank = ps[1 + kc % 2]
                    kb.wait("tensor", "SS")
                    kb.transp(bank[:, 0:128], srcb[:, kc * T + tg * 128:kc * T + tg * 128 + 128],
                              cident[:, :], mark=True)
                    kb.wait("scalar", "TT")
                    kb.scopy(fscr[:, off + kc * 128:off + kc * 128 + 128], bank[:, 0:128])
                if DOST:
                    kb.wait("gpsimd", "SS")
                    r0 = (b * 16 + tg) * 128
                    kb.store(dst_dram[r0:r0 + 128, :], fscr[:, off:off + 1024])
            if DOST:
                kb.wait("scalar", "SD")

    def cross_reduce(dst, src, op, bA, bB):
        kb.wait("gpsimd", "VV")
        kb.store(small[0:1, 680:808], src)
        kb.wait("vector", "SD")
        kb.red(cell(14), small[0:1, 680:808], op)
        kb.wait("tensor", "VV")
        kb.mm(bB[:, 0:1], cones[0:1, :], cell(14), mark=True)
        kb.wait("vector", "TT")
        kb.vcopy(dst, bB[:, 0:1])

    def routing():
        G = lambda c0, w: gscr[:, c0:c0 + w]
        s_all = lambda e: gscr[:, e * 256:e * 256 + 256]
        mem = lambda e: fscr[:, e * 256:e * 256 + 256]
        s_cur = G(2048, 256)
        dump = G(2304, 256)
        eq_t = G(2560, 256)
        gt_t = G(2816, 256)
        u_t = G(3072, 256)
        taken = G(3328, 256)
        scn = G(3584, 256)
        zrow = qkb[0:1, 2048:2304]

        if _RSTOP <= -1:
            return
        kb.cc("AllGather", scr_blk.ap(), scr_g.ap())
        kb.wait("sync", "CC")
        if _RSTOP <= -2:
            return
        for c in range(8):
            kb.load(gscr[c * 16:(c + 1) * 16, 0:2048].rearrange("p (e f) -> p e f", e=8),
                    scr_g.ap().rearrange("c e (pr f) -> c pr e f", pr=16)[c, :, :, :])
        kb.memset(zrow, 0.0)
        kb.memset(taken, 0.0)
        kb.wait("vector", "LD")
        if _RSTOP <= 0:
            return
        LO, HI, TH, PRED, CNT, M_, VREP, NOTP, D_ = (p1(0), p1(1), p1(2), p1(3),
                                                     p1(4), p1(5), p1(6), p1(7), p1(8))
        for e in range(8):
            kb.wait("tensor", "VV")
            if e == 0:
                kb.vcopy(s_cur, s_all(0))
            elif e < 7:
                kb.stt(s_cur, taken, -2e30, s_all(e), Alu.mult, Alu.add)
            if e == 7:
                kb.ts(mem(7), taken, -1.0, 1.0, Alu.mult, Alu.add)
            else:
                kb.stt(u_t, taken, 4e30, s_cur, Alu.mult, Alu.add)
                kb.red(D_, u_t, Alu.min)
                cross_reduce(LO, D_, Alu.min, ps[0], ps[1])
                kb.red(D_, s_cur, Alu.max)
                cross_reduce(HI, D_, Alu.max, ps[0], ps[1])
                for _ in range(30):
                    kb.ts(TH, LO, HI, 0.5, Alu.add, Alu.mult)
                    kb.ts(dump, s_cur, TH, 0.0, Alu.is_ge, Alu.add, accum_out=CNT)
                    kb.wait("tensor", "VV")
                    kb.mm(ps[0][:, 0:1], cones[:, :], CNT, mark=True)
                    kb.wait("vector", "TT")
                    kb.ts(PRED, ps[0][:, 0:1], float(CEXP), None, Alu.is_ge)
                    kb.tt(D_, TH, LO, Alu.subtract)
                    kb.tt(D_, PRED, D_, Alu.mult)
                    kb.tt(LO, LO, D_, Alu.add)
                    kb.ts(NOTP, PRED, -1.0, 1.0, Alu.mult, Alu.add)
                    kb.tt(D_, TH, HI, Alu.subtract)
                    kb.tt(D_, NOTP, D_, Alu.mult)
                    kb.tt(HI, HI, D_, Alu.add)
                kb.ts(dump, s_cur, LO, None, Alu.is_lt)
                kb.stt(u_t, dump, 4e30, s_cur, Alu.mult, Alu.add)
                kb.red(D_, u_t, Alu.min)
                cross_reduce(VREP, D_, Alu.min, ps[0], ps[1])
                kb.ts(dump, s_cur, VREP, 0.0, Alu.is_gt, Alu.add, accum_out=CNT)
                kb.wait("tensor", "VV")
                kb.mm(ps[0][:, 0:1], cones[:, :], CNT, mark=True)
                kb.wait("vector", "TT")
                kb.ts(M_, ps[0][:, 0:1], -1.0, float(CEXP), Alu.mult, Alu.add)
                kb.ts(eq_t, s_cur, VREP, None, Alu.is_equal)
                kb.scan(scn, eq_t, eq_t, 0.0, Alu.add, Alu.bypass)
                kb.vcopy(p1(9), scn[:, 255:256])
                kb.tt(scn, scn, eq_t, Alu.subtract)
                kb.wait("tensor", "VV")
                kb.mm(ps[2][:, 0:1], clmask[:, :], p1(9), mark=True)
                kb.wait("vector", "TT")
                kb.ts(scn, scn, ps[2][:, 0:1], None, Alu.add)
                kb.ts(dump, scn, M_, None, Alu.is_lt)
                kb.ts(gt_t, s_cur, VREP, None, Alu.is_gt)
                kb.tt(eq_t, eq_t, dump, Alu.mult)
                kb.tt(mem(e), gt_t, eq_t, Alu.add)
            if e < 7:
                kb.tt(taken, taken, mem(e), Alu.add)
        if _RSTOP <= 1:
            return
        hrow_full = G(2048, 256)
        lp_full = G(2304, 256)
        kb.wait("vector", "TT")
        kb.memset(hrow_full, 0.0)
        kb.memset(lp_full, 0.0)
        bp8m = qkb[0:8, 2304:2312]
        st8m = qkb[0:8, 2312:2320]
        cnt8 = qkb[0:8, 2320:2328]
        run8 = qkb[0:8, 2328:2336]
        c1 = p1(10)
        c3 = p1(11)
        kb.memset(run8[:, 0:1], 0.0)
        for e in range(8):
            kb.wait("tensor", "VV")
            kb.scan(scn, mem(e), mem(e), 0.0, Alu.add, Alu.bypass)
            kb.vcopy(p1(9), scn[:, 255:256])
            kb.tt(scn, scn, mem(e), Alu.subtract)
            kb.wait("tensor", "VV")
            kb.mm(ps[2][:, 0:1], clmask[:, :], p1(9), start=True, stop=True)
            kb.mm(ps[3][:, 0:1], clb[:, :], p1(9), start=True, stop=True)
            kb.mm(ps[4][0:8, 0:1], cmisc[:, 1:9], p1(9), mark=True)
            kb.wait("vector", "TT")
            kb.vcopy(p1(12)[0:8, :], ps[4][0:8, 0:1])
            kb.vcopy(qkb[0:8, 2320 + e:2321 + e], p1(12)[0:8, :])
            kb.vcopy(st8m[:, e:e + 1], run8[:, 0:1])
            kb.tt(run8[:, 0:1], run8[:, 0:1], p1(12)[0:8, :], Alu.add)
            kb.wait("tensor", "VV")
            kb.mm(ps[5][0:8, 0:1], clmask[0:8, 0:8], p1(12)[0:8, :], mark=True)
            kb.wait("vector", "TT")
            kb.vcopy(bp8m[:, e:e + 1], ps[5][0:8, 0:1])
            kb.ts(u_t, scn, ps[2][:, 0:1], float(CEXP * e), Alu.add, Alu.add)
            kb.tt(taken, mem(e), u_t, Alu.mult)
            kb.tt(hrow_full, hrow_full, taken, Alu.add)
            kb.vcopy(c1, ps[2][:, 0:1])
            kb.tt(c1, c1, ps[3][:, 0:1], Alu.subtract)
            kb.wait("tensor", "VV")
            kb.mm(ps[6][:, 0:1], cexp16[:, :], st8m[:, e:e + 1], mark=True)
            kb.wait("vector", "TT")
            kb.tt(c3, c1, ps[6][:, 0:1], Alu.add)
            kb.ts(u_t, scn, c3, None, Alu.add)
            kb.tt(taken, mem(e), u_t, Alu.mult)
            kb.tt(lp_full, lp_full, taken, Alu.add)
        if _RSTOP <= 2:
            return
        kb.wait("tensor", "VV")
        kb.mm(ps[0][0:16, 0:256], cesel128[:, :], lp_full, mark=True)
        kb.mm(ps[1][0:16, 0:256], cesel128[:, :], hrow_full, mark=True)
        kb.wait("scalar", "TT")
        kb.scopy(gscr[0:16, 2048:2304], ps[0][0:16, 0:256])
        kb.scopy(gscr[0:16, 2304:2560], ps[1][0:16, 0:256])
        kb.wait("gpsimd", "SS")
        for k in range(32):
            pr, half = k // 2, (k % 2) * 128
            kb.store(fscr[:, 2048 + k:2048 + k + 1], gscr[pr:pr + 1, 2048 + half:2048 + half + 128])
            kb.store(fscr[:, 2080 + k:2080 + k + 1], gscr[pr:pr + 1, 2304 + half:2304 + half + 128])
        kb.wait("vector", "SD")
        kb.ts(u32b[:, 0:32], fscr[:, 2048:2080], 0.0, None, Alu.add)
        kb.ts(u32b[:, 32:64], fscr[:, 2080:2112], 0.0, None, Alu.add)
        if _RSTOP <= 3:
            return
        esr = qkb[0:1, 2336:2344]
        kb.wait("gpsimd", "VV")
        kb.store(esr, cesel[:, :])
        kb.wait("tensor", "SD")
        kb.mm(ps[7][0:8, 0:8], cones[0:1, 0:8], esr, mark=True)
        kb.wait("vector", "TT")
        esrep8 = qkb[0:8, 2344:2352]
        kb.vcopy(esrep8, ps[7][0:8, 0:8])
        cumown = p1(13)
        stown = p1(14)
        kb.memset(cumown[0:8, :], 0.0)
        kb.memset(stown[0:8, :], 0.0)
        for e in range(8):
            kb.stt(cumown[0:8, :], bp8m[:, e:e + 1], esrep8[:, e:e + 1], cumown[0:8, :],
                   Alu.mult, Alu.add)
            kb.stt(stown[0:8, :], st8m[:, e:e + 1], esrep8[:, e:e + 1], stown[0:8, :],
                   Alu.mult, Alu.add)
        lam = p1(15)
        kb.tt(lam[0:8, :], stown[0:8, :], cumown[0:8, :], Alu.subtract)
        kb.tt(lam[0:8, :], lam[0:8, :], cmisc[0:8, 46:47], Alu.add)
        kb.wait("gpsimd", "VV")
        kb.store(qkb[0:1, 2352:2360], lam[0:8, :])
        kb.store(qkb[0:1, 2360:2368], cumown[0:8, :])
        kb.wait("tensor", "SD")
        kb.mm(ps[6][:, 0:8], cones[0:1, :], qkb[0:1, 2352:2360])
        kb.mm(ps[7][:, 0:8], cones[0:1, :], qkb[0:1, 2360:2368], mark=True)
        kb.wait("vector", "TT")
        lamrep = G(3584, 8)
        cumrep = G(3592, 8)
        kb.vcopy(lamrep, ps[6][:, 0:8])
        kb.vcopy(cumrep, ps[7][:, 0:8])
        sgrid = G(3600, 32)
        kb.iota(sgrid, [[128, 32]], 0, 1)
        kb.wait("vector", "GG")
        acc32 = G(3632, 32)
        ge_t = lambda c: gscr[:, 3664 + c * 32:3664 + c * 32 + 32]
        kb.memset(acc32, 0.0)
        for c in range(8):
            kb.ts(ge_t(c), sgrid, cumrep[:, c:c + 1], None, Alu.is_ge)
        for c in range(8):
            if c < 7:
                kb.tt(s_cur[:, 0:32], ge_t(c), ge_t(c + 1), Alu.subtract)
                ind = s_cur[:, 0:32]
            else:
                ind = ge_t(7)
            kb.stt(acc32, ind, lamrep[:, c:c + 1], acc32, Alu.mult, Alu.add)
        kb.tt(acc32, sgrid, acc32, Alu.add)
        kb.ts(u32b[:, 64:96], acc32, 0.0, None, Alu.add)
        kb.barrier()
        if stage == "route" and _RSTOP >= 5:
            kb.store(dbg2[0 * 128:1 * 128, :], mem(0))
            kb.memset(gt_t, 0.0)
            kb.vcopy(gt_t[:, 0:64], fscr[:, 2048:2112])
            kb.wait("gpsimd", "VV")
            kb.store(dbg2[1 * 128:2 * 128, :], gt_t)
            kb.store(dbg2[2 * 128:3 * 128, :], mem(7))
            kb.store(dbg2[3 * 128:4 * 128, :], hrow_full)
            kb.store(dbg2[4 * 128:5 * 128, :], lp_full)
            kb.memset(dump, 0.0)
            kb.vcopy(dump[:, 0:32], acc32)
            kb.wait("gpsimd", "VV")
            kb.store(dbg2[5 * 128:6 * 128, :], dump)
            kb.wait("gpsimd", "SD")
        if _RSTOP <= 4:
            return
        for k in range(32):
            tile = fscr[:, (k % 2) * 1024:(k % 2) * 1024 + 1024]
            kb.wait("sync", "SD")
            kb.load(tile, lnf_tm[k * 128:k * 128 + 128, :])
            kb.wait("gpsimd", "LD")
            kb.ind_scatter(agin.ap(), tile, u32b[:, k:k + 1])
        if dbg7 is not None:
            kb.wait("gpsimd", "VV")
            kb.store(dbg7, u32b[:, :])
        kb.cc("AllGather", agin.ap(), agout.ap())
        if dbg6 is not None and _MSTOP == 1:
            kb.wait("gpsimd", "CC")
            kb.store(dbg6[0:256, :], agout.ap()[0:256, :])
            kb.store(dbg6[256:512, :], agout.ap()[CEXP * 3:CEXP * 3 + 256, :])

    def moe():
        kb.wait("gpsimd", "CC")
        kb.wait("tensor", "CC")
        mw1r = di["mw1"].rearrange("(k p) n -> p k n", p=128)
        for st in range(8):
            for g in range(4):
                tile = fscr[:, (g % 2) * 1024:(g % 2) * 1024 + 1024]
                kb.wait("gpsimd", "SS")
                kb.wait("gpsimd", "SD")
                kb.ind_gather(tile, agout.ap(), u32b[:, 64 + st * 4 + g:64 + st * 4 + g + 1])
                if dbg6 is not None and _MSTOP == 5 and st == 4 and g == 0:
                    kb.wait("gpsimd", "SD")
                    kb.store(dbg6[0:128, :], tile)
                    kb.wait("tensor", "SD")
                kb.wait("tensor", "SD")
                for kc in range(8):
                    bank = ps[kc % 2]
                    kb.wait("tensor", "SS")
                    kb.transp(bank[:, 0:128], tile[:, kc * 128:kc * 128 + 128],
                              cident[:, :], mark=True)
                    kb.wait("scalar", "TT")
                    kb.scopy(qkb[:, kc * 512 + g * 128:kc * 512 + g * 128 + 128],
                             bank[:, 0:128])
            if dbg6 is not None and _MSTOP == 3 and st == 4:
                kb.wait("gpsimd", "SS")
                kb.store(dbg6[0:128, :], qkb[:, 0:1024])
                kb.store(dbg6[128:256, :], qkb[:, 1024:2048])
                kb.store(dbg6[256:384, :], qkb[:, 2048:3072])
                kb.store(dbg6[384:512, :], qkb[:, 3072:4096])
                kb.wait("sync", "SD")
            w1evac = []
            w1mm = []
            for hc in range(32):
                wsl = accb[:, (hc % 2) * 1024:(hc % 2) * 1024 + 1024]
                if hc >= 2:
                    kb.wait("sync", "TT", w1mm[hc - 2])
                else:
                    kb.wait("sync", "TT")
                kb.load(wsl.rearrange("p (k n) -> p k n", k=8), mw1r[:, :, hc * 128:hc * 128 + 128])
                kb.wait("tensor", "LD")
                bank = ps[2 + hc % 2]
                if hc < 2:
                    kb.wait("tensor", "SS")
                else:
                    kb.wait("tensor", "SS", w1evac[hc - 2])
                for kc in range(8):
                    kb.mm(bank[:, :], wsl[:, kc * 128:kc * 128 + 128],
                          qkb[:, kc * 512:kc * 512 + 512], start=(kc == 0), stop=(kc == 7))
                kb.tmark()
                w1mm.append(kb.cnt["TT"])
                kb.wait("scalar", "TT")
                kb.act(xbuf[:, hc * 512:hc * 512 + 512], bank[:, :], AF.Gelu_apprx_tanh,
                       bias=small[:, S_MB1 + hc:S_MB1 + hc + 1])
                w1evac.append(kb.cnt["SS"])
            if dbg6 is not None and _MSTOP == 4 and st == 4:
                kb.wait("gpsimd", "SS")
                for q_ in range(4):
                    kb.store(dbg6[q_ * 128:(q_ + 1) * 128, :], xbuf[:, q_ * 1024:(q_ + 1) * 1024])
                kb.wait("sync", "SD")
            kb.wait("tensor", "SS")
            w2mm = []
            for hc in range(32):
                w2sl = accb[:, 2048 + (hc % 2) * 1024:2048 + (hc % 2) * 1024 + 1024]
                if hc >= 2:
                    kb.wait("sync", "TT", w2mm[hc - 2])
                else:
                    kb.wait("sync", "TT")
                kb.load(w2sl, di["mw2"][hc * 128:hc * 128 + 128, :])
                kb.wait("tensor", "LD")
                for oc in range(8):
                    kb.mm(ps[oc][:, :], w2sl[:, oc * 128:oc * 128 + 128],
                          xbuf[:, hc * 512:hc * 512 + 512],
                          start=(hc == 0), stop=(hc == 31))
                kb.tmark()
                w2mm.append(kb.cnt["TT"])
            kb.wait("scalar", "TT")
            for oc in range(8):
                kb.act(qkb[:, oc * 512:oc * 512 + 512], ps[oc][:, :], AF.Identity,
                       bias=small[:, S_MB2 + oc:S_MB2 + oc + 1])
            for g in range(4):
                tile = fscr[:, (g % 2) * 1024:(g % 2) * 1024 + 1024]
                if g >= 2:
                    kb.wait("scalar", "SD")
                for kc in range(8):
                    bank = ps[kc % 2]
                    kb.wait("tensor", "SS")
                    kb.transp(bank[:, 0:128], qkb[:, kc * 512 + g * 128:kc * 512 + g * 128 + 128],
                              cident[:, :], mark=True)
                    kb.wait("scalar", "TT")
                    kb.scopy(tile[:, kc * 128:kc * 128 + 128], bank[:, 0:128])
                kb.wait("gpsimd", "SS")
                kb.store(h_loc.ap()[st * 512 + g * 128:st * 512 + g * 128 + 128, :], tile)
        if dbg6 is not None and _MSTOP == 2:
            kb.wait("gpsimd", "SD")
            kb.store(dbg6[0:256, :], h_loc.ap()[0:256, :])
            kb.store(dbg6[256:512, :], h_loc.ap()[2048:2048 + 256, :])
        kb.cc("AllGather", h_loc.ap(), h_g.ap())

    def final():
        kb.wait("sync", "CC")
        kb.wait("gpsimd", "CC")
        for k in range(32):
            xt = fscr[:, (k % 2) * 1024:(k % 2) * 1024 + 1024]
            ht = fscr[:, 2048 + (k % 2) * 1024:2048 + (k % 2) * 1024 + 1024]
            if k >= 2:
                kb.wait("sync", "VV")
                kb.wait("gpsimd", "VV")
            kb.load(xt, feat_tm[k * 128:k * 128 + 128, :])
            kb.ind_gather(ht, h_g.ap(), u32b[:, 32 + k:32 + k + 1])
            kb.wait("vector", "LD")
            kb.wait("vector", "SD")
            u = gscr[:, (k % 2) * 1024:(k % 2) * 1024 + 1024]
            du = gscr[:, 2048 + (k % 2) * 1024:2048 + (k % 2) * 1024 + 1024]
            kb.stt(u, xt, 2.0, ht, Alu.mult, Alu.add)
            kb.ts(du, u, 0.0, 0.0, Alu.add, Alu.add, accum_out=p1(0))
            kb.stt(du, u, 1.0, u, Alu.mult, Alu.mult, accum_out=p1(1))
            kb.ts(p1(0), p1(0), 1.0 / 1024, None, Alu.mult)
            kb.ts(p1(1), p1(1), 1.0 / 1024, None, Alu.mult)
            kb.tt(p1(2), p1(0), p1(0), Alu.mult)
            kb.tt(p1(1), p1(1), p1(2), Alu.subtract)
            kb.ts(p1(1), p1(1), EPS, None, Alu.add)
            kb.wait("scalar", "VV")
            kb.act(p1(2), p1(1), AF.Ln)
            kb.act(p1(2), p1(2), AF.Exp, scale=-0.5)
            kb.wait("vector", "SS")
            kb.tt(p1(3), p1(2), p1(2), Alu.mult)
            kb.tt(p1(3), p1(3), p1(1), Alu.mult)
            kb.ts(p1(3), p1(3), -0.5, 1.5, Alu.mult, Alu.add)
            kb.tt(p1(2), p1(2), p1(3), Alu.mult)
            kb.ts(u, u, p1(0), p1(2), Alu.subtract, Alu.mult)
            kb.wait("gpsimd", "VV")
            kb.store(out[k * 128:k * 128 + 128, :], u)

    x3v = xbuf.rearrange("p (k t) -> p k t", k=8)
    for b in range(NB):
        kb.wait("sync", "VV")
        kb.wait("sync", "TT")
        kb.wait("sync", "SS")
        kb.wait("sync", "SD")
        kb.load(x3v, di["x0" if b == 0 else "x1"].rearrange("(k p) t -> p k t", p=128))
        kb.barrier()
        attn_block(b, 0)
        kb.barrier()
        fm_ln(accs, xs)
        kb.barrier()
        if SI == 1:
            kb.store(dbg[b * 128:(b + 1) * 128, :], xbuf[:, :])
            continue
        ffn_block(b)
        kb.barrier()
        if SI == 2:
            kb.store(dbg[b * 128:(b + 1) * 128, :], xbuf[:, :])
            if b == 0:
                kb.store(dbg3[128:256, :], small[:, 0:1024])
            continue
        kb.vcopy(accb[:, :], xbuf[:, :])
        kb.barrier()
        fm_ln(xs, xs)
        kb.barrier()
        attn_block(b, 1)
        kb.barrier()
        fm_ln(accs, xs)
        kb.barrier()
        if SI == 3:
            kb.store(dbg[b * 128:(b + 1) * 128, :], xbuf[:, :])
            continue
        l1_tail(b)
        kb.barrier()
    if SI >= 4:
        routing()
        kb.barrier()
    if SI >= 5:
        moe()
        kb.barrier()
        final()
    kb.barrier()
    kb.emit()
    return nc


def _consts():
    f32 = np.float32
    c = {}
    c["ones"] = np.ones((128, 128), f32)
    l = np.zeros((128, 128), f32)
    for k in range(128):
        l[k, k + 1:] = 1.0
    c["lmask"] = l
    c["ident"] = np.eye(128, dtype=f32)
    top = 2595.0 * np.log10(21.0)
    mel = np.power(10.0, np.linspace(0.0, top, 32) / 2595.0) - 1.0
    fb64 = 200.0 * mel / 1000.0
    fbhi32 = np.array(FBJ_BITS, dtype=np.uint32).view(np.float32)
    lo = fb64 - fbhi32.astype(np.float64)
    lim = 0.4999 * np.spacing(np.abs(fbhi32)).astype(np.float64)
    lo = np.clip(lo, -lim, lim)
    idx = (np.arange(128) % 64) // 2
    c["fbhi"] = fbhi32[idx].astype(np.float32).reshape(128, 1)
    c["fblo"] = lo[idx].astype(np.float32).reshape(128, 1)
    c["signrow"] = np.array([[-1.0 if d % 2 == 0 else 1.0] for d in range(128)], f32)
    c["irow32"] = np.arange(32, dtype=f32).reshape(1, 32)
    c["c4096col"] = (np.arange(8, dtype=f32) * 4096.0).reshape(8, 1)
    lb = np.zeros((128, 128), f32)
    for m in range(128):
        lb[: 16 * (m // 16), m] = 1.0
    c["lb"] = lb
    bs = np.zeros((128, 8), f32)
    for p in range(128):
        bs[p, p // 16] = 1.0
    c["bs"] = bs
    e16 = np.zeros((8, 128), f32)
    for p in range(128):
        e16[p // 16, p] = 1.0
    c["exp16"] = e16
    return c


def _colmajor(v, k):
    return np.ascontiguousarray(np.asarray(v, np.float32).reshape(k, 128).T)


def _prep(inputs):
    f32 = np.float32
    A = lambda n: np.asarray(inputs[n], f32)
    x = A("L0_attn_x")
    consts = _consts()
    shared = dict(consts)
    shared.update(
        wqkv=A("L0_attn_w_qkv"), wo0=A("L0_attn_w_o"),
        cnw1=A("L0_ffn_cn_w1"), cnw2=A("L0_ffn_cn_w2"), cnw3=A("L0_ffn_cn_w3"),
        cnw4=A("L0_ffn_cn_w4"),
        cnb1=_colmajor(inputs["L0_ffn_cn_b1"], 8), cnb2=_colmajor(inputs["L0_ffn_cn_b2"], 4),
        cnb3=_colmajor(inputs["L0_ffn_cn_b3"], 2),
        cnb4=A("L0_ffn_cn_b4").reshape(1, 1),
        gcw1=A("L0_ffn_gc_w1"), gcw2=A("L0_ffn_gc_w2"), gcw3=A("L0_ffn_gc_w3"),
        gcb1=_colmajor(inputs["L0_ffn_gc_b1"], 4), gcb2=_colmajor(inputs["L0_ffn_gc_b2"], 2),
        gcb3=A("L0_ffn_gc_b3").reshape(1, 1),
        ar=A("L0_ffn_adaptive_range").reshape(1, 1),
        lnw1=_colmajor(inputs["L1_attn_ln_w"], 8), lnb1=_colmajor(inputs["L1_attn_ln_b"], 8),
        wq1=A("L1_attn_w_q"), wkv1=A("L1_attn_w_kv"), wo1=A("L1_attn_w_o"),
        theta=A("L1_attn_theta").reshape(1, 1),
        centT=np.ascontiguousarray(A("L1_ffn_centroids").T),
        lng=_colmajor(inputs["L1_ffn_ln_g"], 8), lnb2=_colmajor(inputs["L1_ffn_ln_b"], 8),
    )
    w1 = A("L1_ffn_w1")
    b1 = np.asarray(inputs["L1_ffn_b1"], f32)
    w2 = A("L1_ffn_w2")
    b2 = np.asarray(inputs["L1_ffn_b2"], f32)
    in_maps = []
    for c in range(NC8):
        m = dict(shared)
        m["x0"] = np.ascontiguousarray(x[2 * c].T)
        m["x1"] = np.ascontiguousarray(x[2 * c + 1].T)
        m["mw1"] = np.ascontiguousarray(w1[c])
        m["mb1"] = _colmajor(b1[c], 32)
        m["mw2"] = np.ascontiguousarray(w2[c])
        m["mb2"] = _colmajor(b2[c], 8)
        es = np.zeros((8, 1), f32)
        es[c, 0] = 1.0
        m["esel"] = es
        e128 = np.zeros((128, 16), f32)
        for pr in range(16):
            e128[16 * c + pr, pr] = 1.0
        m["esel128"] = e128
        in_maps.append(m)
    return in_maps


LAST_RES = [None]


def kernel(**inputs):
    in_maps = _prep(inputs)
    nc = build(_STAGE)
    res = run_bass_kernel_spmd(nc, in_maps, core_ids=list(range(NC8)))
    LAST_RES[0] = res
    outs = [np.asarray(res.results[c]["out"]) for c in range(NC8)]
    return np.concatenate(outs, axis=0).reshape(16, T, D)


# version 04
# speedup vs baseline: 1.1027x; 1.0084x over previous
import math
import numpy as np
import concourse.bass as bass
import concourse.mybir as mybir
from concourse.bass_utils import run_bass_kernel_spmd

F32 = mybir.dt.float32
U32 = mybir.dt.uint32
AF = mybir.ActivationFunctionType
Alu = mybir.AluOpType
AX = mybir.AxisListType.X

NC8 = 8
T = 2048
D = 1024
HD = 64
NB = 2
NTOK = NB * T
CEXP = 4096
EPS = 1e-5

CW1 = 6.25
CW2 = 0.033203125
CW3 = float(np.float32(2 * math.pi - CW1 - CW2))
INV2PI = float(np.float32(1.0 / (2 * math.pi)))
PIO2 = float(np.float32(math.pi / 2))

_STAGE = "full"
_RSTOP = 9
_TSTOP = 9
_MSTOP = 9

FBJ_BITS = [0x0, 0x3ca9130c, 0x3d31cc5a, 0x3d8c577a, 0x3dc517c7, 0x3e01d9a8,
            0x3e246269, 0x3e4a7b7b, 0x3e748306, 0x3e917073, 0x3eab03d5,
            0x3ec73ae1, 0x3ee65b4f, 0x3f045905, 0x3f1749f5, 0x3f2c2f4b,
            0x3f433ca5, 0x3f5caafa, 0x3f78b92b, 0x3f8bd641, 0x3f9ce8bc,
            0x3fafbe41, 0x3fc48550, 0x3fdb7148, 0x3ff4bac7, 0x40085028,
            0x4017b369, 0x4028ad2a, 0x403b6764, 0x40501060, 0x4066db22,
            0x40800002]


class KB:
    ENGS = ("sync", "scalar", "vector", "tensor", "gpsimd")

    def __init__(self, nc):
        self.nc = nc
        self.q = {e: [] for e in self.ENGS}
        self.sems = {}
        self.cnt = {}
        for s in ("LD", "SD", "TT", "TA", "VV", "SS", "GG", "CC"):
            self.sems[s] = nc.semaphore(s).__enter__()
            self.cnt[s] = 0

    def _rec(self, eng, fn, inc, n):
        if inc is not None:
            self.cnt[inc] += n
        self.q[eng].append((fn, inc, n))

    def wait(self, eng, sem, val=None):
        v = self.cnt[sem] if val is None else val
        h = self.sems[sem]
        self.q[eng].append((lambda e, h=h, v=v: e.wait_ge(h, v), None, 0))

    def wait_all(self, eng):
        for s in ("LD", "SD", "TT", "TA", "VV", "SS", "GG", "CC"):
            if self.cnt[s] > 0:
                self.wait(eng, s)

    def barrier(self):
        for e in self.ENGS:
            self.wait_all(e)

    def load(self, out, in_, slow=False):
        if slow:
            self._rec("sync", lambda e, o=out, i=in_: e.dma_start(out=o, in_=i, allow_slow_non_contiguous=True), "LD", 16)
        else:
            self._rec("sync", lambda e, o=out, i=in_: e.dma_start(out=o, in_=i), "LD", 16)

    def store(self, out, in_):
        self._rec("gpsimd", lambda e, o=out, i=in_: e.dma_start(out=o, in_=i), "SD", 16)

    def ind_gather(self, out, dram_ap, idx_ap):
        self._rec("gpsimd", lambda e, o=out, d=dram_ap, ix=idx_ap: e.indirect_dma_start(
            out=o, out_offset=None, in_=d,
            in_offset=bass.IndirectOffsetOnAxis(ap=ix, axis=0)), "SD", 16)

    def ind_scatter(self, dram_ap, in_, idx_ap):
        self._rec("gpsimd", lambda e, d=dram_ap, i=in_, ix=idx_ap: e.indirect_dma_start(
            out=d, out_offset=bass.IndirectOffsetOnAxis(ap=ix, axis=0),
            in_=i, in_offset=None), "SD", 16)

    def cc(self, kind, in_ap, out_ap):
        self.wait("gpsimd", "SD")
        self._rec("gpsimd", lambda e, k=kind, i=in_ap, o=out_ap: e.collective_compute(
            k, Alu.bypass, replica_groups=[list(range(NC8))],
            ins=[i.opt()], outs=[o.opt()]), "CC", 1)

    def mm(self, out, lhsT, rhs, start=True, stop=True, tp=None, mark=False, sem=None):
        s_ = sem if sem is not None else ("TT" if mark else None)
        if tp is None:
            self._rec("tensor", lambda e, o=out, l=lhsT, r=rhs, s1=start, s2=stop:
                      e.matmul(o, l, r, start=s1, stop=s2), s_, 1)
        else:
            self._rec("tensor", lambda e, o=out, l=lhsT, r=rhs, s1=start, s2=stop, t=tp:
                      e.matmul(o, l, r, start=s1, stop=s2, tile_position=t), s_, 1)

    def transp(self, out, in_, ident, mark=False):
        self._rec("tensor", lambda e, o=out, i=in_, d=ident: e.transpose(o, i, d),
                  "TT" if mark else None, 1)

    def tmark(self):
        for j in range(len(self.q["tensor"]) - 1, -1, -1):
            fn, inc, n = self.q["tensor"][j]
            if n == 0:
                continue
            assert inc is None
            self.cnt["TT"] += 1
            self.q["tensor"][j] = (fn, "TT", 1)
            return

    def act(self, out, in_, func, bias=0.0, scale=1.0, accum_out=None):
        self._rec("scalar", lambda e, o=out, i=in_, f=func, b=bias, s=scale, a=accum_out:
                  e.activation(out=o, in_=i, func=f, bias=b, scale=s, accum_out=a), "SS", 1)

    def scopy(self, out, in_):
        self._rec("scalar", lambda e, o=out, i=in_: e.copy(out=o, in_=i), "SS", 1)

    def _v(self, fn):
        self._rec("vector", fn, "VV", 1)

    def _md(self, out):
        try:
            n = 1
            for d in out.shape[1:]:
                n *= d
        except Exception:
            n = 999
        if n <= 64:
            self._v(lambda e: e.drain())

    def tt(self, out, a, b, op):
        self._v(lambda e, o=out, x=a, y=b, p=op: e.tensor_tensor(o, x, y, p))
        self._md(out)

    def ts(self, out, in0, s1, s2, op0, op1=None, accum_out=None, safe=False):
        if not safe and (not isinstance(s1, (int, float)) or
                         (s2 is not None and not isinstance(s2, (int, float)))):
            self._v(lambda e: e.drain())
        if op1 is None:
            self._v(lambda e, o=out, i=in0, a=s1, p0=op0: e.tensor_scalar(o, i, a, None, op0=p0))
        else:
            self._v(lambda e, o=out, i=in0, a=s1, b=s2, p0=op0, p1=op1, ac=accum_out:
                    e.tensor_scalar(o, i, a, b, op0=p0, op1=p1, accum_out=ac))
        if accum_out is not None:
            self._v(lambda e: e.drain())
        else:
            self._md(out)

    def stt(self, out, in0, scalar, in1, op0, op1, accum_out=None, safe=False):
        if not safe and not isinstance(scalar, (int, float)):
            self._v(lambda e: e.drain())
        self._v(lambda e, o=out, i=in0, s=scalar, j=in1, p0=op0, p1=op1, ac=accum_out:
                e.scalar_tensor_tensor(o, i, s, j, op0=p0, op1=p1, accum_out=ac))
        if accum_out is not None:
            self._v(lambda e: e.drain())
        else:
            self._md(out)

    def red(self, out, in_, op):
        self._v(lambda e, o=out, i=in_, p=op: e.tensor_reduce(o, i, axis=AX, op=p))
        self._v(lambda e: e.drain())

    def vcopy(self, out, in_):
        self._v(lambda e, o=out, i=in_: e.tensor_copy(out=o, in_=i))
        self._md(out)

    def memset(self, ap, c):
        self._v(lambda e, a=ap, v=c: e.memset(a, v))
        self._md(ap)

    def recip(self, out, in_):
        self._v(lambda e, o=out, i=in_: e.reciprocal(out=o, in_=i))
        self._v(lambda e: e.drain())

    def scan(self, out, d0, d1, init, op0, op1):
        self._v(lambda e, o=out, a=d0, b=d1, i=init, p0=op0, p1=op1:
                e.tensor_tensor_scan(o, a, b, i, op0=p0, op1=p1))
        self._v(lambda e: e.drain())

    def shuf(self, out, in_, mask):
        self._v(lambda e, o=out, i=in_, m=mask: e.stream_shuffle(o, i, m))

    def _g(self, fn):
        self._rec("gpsimd", fn, "GG", 1)

    def iota(self, out, pattern, base=0, chm=0):
        self._g(lambda e, o=out, p=pattern, b=base, c=chm: e.iota(
            o, pattern=p, base=b, channel_multiplier=c, allow_small_or_imprecise_dtypes=True))

    def affine(self, out, in_, pattern, cmp, fill, base, chm):
        self._g(lambda e, o=out, i=in_, p=pattern, c=cmp, f=fill, b=base, m=chm:
                e.affine_select(o, i, pattern=p, compare_op=c, fill=f, base=b, channel_multiplier=m))

    def gtt(self, out, a, b, op):
        self._g(lambda e, o=out, x=a, y=b, p=op: e.tensor_tensor(o, x, y, p))

    def emit(self):
        blk = self.nc.Block().__enter__()
        for eng in self.ENGS:
            items = self.q[eng]
            sems = self.sems

            def body(e, items=items, sems=sems):
                for fn, inc, n in items:
                    ins = fn(e)
                    if inc is not None:
                        ins.then_inc(sems[inc], n)

            getattr(blk, eng)(body)
        blk.__exit__(None, None, None)


def stage_idx(s):
    return {"x1": 1, "x2": 2, "x3": 3, "route": 4, "m1": 5, "full": 5}[s]


def build(stage):
    SI = stage_idx(stage)
    nc = bass.Bass("TRN2", target_bir_lowering=False)
    di = {}

    def IN(name, shape, dt=F32):
        di[name] = nc.dram_tensor(name, list(shape), dt, kind="ExternalInput").ap()
        return di[name]

    IN("x0", [D, T]); IN("x1", [D, T])
    IN("wqkv", [D, 3 * D]); IN("wo0", [D, D])
    IN("cnw1", [D, D]); IN("cnw2", [D, 512]); IN("cnw3", [512, 256]); IN("cnw4", [256, 1])
    IN("cnb1", [128, 8]); IN("cnb2", [128, 4]); IN("cnb3", [128, 2]); IN("cnb4", [1, 1])
    IN("gcw1", [D, 512]); IN("gcw2", [512, 256]); IN("gcw3", [256, 1])
    IN("gcb1", [128, 4]); IN("gcb2", [128, 2]); IN("gcb3", [1, 1]); IN("ar", [1, 1])
    IN("lnw1", [128, 8]); IN("lnb1", [128, 8])
    IN("wq1", [D, D]); IN("wkv1", [D, 2 * D]); IN("wo1", [D, D]); IN("theta", [1, 1])
    IN("centT", [D, 8]); IN("lng", [128, 8]); IN("lnb2", [128, 8])
    IN("mw1", [D, 4 * D]); IN("mb1", [128, 32]); IN("mw2", [4 * D, D]); IN("mb2", [128, 8])
    IN("esel", [8, 1])
    IN("ones", [128, 128]); IN("lmask", [128, 128]); IN("ident", [128, 128])
    IN("fbhi", [128, 1]); IN("fblo", [128, 1]); IN("signrow", [128, 1]); IN("irow32", [1, 32])
    IN("c4096col", [8, 1]); IN("lb", [128, 128]); IN("bs", [128, 8])
    IN("exp16", [8, 128]); IN("esel128", [128, 16])

    out = nc.dram_tensor("out", [NTOK, D], F32, kind="ExternalOutput").ap()
    dbg = None
    if SI < 5:
        dbg = nc.dram_tensor("dbg", [NB * 128, 8 * T], F32, kind="ExternalOutput").ap()
    dbg3 = None
    if stage == "x2":
        dbg3 = nc.dram_tensor("dbg3", [2 * 128, 1024], F32, kind="ExternalOutput").ap()
    dbg4 = None
    dbg5 = None
    if stage == "x3":
        dbg4 = nc.dram_tensor("dbg4", [128, 4096], F32, kind="ExternalOutput").ap()
        dbg5 = nc.dram_tensor("dbg5", [128, 2048], F32, kind="ExternalOutput").ap()
    dbg6 = None
    dbg7 = None
    if stage == "m1":
        dbg6 = nc.dram_tensor("dbg6", [512, 1024], F32, kind="ExternalOutput").ap()
        dbg7 = nc.dram_tensor("dbg7", [128, 96], U32, kind="ExternalOutput").ap()
    dbg2 = None
    if stage == "route":
        dbg2 = nc.dram_tensor("dbg2", [6 * 128, 256], F32, kind="ExternalOutput").ap()

    wq1p = nc.dram_tensor("wq1p", [D, D], F32).ap()
    wkv1p = nc.dram_tensor("wkv1p", [D, 2 * D], F32).ap()
    park1 = nc.dram_tensor("park1", [128, 8, T], F32).ap()
    feat_tm = nc.dram_tensor("feat_tm", [NTOK, D], F32).ap()
    lnf_tm = nc.dram_tensor("lnf_tm", [NTOK, D], F32).ap()
    agin = nc.dram_tensor("agin", [CEXP, D], F32)
    agout = nc.dram_tensor("agout", [NC8 * CEXP, D], F32)
    scr_blk = nc.dram_tensor("scr_blk", [8, NTOK], F32)
    scr_g = nc.dram_tensor("scr_g", [NC8, 8, NTOK], F32)
    h_loc = nc.dram_tensor("h_loc", [CEXP, D], F32)
    h_g = nc.dram_tensor("h_g", [NC8 * CEXP, D], F32)

    xbuf = nc.sbuf_tensor("xbuf", [128, 16384], F32).__enter__()
    accb = nc.sbuf_tensor("accb", [128, 16384], F32).__enter__()
    qkb = nc.sbuf_tensor("qkb", [128, 4096], F32).__enter__()
    vtb = nc.sbuf_tensor("vtb", [128, 2080], F32).__enter__()
    fscr = nc.sbuf_tensor("fscr", [128, 4096], F32).__enter__()
    gscr = nc.sbuf_tensor("gscr", [128, 4096], F32).__enter__()
    wbuf = nc.sbuf_tensor("wbuf", [128, 4096], F32).__enter__()
    small = nc.sbuf_tensor("small", [128, 1024], F32).__enter__()
    u32b = nc.sbuf_tensor("u32b", [128, 96], U32).__enter__()
    cones = nc.sbuf_tensor("cones", [128, 128], F32).__enter__()
    clmask = nc.sbuf_tensor("clmask", [128, 128], F32).__enter__()
    cident = nc.sbuf_tensor("cident", [128, 128], F32).__enter__()
    cmisc = nc.sbuf_tensor("cmisc", [128, 48], F32).__enter__()
    clb = nc.sbuf_tensor("clb", [128, 128], F32).__enter__()
    cexp16 = nc.sbuf_tensor("cexp16", [8, 128], F32).__enter__()
    cesel128 = nc.sbuf_tensor("cesel128", [128, 16], F32).__enter__()
    ccent = nc.sbuf_tensor("ccent", [128, 64], F32).__enter__()
    cesel = nc.sbuf_tensor("cesel", [8, 1], F32).__enter__()

    ps = [nc.psum_tensor("psb%d" % i, [128, 512], F32).__enter__() for i in range(8)]

    kb = KB(nc)

    S_CNB1, S_CNB2, S_CNB3, S_GCB1, S_GCB2 = 0, 8, 12, 14, 18
    S_CELL = 20
    S_LNW1, S_LNB1, S_LNG, S_LNB2 = 40, 48, 56, 64
    S_BCQ, S_BCK, S_BCV = 72, 80, 88
    S_BVROW = 96
    S_MB1, S_MB2 = 224, 256
    S_P1 = 264
    S_ROWS = 296

    def cell(i):
        return small[0:1, S_CELL + i:S_CELL + i + 1]

    def p1(i):
        return small[:, S_P1 + i:S_P1 + i + 1]

    def srow(i, w=512):
        return small[0:1, S_ROWS + i * 512:S_ROWS + i * 512 + w]

    def xs(kc, lo, hi):
        return xbuf[:, kc * T + lo: kc * T + hi]

    def accs(kc, lo, hi):
        return accb[:, kc * T + lo: kc * T + hi]

    kb.load(cones[:, :], di["ones"])
    kb.load(clmask[:, :], di["lmask"])
    kb.load(cident[:, :], di["ident"])
    kb.load(cmisc[:, 44:45], di["fbhi"])
    kb.load(cmisc[:, 45:46], di["fblo"])
    kb.load(cmisc[:, 0:1], di["signrow"])
    kb.load(clb[:, :], di["lb"])
    kb.load(cmisc[0:8, 46:47], di["c4096col"])
    kb.load(cmisc[:, 1:9], di["bs"])
    kb.load(cexp16[:, :], di["exp16"])
    kb.load(cesel128[:, :], di["esel128"])
    kb.load(cmisc[0:1, 9:41], di["irow32"])
    kb.load(small[:, S_CNB1:S_CNB1 + 8], di["cnb1"])
    kb.load(small[:, S_CNB2:S_CNB2 + 4], di["cnb2"])
    kb.load(small[:, S_CNB3:S_CNB3 + 2], di["cnb3"])
    kb.load(small[:, S_GCB1:S_GCB1 + 4], di["gcb1"])
    kb.load(small[:, S_GCB2:S_GCB2 + 2], di["gcb2"])
    kb.load(cell(0), di["cnb4"])
    kb.load(cell(1), di["gcb3"])
    kb.load(cell(2), di["ar"])
    kb.load(cell(3), di["theta"])
    kb.load(small[:, S_LNW1:S_LNW1 + 8], di["lnw1"])
    kb.load(small[:, S_LNB1:S_LNB1 + 8], di["lnb1"])
    kb.load(small[:, S_LNG:S_LNG + 8], di["lng"])
    kb.load(small[:, S_LNB2:S_LNB2 + 8], di["lnb2"])
    kb.load(small[:, S_MB1:S_MB1 + 32], di["mb1"])
    kb.load(small[:, S_MB2:S_MB2 + 8], di["mb2"])
    kb.load(cesel[:, :], di["esel"])

    kb.load(ccent[:, 0:64].rearrange("p (k n) -> p k n", k=8), di["centT"].rearrange("(k p) n -> p k n", p=128))
    kb.barrier()

    first = [True]
    oc_list = [("q", oc) for oc in range(8)] + [("kv", oc) for oc in range(16)]
    for idx, (which, oc) in enumerate(oc_list):
        wsrc = di["wq1"] if which == "q" else di["wkv1"]
        wdst = wq1p if which == "q" else wkv1p
        half = idx % 2
        wb = wbuf[:, half * 2048: half * 2048 + 1024]
        src = wsrc.rearrange("(k p) n -> p k n", p=128)[:, :, oc * 128:(oc + 1) * 128]
        kb.load(wb.rearrange("p (k n) -> p k n", k=8), src)
        kb.wait("tensor", "LD")
        col = oc if which == "q" else 8 + oc
        for kc in range(8):
            kb.mm(ps[0][:, col:col + 1], wb[:, kc * 128:(kc + 1) * 128],
                  small[:, S_LNB1 + kc:S_LNB1 + kc + 1],
                  start=first[0], stop=(idx == len(oc_list) - 1 and kc == 7))
            first[0] = False
        kb.tmark()
        kb.wait("vector", "TT")
        for kc in range(8):
            kb.ts(wb[:, kc * 128:(kc + 1) * 128], wb[:, kc * 128:(kc + 1) * 128],
                  small[:, S_LNW1 + kc:S_LNW1 + kc + 1], None, Alu.mult)
        kb.wait("gpsimd", "VV")
        kb.store(wdst.rearrange("(k p) n -> p k n", p=128)[:, :, oc * 128:(oc + 1) * 128],
                 wb.rearrange("p (k n) -> p k n", k=8))
        kb.wait("sync", "SD")
    kb.wait("scalar", "TT")
    kb.scopy(small[:, S_BCQ:S_BCQ + 24], ps[0][:, 0:24])
    kb.barrier()

    kb.memset(cmisc[:, 42:43], 1.0)
    kb.memset(cmisc[:, 43:44], PIO2)
    RA = small[0:1, 808:816]
    RB = small[0:1, 816:824]
    kb.memset(RA[0:1, 0:1], 220.0)
    kb.recip(RB, RA)
    kb.tt(cell(8), cell(3), RB[0:1, 0:1], Alu.mult)
    kb.tt(p1(15), cmisc[:, 44:45], cmisc[:, 45:46], Alu.add)
    kb.wait("tensor", "VV")
    kb.mm(ps[3][:, 0:1], cones[0:1, :], cell(8), mark=True)
    kb.wait("vector", "TT")
    kb.tt(p1(1), p1(15), ps[3][:, 0:1], Alu.mult)
    kb.iota(fscr[:, 0:2048], [[1, 2048]], 0, 0)
    kb.wait("vector", "GG")
    kb.ts(qkb[:, 2048:4096], fscr[:, 0:2048], p1(1), None, Alu.mult)
    kb.ts(fscr[:, 2048:4096], qkb[:, 2048:4096], INV2PI, 0.5, Alu.mult, Alu.add)
    kb.vcopy(qkb[:, 0:2048], fscr[:, 2048:4096])
    for sh in range(16, -1, -1):
        c = float(2 ** sh)
        kb.ts(fscr[:, 0:2048], fscr[:, 2048:4096], c, None, Alu.is_ge)
        kb.stt(fscr[:, 2048:4096], fscr[:, 0:2048], -c, fscr[:, 2048:4096], Alu.mult, Alu.add)
    kb.tt(fscr[:, 0:2048], qkb[:, 0:2048], fscr[:, 2048:4096], Alu.subtract)
    kb.stt(fscr[:, 2048:4096], fscr[:, 0:2048], -CW1, qkb[:, 2048:4096], Alu.mult, Alu.add)
    kb.stt(fscr[:, 2048:4096], fscr[:, 0:2048], -CW2, fscr[:, 2048:4096], Alu.mult, Alu.add)
    kb.stt(fscr[:, 2048:4096], fscr[:, 0:2048], -CW3, fscr[:, 2048:4096], Alu.mult, Alu.add)
    kb.wait("scalar", "VV")
    kb.act(gscr[:, 2048:4096], fscr[:, 2048:4096], AF.Sin)
    kb.act(fscr[:, 0:2048], fscr[:, 2048:4096], AF.Abs)
    kb.act(gscr[:, 0:2048], fscr[:, 0:2048], AF.Sin, bias=cmisc[:, 43:44], scale=-1.0)
    kb.wait("vector", "SS")
    kb.ts(gscr[:, 2048:4096], gscr[:, 2048:4096], cmisc[:, 0:1], None, Alu.mult)
    kb.barrier()
    if dbg4 is not None:
        kb.store(dbg4, gscr[:, :])
        kb.wait("sync", "SD")
    kb.barrier()

    def wrow(i, w=512):
        return wbuf[0:1, i * 512:i * 512 + w]

    def fm_ln(src, dst, gamma=None, beta=None):
        for tc in range(4):
            lo, hi = tc * 512, tc * 512 + 512
            kb.wait("tensor", "VV")
            for kc in range(8):
                sq = fscr[:, (kc % 2) * 512:(kc % 2) * 512 + 512]
                if kc >= 2:
                    kb.wait("scalar", "TT")
                kb.act(sq, src(kc, lo, hi), AF.Square)
                kb.wait("tensor", "SS")
                kb.mm(ps[4][0:1, :], cones[:, 0:1], src(kc, lo, hi),
                      start=(kc == 0), stop=(kc == 7))
                kb.mm(ps[5][0:1, :], cones[:, 0:1], sq,
                      start=(kc == 0), stop=(kc == 7), mark=True)
            kb.wait("vector", "TT")
            kb.ts(wrow(0), ps[4][0:1, :], 1.0 / 1024, None, Alu.mult)
            kb.ts(wrow(1), ps[5][0:1, :], 1.0 / 1024, None, Alu.mult)
            kb.tt(wrow(2), wrow(0), wrow(0), Alu.mult)
            kb.tt(wrow(1), wrow(1), wrow(2), Alu.subtract)
            kb.ts(wrow(1), wrow(1), EPS, None, Alu.add)
            kb.wait("scalar", "VV")
            kb.act(wrow(2), wrow(1), AF.Ln)
            kb.act(wrow(2), wrow(2), AF.Exp, scale=-0.5)
            kb.wait("vector", "SS")
            kb.tt(wrow(3), wrow(2), wrow(2), Alu.mult)
            kb.tt(wrow(3), wrow(3), wrow(1), Alu.mult)
            kb.ts(wrow(3), wrow(3), -0.5, 1.5, Alu.mult, Alu.add)
            kb.tt(wrow(2), wrow(2), wrow(3), Alu.mult)
            kb.tt(wrow(0), wrow(0), wrow(2), Alu.mult)
            kb.wait("tensor", "VV")
            kb.mm(ps[6][:, :], cones[0:1, :], wrow(2), mark=True)
            kb.mm(ps[7][:, :], cones[0:1, :], wrow(0), mark=True)
            kb.wait("vector", "TT")
            for kc in range(8):
                tmp = fscr[:, 1024:1536]
                kb.tt(tmp, src(kc, lo, hi), ps[6][:, :], Alu.mult)
                kb.tt(dst(kc, lo, hi), tmp, ps[7][:, :], Alu.subtract)
                if gamma is not None:
                    kb.ts(dst(kc, lo, hi), dst(kc, lo, hi),
                          small[:, gamma + kc:gamma + kc + 1],
                          small[:, beta + kc:beta + kc + 1], Alu.mult, Alu.add)

    def attn_block(b, layer):
        attn_block.b = b
        wom = [0, 0]
        qkmm, vmm, womtt = [], [], []
        for hp in range(8):
            if hp == 0:
                kb.wait("sync", "TT")
                kb.wait("sync", "VV")
            else:
                kb.wait("sync", "TT", qkmm[hp - 1])
            if layer == 0:
                src4 = di["wqkv"].rearrange("(k p) n -> p k n", p=128)
                def l0w(j, base):
                    dst4 = wbuf[:, j * 1024:(j + 1) * 1024].rearrange("p (k h c) -> p k h c", k=8, h=2)
                    for hh in range(2):
                        kb.load(dst4[:, :, hh, :], src4[:, :, base + hh * 192:base + hh * 192 + 64])
                l0w(0, 2 * hp * 192)
                l0w(1, 2 * hp * 192 + 64)
                if hp > 0:
                    kb.wait("sync", "TT", vmm[hp - 1])
                l0w(2, 2 * hp * 192 + 128)
                ldqkv = kb.cnt["LD"]
                if hp > 0:
                    kb.wait("sync", "TT", womtt[hp - 1])
                kb.load(wbuf[:, 3072:4096], di["wo0"][hp * 128:(hp + 1) * 128, :])
            else:
                kb.load(wbuf[:, 0:1024].rearrange("p (k n) -> p k n", k=8),
                        wq1p.rearrange("(k p) n -> p k n", p=128)[:, :, hp * 128:(hp + 1) * 128])
                kb.load(wbuf[:, 1024:2048].rearrange("p (k n) -> p k n", k=8),
                        wkv1p.rearrange("(k p) n -> p k n", p=128)[:, :, hp * 128:(hp + 1) * 128])
                if hp > 0:
                    kb.wait("sync", "TT", vmm[hp - 1])
                kb.load(wbuf[:, 2048:3072].rearrange("p (k n) -> p k n", k=8),
                        wkv1p.rearrange("(k p) n -> p k n", p=128)[:, :, 1024 + hp * 128:1024 + hp * 128 + 128])
                ldqkv = kb.cnt["LD"]
                if hp > 0:
                    kb.wait("sync", "TT", womtt[hp - 1])
                kb.load(wbuf[:, 3072:4096], di["wo1"][hp * 128:(hp + 1) * 128, :])
            ldwo = kb.cnt["LD"]
            if layer == 1:
                kb.store(small[0:1, S_BVROW:S_BVROW + 128], small[:, S_BCV + hp:S_BCV + hp + 1])
            kb.wait("tensor", "LD", ldqkv)
            for tc in range(4):
                lo, hi = tc * 512, tc * 512 + 512
                if tc > 0:
                    kb.wait("tensor", "SS")
                for kc in range(8):
                    kb.mm(ps[0][:, :], wbuf[:, kc * 128:kc * 128 + 128], xs(kc, lo, hi),
                          start=(kc == 0), stop=(kc == 7))
                for kc in range(8):
                    kb.mm(ps[1][:, :], wbuf[:, 1024 + kc * 128:1024 + kc * 128 + 128], xs(kc, lo, hi),
                          start=(kc == 0), stop=(kc == 7), mark=True)
                kb.wait("scalar", "TT")
                if layer == 0:
                    kb.scopy(qkb[:, lo:hi], ps[0][:, :])
                    kb.scopy(qkb[:, 2048 + lo:2048 + hi], ps[1][:, :])
                else:
                    kb.act(qkb[:, lo:hi], ps[0][:, :], AF.Identity,
                           bias=small[:, S_BCQ + hp:S_BCQ + hp + 1])
                    kb.act(qkb[:, 2048 + lo:2048 + hi], ps[1][:, :], AF.Identity,
                           bias=small[:, S_BCK + hp:S_BCK + hp + 1])
            qkmm.append(kb.cnt["TT"])
            kb.memset(vtb[:, :], 1.0)
            kb.wait("tensor", "VV")
            kb.wait("scalar", "VV")
            if layer == 1:
                kb.wait("tensor", "SD")
            vt3 = vtb.rearrange("p (t c) -> p t c", c=130)
            for tkc in range(16):
                bank = ps[2 + (tkc % 2)]
                if tkc >= 2:
                    kb.wait("tensor", "SS")
                for kc in range(8):
                    kb.mm(bank[:, 0:128], xbuf[:, kc * T + tkc * 128:kc * T + tkc * 128 + 128],
                          wbuf[:, 2048 + kc * 128:2048 + kc * 128 + 128],
                          start=(kc == 0), stop=(kc == 7 and layer == 0))
                if layer == 1:
                    kb.mm(bank[:, 0:128], cones[0:1, 0:128],
                          small[0:1, S_BVROW:S_BVROW + 128], start=False, stop=True)
                kb.tmark()
                kb.wait("scalar", "TT")
                kb.scopy(vt3[:, tkc, 0:64], bank[:, 0:64])
                kb.scopy(vt3[:, tkc, 65:129], bank[:, 64:128])
            vmm.append(kb.cnt["TT"])
            if layer == 1:
                kb.wait("vector", "SS")
                for base in (0, 2048):
                    q = qkb[:, base:base + 2048]
                    kb.shuf(fscr[:, 0:2048], q, [i ^ 1 for i in range(32)])
                    kb.tt(q, q, gscr[:, 0:2048], Alu.mult)
                    kb.tt(fscr[:, 0:2048], fscr[:, 0:2048], gscr[:, 2048:4096], Alu.mult)
                    kb.tt(q, q, fscr[:, 0:2048], Alu.add)
                kb.wait("tensor", "VV")
            if layer == 1:
                kb.wait("tensor", "SS")
                for h, bank, dst in ((0, ps[6], 10), (1, ps[7], 11)):
                    for tkc in range(16):
                        kb.mm(bank[0:65, 0:1], vt3[:, tkc, h * 65:h * 65 + 65], cones[:, 0:1],
                              start=(tkc == 0), stop=(tkc == 15))
                    kb.tmark()
                kb.wait("scalar", "TT")
                kb.scopy(p1(10)[0:65, :], ps[6][0:65, 0:1])
                kb.scopy(p1(11)[0:65, :], ps[7][0:65, 0:1])
            scale = (1.0 / 32.0) if layer == 0 else 0.125
            for qt in range(4):
                kmax = 4 * qt + 4 if layer == 0 else 16
                kb.wait("tensor", "VV")
                kb.wait("tensor", "TA")
                sevac = []
                avm = []
                for kc in range(kmax):
                    sb0 = ps[0] if kc % 2 == 0 else ps[2]
                    sb1 = ps[1] if kc % 2 == 0 else ps[3]
                    if kc >= 2:
                        kb.wait("tensor", "SS", sevac[kc - 2])
                    kb.mm(sb0[:, :], qkb[0:64, 2048 + kc * 128:2048 + kc * 128 + 128],
                          qkb[0:64, qt * 512:qt * 512 + 512], tp=(0, 0))
                    kb.tmark()
                    kb.mm(sb1[:, :], qkb[64:128, 2048 + kc * 128:2048 + kc * 128 + 128],
                          qkb[64:128, qt * 512:qt * 512 + 512], tp=(64, 0))
                    kb.tmark()
                    if kc >= 2:
                        kb.wait("scalar", "TA", avm[kc - 2])
                    kb.wait("scalar", "TT")
                    p0 = fscr[:, (kc % 2) * 512:(kc % 2) * 512 + 512]
                    p1_ = fscr[:, 1024 + (kc % 2) * 512:1024 + (kc % 2) * 512 + 512]
                    if layer == 0:
                        kb.act(p0, sb0[:, :], AF.Exp, scale=scale)
                        kb.act(p1_, sb1[:, :], AF.Exp, scale=scale)
                        if kc >= 4 * qt:
                            kb.wait("gpsimd", "SS")
                            kb.affine(p0, p0, [[1, 512]], Alu.is_ge, 0.0,
                                      qt * 512 - kc * 128, -1)
                            kb.affine(p1_, p1_, [[1, 512]], Alu.is_ge, 0.0,
                                      qt * 512 - kc * 128, -1)
                            kb.wait("tensor", "GG")
                    else:
                        kb.act(p0, sb0[:, :], AF.Square, scale=scale, bias=cmisc[:, 42:43])
                        kb.act(p1_, sb1[:, :], AF.Square, scale=scale, bias=cmisc[:, 42:43])
                    kb.wait("tensor", "SS")
                    kb.mm(ps[4][0:65, :], vt3[:, kc, 0:65], p0,
                          start=(kc == 0), stop=(kc == kmax - 1))
                    kb.mm(ps[5][0:65, :], vt3[:, kc, 65:130], p1_,
                          start=(kc == 0), stop=(kc == kmax - 1), sem="TA")
                    avm.append(kb.cnt["TA"])
                    sevac.append(kb.cnt["SS"])
                kb.wait("vector", "TT")
                kb.wait("vector", "TA")
                for h, ob in ((0, ps[4]), (1, ps[5])):
                    rr = small[0:1, S_ROWS:S_ROWS + 512]
                    if h == 1:
                        kb.wait("vector", "TT")
                    if layer == 0:
                        kb.recip(rr, ob[64:65, :])
                    else:
                        kb.ts(rr, ob[64:65, :], 2048.0, None, Alu.add)
                        kb.recip(rr, rr)
                    kb.wait("tensor", "VV")
                    rb = ps[6] if h == 0 else ps[7]
                    kb.mm(rb[0:64, :], cones[0:1, 0:64], rr, mark=True)
                    kb.wait("scalar", "TT")
                    kb.scopy(fscr[0:64, 0:512], rb[0:64, :])
                    kb.wait("vector", "SS")
                    dst = fscr[h * 64:h * 64 + 64, 2048 + qt * 512:2048 + qt * 512 + 512]
                    if layer == 0:
                        kb.tt(dst, ob[0:64, :], fscr[0:64, 0:512], Alu.mult)
                    else:
                        kb.stt(dst, ob[0:64, :], p1(10 + h)[0:64, :],
                               fscr[0:64, 0:512], Alu.add, Alu.mult)
            kb.wait("tensor", "LD", ldwo)
            kb.wait("tensor", "VV")
            if layer == 1 and hp == 0 and attn_block.b == 0 and dbg5 is not None:
                kb.wait("gpsimd", "VV")
                kb.store(dbg5, fscr[:, 2048:4096])
                kb.wait("tensor", "SD")
            for tc in range(4):
                for oc in range(8):
                    bank = ps[2 + (oc % 2)]
                    if tc * 8 + oc >= 2:
                        kb.wait("tensor", "VV", wom[-2])
                    kb.mm(bank[:, :], wbuf[:, 3072 + oc * 128:3072 + oc * 128 + 128],
                          fscr[:, 2048 + tc * 512:2048 + tc * 512 + 512], mark=True)
                    kb.wait("vector", "TT")
                    src1 = xs(oc, tc * 512, tc * 512 + 512) if (layer == 0 and hp == 0) \
                        else accs(oc, tc * 512, tc * 512 + 512)
                    kb.tt(accs(oc, tc * 512, tc * 512 + 512), bank[:, :], src1, Alu.add)
                    wom.append(kb.cnt["VV"])
            womtt.append(kb.cnt["TT"])

    def ffn_block(b):
        ffn_block.b = b
        x3 = xbuf.rearrange("p (k t) -> p k t", k=8)
        kb.wait("gpsimd", "VV")
        kb.wait("gpsimd", "SS")
        kb.store(park1, x3)
        kb.vcopy(accb[:, :], xbuf[:, :])
        for kc in range(8):
            kb.red(p1(24 + kc), xs(kc, 0, T), Alu.add)
        kb.ts(small[:, S_P1 + 24:S_P1 + 32], small[:, S_P1 + 24:S_P1 + 32],
              1.0 / T, None, Alu.mult)
        if dbg3 is not None and ffn_block.b == 0:
            kb.wait("gpsimd", "VV")
            kb.store(dbg3[0:128, :], small[:, 0:1024])
        kb.wait("sync", "TT")
        kb.load(wbuf[:, 0:4096].rearrange("p (k n) -> p k n", k=8),
                di["gcw1"].rearrange("(k p) n -> p k n", p=128))
        kb.wait("tensor", "LD")
        kb.wait("tensor", "VV")
        first = True
        for oc in range(4):
            for kc in range(8):
                kb.mm(ps[0][:, oc:oc + 1], wbuf[:, kc * 512 + oc * 128:kc * 512 + oc * 128 + 128],
                      p1(24 + kc), start=first, stop=(oc == 3 and kc == 7))
                first = False
        kb.tmark()
        kb.wait("scalar", "TT")
        for oc in range(4):
            kb.act(p1(16 + oc), ps[0][:, oc:oc + 1], AF.Gelu,
                   bias=small[:, S_GCB1 + oc:S_GCB1 + oc + 1])
        kb.wait("sync", "TT")
        kb.load(wbuf[:, 0:1024].rearrange("p (k n) -> p k n", k=4),
                di["gcw2"].rearrange("(k p) n -> p k n", p=128))
        kb.wait("tensor", "LD")
        kb.wait("tensor", "SS")
        first = True
        for oc in range(2):
            for kc in range(4):
                kb.mm(ps[1][:, oc:oc + 1], wbuf[:, kc * 256 + oc * 128:kc * 256 + oc * 128 + 128],
                      p1(16 + kc), start=first, stop=(oc == 1 and kc == 3))
                first = False
        kb.tmark()
        kb.wait("scalar", "TT")
        for oc in range(2):
            kb.act(p1(20 + oc), ps[1][:, oc:oc + 1], AF.Gelu,
                   bias=small[:, S_GCB2 + oc:S_GCB2 + oc + 1])
        kb.wait("sync", "TT")
        kb.load(wbuf[:, 0:2].rearrange("p (k n) -> p k n", k=2),
                di["gcw3"].rearrange("(k p) n -> p k n", p=128), slow=True)
        kb.wait("tensor", "LD")
        kb.wait("tensor", "SS")
        for kc in range(2):
            kb.mm(ps[2][0:1, 0:1], wbuf[:, kc:kc + 1], p1(20 + kc),
                  start=(kc == 0), stop=(kc == 1))
        kb.tmark()
        kb.wait("scalar", "TT")
        kb.act(cell(4), ps[2][0:1, 0:1], AF.Sigmoid, bias=cell(1))
        kb.wait("vector", "SS")
        kb.ts(cell(5), cell(4), 0.7, None, Alu.mult)
        kb.wait("sync", "SD")
        kb.wait("sync", "VV")
        kb.load(xbuf[:, 8192:16384].rearrange("p (k n) -> p k n", k=8),
                di["cnw1"].rearrange("(k p) n -> p k n", p=128))
        kb.load(qkb[:, 0:4096].rearrange("p (k n) -> p k n", k=8),
                di["cnw2"].rearrange("(k p) n -> p k n", p=128))
        kb.load(vtb[:, 0:1024].rearrange("p (k n) -> p k n", k=4),
                di["cnw3"].rearrange("(k p) n -> p k n", p=128))
        kb.load(vtb[:, 1024:1026].rearrange("p (k n) -> p k n", k=2),
                di["cnw4"].rearrange("(k p) n -> p k n", p=128), slow=True)
        kb.wait("tensor", "LD")
        for cyc in range(3):
            for tc in range(4):
                lo, hi = tc * 512, tc * 512 + 512
                kb.wait("tensor", "VV")
                kb.wait("tensor", "SS")
                h1e = []
                for oc in range(8):
                    bank = ps[oc % 2]
                    if oc >= 2:
                        kb.wait("tensor", "SS", h1e[oc - 2])
                    for kc in range(8):
                        kb.mm(bank[:, :], xbuf[:, 8192 + kc * 1024 + oc * 128:8192 + kc * 1024 + oc * 128 + 128],
                              accs(kc, lo, hi), start=(kc == 0), stop=(kc == 7))
                    kb.tmark()
                    kb.wait("scalar", "TT")
                    kb.act(xbuf[:, oc * 512:oc * 512 + 512], bank[:, :], AF.Gelu,
                           bias=small[:, S_CNB1 + oc:S_CNB1 + oc + 1])
                    h1e.append(kb.cnt["SS"])
                for oc in range(4):
                    bank = ps[2 + oc % 2]
                    if oc >= 2:
                        kb.wait("tensor", "SS")
                    for kc in range(8):
                        kb.mm(bank[:, :], qkb[:, kc * 512 + oc * 128:kc * 512 + oc * 128 + 128],
                              xbuf[:, kc * 512:kc * 512 + 512], start=(kc == 0), stop=(kc == 7))
                    kb.tmark()
                    kb.wait("scalar", "TT")
                    kb.act(xbuf[:, 4096 + oc * 512:4096 + oc * 512 + 512], bank[:, :], AF.Gelu,
                           bias=small[:, S_CNB2 + oc:S_CNB2 + oc + 1])
                for oc in range(2):
                    bank = ps[4 + oc % 2]
                    for kc in range(4):
                        kb.mm(bank[:, :], vtb[:, kc * 256 + oc * 128:kc * 256 + oc * 128 + 128],
                              xbuf[:, 4096 + kc * 512:4096 + kc * 512 + 512],
                              start=(kc == 0), stop=(kc == 3))
                    kb.tmark()
                    kb.wait("scalar", "TT")
                    kb.act(xbuf[:, 6144 + oc * 512:6144 + oc * 512 + 512], bank[:, :], AF.Gelu,
                           bias=small[:, S_CNB3 + oc:S_CNB3 + oc + 1])
                for kc in range(2):
                    kb.mm(ps[6][0:1, :], vtb[:, 1024 + kc:1024 + kc + 1],
                          xbuf[:, 6144 + kc * 512:6144 + kc * 512 + 512],
                          start=(kc == 0), stop=(kc == 1))
                kb.tmark()
                kb.wait("scalar", "TT")
                kb.act(wrow(0), ps[6][0:1, :], AF.Sigmoid, bias=cell(0))
                kb.wait("vector", "SS")
                kb.ts(wrow(1), wrow(0), 0.3, cell(5), Alu.mult, Alu.add)
                kb.ts(wrow(1), wrow(1), 0.5, cell(2), Alu.subtract, Alu.mult)
                kb.ts(wrow(1), wrow(1), 2.0, 0.1, Alu.mult, Alu.add)
                kb.wait("tensor", "VV")
                kb.mm(ps[7][:, :], cones[0:1, :], wrow(1), mark=True)
                kb.wait("vector", "TT")
                for j in range(4):
                    sgn = 1.0 if j < 2 else -1.0
                    a = accs(j, lo, hi)
                    bb = accs(j + 4, lo, hi)
                    kb.tt(fscr[:, 0:512], bb, ps[7][:, :], Alu.mult)
                    kb.tt(fscr[:, 512:1024], a, ps[7][:, :], Alu.mult)
                    kb.stt(a, fscr[:, 0:512], sgn, a, Alu.mult, Alu.add)
                    kb.stt(bb, fscr[:, 512:1024], -sgn, bb, Alu.mult, Alu.add)
        for tc in range(4):
            lo, hi = tc * 512, tc * 512 + 512
            for kc in range(8):
                tile = fscr[:, 1024 + (kc % 2) * 512:1024 + (kc % 2) * 512 + 512]
                if tc * 8 + kc >= 2:
                    kb.wait("sync", "VV")
                kb.load(tile, park1[:, kc, lo:hi])
                kb.wait("vector", "LD")
                kb.tt(xs(kc, lo, hi), tile, accs(kc, lo, hi), Alu.add)
        kb.barrier()
        fm_ln(xs, xs)

    def l1_tail(b):
        fm_ln(xs, accs, gamma=S_LNG, beta=S_LNB2)
        kb.barrier()
        if _TSTOP <= 1:
            return
        for tc in range(4):
            lo, hi = tc * 512, tc * 512 + 512
            for kc in range(8):
                kb.mm(ps[0][0:8, :], ccent[:, kc * 8:kc * 8 + 8], accs(kc, lo, hi),
                      start=(kc == 0), stop=(kc == 7))
            kb.tmark()
            kb.wait("scalar", "TT")
            kb.scopy(fscr[0:8, 0:512], ps[0][0:8, :])
            kb.wait("gpsimd", "SS")
            j0 = b * T + tc * 512
            kb.store(scr_blk.ap()[:, j0:j0 + 512], fscr[0:8, 0:512])
            kb.wait("tensor", "SD")
        if _TSTOP <= 2:
            return
        NTG = 1 if _TSTOP == 4 else 16
        DOST = _TSTOP != 5
        for tg in range(NTG):
            for (srcb, dst_dram, off) in ((xbuf, feat_tm, 1024), (accb, lnf_tm, 2048)):
                for kc in range(8):
                    bank = ps[1 + kc % 2]
                    kb.wait("tensor", "SS")
                    kb.transp(bank[:, 0:128], srcb[:, kc * T + tg * 128:kc * T + tg * 128 + 128],
                              cident[:, :], mark=True)
                    kb.wait("scalar", "TT")
                    kb.scopy(fscr[:, off + kc * 128:off + kc * 128 + 128], bank[:, 0:128])
                if DOST:
                    kb.wait("gpsimd", "SS")
                    r0 = (b * 16 + tg) * 128
                    kb.store(dst_dram[r0:r0 + 128, :], fscr[:, off:off + 1024])
            if DOST:
                kb.wait("scalar", "SD")

    def cross_reduce(dst, src, op, bA, bB):
        kb.wait("gpsimd", "VV")
        kb.store(small[0:1, 680:808], src)
        kb.wait("vector", "SD")
        kb.red(cell(14), small[0:1, 680:808], op)
        kb.wait("tensor", "VV")
        kb.mm(bB[:, 0:1], cones[0:1, :], cell(14), mark=True)
        kb.wait("vector", "TT")
        kb.vcopy(dst, bB[:, 0:1])

    def routing():
        G = lambda c0, w: gscr[:, c0:c0 + w]
        s_all = lambda e: gscr[:, e * 256:e * 256 + 256]
        mem = lambda e: fscr[:, e * 256:e * 256 + 256]
        s_cur = G(2048, 256)
        dump = G(2304, 256)
        eq_t = G(2560, 256)
        gt_t = G(2816, 256)
        u_t = G(3072, 256)
        taken = G(3328, 256)
        scn = G(3584, 256)
        zrow = qkb[0:1, 2048:2304]

        if _RSTOP <= -1:
            return
        kb.cc("AllGather", scr_blk.ap(), scr_g.ap())
        kb.wait("sync", "CC")
        if _RSTOP <= -2:
            return
        for c in range(8):
            kb.load(gscr[c * 16:(c + 1) * 16, 0:2048].rearrange("p (e f) -> p e f", e=8),
                    scr_g.ap().rearrange("c e (pr f) -> c pr e f", pr=16)[c, :, :, :])
        kb.memset(zrow, 0.0)
        kb.memset(taken, 0.0)
        kb.wait("vector", "LD")
        if _RSTOP <= 0:
            return
        LO, HI, TH, PRED, CNT, M_, VREP, NOTP, D_ = (p1(0), p1(1), p1(2), p1(3),
                                                     p1(4), p1(5), p1(6), p1(7), p1(8))
        for e in range(8):
            kb.wait("tensor", "VV")
            if e == 0:
                kb.vcopy(s_cur, s_all(0))
            elif e < 7:
                kb.stt(s_cur, taken, -2e30, s_all(e), Alu.mult, Alu.add)
            if e == 7:
                kb.ts(mem(7), taken, -1.0, 1.0, Alu.mult, Alu.add)
            else:
                kb.stt(u_t, taken, 4e30, s_cur, Alu.mult, Alu.add)
                kb.red(D_, u_t, Alu.min)
                cross_reduce(LO, D_, Alu.min, ps[0], ps[1])
                kb.red(D_, s_cur, Alu.max)
                cross_reduce(HI, D_, Alu.max, ps[0], ps[1])
                for _ in range(30):
                    kb.ts(TH, LO, HI, 0.5, Alu.add, Alu.mult)
                    kb.ts(dump, s_cur, TH, 0.0, Alu.is_ge, Alu.add, accum_out=CNT)
                    kb.wait("tensor", "VV")
                    kb.mm(ps[0][:, 0:1], cones[:, :], CNT, mark=True)
                    kb.wait("vector", "TT")
                    kb.ts(PRED, ps[0][:, 0:1], float(CEXP), None, Alu.is_ge)
                    kb.tt(D_, TH, LO, Alu.subtract)
                    kb.tt(D_, PRED, D_, Alu.mult)
                    kb.tt(LO, LO, D_, Alu.add)
                    kb.ts(NOTP, PRED, -1.0, 1.0, Alu.mult, Alu.add)
                    kb.tt(D_, TH, HI, Alu.subtract)
                    kb.tt(D_, NOTP, D_, Alu.mult)
                    kb.tt(HI, HI, D_, Alu.add)
                kb.ts(dump, s_cur, LO, None, Alu.is_lt)
                kb.stt(u_t, dump, 4e30, s_cur, Alu.mult, Alu.add)
                kb.red(D_, u_t, Alu.min)
                cross_reduce(VREP, D_, Alu.min, ps[0], ps[1])
                kb.ts(dump, s_cur, VREP, 0.0, Alu.is_gt, Alu.add, accum_out=CNT)
                kb.wait("tensor", "VV")
                kb.mm(ps[0][:, 0:1], cones[:, :], CNT, mark=True)
                kb.wait("vector", "TT")
                kb.ts(M_, ps[0][:, 0:1], -1.0, float(CEXP), Alu.mult, Alu.add)
                kb.ts(eq_t, s_cur, VREP, None, Alu.is_equal)
                kb.scan(scn, eq_t, eq_t, 0.0, Alu.add, Alu.bypass)
                kb.vcopy(p1(9), scn[:, 255:256])
                kb.tt(scn, scn, eq_t, Alu.subtract)
                kb.wait("tensor", "VV")
                kb.mm(ps[2][:, 0:1], clmask[:, :], p1(9), mark=True)
                kb.wait("vector", "TT")
                kb.ts(scn, scn, ps[2][:, 0:1], None, Alu.add)
                kb.ts(dump, scn, M_, None, Alu.is_lt)
                kb.ts(gt_t, s_cur, VREP, None, Alu.is_gt)
                kb.tt(eq_t, eq_t, dump, Alu.mult)
                kb.tt(mem(e), gt_t, eq_t, Alu.add)
            if e < 7:
                kb.tt(taken, taken, mem(e), Alu.add)
        if _RSTOP <= 1:
            return
        hrow_full = G(2048, 256)
        lp_full = G(2304, 256)
        kb.wait("vector", "TT")
        kb.memset(hrow_full, 0.0)
        kb.memset(lp_full, 0.0)
        bp8m = qkb[0:8, 2304:2312]
        st8m = qkb[0:8, 2312:2320]
        cnt8 = qkb[0:8, 2320:2328]
        run8 = qkb[0:8, 2328:2336]
        c1 = p1(10)
        c3 = p1(11)
        kb.memset(run8[:, 0:1], 0.0)
        for e in range(8):
            kb.wait("tensor", "VV")
            kb.scan(scn, mem(e), mem(e), 0.0, Alu.add, Alu.bypass)
            kb.vcopy(p1(9), scn[:, 255:256])
            kb.tt(scn, scn, mem(e), Alu.subtract)
            kb.wait("tensor", "VV")
            kb.mm(ps[2][:, 0:1], clmask[:, :], p1(9), start=True, stop=True)
            kb.mm(ps[3][:, 0:1], clb[:, :], p1(9), start=True, stop=True)
            kb.mm(ps[4][0:8, 0:1], cmisc[:, 1:9], p1(9), mark=True)
            kb.wait("vector", "TT")
            kb.vcopy(p1(12)[0:8, :], ps[4][0:8, 0:1])
            kb.vcopy(qkb[0:8, 2320 + e:2321 + e], p1(12)[0:8, :])
            kb.vcopy(st8m[:, e:e + 1], run8[:, 0:1])
            kb.tt(run8[:, 0:1], run8[:, 0:1], p1(12)[0:8, :], Alu.add)
            kb.wait("tensor", "VV")
            kb.mm(ps[5][0:8, 0:1], clmask[0:8, 0:8], p1(12)[0:8, :], mark=True)
            kb.wait("vector", "TT")
            kb.vcopy(bp8m[:, e:e + 1], ps[5][0:8, 0:1])
            kb.ts(u_t, scn, ps[2][:, 0:1], float(CEXP * e), Alu.add, Alu.add)
            kb.tt(taken, mem(e), u_t, Alu.mult)
            kb.tt(hrow_full, hrow_full, taken, Alu.add)
            kb.vcopy(c1, ps[2][:, 0:1])
            kb.tt(c1, c1, ps[3][:, 0:1], Alu.subtract)
            kb.wait("tensor", "VV")
            kb.mm(ps[6][:, 0:1], cexp16[:, :], st8m[:, e:e + 1], mark=True)
            kb.wait("vector", "TT")
            kb.tt(c3, c1, ps[6][:, 0:1], Alu.add)
            kb.ts(u_t, scn, c3, None, Alu.add)
            kb.tt(taken, mem(e), u_t, Alu.mult)
            kb.tt(lp_full, lp_full, taken, Alu.add)
        if _RSTOP <= 2:
            return
        kb.wait("tensor", "VV")
        kb.mm(ps[0][0:16, 0:256], cesel128[:, :], lp_full, mark=True)
        kb.mm(ps[1][0:16, 0:256], cesel128[:, :], hrow_full, mark=True)
        kb.wait("scalar", "TT")
        kb.scopy(gscr[0:16, 2048:2304], ps[0][0:16, 0:256])
        kb.scopy(gscr[0:16, 2304:2560], ps[1][0:16, 0:256])
        kb.wait("gpsimd", "SS")
        for k in range(32):
            pr, half = k // 2, (k % 2) * 128
            kb.store(fscr[:, 2048 + k:2048 + k + 1], gscr[pr:pr + 1, 2048 + half:2048 + half + 128])
            kb.store(fscr[:, 2080 + k:2080 + k + 1], gscr[pr:pr + 1, 2304 + half:2304 + half + 128])
        kb.wait("vector", "SD")
        kb.ts(u32b[:, 0:32], fscr[:, 2048:2080], 0.0, None, Alu.add)
        kb.ts(u32b[:, 32:64], fscr[:, 2080:2112], 0.0, None, Alu.add)
        if _RSTOP <= 3:
            return
        esr = qkb[0:1, 2336:2344]
        kb.wait("gpsimd", "VV")
        kb.store(esr, cesel[:, :])
        kb.wait("tensor", "SD")
        kb.mm(ps[7][0:8, 0:8], cones[0:1, 0:8], esr, mark=True)
        kb.wait("vector", "TT")
        esrep8 = qkb[0:8, 2344:2352]
        kb.vcopy(esrep8, ps[7][0:8, 0:8])
        cumown = p1(13)
        stown = p1(14)
        kb.memset(cumown[0:8, :], 0.0)
        kb.memset(stown[0:8, :], 0.0)
        for e in range(8):
            kb.stt(cumown[0:8, :], bp8m[:, e:e + 1], esrep8[:, e:e + 1], cumown[0:8, :],
                   Alu.mult, Alu.add)
            kb.stt(stown[0:8, :], st8m[:, e:e + 1], esrep8[:, e:e + 1], stown[0:8, :],
                   Alu.mult, Alu.add)
        lam = p1(15)
        kb.tt(lam[0:8, :], stown[0:8, :], cumown[0:8, :], Alu.subtract)
        kb.tt(lam[0:8, :], lam[0:8, :], cmisc[0:8, 46:47], Alu.add)
        kb.wait("gpsimd", "VV")
        kb.store(qkb[0:1, 2352:2360], lam[0:8, :])
        kb.store(qkb[0:1, 2360:2368], cumown[0:8, :])
        kb.wait("tensor", "SD")
        kb.mm(ps[6][:, 0:8], cones[0:1, :], qkb[0:1, 2352:2360])
        kb.mm(ps[7][:, 0:8], cones[0:1, :], qkb[0:1, 2360:2368], mark=True)
        kb.wait("vector", "TT")
        lamrep = G(3584, 8)
        cumrep = G(3592, 8)
        kb.vcopy(lamrep, ps[6][:, 0:8])
        kb.vcopy(cumrep, ps[7][:, 0:8])
        sgrid = G(3600, 32)
        kb.iota(sgrid, [[128, 32]], 0, 1)
        kb.wait("vector", "GG")
        acc32 = G(3632, 32)
        ge_t = lambda c: gscr[:, 3664 + c * 32:3664 + c * 32 + 32]
        kb.memset(acc32, 0.0)
        for c in range(8):
            kb.ts(ge_t(c), sgrid, cumrep[:, c:c + 1], None, Alu.is_ge)
        for c in range(8):
            if c < 7:
                kb.tt(s_cur[:, 0:32], ge_t(c), ge_t(c + 1), Alu.subtract)
                ind = s_cur[:, 0:32]
            else:
                ind = ge_t(7)
            kb.stt(acc32, ind, lamrep[:, c:c + 1], acc32, Alu.mult, Alu.add)
        kb.tt(acc32, sgrid, acc32, Alu.add)
        kb.ts(u32b[:, 64:96], acc32, 0.0, None, Alu.add)
        kb.barrier()
        if stage == "route" and _RSTOP >= 5:
            kb.store(dbg2[0 * 128:1 * 128, :], mem(0))
            kb.memset(gt_t, 0.0)
            kb.vcopy(gt_t[:, 0:64], fscr[:, 2048:2112])
            kb.wait("gpsimd", "VV")
            kb.store(dbg2[1 * 128:2 * 128, :], gt_t)
            kb.store(dbg2[2 * 128:3 * 128, :], mem(7))
            kb.store(dbg2[3 * 128:4 * 128, :], hrow_full)
            kb.store(dbg2[4 * 128:5 * 128, :], lp_full)
            kb.memset(dump, 0.0)
            kb.vcopy(dump[:, 0:32], acc32)
            kb.wait("gpsimd", "VV")
            kb.store(dbg2[5 * 128:6 * 128, :], dump)
            kb.wait("gpsimd", "SD")
        if _RSTOP <= 4:
            return
        for k in range(32):
            tile = fscr[:, (k % 2) * 1024:(k % 2) * 1024 + 1024]
            kb.wait("sync", "SD")
            kb.load(tile, lnf_tm[k * 128:k * 128 + 128, :])
            kb.wait("gpsimd", "LD")
            kb.ind_scatter(agin.ap(), tile, u32b[:, k:k + 1])
        if dbg7 is not None:
            kb.wait("gpsimd", "VV")
            kb.store(dbg7, u32b[:, :])
        kb.cc("AllGather", agin.ap(), agout.ap())
        if dbg6 is not None and _MSTOP == 1:
            kb.wait("gpsimd", "CC")
            kb.store(dbg6[0:256, :], agout.ap()[0:256, :])
            kb.store(dbg6[256:512, :], agout.ap()[CEXP * 3:CEXP * 3 + 256, :])

    def moe():
        kb.wait("gpsimd", "CC")
        kb.wait("tensor", "CC")
        mw1r = di["mw1"].rearrange("(k p) n -> p k n", p=128)
        for st in range(8):
            for g in range(4):
                tile = fscr[:, (g % 2) * 1024:(g % 2) * 1024 + 1024]
                kb.wait("gpsimd", "SS")
                kb.wait("gpsimd", "SD")
                kb.ind_gather(tile, agout.ap(), u32b[:, 64 + st * 4 + g:64 + st * 4 + g + 1])
                if dbg6 is not None and _MSTOP == 5 and st == 4 and g == 0:
                    kb.wait("gpsimd", "SD")
                    kb.store(dbg6[0:128, :], tile)
                    kb.wait("tensor", "SD")
                kb.wait("tensor", "SD")
                for kc in range(8):
                    bank = ps[kc % 2]
                    kb.wait("tensor", "SS")
                    kb.transp(bank[:, 0:128], tile[:, kc * 128:kc * 128 + 128],
                              cident[:, :], mark=True)
                    kb.wait("scalar", "TT")
                    kb.scopy(qkb[:, kc * 512 + g * 128:kc * 512 + g * 128 + 128],
                             bank[:, 0:128])
            if dbg6 is not None and _MSTOP == 3 and st == 4:
                kb.wait("gpsimd", "SS")
                kb.store(dbg6[0:128, :], qkb[:, 0:1024])
                kb.store(dbg6[128:256, :], qkb[:, 1024:2048])
                kb.store(dbg6[256:384, :], qkb[:, 2048:3072])
                kb.store(dbg6[384:512, :], qkb[:, 3072:4096])
                kb.wait("sync", "SD")
            w1evac = []
            w1mm = []
            for hc in range(32):
                wsl = accb[:, (hc % 2) * 1024:(hc % 2) * 1024 + 1024]
                if hc >= 2:
                    kb.wait("sync", "TT", w1mm[hc - 2])
                else:
                    kb.wait("sync", "TT")
                kb.load(wsl.rearrange("p (k n) -> p k n", k=8), mw1r[:, :, hc * 128:hc * 128 + 128])
                kb.wait("tensor", "LD")
                bank = ps[2 + hc % 2]
                if hc < 2:
                    kb.wait("tensor", "SS")
                else:
                    kb.wait("tensor", "SS", w1evac[hc - 2])
                for kc in range(8):
                    kb.mm(bank[:, :], wsl[:, kc * 128:kc * 128 + 128],
                          qkb[:, kc * 512:kc * 512 + 512], start=(kc == 0), stop=(kc == 7))
                kb.tmark()
                w1mm.append(kb.cnt["TT"])
                kb.wait("scalar", "TT")
                kb.act(xbuf[:, hc * 512:hc * 512 + 512], bank[:, :], AF.Gelu_apprx_tanh,
                       bias=small[:, S_MB1 + hc:S_MB1 + hc + 1])
                w1evac.append(kb.cnt["SS"])
            if dbg6 is not None and _MSTOP == 4 and st == 4:
                kb.wait("gpsimd", "SS")
                for q_ in range(4):
                    kb.store(dbg6[q_ * 128:(q_ + 1) * 128, :], xbuf[:, q_ * 1024:(q_ + 1) * 1024])
                kb.wait("sync", "SD")
            kb.wait("tensor", "SS")
            w2mm = []
            for hc in range(32):
                w2sl = accb[:, 2048 + (hc % 2) * 1024:2048 + (hc % 2) * 1024 + 1024]
                if hc >= 2:
                    kb.wait("sync", "TT", w2mm[hc - 2])
                else:
                    kb.wait("sync", "TT")
                kb.load(w2sl, di["mw2"][hc * 128:hc * 128 + 128, :])
                kb.wait("tensor", "LD")
                for oc in range(8):
                    kb.mm(ps[oc][:, :], w2sl[:, oc * 128:oc * 128 + 128],
                          xbuf[:, hc * 512:hc * 512 + 512],
                          start=(hc == 0), stop=(hc == 31))
                kb.tmark()
                w2mm.append(kb.cnt["TT"])
            kb.wait("scalar", "TT")
            for oc in range(8):
                kb.act(qkb[:, oc * 512:oc * 512 + 512], ps[oc][:, :], AF.Identity,
                       bias=small[:, S_MB2 + oc:S_MB2 + oc + 1])
            for g in range(4):
                tile = fscr[:, (g % 2) * 1024:(g % 2) * 1024 + 1024]
                if g >= 2:
                    kb.wait("scalar", "SD")
                for kc in range(8):
                    bank = ps[kc % 2]
                    kb.wait("tensor", "SS")
                    kb.transp(bank[:, 0:128], qkb[:, kc * 512 + g * 128:kc * 512 + g * 128 + 128],
                              cident[:, :], mark=True)
                    kb.wait("scalar", "TT")
                    kb.scopy(tile[:, kc * 128:kc * 128 + 128], bank[:, 0:128])
                kb.wait("gpsimd", "SS")
                kb.store(h_loc.ap()[st * 512 + g * 128:st * 512 + g * 128 + 128, :], tile)
        if dbg6 is not None and _MSTOP == 2:
            kb.wait("gpsimd", "SD")
            kb.store(dbg6[0:256, :], h_loc.ap()[0:256, :])
            kb.store(dbg6[256:512, :], h_loc.ap()[2048:2048 + 256, :])
        kb.cc("AllGather", h_loc.ap(), h_g.ap())

    def final():
        kb.wait("sync", "CC")
        kb.wait("gpsimd", "CC")
        for k in range(32):
            xt = fscr[:, (k % 2) * 1024:(k % 2) * 1024 + 1024]
            ht = fscr[:, 2048 + (k % 2) * 1024:2048 + (k % 2) * 1024 + 1024]
            if k >= 2:
                kb.wait("sync", "VV")
                kb.wait("gpsimd", "VV")
            kb.load(xt, feat_tm[k * 128:k * 128 + 128, :])
            kb.ind_gather(ht, h_g.ap(), u32b[:, 32 + k:32 + k + 1])
            kb.wait("vector", "LD")
            kb.wait("vector", "SD")
            u = gscr[:, (k % 2) * 1024:(k % 2) * 1024 + 1024]
            du = gscr[:, 2048 + (k % 2) * 1024:2048 + (k % 2) * 1024 + 1024]
            kb.stt(u, xt, 2.0, ht, Alu.mult, Alu.add)
            kb.ts(du, u, 0.0, 0.0, Alu.add, Alu.add, accum_out=p1(0))
            kb.stt(du, u, 1.0, u, Alu.mult, Alu.mult, accum_out=p1(1))
            kb.ts(p1(0), p1(0), 1.0 / 1024, None, Alu.mult)
            kb.ts(p1(1), p1(1), 1.0 / 1024, None, Alu.mult)
            kb.tt(p1(2), p1(0), p1(0), Alu.mult)
            kb.tt(p1(1), p1(1), p1(2), Alu.subtract)
            kb.ts(p1(1), p1(1), EPS, None, Alu.add)
            kb.wait("scalar", "VV")
            kb.act(p1(2), p1(1), AF.Ln)
            kb.act(p1(2), p1(2), AF.Exp, scale=-0.5)
            kb.wait("vector", "SS")
            kb.tt(p1(3), p1(2), p1(2), Alu.mult)
            kb.tt(p1(3), p1(3), p1(1), Alu.mult)
            kb.ts(p1(3), p1(3), -0.5, 1.5, Alu.mult, Alu.add)
            kb.tt(p1(2), p1(2), p1(3), Alu.mult)
            kb.ts(u, u, p1(0), p1(2), Alu.subtract, Alu.mult)
            kb.wait("gpsimd", "VV")
            kb.store(out[k * 128:k * 128 + 128, :], u)

    x3v = xbuf.rearrange("p (k t) -> p k t", k=8)
    for b in range(NB):
        kb.wait("sync", "VV")
        kb.wait("sync", "TT")
        kb.wait("sync", "SS")
        kb.wait("sync", "SD")
        kb.load(x3v, di["x0" if b == 0 else "x1"].rearrange("(k p) t -> p k t", p=128))
        kb.barrier()
        attn_block(b, 0)
        kb.barrier()
        fm_ln(accs, xs)
        kb.barrier()
        if SI == 1:
            kb.store(dbg[b * 128:(b + 1) * 128, :], xbuf[:, :])
            continue
        ffn_block(b)
        kb.barrier()
        if SI == 2:
            kb.store(dbg[b * 128:(b + 1) * 128, :], xbuf[:, :])
            if b == 0:
                kb.store(dbg3[128:256, :], small[:, 0:1024])
            continue
        kb.vcopy(accb[:, :], xbuf[:, :])
        kb.barrier()
        fm_ln(xs, xs)
        kb.barrier()
        attn_block(b, 1)
        kb.barrier()
        fm_ln(accs, xs)
        kb.barrier()
        if SI == 3:
            kb.store(dbg[b * 128:(b + 1) * 128, :], xbuf[:, :])
            continue
        l1_tail(b)
        kb.barrier()
    if SI >= 4:
        routing()
        kb.barrier()
    if SI >= 5:
        moe()
        kb.barrier()
        final()
    kb.barrier()
    kb.emit()
    return nc


def _consts():
    f32 = np.float32
    c = {}
    c["ones"] = np.ones((128, 128), f32)
    l = np.zeros((128, 128), f32)
    for k in range(128):
        l[k, k + 1:] = 1.0
    c["lmask"] = l
    c["ident"] = np.eye(128, dtype=f32)
    top = 2595.0 * np.log10(21.0)
    mel = np.power(10.0, np.linspace(0.0, top, 32) / 2595.0) - 1.0
    fb64 = 200.0 * mel / 1000.0
    fbhi32 = np.array(FBJ_BITS, dtype=np.uint32).view(np.float32)
    lo = fb64 - fbhi32.astype(np.float64)
    lim = 0.4999 * np.spacing(np.abs(fbhi32)).astype(np.float64)
    lo = np.clip(lo, -lim, lim)
    idx = (np.arange(128) % 64) // 2
    c["fbhi"] = fbhi32[idx].astype(np.float32).reshape(128, 1)
    c["fblo"] = lo[idx].astype(np.float32).reshape(128, 1)
    c["signrow"] = np.array([[-1.0 if d % 2 == 0 else 1.0] for d in range(128)], f32)
    c["irow32"] = np.arange(32, dtype=f32).reshape(1, 32)
    c["c4096col"] = (np.arange(8, dtype=f32) * 4096.0).reshape(8, 1)
    lb = np.zeros((128, 128), f32)
    for m in range(128):
        lb[: 16 * (m // 16), m] = 1.0
    c["lb"] = lb
    bs = np.zeros((128, 8), f32)
    for p in range(128):
        bs[p, p // 16] = 1.0
    c["bs"] = bs
    e16 = np.zeros((8, 128), f32)
    for p in range(128):
        e16[p // 16, p] = 1.0
    c["exp16"] = e16
    return c


def _colmajor(v, k):
    return np.ascontiguousarray(np.asarray(v, np.float32).reshape(k, 128).T)


def _prep(inputs):
    f32 = np.float32
    A = lambda n: np.asarray(inputs[n], f32)
    x = A("L0_attn_x")
    consts = _consts()
    shared = dict(consts)
    shared.update(
        wqkv=A("L0_attn_w_qkv"), wo0=A("L0_attn_w_o"),
        cnw1=A("L0_ffn_cn_w1"), cnw2=A("L0_ffn_cn_w2"), cnw3=A("L0_ffn_cn_w3"),
        cnw4=A("L0_ffn_cn_w4"),
        cnb1=_colmajor(inputs["L0_ffn_cn_b1"], 8), cnb2=_colmajor(inputs["L0_ffn_cn_b2"], 4),
        cnb3=_colmajor(inputs["L0_ffn_cn_b3"], 2),
        cnb4=A("L0_ffn_cn_b4").reshape(1, 1),
        gcw1=A("L0_ffn_gc_w1"), gcw2=A("L0_ffn_gc_w2"), gcw3=A("L0_ffn_gc_w3"),
        gcb1=_colmajor(inputs["L0_ffn_gc_b1"], 4), gcb2=_colmajor(inputs["L0_ffn_gc_b2"], 2),
        gcb3=A("L0_ffn_gc_b3").reshape(1, 1),
        ar=A("L0_ffn_adaptive_range").reshape(1, 1),
        lnw1=_colmajor(inputs["L1_attn_ln_w"], 8), lnb1=_colmajor(inputs["L1_attn_ln_b"], 8),
        wq1=A("L1_attn_w_q"), wkv1=A("L1_attn_w_kv"), wo1=A("L1_attn_w_o"),
        theta=A("L1_attn_theta").reshape(1, 1),
        centT=np.ascontiguousarray(A("L1_ffn_centroids").T),
        lng=_colmajor(inputs["L1_ffn_ln_g"], 8), lnb2=_colmajor(inputs["L1_ffn_ln_b"], 8),
    )
    w1 = A("L1_ffn_w1")
    b1 = np.asarray(inputs["L1_ffn_b1"], f32)
    w2 = A("L1_ffn_w2")
    b2 = np.asarray(inputs["L1_ffn_b2"], f32)
    in_maps = []
    for c in range(NC8):
        m = dict(shared)
        m["x0"] = np.ascontiguousarray(x[2 * c].T)
        m["x1"] = np.ascontiguousarray(x[2 * c + 1].T)
        m["mw1"] = np.ascontiguousarray(w1[c])
        m["mb1"] = _colmajor(b1[c], 32)
        m["mw2"] = np.ascontiguousarray(w2[c])
        m["mb2"] = _colmajor(b2[c], 8)
        es = np.zeros((8, 1), f32)
        es[c, 0] = 1.0
        m["esel"] = es
        e128 = np.zeros((128, 16), f32)
        for pr in range(16):
            e128[16 * c + pr, pr] = 1.0
        m["esel128"] = e128
        in_maps.append(m)
    return in_maps


LAST_RES = [None]


def kernel(**inputs):
    in_maps = _prep(inputs)
    nc = build(_STAGE)
    res = run_bass_kernel_spmd(nc, in_maps, core_ids=list(range(NC8)))
    LAST_RES[0] = res
    outs = [np.asarray(res.results[c]["out"]) for c in range(NC8)]
    return np.concatenate(outs, axis=0).reshape(16, T, D)


# version 05
# speedup vs baseline: 1.1068x; 1.0037x over previous
import math
import numpy as np
import concourse.bass as bass
import concourse.mybir as mybir
from concourse.bass_utils import run_bass_kernel_spmd

F32 = mybir.dt.float32
U32 = mybir.dt.uint32
AF = mybir.ActivationFunctionType
Alu = mybir.AluOpType
AX = mybir.AxisListType.X

NC8 = 8
T = 2048
D = 1024
HD = 64
NB = 2
NTOK = NB * T
CEXP = 4096
EPS = 1e-5

CW1 = 6.25
CW2 = 0.033203125
CW3 = float(np.float32(2 * math.pi - CW1 - CW2))
INV2PI = float(np.float32(1.0 / (2 * math.pi)))
PIO2 = float(np.float32(math.pi / 2))

_STAGE = "full"
_RSTOP = 9
_TSTOP = 9
_MSTOP = 9

FBJ_BITS = [0x0, 0x3ca9130c, 0x3d31cc5a, 0x3d8c577a, 0x3dc517c7, 0x3e01d9a8,
            0x3e246269, 0x3e4a7b7b, 0x3e748306, 0x3e917073, 0x3eab03d5,
            0x3ec73ae1, 0x3ee65b4f, 0x3f045905, 0x3f1749f5, 0x3f2c2f4b,
            0x3f433ca5, 0x3f5caafa, 0x3f78b92b, 0x3f8bd641, 0x3f9ce8bc,
            0x3fafbe41, 0x3fc48550, 0x3fdb7148, 0x3ff4bac7, 0x40085028,
            0x4017b369, 0x4028ad2a, 0x403b6764, 0x40501060, 0x4066db22,
            0x40800002]


class KB:
    ENGS = ("sync", "scalar", "vector", "tensor", "gpsimd")

    def __init__(self, nc):
        self.nc = nc
        self.q = {e: [] for e in self.ENGS}
        self.sems = {}
        self.cnt = {}
        for s in ("LD", "SD", "TT", "TA", "VV", "SS", "GG", "CC"):
            self.sems[s] = nc.semaphore(s).__enter__()
            self.cnt[s] = 0

    def _rec(self, eng, fn, inc, n):
        if inc is not None:
            self.cnt[inc] += n
        self.q[eng].append((fn, inc, n))

    def wait(self, eng, sem, val=None):
        v = self.cnt[sem] if val is None else val
        h = self.sems[sem]
        self.q[eng].append((lambda e, h=h, v=v: e.wait_ge(h, v), None, 0))

    def wait_all(self, eng):
        for s in ("LD", "SD", "TT", "TA", "VV", "SS", "GG", "CC"):
            if self.cnt[s] > 0:
                self.wait(eng, s)

    def barrier(self):
        for e in self.ENGS:
            self.wait_all(e)

    def load(self, out, in_, slow=False):
        if slow:
            self._rec("sync", lambda e, o=out, i=in_: e.dma_start(out=o, in_=i, allow_slow_non_contiguous=True), "LD", 16)
        else:
            self._rec("sync", lambda e, o=out, i=in_: e.dma_start(out=o, in_=i), "LD", 16)

    def store(self, out, in_):
        self._rec("gpsimd", lambda e, o=out, i=in_: e.dma_start(out=o, in_=i), "SD", 16)

    def ind_gather(self, out, dram_ap, idx_ap):
        self._rec("gpsimd", lambda e, o=out, d=dram_ap, ix=idx_ap: e.indirect_dma_start(
            out=o, out_offset=None, in_=d,
            in_offset=bass.IndirectOffsetOnAxis(ap=ix, axis=0)), "SD", 16)

    def ind_scatter(self, dram_ap, in_, idx_ap):
        self._rec("gpsimd", lambda e, d=dram_ap, i=in_, ix=idx_ap: e.indirect_dma_start(
            out=d, out_offset=bass.IndirectOffsetOnAxis(ap=ix, axis=0),
            in_=i, in_offset=None), "SD", 16)

    def cc(self, kind, in_ap, out_ap):
        self.wait("gpsimd", "SD")
        self._rec("gpsimd", lambda e, k=kind, i=in_ap, o=out_ap: e.collective_compute(
            k, Alu.bypass, replica_groups=[list(range(NC8))],
            ins=[i.opt()], outs=[o.opt()]), "CC", 1)

    def mm(self, out, lhsT, rhs, start=True, stop=True, tp=None, mark=False, sem=None):
        s_ = sem if sem is not None else ("TT" if mark else None)
        if tp is None:
            self._rec("tensor", lambda e, o=out, l=lhsT, r=rhs, s1=start, s2=stop:
                      e.matmul(o, l, r, start=s1, stop=s2), s_, 1)
        else:
            self._rec("tensor", lambda e, o=out, l=lhsT, r=rhs, s1=start, s2=stop, t=tp:
                      e.matmul(o, l, r, start=s1, stop=s2, tile_position=t), s_, 1)

    def transp(self, out, in_, ident, mark=False):
        self._rec("tensor", lambda e, o=out, i=in_, d=ident: e.transpose(o, i, d),
                  "TT" if mark else None, 1)

    def tmark(self):
        for j in range(len(self.q["tensor"]) - 1, -1, -1):
            fn, inc, n = self.q["tensor"][j]
            if n == 0:
                continue
            assert inc is None
            self.cnt["TT"] += 1
            self.q["tensor"][j] = (fn, "TT", 1)
            return

    def act(self, out, in_, func, bias=0.0, scale=1.0, accum_out=None):
        self._rec("scalar", lambda e, o=out, i=in_, f=func, b=bias, s=scale, a=accum_out:
                  e.activation(out=o, in_=i, func=f, bias=b, scale=s, accum_out=a), "SS", 1)

    def scopy(self, out, in_):
        self._rec("scalar", lambda e, o=out, i=in_: e.copy(out=o, in_=i), "SS", 1)

    def _v(self, fn):
        self._rec("vector", fn, "VV", 1)

    def _md(self, out):
        try:
            n = 1
            for d in out.shape[1:]:
                n *= d
        except Exception:
            n = 999
        if n <= 64:
            self._v(lambda e: e.drain())

    def tt(self, out, a, b, op):
        self._v(lambda e, o=out, x=a, y=b, p=op: e.tensor_tensor(o, x, y, p))
        self._md(out)

    def ts(self, out, in0, s1, s2, op0, op1=None, accum_out=None, safe=False):
        if not safe and (not isinstance(s1, (int, float)) or
                         (s2 is not None and not isinstance(s2, (int, float)))):
            self._v(lambda e: e.drain())
        if op1 is None:
            self._v(lambda e, o=out, i=in0, a=s1, p0=op0: e.tensor_scalar(o, i, a, None, op0=p0))
        else:
            self._v(lambda e, o=out, i=in0, a=s1, b=s2, p0=op0, p1=op1, ac=accum_out:
                    e.tensor_scalar(o, i, a, b, op0=p0, op1=p1, accum_out=ac))
        if accum_out is not None:
            self._v(lambda e: e.drain())
        else:
            self._md(out)

    def stt(self, out, in0, scalar, in1, op0, op1, accum_out=None, safe=False):
        if not safe and not isinstance(scalar, (int, float)):
            self._v(lambda e: e.drain())
        self._v(lambda e, o=out, i=in0, s=scalar, j=in1, p0=op0, p1=op1, ac=accum_out:
                e.scalar_tensor_tensor(o, i, s, j, op0=p0, op1=p1, accum_out=ac))
        if accum_out is not None:
            self._v(lambda e: e.drain())
        else:
            self._md(out)

    def red(self, out, in_, op):
        self._v(lambda e, o=out, i=in_, p=op: e.tensor_reduce(o, i, axis=AX, op=p))
        self._v(lambda e: e.drain())

    def vcopy(self, out, in_):
        self._v(lambda e, o=out, i=in_: e.tensor_copy(out=o, in_=i))
        self._md(out)

    def memset(self, ap, c):
        self._v(lambda e, a=ap, v=c: e.memset(a, v))
        self._md(ap)

    def recip(self, out, in_):
        self._v(lambda e, o=out, i=in_: e.reciprocal(out=o, in_=i))
        self._v(lambda e: e.drain())

    def scan(self, out, d0, d1, init, op0, op1):
        self._v(lambda e, o=out, a=d0, b=d1, i=init, p0=op0, p1=op1:
                e.tensor_tensor_scan(o, a, b, i, op0=p0, op1=p1))
        self._v(lambda e: e.drain())

    def shuf(self, out, in_, mask):
        self._v(lambda e, o=out, i=in_, m=mask: e.stream_shuffle(o, i, m))

    def _g(self, fn):
        self._rec("gpsimd", fn, "GG", 1)

    def iota(self, out, pattern, base=0, chm=0):
        self._g(lambda e, o=out, p=pattern, b=base, c=chm: e.iota(
            o, pattern=p, base=b, channel_multiplier=c, allow_small_or_imprecise_dtypes=True))

    def affine(self, out, in_, pattern, cmp, fill, base, chm):
        self._g(lambda e, o=out, i=in_, p=pattern, c=cmp, f=fill, b=base, m=chm:
                e.affine_select(o, i, pattern=p, compare_op=c, fill=f, base=b, channel_multiplier=m))

    def gtt(self, out, a, b, op):
        self._g(lambda e, o=out, x=a, y=b, p=op: e.tensor_tensor(o, x, y, p))

    def emit(self):
        blk = self.nc.Block().__enter__()
        for eng in self.ENGS:
            items = self.q[eng]
            sems = self.sems

            def body(e, items=items, sems=sems):
                for fn, inc, n in items:
                    ins = fn(e)
                    if inc is not None:
                        ins.then_inc(sems[inc], n)

            getattr(blk, eng)(body)
        blk.__exit__(None, None, None)


def stage_idx(s):
    return {"x1": 1, "x2": 2, "x3": 3, "route": 4, "m1": 5, "full": 5}[s]


def build(stage):
    SI = stage_idx(stage)
    nc = bass.Bass("TRN2", target_bir_lowering=False)
    di = {}

    def IN(name, shape, dt=F32):
        di[name] = nc.dram_tensor(name, list(shape), dt, kind="ExternalInput").ap()
        return di[name]

    IN("x0", [D, T]); IN("x1", [D, T])
    IN("wqkv", [D, 3 * D]); IN("wo0", [D, D])
    IN("cnw1", [D, D]); IN("cnw2", [D, 512]); IN("cnw3", [512, 256]); IN("cnw4", [256, 1])
    IN("cnb1", [128, 8]); IN("cnb2", [128, 4]); IN("cnb3", [128, 2]); IN("cnb4", [1, 1])
    IN("gcw1", [D, 512]); IN("gcw2", [512, 256]); IN("gcw3", [256, 1])
    IN("gcb1", [128, 4]); IN("gcb2", [128, 2]); IN("gcb3", [1, 1]); IN("ar", [1, 1])
    IN("lnw1", [128, 8]); IN("lnb1", [128, 8])
    IN("wq1", [D, D]); IN("wkv1", [D, 2 * D]); IN("wo1", [D, D]); IN("theta", [1, 1])
    IN("centT", [D, 8]); IN("lng", [128, 8]); IN("lnb2", [128, 8])
    IN("mw1", [D, 4 * D]); IN("mb1", [128, 32]); IN("mw2", [4 * D, D]); IN("mb2", [128, 8])
    IN("esel", [8, 1])
    IN("ones", [128, 128]); IN("lmask", [128, 128]); IN("ident", [128, 128])
    IN("fbhi", [128, 1]); IN("fblo", [128, 1]); IN("signrow", [128, 1]); IN("irow32", [1, 32])
    IN("c4096col", [8, 1]); IN("lb", [128, 128]); IN("bs", [128, 8])
    IN("exp16", [8, 128]); IN("esel128", [128, 16])

    out = nc.dram_tensor("out", [NTOK, D], F32, kind="ExternalOutput").ap()
    dbg = None
    if SI < 5:
        dbg = nc.dram_tensor("dbg", [NB * 128, 8 * T], F32, kind="ExternalOutput").ap()
    dbg3 = None
    if stage == "x2":
        dbg3 = nc.dram_tensor("dbg3", [2 * 128, 1024], F32, kind="ExternalOutput").ap()
    dbg4 = None
    dbg5 = None
    if stage == "x3":
        dbg4 = nc.dram_tensor("dbg4", [128, 4096], F32, kind="ExternalOutput").ap()
        dbg5 = nc.dram_tensor("dbg5", [128, 2048], F32, kind="ExternalOutput").ap()
    dbg6 = None
    dbg7 = None
    if stage == "m1":
        dbg6 = nc.dram_tensor("dbg6", [512, 1024], F32, kind="ExternalOutput").ap()
        dbg7 = nc.dram_tensor("dbg7", [128, 96], U32, kind="ExternalOutput").ap()
    dbg2 = None
    if stage == "route":
        dbg2 = nc.dram_tensor("dbg2", [6 * 128, 256], F32, kind="ExternalOutput").ap()

    wq1p = nc.dram_tensor("wq1p", [D, D], F32).ap()
    wkv1p = nc.dram_tensor("wkv1p", [D, 2 * D], F32).ap()
    park1 = nc.dram_tensor("park1", [128, 8, T], F32).ap()
    feat_tm = nc.dram_tensor("feat_tm", [NTOK, D], F32).ap()
    lnf_tm = nc.dram_tensor("lnf_tm", [NTOK, D], F32).ap()
    agin = nc.dram_tensor("agin", [CEXP, D], F32)
    agout = nc.dram_tensor("agout", [NC8 * CEXP, D], F32)
    scr_blk = nc.dram_tensor("scr_blk", [8, NTOK], F32)
    scr_g = nc.dram_tensor("scr_g", [NC8, 8, NTOK], F32)
    h_loc = nc.dram_tensor("h_loc", [CEXP, D], F32)
    h_g = nc.dram_tensor("h_g", [NC8 * CEXP, D], F32)

    xbuf = nc.sbuf_tensor("xbuf", [128, 16384], F32).__enter__()
    accb = nc.sbuf_tensor("accb", [128, 16384], F32).__enter__()
    qkb = nc.sbuf_tensor("qkb", [128, 4096], F32).__enter__()
    vtb = nc.sbuf_tensor("vtb", [128, 2080], F32).__enter__()
    fscr = nc.sbuf_tensor("fscr", [128, 4096], F32).__enter__()
    gscr = nc.sbuf_tensor("gscr", [128, 4096], F32).__enter__()
    wbuf = nc.sbuf_tensor("wbuf", [128, 4096], F32).__enter__()
    small = nc.sbuf_tensor("small", [128, 1024], F32).__enter__()
    u32b = nc.sbuf_tensor("u32b", [128, 96], U32).__enter__()
    cones = nc.sbuf_tensor("cones", [128, 128], F32).__enter__()
    clmask = nc.sbuf_tensor("clmask", [128, 128], F32).__enter__()
    cident = nc.sbuf_tensor("cident", [128, 128], F32).__enter__()
    cmisc = nc.sbuf_tensor("cmisc", [128, 48], F32).__enter__()
    clb = nc.sbuf_tensor("clb", [128, 128], F32).__enter__()
    cexp16 = nc.sbuf_tensor("cexp16", [8, 128], F32).__enter__()
    cesel128 = nc.sbuf_tensor("cesel128", [128, 16], F32).__enter__()
    ccent = nc.sbuf_tensor("ccent", [128, 64], F32).__enter__()
    cesel = nc.sbuf_tensor("cesel", [8, 1], F32).__enter__()

    ps = [nc.psum_tensor("psb%d" % i, [128, 512], F32).__enter__() for i in range(8)]

    kb = KB(nc)

    S_CNB1, S_CNB2, S_CNB3, S_GCB1, S_GCB2 = 0, 8, 12, 14, 18
    S_CELL = 20
    S_LNW1, S_LNB1, S_LNG, S_LNB2 = 40, 48, 56, 64
    S_BCQ, S_BCK, S_BCV = 72, 80, 88
    S_BVROW = 96
    S_MB1, S_MB2 = 224, 256
    S_P1 = 264
    S_ROWS = 296

    def cell(i):
        return small[0:1, S_CELL + i:S_CELL + i + 1]

    def p1(i):
        return small[:, S_P1 + i:S_P1 + i + 1]

    def srow(i, w=512):
        return small[0:1, S_ROWS + i * 512:S_ROWS + i * 512 + w]

    def xs(kc, lo, hi):
        return xbuf[:, kc * T + lo: kc * T + hi]

    def accs(kc, lo, hi):
        return accb[:, kc * T + lo: kc * T + hi]

    kb.load(cones[:, :], di["ones"])
    kb.load(clmask[:, :], di["lmask"])
    kb.load(cident[:, :], di["ident"])
    kb.load(cmisc[:, 44:45], di["fbhi"])
    kb.load(cmisc[:, 45:46], di["fblo"])
    kb.load(cmisc[:, 0:1], di["signrow"])
    kb.load(clb[:, :], di["lb"])
    kb.load(cmisc[0:8, 46:47], di["c4096col"])
    kb.load(cmisc[:, 1:9], di["bs"])
    kb.load(cexp16[:, :], di["exp16"])
    kb.load(cesel128[:, :], di["esel128"])
    kb.load(cmisc[0:1, 9:41], di["irow32"])
    kb.load(small[:, S_CNB1:S_CNB1 + 8], di["cnb1"])
    kb.load(small[:, S_CNB2:S_CNB2 + 4], di["cnb2"])
    kb.load(small[:, S_CNB3:S_CNB3 + 2], di["cnb3"])
    kb.load(small[:, S_GCB1:S_GCB1 + 4], di["gcb1"])
    kb.load(small[:, S_GCB2:S_GCB2 + 2], di["gcb2"])
    kb.load(cell(0), di["cnb4"])
    kb.load(cell(1), di["gcb3"])
    kb.load(cell(2), di["ar"])
    kb.load(cell(3), di["theta"])
    kb.load(small[:, S_LNW1:S_LNW1 + 8], di["lnw1"])
    kb.load(small[:, S_LNB1:S_LNB1 + 8], di["lnb1"])
    kb.load(small[:, S_LNG:S_LNG + 8], di["lng"])
    kb.load(small[:, S_LNB2:S_LNB2 + 8], di["lnb2"])
    kb.load(small[:, S_MB1:S_MB1 + 32], di["mb1"])
    kb.load(small[:, S_MB2:S_MB2 + 8], di["mb2"])
    kb.load(cesel[:, :], di["esel"])

    kb.load(ccent[:, 0:64].rearrange("p (k n) -> p k n", k=8), di["centT"].rearrange("(k p) n -> p k n", p=128))
    kb.barrier()

    first = [True]
    oc_list = [("q", oc) for oc in range(8)] + [("kv", oc) for oc in range(16)]
    for idx, (which, oc) in enumerate(oc_list):
        wsrc = di["wq1"] if which == "q" else di["wkv1"]
        wdst = wq1p if which == "q" else wkv1p
        half = idx % 2
        wb = wbuf[:, half * 2048: half * 2048 + 1024]
        src = wsrc.rearrange("(k p) n -> p k n", p=128)[:, :, oc * 128:(oc + 1) * 128]
        kb.load(wb.rearrange("p (k n) -> p k n", k=8), src)
        kb.wait("tensor", "LD")
        col = oc if which == "q" else 8 + oc
        for kc in range(8):
            kb.mm(ps[0][:, col:col + 1], wb[:, kc * 128:(kc + 1) * 128],
                  small[:, S_LNB1 + kc:S_LNB1 + kc + 1],
                  start=first[0], stop=(idx == len(oc_list) - 1 and kc == 7))
            first[0] = False
        kb.tmark()
        kb.wait("vector", "TT")
        for kc in range(8):
            kb.ts(wb[:, kc * 128:(kc + 1) * 128], wb[:, kc * 128:(kc + 1) * 128],
                  small[:, S_LNW1 + kc:S_LNW1 + kc + 1], None, Alu.mult)
        kb.wait("gpsimd", "VV")
        kb.store(wdst.rearrange("(k p) n -> p k n", p=128)[:, :, oc * 128:(oc + 1) * 128],
                 wb.rearrange("p (k n) -> p k n", k=8))
        kb.wait("sync", "SD")
    kb.wait("scalar", "TT")
    kb.scopy(small[:, S_BCQ:S_BCQ + 24], ps[0][:, 0:24])
    kb.barrier()

    kb.memset(cmisc[:, 42:43], 1.0)
    kb.memset(cmisc[:, 43:44], PIO2)
    RA = small[0:1, 808:816]
    RB = small[0:1, 816:824]
    kb.memset(RA[0:1, 0:1], 220.0)
    kb.recip(RB, RA)
    kb.tt(cell(8), cell(3), RB[0:1, 0:1], Alu.mult)
    kb.tt(p1(15), cmisc[:, 44:45], cmisc[:, 45:46], Alu.add)
    kb.wait("tensor", "VV")
    kb.mm(ps[3][:, 0:1], cones[0:1, :], cell(8), mark=True)
    kb.wait("vector", "TT")
    kb.tt(p1(1), p1(15), ps[3][:, 0:1], Alu.mult)
    kb.iota(fscr[:, 0:2048], [[1, 2048]], 0, 0)
    kb.wait("vector", "GG")
    kb.ts(qkb[:, 2048:4096], fscr[:, 0:2048], p1(1), None, Alu.mult)
    kb.ts(fscr[:, 2048:4096], qkb[:, 2048:4096], INV2PI, 0.5, Alu.mult, Alu.add)
    kb.vcopy(qkb[:, 0:2048], fscr[:, 2048:4096])
    for sh in range(16, -1, -1):
        c = float(2 ** sh)
        kb.ts(fscr[:, 0:2048], fscr[:, 2048:4096], c, None, Alu.is_ge)
        kb.stt(fscr[:, 2048:4096], fscr[:, 0:2048], -c, fscr[:, 2048:4096], Alu.mult, Alu.add)
    kb.tt(fscr[:, 0:2048], qkb[:, 0:2048], fscr[:, 2048:4096], Alu.subtract)
    kb.stt(fscr[:, 2048:4096], fscr[:, 0:2048], -CW1, qkb[:, 2048:4096], Alu.mult, Alu.add)
    kb.stt(fscr[:, 2048:4096], fscr[:, 0:2048], -CW2, fscr[:, 2048:4096], Alu.mult, Alu.add)
    kb.stt(fscr[:, 2048:4096], fscr[:, 0:2048], -CW3, fscr[:, 2048:4096], Alu.mult, Alu.add)
    kb.wait("scalar", "VV")
    kb.act(gscr[:, 2048:4096], fscr[:, 2048:4096], AF.Sin)
    kb.act(fscr[:, 0:2048], fscr[:, 2048:4096], AF.Abs)
    kb.act(gscr[:, 0:2048], fscr[:, 0:2048], AF.Sin, bias=cmisc[:, 43:44], scale=-1.0)
    kb.wait("vector", "SS")
    kb.ts(gscr[:, 2048:4096], gscr[:, 2048:4096], cmisc[:, 0:1], None, Alu.mult)
    kb.barrier()
    if dbg4 is not None:
        kb.store(dbg4, gscr[:, :])
        kb.wait("sync", "SD")
    kb.barrier()

    def wrow(i, w=512):
        return wbuf[0:1, i * 512:i * 512 + w]

    def fm_ln(src, dst, gamma=None, beta=None):
        for tc in range(4):
            lo, hi = tc * 512, tc * 512 + 512
            kb.wait("tensor", "VV")
            for kc in range(8):
                sq = fscr[:, (kc % 2) * 512:(kc % 2) * 512 + 512]
                if kc >= 2:
                    kb.wait("scalar", "TT")
                kb.act(sq, src(kc, lo, hi), AF.Square)
                kb.wait("tensor", "SS")
                kb.mm(ps[4][0:1, :], cones[:, 0:1], src(kc, lo, hi),
                      start=(kc == 0), stop=(kc == 7))
                kb.mm(ps[5][0:1, :], cones[:, 0:1], sq,
                      start=(kc == 0), stop=(kc == 7), mark=True)
            kb.wait("vector", "TT")
            kb.ts(wrow(0), ps[4][0:1, :], 1.0 / 1024, None, Alu.mult)
            kb.ts(wrow(1), ps[5][0:1, :], 1.0 / 1024, None, Alu.mult)
            kb.tt(wrow(2), wrow(0), wrow(0), Alu.mult)
            kb.tt(wrow(1), wrow(1), wrow(2), Alu.subtract)
            kb.ts(wrow(1), wrow(1), EPS, None, Alu.add)
            kb.wait("scalar", "VV")
            kb.act(wrow(2), wrow(1), AF.Ln)
            kb.act(wrow(2), wrow(2), AF.Exp, scale=-0.5)
            kb.wait("vector", "SS")
            kb.tt(wrow(3), wrow(2), wrow(2), Alu.mult)
            kb.tt(wrow(3), wrow(3), wrow(1), Alu.mult)
            kb.ts(wrow(3), wrow(3), -0.5, 1.5, Alu.mult, Alu.add)
            kb.tt(wrow(2), wrow(2), wrow(3), Alu.mult)
            kb.tt(wrow(0), wrow(0), wrow(2), Alu.mult)
            kb.wait("tensor", "VV")
            kb.mm(ps[6][:, :], cones[0:1, :], wrow(2), mark=True)
            kb.mm(ps[7][:, :], cones[0:1, :], wrow(0), mark=True)
            kb.wait("vector", "TT")
            for kc in range(8):
                tmp = fscr[:, 1024:1536]
                kb.tt(tmp, src(kc, lo, hi), ps[6][:, :], Alu.mult)
                kb.tt(dst(kc, lo, hi), tmp, ps[7][:, :], Alu.subtract)
                if gamma is not None:
                    kb.ts(dst(kc, lo, hi), dst(kc, lo, hi),
                          small[:, gamma + kc:gamma + kc + 1],
                          small[:, beta + kc:beta + kc + 1], Alu.mult, Alu.add)

    def attn_block(b, layer):
        attn_block.b = b
        wom = [0, 0]
        qkmm, vmm, womtt = [], [], []
        for hp in range(8):
            if hp == 0:
                kb.wait("sync", "TT")
                kb.wait("sync", "VV")
            else:
                kb.wait("sync", "TT", qkmm[hp - 1])
            if layer == 0:
                src4 = di["wqkv"].rearrange("(k p) n -> p k n", p=128)
                def l0w(j, base):
                    dst4 = wbuf[:, j * 1024:(j + 1) * 1024].rearrange("p (k h c) -> p k h c", k=8, h=2)
                    for hh in range(2):
                        kb.load(dst4[:, :, hh, :], src4[:, :, base + hh * 192:base + hh * 192 + 64])
                l0w(0, 2 * hp * 192)
                l0w(1, 2 * hp * 192 + 64)
                if hp > 0:
                    kb.wait("sync", "TT", vmm[hp - 1])
                l0w(2, 2 * hp * 192 + 128)
                ldqkv = kb.cnt["LD"]
                if hp > 0:
                    kb.wait("sync", "TT", womtt[hp - 1])
                kb.load(wbuf[:, 3072:4096], di["wo0"][hp * 128:(hp + 1) * 128, :])
            else:
                kb.load(wbuf[:, 0:1024].rearrange("p (k n) -> p k n", k=8),
                        wq1p.rearrange("(k p) n -> p k n", p=128)[:, :, hp * 128:(hp + 1) * 128])
                kb.load(wbuf[:, 1024:2048].rearrange("p (k n) -> p k n", k=8),
                        wkv1p.rearrange("(k p) n -> p k n", p=128)[:, :, hp * 128:(hp + 1) * 128])
                if hp > 0:
                    kb.wait("sync", "TT", vmm[hp - 1])
                kb.load(wbuf[:, 2048:3072].rearrange("p (k n) -> p k n", k=8),
                        wkv1p.rearrange("(k p) n -> p k n", p=128)[:, :, 1024 + hp * 128:1024 + hp * 128 + 128])
                ldqkv = kb.cnt["LD"]
                if hp > 0:
                    kb.wait("sync", "TT", womtt[hp - 1])
                kb.load(wbuf[:, 3072:4096], di["wo1"][hp * 128:(hp + 1) * 128, :])
            ldwo = kb.cnt["LD"]
            if layer == 1:
                kb.store(small[0:1, S_BVROW:S_BVROW + 128], small[:, S_BCV + hp:S_BCV + hp + 1])
            kb.wait("tensor", "LD", ldqkv)
            for tc in range(4):
                lo, hi = tc * 512, tc * 512 + 512
                if tc > 0:
                    kb.wait("tensor", "SS")
                for kc in range(8):
                    kb.mm(ps[0][:, :], wbuf[:, kc * 128:kc * 128 + 128], xs(kc, lo, hi),
                          start=(kc == 0), stop=(kc == 7))
                for kc in range(8):
                    kb.mm(ps[1][:, :], wbuf[:, 1024 + kc * 128:1024 + kc * 128 + 128], xs(kc, lo, hi),
                          start=(kc == 0), stop=(kc == 7), mark=True)
                kb.wait("scalar", "TT")
                if layer == 0:
                    kb.scopy(qkb[:, lo:hi], ps[0][:, :])
                    kb.scopy(qkb[:, 2048 + lo:2048 + hi], ps[1][:, :])
                else:
                    kb.act(qkb[:, lo:hi], ps[0][:, :], AF.Identity,
                           bias=small[:, S_BCQ + hp:S_BCQ + hp + 1])
                    kb.act(qkb[:, 2048 + lo:2048 + hi], ps[1][:, :], AF.Identity,
                           bias=small[:, S_BCK + hp:S_BCK + hp + 1])
            qkmm.append(kb.cnt["TT"])
            kb.memset(vtb[:, :], 1.0)
            kb.wait("tensor", "VV")
            kb.wait("scalar", "VV")
            if layer == 1:
                kb.wait("tensor", "SD")
            vt3 = vtb.rearrange("p (t c) -> p t c", c=130)
            for tkc in range(16):
                bank = ps[2 + (tkc % 2)]
                if tkc >= 2:
                    kb.wait("tensor", "SS")
                for kc in range(8):
                    kb.mm(bank[:, 0:128], xbuf[:, kc * T + tkc * 128:kc * T + tkc * 128 + 128],
                          wbuf[:, 2048 + kc * 128:2048 + kc * 128 + 128],
                          start=(kc == 0), stop=(kc == 7 and layer == 0))
                if layer == 1:
                    kb.mm(bank[:, 0:128], cones[0:1, 0:128],
                          small[0:1, S_BVROW:S_BVROW + 128], start=False, stop=True)
                kb.tmark()
                kb.wait("scalar", "TT")
                kb.scopy(vt3[:, tkc, 0:64], bank[:, 0:64])
                kb.scopy(vt3[:, tkc, 65:129], bank[:, 64:128])
            vmm.append(kb.cnt["TT"])
            if layer == 1:
                kb.wait("vector", "SS")
                for base in (0, 2048):
                    q = qkb[:, base:base + 2048]
                    kb.shuf(fscr[:, 0:2048], q, [i ^ 1 for i in range(32)])
                    kb.tt(q, q, gscr[:, 0:2048], Alu.mult)
                    kb.tt(fscr[:, 0:2048], fscr[:, 0:2048], gscr[:, 2048:4096], Alu.mult)
                    kb.tt(q, q, fscr[:, 0:2048], Alu.add)
                kb.wait("tensor", "VV")
            if layer == 1:
                kb.wait("tensor", "SS")
                for h, bank, dst in ((0, ps[6], 10), (1, ps[7], 11)):
                    for tkc in range(16):
                        kb.mm(bank[0:65, 0:1], vt3[:, tkc, h * 65:h * 65 + 65], cones[:, 0:1],
                              start=(tkc == 0), stop=(tkc == 15))
                    kb.tmark()
                kb.wait("scalar", "TT")
                kb.scopy(p1(10)[0:65, :], ps[6][0:65, 0:1])
                kb.scopy(p1(11)[0:65, :], ps[7][0:65, 0:1])
            scale = (1.0 / 32.0) if layer == 0 else 0.125
            for qt in range(4):
                kmax = 4 * qt + 4 if layer == 0 else 16
                kb.wait("tensor", "VV")
                kb.wait("tensor", "TA")
                sevac = []
                avm = []
                for kc in range(kmax):
                    sb0 = ps[0] if kc % 2 == 0 else ps[2]
                    sb1 = ps[1] if kc % 2 == 0 else ps[3]
                    if kc >= 2:
                        kb.wait("tensor", "SS", sevac[kc - 2])
                    kb.mm(sb0[:, :], qkb[0:64, 2048 + kc * 128:2048 + kc * 128 + 128],
                          qkb[0:64, qt * 512:qt * 512 + 512], tp=(0, 0))
                    kb.tmark()
                    kb.mm(sb1[:, :], qkb[64:128, 2048 + kc * 128:2048 + kc * 128 + 128],
                          qkb[64:128, qt * 512:qt * 512 + 512], tp=(64, 0))
                    kb.tmark()
                    if kc >= 2:
                        kb.wait("scalar", "TA", avm[kc - 2])
                    kb.wait("scalar", "TT")
                    p0 = fscr[:, (kc % 2) * 512:(kc % 2) * 512 + 512]
                    p1_ = fscr[:, 1024 + (kc % 2) * 512:1024 + (kc % 2) * 512 + 512]
                    if layer == 0:
                        kb.act(p0, sb0[:, :], AF.Exp, scale=scale)
                        kb.act(p1_, sb1[:, :], AF.Exp, scale=scale)
                        if kc >= 4 * qt:
                            kb.wait("gpsimd", "SS")
                            kb.affine(p0, p0, [[1, 512]], Alu.is_ge, 0.0,
                                      qt * 512 - kc * 128, -1)
                            kb.affine(p1_, p1_, [[1, 512]], Alu.is_ge, 0.0,
                                      qt * 512 - kc * 128, -1)
                            kb.wait("tensor", "GG")
                    else:
                        kb.act(p0, sb0[:, :], AF.Square, scale=scale, bias=cmisc[:, 42:43])
                        kb.act(p1_, sb1[:, :], AF.Square, scale=scale, bias=cmisc[:, 42:43])
                    kb.wait("tensor", "SS")
                    kb.mm(ps[4][0:65, :], vt3[:, kc, 0:65], p0,
                          start=(kc == 0), stop=(kc == kmax - 1))
                    kb.mm(ps[5][0:65, :], vt3[:, kc, 65:130], p1_,
                          start=(kc == 0), stop=(kc == kmax - 1), sem="TA")
                    avm.append(kb.cnt["TA"])
                    sevac.append(kb.cnt["SS"])
                kb.wait("vector", "TT")
                kb.wait("vector", "TA")
                for h, ob in ((0, ps[4]), (1, ps[5])):
                    rr = small[0:1, S_ROWS:S_ROWS + 512]
                    if h == 1:
                        kb.wait("vector", "TT")
                    if layer == 0:
                        kb.recip(rr, ob[64:65, :])
                    else:
                        kb.ts(rr, ob[64:65, :], 2048.0, None, Alu.add)
                        kb.recip(rr, rr)
                    kb.wait("tensor", "VV")
                    rb = ps[6] if h == 0 else ps[7]
                    kb.mm(rb[0:64, :], cones[0:1, 0:64], rr, mark=True)
                    kb.wait("scalar", "TT")
                    kb.scopy(fscr[0:64, 0:512], rb[0:64, :])
                    kb.wait("vector", "SS")
                    dst = fscr[h * 64:h * 64 + 64, 2048 + qt * 512:2048 + qt * 512 + 512]
                    if layer == 0:
                        kb.tt(dst, ob[0:64, :], fscr[0:64, 0:512], Alu.mult)
                    else:
                        kb.stt(dst, ob[0:64, :], p1(10 + h)[0:64, :],
                               fscr[0:64, 0:512], Alu.add, Alu.mult)
            kb.wait("tensor", "LD", ldwo)
            kb.wait("tensor", "VV")
            if layer == 1 and hp == 0 and attn_block.b == 0 and dbg5 is not None:
                kb.wait("gpsimd", "VV")
                kb.store(dbg5, fscr[:, 2048:4096])
                kb.wait("tensor", "SD")
            for tc in range(4):
                for oc in range(8):
                    bank = ps[2 + (oc % 2)]
                    if tc * 8 + oc >= 2:
                        kb.wait("tensor", "VV", wom[-2])
                    kb.mm(bank[:, :], wbuf[:, 3072 + oc * 128:3072 + oc * 128 + 128],
                          fscr[:, 2048 + tc * 512:2048 + tc * 512 + 512], mark=True)
                    kb.wait("vector", "TT")
                    src1 = xs(oc, tc * 512, tc * 512 + 512) if (layer == 0 and hp == 0) \
                        else accs(oc, tc * 512, tc * 512 + 512)
                    kb.tt(accs(oc, tc * 512, tc * 512 + 512), bank[:, :], src1, Alu.add)
                    wom.append(kb.cnt["VV"])
            womtt.append(kb.cnt["TT"])

    def ffn_block(b):
        ffn_block.b = b
        x3 = xbuf.rearrange("p (k t) -> p k t", k=8)
        kb.wait("gpsimd", "VV")
        kb.wait("gpsimd", "SS")
        kb.store(park1, x3)
        kb.vcopy(accb[:, :], xbuf[:, :])
        for kc in range(8):
            kb.red(p1(24 + kc), xs(kc, 0, T), Alu.add)
        kb.ts(small[:, S_P1 + 24:S_P1 + 32], small[:, S_P1 + 24:S_P1 + 32],
              1.0 / T, None, Alu.mult)
        if dbg3 is not None and ffn_block.b == 0:
            kb.wait("gpsimd", "VV")
            kb.store(dbg3[0:128, :], small[:, 0:1024])
        kb.wait("sync", "TT")
        kb.load(wbuf[:, 0:4096].rearrange("p (k n) -> p k n", k=8),
                di["gcw1"].rearrange("(k p) n -> p k n", p=128))
        kb.wait("tensor", "LD")
        kb.wait("tensor", "VV")
        first = True
        for oc in range(4):
            for kc in range(8):
                kb.mm(ps[0][:, oc:oc + 1], wbuf[:, kc * 512 + oc * 128:kc * 512 + oc * 128 + 128],
                      p1(24 + kc), start=first, stop=(oc == 3 and kc == 7))
                first = False
        kb.tmark()
        kb.wait("scalar", "TT")
        for oc in range(4):
            kb.act(p1(16 + oc), ps[0][:, oc:oc + 1], AF.Gelu,
                   bias=small[:, S_GCB1 + oc:S_GCB1 + oc + 1])
        kb.wait("sync", "TT")
        kb.load(wbuf[:, 0:1024].rearrange("p (k n) -> p k n", k=4),
                di["gcw2"].rearrange("(k p) n -> p k n", p=128))
        kb.wait("tensor", "LD")
        kb.wait("tensor", "SS")
        first = True
        for oc in range(2):
            for kc in range(4):
                kb.mm(ps[1][:, oc:oc + 1], wbuf[:, kc * 256 + oc * 128:kc * 256 + oc * 128 + 128],
                      p1(16 + kc), start=first, stop=(oc == 1 and kc == 3))
                first = False
        kb.tmark()
        kb.wait("scalar", "TT")
        for oc in range(2):
            kb.act(p1(20 + oc), ps[1][:, oc:oc + 1], AF.Gelu,
                   bias=small[:, S_GCB2 + oc:S_GCB2 + oc + 1])
        kb.wait("sync", "TT")
        kb.load(wbuf[:, 0:2].rearrange("p (k n) -> p k n", k=2),
                di["gcw3"].rearrange("(k p) n -> p k n", p=128), slow=True)
        kb.wait("tensor", "LD")
        kb.wait("tensor", "SS")
        for kc in range(2):
            kb.mm(ps[2][0:1, 0:1], wbuf[:, kc:kc + 1], p1(20 + kc),
                  start=(kc == 0), stop=(kc == 1))
        kb.tmark()
        kb.wait("scalar", "TT")
        kb.act(cell(4), ps[2][0:1, 0:1], AF.Sigmoid, bias=cell(1))
        kb.wait("vector", "SS")
        kb.ts(cell(5), cell(4), 0.7, None, Alu.mult)
        kb.wait("sync", "SD")
        kb.wait("sync", "VV")
        kb.load(xbuf[:, 8192:16384].rearrange("p (k n) -> p k n", k=8),
                di["cnw1"].rearrange("(k p) n -> p k n", p=128))
        kb.load(qkb[:, 0:4096].rearrange("p (k n) -> p k n", k=8),
                di["cnw2"].rearrange("(k p) n -> p k n", p=128))
        kb.load(vtb[:, 0:1024].rearrange("p (k n) -> p k n", k=4),
                di["cnw3"].rearrange("(k p) n -> p k n", p=128))
        kb.load(vtb[:, 1024:1026].rearrange("p (k n) -> p k n", k=2),
                di["cnw4"].rearrange("(k p) n -> p k n", p=128), slow=True)
        kb.wait("tensor", "LD")
        for cyc in range(3):
            for tc in range(4):
                lo, hi = tc * 512, tc * 512 + 512
                kb.wait("tensor", "VV")
                kb.wait("tensor", "SS")
                h1e = []
                for oc in range(8):
                    bank = ps[oc % 2]
                    if oc >= 2:
                        kb.wait("tensor", "SS", h1e[oc - 2])
                    for kc in range(8):
                        kb.mm(bank[:, :], xbuf[:, 8192 + kc * 1024 + oc * 128:8192 + kc * 1024 + oc * 128 + 128],
                              accs(kc, lo, hi), start=(kc == 0), stop=(kc == 7))
                    kb.tmark()
                    kb.wait("scalar", "TT")
                    kb.act(xbuf[:, oc * 512:oc * 512 + 512], bank[:, :], AF.Gelu,
                           bias=small[:, S_CNB1 + oc:S_CNB1 + oc + 1])
                    h1e.append(kb.cnt["SS"])
                for oc in range(4):
                    bank = ps[2 + oc % 2]
                    if oc >= 2:
                        kb.wait("tensor", "SS")
                    for kc in range(8):
                        kb.mm(bank[:, :], qkb[:, kc * 512 + oc * 128:kc * 512 + oc * 128 + 128],
                              xbuf[:, kc * 512:kc * 512 + 512], start=(kc == 0), stop=(kc == 7))
                    kb.tmark()
                    kb.wait("scalar", "TT")
                    kb.act(xbuf[:, 4096 + oc * 512:4096 + oc * 512 + 512], bank[:, :], AF.Gelu,
                           bias=small[:, S_CNB2 + oc:S_CNB2 + oc + 1])
                for oc in range(2):
                    bank = ps[4 + oc % 2]
                    for kc in range(4):
                        kb.mm(bank[:, :], vtb[:, kc * 256 + oc * 128:kc * 256 + oc * 128 + 128],
                              xbuf[:, 4096 + kc * 512:4096 + kc * 512 + 512],
                              start=(kc == 0), stop=(kc == 3))
                    kb.tmark()
                    kb.wait("scalar", "TT")
                    kb.act(xbuf[:, 6144 + oc * 512:6144 + oc * 512 + 512], bank[:, :], AF.Gelu,
                           bias=small[:, S_CNB3 + oc:S_CNB3 + oc + 1])
                for kc in range(2):
                    kb.mm(ps[6][0:1, :], vtb[:, 1024 + kc:1024 + kc + 1],
                          xbuf[:, 6144 + kc * 512:6144 + kc * 512 + 512],
                          start=(kc == 0), stop=(kc == 1))
                kb.tmark()
                kb.wait("scalar", "TT")
                kb.act(wrow(0), ps[6][0:1, :], AF.Sigmoid, bias=cell(0))
                kb.wait("vector", "SS")
                kb.ts(wrow(1), wrow(0), 0.3, cell(5), Alu.mult, Alu.add)
                kb.ts(wrow(1), wrow(1), 0.5, cell(2), Alu.subtract, Alu.mult)
                kb.ts(wrow(1), wrow(1), 2.0, 0.1, Alu.mult, Alu.add)
                kb.wait("tensor", "VV")
                kb.mm(ps[7][:, :], cones[0:1, :], wrow(1), mark=True)
                kb.wait("vector", "TT")
                for j in range(4):
                    sgn = 1.0 if j < 2 else -1.0
                    a = accs(j, lo, hi)
                    bb = accs(j + 4, lo, hi)
                    kb.tt(fscr[:, 0:512], bb, ps[7][:, :], Alu.mult)
                    kb.tt(fscr[:, 512:1024], a, ps[7][:, :], Alu.mult)
                    kb.stt(a, fscr[:, 0:512], sgn, a, Alu.mult, Alu.add)
                    kb.stt(bb, fscr[:, 512:1024], -sgn, bb, Alu.mult, Alu.add)
        for tc in range(4):
            lo, hi = tc * 512, tc * 512 + 512
            for kc in range(8):
                tile = fscr[:, 1024 + (kc % 2) * 512:1024 + (kc % 2) * 512 + 512]
                if tc * 8 + kc >= 2:
                    kb.wait("sync", "VV")
                kb.load(tile, park1[:, kc, lo:hi])
                kb.wait("vector", "LD")
                kb.tt(xs(kc, lo, hi), tile, accs(kc, lo, hi), Alu.add)
        kb.barrier()
        fm_ln(xs, xs)

    def l1_tail(b):
        fm_ln(xs, accs, gamma=S_LNG, beta=S_LNB2)
        kb.barrier()
        if _TSTOP <= 1:
            return
        for tc in range(4):
            lo, hi = tc * 512, tc * 512 + 512
            for kc in range(8):
                kb.mm(ps[0][0:8, :], ccent[:, kc * 8:kc * 8 + 8], accs(kc, lo, hi),
                      start=(kc == 0), stop=(kc == 7))
            kb.tmark()
            kb.wait("scalar", "TT")
            kb.scopy(fscr[0:8, 0:512], ps[0][0:8, :])
            kb.wait("gpsimd", "SS")
            j0 = b * T + tc * 512
            kb.store(scr_blk.ap()[:, j0:j0 + 512], fscr[0:8, 0:512])
            kb.wait("tensor", "SD")
        if _TSTOP <= 2:
            return
        NTG = 1 if _TSTOP == 4 else 16
        DOST = _TSTOP != 5
        for tg in range(NTG):
            for (srcb, dst_dram, off) in ((xbuf, feat_tm, 1024), (accb, lnf_tm, 2048)):
                for kc in range(8):
                    bank = ps[1 + kc % 2]
                    kb.wait("tensor", "SS")
                    kb.transp(bank[:, 0:128], srcb[:, kc * T + tg * 128:kc * T + tg * 128 + 128],
                              cident[:, :], mark=True)
                    kb.wait("scalar", "TT")
                    kb.scopy(fscr[:, off + kc * 128:off + kc * 128 + 128], bank[:, 0:128])
                if DOST:
                    kb.wait("gpsimd", "SS")
                    r0 = (b * 16 + tg) * 128
                    kb.store(dst_dram[r0:r0 + 128, :], fscr[:, off:off + 1024])
            if DOST:
                kb.wait("scalar", "SD")

    def cross_reduce(dst, src, op, bA, bB):
        kb.wait("gpsimd", "VV")
        kb.store(small[0:1, 680:808], src)
        kb.wait("vector", "SD")
        kb.red(cell(14), small[0:1, 680:808], op)
        kb.wait("tensor", "VV")
        kb.mm(bB[:, 0:1], cones[0:1, :], cell(14), mark=True)
        kb.wait("vector", "TT")
        kb.vcopy(dst, bB[:, 0:1])

    def routing():
        G = lambda c0, w: gscr[:, c0:c0 + w]
        s_all = lambda e: gscr[:, e * 256:e * 256 + 256]
        mem = lambda e: fscr[:, e * 256:e * 256 + 256]
        s_cur = G(2048, 256)
        dump = G(2304, 256)
        eq_t = G(2560, 256)
        gt_t = G(2816, 256)
        u_t = G(3072, 256)
        taken = G(3328, 256)
        scn = G(3584, 256)
        zrow = qkb[0:1, 2048:2304]

        if _RSTOP <= -1:
            return
        kb.cc("AllGather", scr_blk.ap(), scr_g.ap())
        kb.wait("sync", "CC")
        if _RSTOP <= -2:
            return
        for c in range(8):
            kb.load(gscr[c * 16:(c + 1) * 16, 0:2048].rearrange("p (e f) -> p e f", e=8),
                    scr_g.ap().rearrange("c e (pr f) -> c pr e f", pr=16)[c, :, :, :])
        kb.memset(zrow, 0.0)
        kb.memset(taken, 0.0)
        kb.wait("vector", "LD")
        if _RSTOP <= 0:
            return
        LO, HI, TH, PRED, CNT, M_, VREP, NOTP, D_ = (p1(0), p1(1), p1(2), p1(3),
                                                     p1(4), p1(5), p1(6), p1(7), p1(8))
        for e in range(8):
            kb.wait("tensor", "VV")
            if e == 0:
                kb.vcopy(s_cur, s_all(0))
            elif e < 7:
                kb.stt(s_cur, taken, -2e30, s_all(e), Alu.mult, Alu.add)
            if e == 7:
                kb.ts(mem(7), taken, -1.0, 1.0, Alu.mult, Alu.add)
            else:
                kb.stt(u_t, taken, 4e30, s_cur, Alu.mult, Alu.add)
                kb.red(D_, u_t, Alu.min)
                cross_reduce(LO, D_, Alu.min, ps[0], ps[1])
                kb.red(D_, s_cur, Alu.max)
                cross_reduce(HI, D_, Alu.max, ps[0], ps[1])
                for _ in range(30):
                    kb.ts(TH, LO, HI, 0.5, Alu.add, Alu.mult)
                    kb.ts(dump, s_cur, TH, 0.0, Alu.is_ge, Alu.add, accum_out=CNT)
                    kb.wait("tensor", "VV")
                    kb.mm(ps[0][:, 0:1], cones[:, :], CNT, mark=True)
                    kb.wait("vector", "TT")
                    kb.ts(PRED, ps[0][:, 0:1], float(CEXP), None, Alu.is_ge)
                    kb.tt(D_, TH, LO, Alu.subtract)
                    kb.tt(D_, PRED, D_, Alu.mult)
                    kb.tt(LO, LO, D_, Alu.add)
                    kb.ts(NOTP, PRED, -1.0, 1.0, Alu.mult, Alu.add)
                    kb.tt(D_, TH, HI, Alu.subtract)
                    kb.tt(D_, NOTP, D_, Alu.mult)
                    kb.tt(HI, HI, D_, Alu.add)
                kb.ts(dump, s_cur, LO, None, Alu.is_lt)
                kb.stt(u_t, dump, 4e30, s_cur, Alu.mult, Alu.add)
                kb.red(D_, u_t, Alu.min)
                cross_reduce(VREP, D_, Alu.min, ps[0], ps[1])
                kb.ts(dump, s_cur, VREP, 0.0, Alu.is_gt, Alu.add, accum_out=CNT)
                kb.wait("tensor", "VV")
                kb.mm(ps[0][:, 0:1], cones[:, :], CNT, mark=True)
                kb.wait("vector", "TT")
                kb.ts(M_, ps[0][:, 0:1], -1.0, float(CEXP), Alu.mult, Alu.add)
                kb.ts(eq_t, s_cur, VREP, None, Alu.is_equal)
                kb.scan(scn, eq_t, eq_t, 0.0, Alu.add, Alu.bypass)
                kb.vcopy(p1(9), scn[:, 255:256])
                kb.tt(scn, scn, eq_t, Alu.subtract)
                kb.wait("tensor", "VV")
                kb.mm(ps[2][:, 0:1], clmask[:, :], p1(9), mark=True)
                kb.wait("vector", "TT")
                kb.ts(scn, scn, ps[2][:, 0:1], None, Alu.add)
                kb.ts(dump, scn, M_, None, Alu.is_lt)
                kb.ts(gt_t, s_cur, VREP, None, Alu.is_gt)
                kb.tt(eq_t, eq_t, dump, Alu.mult)
                kb.tt(mem(e), gt_t, eq_t, Alu.add)
            if e < 7:
                kb.tt(taken, taken, mem(e), Alu.add)
        if _RSTOP <= 1:
            return
        hrow_full = G(2048, 256)
        lp_full = G(2304, 256)
        kb.wait("vector", "TT")
        kb.memset(hrow_full, 0.0)
        kb.memset(lp_full, 0.0)
        bp8m = qkb[0:8, 2304:2312]
        st8m = qkb[0:8, 2312:2320]
        cnt8 = qkb[0:8, 2320:2328]
        run8 = qkb[0:8, 2328:2336]
        c1 = p1(10)
        c3 = p1(11)
        kb.memset(run8[:, 0:1], 0.0)
        for e in range(8):
            kb.wait("tensor", "VV")
            kb.scan(scn, mem(e), mem(e), 0.0, Alu.add, Alu.bypass)
            kb.vcopy(p1(9), scn[:, 255:256])
            kb.tt(scn, scn, mem(e), Alu.subtract)
            kb.wait("tensor", "VV")
            kb.mm(ps[2][:, 0:1], clmask[:, :], p1(9), start=True, stop=True)
            kb.mm(ps[3][:, 0:1], clb[:, :], p1(9), start=True, stop=True)
            kb.mm(ps[4][0:8, 0:1], cmisc[:, 1:9], p1(9), mark=True)
            kb.wait("vector", "TT")
            kb.vcopy(p1(12)[0:8, :], ps[4][0:8, 0:1])
            kb.vcopy(qkb[0:8, 2320 + e:2321 + e], p1(12)[0:8, :])
            kb.vcopy(st8m[:, e:e + 1], run8[:, 0:1])
            kb.tt(run8[:, 0:1], run8[:, 0:1], p1(12)[0:8, :], Alu.add)
            kb.wait("tensor", "VV")
            kb.mm(ps[5][0:8, 0:1], clmask[0:8, 0:8], p1(12)[0:8, :], mark=True)
            kb.wait("vector", "TT")
            kb.vcopy(bp8m[:, e:e + 1], ps[5][0:8, 0:1])
            kb.ts(u_t, scn, ps[2][:, 0:1], float(CEXP * e), Alu.add, Alu.add)
            kb.tt(taken, mem(e), u_t, Alu.mult)
            kb.tt(hrow_full, hrow_full, taken, Alu.add)
            kb.vcopy(c1, ps[2][:, 0:1])
            kb.tt(c1, c1, ps[3][:, 0:1], Alu.subtract)
            kb.wait("tensor", "VV")
            kb.mm(ps[6][:, 0:1], cexp16[:, :], st8m[:, e:e + 1], mark=True)
            kb.wait("vector", "TT")
            kb.tt(c3, c1, ps[6][:, 0:1], Alu.add)
            kb.ts(u_t, scn, c3, None, Alu.add)
            kb.tt(taken, mem(e), u_t, Alu.mult)
            kb.tt(lp_full, lp_full, taken, Alu.add)
        if _RSTOP <= 2:
            return
        kb.wait("tensor", "VV")
        kb.mm(ps[0][0:16, 0:256], cesel128[:, :], lp_full, mark=True)
        kb.mm(ps[1][0:16, 0:256], cesel128[:, :], hrow_full, mark=True)
        kb.wait("scalar", "TT")
        kb.scopy(gscr[0:16, 2048:2304], ps[0][0:16, 0:256])
        kb.scopy(gscr[0:16, 2304:2560], ps[1][0:16, 0:256])
        kb.wait("gpsimd", "SS")
        for k in range(32):
            pr, half = k // 2, (k % 2) * 128
            kb.store(fscr[:, 2048 + k:2048 + k + 1], gscr[pr:pr + 1, 2048 + half:2048 + half + 128])
            kb.store(fscr[:, 2080 + k:2080 + k + 1], gscr[pr:pr + 1, 2304 + half:2304 + half + 128])
        kb.wait("vector", "SD")
        kb.ts(u32b[:, 0:32], fscr[:, 2048:2080], 0.0, None, Alu.add)
        kb.ts(u32b[:, 32:64], fscr[:, 2080:2112], 0.0, None, Alu.add)
        if _RSTOP <= 3:
            return
        esr = qkb[0:1, 2336:2344]
        kb.wait("gpsimd", "VV")
        kb.store(esr, cesel[:, :])
        kb.wait("tensor", "SD")
        kb.mm(ps[7][0:8, 0:8], cones[0:1, 0:8], esr, mark=True)
        kb.wait("vector", "TT")
        esrep8 = qkb[0:8, 2344:2352]
        kb.vcopy(esrep8, ps[7][0:8, 0:8])
        cumown = p1(13)
        stown = p1(14)
        kb.memset(cumown[0:8, :], 0.0)
        kb.memset(stown[0:8, :], 0.0)
        for e in range(8):
            kb.stt(cumown[0:8, :], bp8m[:, e:e + 1], esrep8[:, e:e + 1], cumown[0:8, :],
                   Alu.mult, Alu.add)
            kb.stt(stown[0:8, :], st8m[:, e:e + 1], esrep8[:, e:e + 1], stown[0:8, :],
                   Alu.mult, Alu.add)
        lam = p1(15)
        kb.tt(lam[0:8, :], stown[0:8, :], cumown[0:8, :], Alu.subtract)
        kb.tt(lam[0:8, :], lam[0:8, :], cmisc[0:8, 46:47], Alu.add)
        kb.wait("gpsimd", "VV")
        kb.store(qkb[0:1, 2352:2360], lam[0:8, :])
        kb.store(qkb[0:1, 2360:2368], cumown[0:8, :])
        kb.wait("tensor", "SD")
        kb.mm(ps[6][:, 0:8], cones[0:1, :], qkb[0:1, 2352:2360])
        kb.mm(ps[7][:, 0:8], cones[0:1, :], qkb[0:1, 2360:2368], mark=True)
        kb.wait("vector", "TT")
        lamrep = G(3584, 8)
        cumrep = G(3592, 8)
        kb.vcopy(lamrep, ps[6][:, 0:8])
        kb.vcopy(cumrep, ps[7][:, 0:8])
        sgrid = G(3600, 32)
        kb.iota(sgrid, [[128, 32]], 0, 1)
        kb.wait("vector", "GG")
        acc32 = G(3632, 32)
        ge_t = lambda c: gscr[:, 3664 + c * 32:3664 + c * 32 + 32]
        kb.memset(acc32, 0.0)
        for c in range(8):
            kb.ts(ge_t(c), sgrid, cumrep[:, c:c + 1], None, Alu.is_ge)
        for c in range(8):
            if c < 7:
                kb.tt(s_cur[:, 0:32], ge_t(c), ge_t(c + 1), Alu.subtract)
                ind = s_cur[:, 0:32]
            else:
                ind = ge_t(7)
            kb.stt(acc32, ind, lamrep[:, c:c + 1], acc32, Alu.mult, Alu.add)
        kb.tt(acc32, sgrid, acc32, Alu.add)
        kb.ts(u32b[:, 64:96], acc32, 0.0, None, Alu.add)
        kb.barrier()
        if stage == "route" and _RSTOP >= 5:
            kb.store(dbg2[0 * 128:1 * 128, :], mem(0))
            kb.memset(gt_t, 0.0)
            kb.vcopy(gt_t[:, 0:64], fscr[:, 2048:2112])
            kb.wait("gpsimd", "VV")
            kb.store(dbg2[1 * 128:2 * 128, :], gt_t)
            kb.store(dbg2[2 * 128:3 * 128, :], mem(7))
            kb.store(dbg2[3 * 128:4 * 128, :], hrow_full)
            kb.store(dbg2[4 * 128:5 * 128, :], lp_full)
            kb.memset(dump, 0.0)
            kb.vcopy(dump[:, 0:32], acc32)
            kb.wait("gpsimd", "VV")
            kb.store(dbg2[5 * 128:6 * 128, :], dump)
            kb.wait("gpsimd", "SD")
        if _RSTOP <= 4:
            return
        for k in range(32):
            tile = fscr[:, (k % 2) * 1024:(k % 2) * 1024 + 1024]
            kb.wait("sync", "SD")
            kb.load(tile, lnf_tm[k * 128:k * 128 + 128, :])
            kb.wait("gpsimd", "LD")
            kb.ind_scatter(agin.ap(), tile, u32b[:, k:k + 1])
        if dbg7 is not None:
            kb.wait("gpsimd", "VV")
            kb.store(dbg7, u32b[:, :])
        kb.cc("AllGather", agin.ap(), agout.ap())
        if dbg6 is not None and _MSTOP == 1:
            kb.wait("gpsimd", "CC")
            kb.store(dbg6[0:256, :], agout.ap()[0:256, :])
            kb.store(dbg6[256:512, :], agout.ap()[CEXP * 3:CEXP * 3 + 256, :])

    def moe():
        kb.wait("gpsimd", "CC")
        kb.wait("tensor", "CC")
        mw1r = di["mw1"].rearrange("(k p) n -> p k n", p=128)
        for st in range(8):
            for g in range(4):
                tile = fscr[:, (g % 2) * 1024:(g % 2) * 1024 + 1024]
                kb.wait("gpsimd", "SS")
                kb.wait("gpsimd", "SD")
                kb.ind_gather(tile, agout.ap(), u32b[:, 64 + st * 4 + g:64 + st * 4 + g + 1])
                if dbg6 is not None and _MSTOP == 5 and st == 4 and g == 0:
                    kb.wait("gpsimd", "SD")
                    kb.store(dbg6[0:128, :], tile)
                    kb.wait("tensor", "SD")
                kb.wait("tensor", "SD")
                for kc in range(8):
                    bank = ps[kc % 2]
                    kb.wait("tensor", "SS")
                    kb.transp(bank[:, 0:128], tile[:, kc * 128:kc * 128 + 128],
                              cident[:, :], mark=True)
                    kb.wait("scalar", "TT")
                    kb.scopy(qkb[:, kc * 512 + g * 128:kc * 512 + g * 128 + 128],
                             bank[:, 0:128])
            if dbg6 is not None and _MSTOP == 3 and st == 4:
                kb.wait("gpsimd", "SS")
                kb.store(dbg6[0:128, :], qkb[:, 0:1024])
                kb.store(dbg6[128:256, :], qkb[:, 1024:2048])
                kb.store(dbg6[256:384, :], qkb[:, 2048:3072])
                kb.store(dbg6[384:512, :], qkb[:, 3072:4096])
                kb.wait("sync", "SD")
            w1evac = []
            w1mm = []
            for hc in range(32):
                wsl = accb[:, (hc % 2) * 1024:(hc % 2) * 1024 + 1024]
                if hc >= 2:
                    kb.wait("sync", "TT", w1mm[hc - 2])
                else:
                    kb.wait("sync", "TT")
                kb.load(wsl.rearrange("p (k n) -> p k n", k=8), mw1r[:, :, hc * 128:hc * 128 + 128])
                kb.wait("tensor", "LD")
                bank = ps[2 + hc % 2]
                if hc < 2:
                    kb.wait("tensor", "SS")
                else:
                    kb.wait("tensor", "SS", w1evac[hc - 2])
                for kc in range(8):
                    kb.mm(bank[:, :], wsl[:, kc * 128:kc * 128 + 128],
                          qkb[:, kc * 512:kc * 512 + 512], start=(kc == 0), stop=(kc == 7))
                kb.tmark()
                w1mm.append(kb.cnt["TT"])
                kb.wait("scalar", "TT")
                kb.act(xbuf[:, hc * 512:hc * 512 + 512], bank[:, :], AF.Gelu_apprx_tanh,
                       bias=small[:, S_MB1 + hc:S_MB1 + hc + 1])
                w1evac.append(kb.cnt["SS"])
            if dbg6 is not None and _MSTOP == 4 and st == 4:
                kb.wait("gpsimd", "SS")
                for q_ in range(4):
                    kb.store(dbg6[q_ * 128:(q_ + 1) * 128, :], xbuf[:, q_ * 1024:(q_ + 1) * 1024])
                kb.wait("sync", "SD")
            kb.wait("tensor", "SS")
            w2mm = []
            for hc in range(32):
                w2sl = accb[:, 2048 + (hc % 2) * 1024:2048 + (hc % 2) * 1024 + 1024]
                if hc >= 2:
                    kb.wait("sync", "TT", w2mm[hc - 2])
                else:
                    kb.wait("sync", "TT")
                kb.load(w2sl, di["mw2"][hc * 128:hc * 128 + 128, :])
                kb.wait("tensor", "LD")
                for oc in range(8):
                    kb.mm(ps[oc][:, :], w2sl[:, oc * 128:oc * 128 + 128],
                          xbuf[:, hc * 512:hc * 512 + 512],
                          start=(hc == 0), stop=(hc == 31))
                kb.tmark()
                w2mm.append(kb.cnt["TT"])
            kb.wait("scalar", "TT")
            for oc in range(8):
                kb.act(qkb[:, oc * 512:oc * 512 + 512], ps[oc][:, :], AF.Identity,
                       bias=small[:, S_MB2 + oc:S_MB2 + oc + 1])
            for g in range(4):
                tile = fscr[:, (g % 2) * 1024:(g % 2) * 1024 + 1024]
                if g >= 2:
                    kb.wait("scalar", "SD")
                for kc in range(8):
                    bank = ps[kc % 2]
                    kb.wait("tensor", "SS")
                    kb.transp(bank[:, 0:128], qkb[:, kc * 512 + g * 128:kc * 512 + g * 128 + 128],
                              cident[:, :], mark=True)
                    kb.wait("scalar", "TT")
                    kb.scopy(tile[:, kc * 128:kc * 128 + 128], bank[:, 0:128])
                kb.wait("gpsimd", "SS")
                kb.store(h_loc.ap()[st * 512 + g * 128:st * 512 + g * 128 + 128, :], tile)
        if dbg6 is not None and _MSTOP == 2:
            kb.wait("gpsimd", "SD")
            kb.store(dbg6[0:256, :], h_loc.ap()[0:256, :])
            kb.store(dbg6[256:512, :], h_loc.ap()[2048:2048 + 256, :])
        kb.cc("AllGather", h_loc.ap(), h_g.ap())

    def final():
        def xslot(g, j):
            return xbuf[:, ((g % 2) * 4 + j) * 1024:((g % 2) * 4 + j) * 1024 + 1024]

        def hslot(g, j):
            return accb[:, ((g % 2) * 4 + j) * 1024:((g % 2) * 4 + j) * 1024 + 1024]

        def uslot(j):
            return qkb[:, j * 1024:j * 1024 + 1024]

        du = vtb[:, 0:1024]
        kb.wait("sync", "TT")
        kb.wait("sync", "SS")
        kb.wait("sync", "VV")
        for g in range(2):
            for j in range(4):
                kb.load(xslot(g, j), feat_tm[(g * 4 + j) * 128:(g * 4 + j) * 128 + 128, :])
        kb.wait("gpsimd", "CC")
        kb.wait("gpsimd", "TT")
        kb.wait("gpsimd", "VV")
        ucons = []
        for g in range(8):
            if g >= 2:
                kb.wait("gpsimd", "VV", ucons[g - 2])
            for j in range(4):
                kb.ind_gather(hslot(g, j), h_g.ap(), u32b[:, 32 + g * 4 + j:32 + g * 4 + j + 1])
            gLD = kb.cnt["LD"]
            gSD = kb.cnt["SD"]
            kb.wait("vector", "LD", gLD)
            kb.wait("vector", "SD", gSD)
            if g == 0:
                kb.wait("vector", "TT")
                kb.wait("vector", "SS")
            for j in range(4):
                kb.stt(uslot(j), xslot(g, j), 2.0, hslot(g, j), Alu.mult, Alu.add)
            ucons.append(kb.cnt["VV"])
            for j in range(4):
                kb.ts(du, uslot(j), 0.0, 0.0, Alu.add, Alu.add, accum_out=p1(0 + j))
                kb.stt(du, uslot(j), 1.0, uslot(j), Alu.mult, Alu.mult, accum_out=p1(8 + j))
            mu4 = small[:, S_P1 + 0:S_P1 + 4]
            m2 = small[:, S_P1 + 8:S_P1 + 12]
            v4 = small[:, S_P1 + 16:S_P1 + 20]
            r4 = small[:, S_P1 + 20:S_P1 + 24]
            t4_ = small[:, S_P1 + 24:S_P1 + 28]
            kb.ts(mu4, mu4, 1.0 / 1024, None, Alu.mult)
            kb.ts(m2, m2, 1.0 / 1024, None, Alu.mult)
            kb.tt(v4, mu4, mu4, Alu.mult)
            kb.tt(v4, m2, v4, Alu.subtract)
            kb.ts(v4, v4, EPS, None, Alu.add)
            kb.wait("scalar", "VV")
            kb.act(r4, v4, AF.Ln)
            kb.act(r4, r4, AF.Exp, scale=-0.5)
            kb.wait("vector", "SS")
            kb.tt(t4_, r4, r4, Alu.mult)
            kb.tt(t4_, t4_, v4, Alu.mult)
            kb.ts(t4_, t4_, -0.5, 1.5, Alu.mult, Alu.add)
            kb.tt(r4, r4, t4_, Alu.mult)
            for j in range(4):
                kb.ts(uslot(j), uslot(j), p1(0 + j), p1(20 + j), Alu.subtract, Alu.mult)
                kb.wait("gpsimd", "VV")
                kb.store(out[(g * 4 + j) * 128:(g * 4 + j) * 128 + 128, :], uslot(j))
            if g + 2 < 8:
                kb.wait("sync", "VV", ucons[g])
                for j in range(4):
                    kb.load(xslot(g + 2, j), feat_tm[((g + 2) * 4 + j) * 128:((g + 2) * 4 + j) * 128 + 128, :])

    x3v = xbuf.rearrange("p (k t) -> p k t", k=8)
    for b in range(NB):
        kb.wait("sync", "VV")
        kb.wait("sync", "TT")
        kb.wait("sync", "SS")
        kb.wait("sync", "SD")
        kb.load(x3v, di["x0" if b == 0 else "x1"].rearrange("(k p) t -> p k t", p=128))
        kb.barrier()
        attn_block(b, 0)
        kb.barrier()
        fm_ln(accs, xs)
        kb.barrier()
        if SI == 1:
            kb.store(dbg[b * 128:(b + 1) * 128, :], xbuf[:, :])
            continue
        ffn_block(b)
        kb.barrier()
        if SI == 2:
            kb.store(dbg[b * 128:(b + 1) * 128, :], xbuf[:, :])
            if b == 0:
                kb.store(dbg3[128:256, :], small[:, 0:1024])
            continue
        kb.vcopy(accb[:, :], xbuf[:, :])
        kb.barrier()
        fm_ln(xs, xs)
        kb.barrier()
        attn_block(b, 1)
        kb.barrier()
        fm_ln(accs, xs)
        kb.barrier()
        if SI == 3:
            kb.store(dbg[b * 128:(b + 1) * 128, :], xbuf[:, :])
            continue
        l1_tail(b)
        kb.barrier()
    if SI >= 4:
        routing()
        kb.barrier()
    if SI >= 5:
        moe()
        final()
    kb.barrier()
    kb.emit()
    return nc


def _consts():
    f32 = np.float32
    c = {}
    c["ones"] = np.ones((128, 128), f32)
    l = np.zeros((128, 128), f32)
    for k in range(128):
        l[k, k + 1:] = 1.0
    c["lmask"] = l
    c["ident"] = np.eye(128, dtype=f32)
    top = 2595.0 * np.log10(21.0)
    mel = np.power(10.0, np.linspace(0.0, top, 32) / 2595.0) - 1.0
    fb64 = 200.0 * mel / 1000.0
    fbhi32 = np.array(FBJ_BITS, dtype=np.uint32).view(np.float32)
    lo = fb64 - fbhi32.astype(np.float64)
    lim = 0.4999 * np.spacing(np.abs(fbhi32)).astype(np.float64)
    lo = np.clip(lo, -lim, lim)
    idx = (np.arange(128) % 64) // 2
    c["fbhi"] = fbhi32[idx].astype(np.float32).reshape(128, 1)
    c["fblo"] = lo[idx].astype(np.float32).reshape(128, 1)
    c["signrow"] = np.array([[-1.0 if d % 2 == 0 else 1.0] for d in range(128)], f32)
    c["irow32"] = np.arange(32, dtype=f32).reshape(1, 32)
    c["c4096col"] = (np.arange(8, dtype=f32) * 4096.0).reshape(8, 1)
    lb = np.zeros((128, 128), f32)
    for m in range(128):
        lb[: 16 * (m // 16), m] = 1.0
    c["lb"] = lb
    bs = np.zeros((128, 8), f32)
    for p in range(128):
        bs[p, p // 16] = 1.0
    c["bs"] = bs
    e16 = np.zeros((8, 128), f32)
    for p in range(128):
        e16[p // 16, p] = 1.0
    c["exp16"] = e16
    return c


def _colmajor(v, k):
    return np.ascontiguousarray(np.asarray(v, np.float32).reshape(k, 128).T)


def _prep(inputs):
    f32 = np.float32
    A = lambda n: np.asarray(inputs[n], f32)
    x = A("L0_attn_x")
    consts = _consts()
    shared = dict(consts)
    shared.update(
        wqkv=A("L0_attn_w_qkv"), wo0=A("L0_attn_w_o"),
        cnw1=A("L0_ffn_cn_w1"), cnw2=A("L0_ffn_cn_w2"), cnw3=A("L0_ffn_cn_w3"),
        cnw4=A("L0_ffn_cn_w4"),
        cnb1=_colmajor(inputs["L0_ffn_cn_b1"], 8), cnb2=_colmajor(inputs["L0_ffn_cn_b2"], 4),
        cnb3=_colmajor(inputs["L0_ffn_cn_b3"], 2),
        cnb4=A("L0_ffn_cn_b4").reshape(1, 1),
        gcw1=A("L0_ffn_gc_w1"), gcw2=A("L0_ffn_gc_w2"), gcw3=A("L0_ffn_gc_w3"),
        gcb1=_colmajor(inputs["L0_ffn_gc_b1"], 4), gcb2=_colmajor(inputs["L0_ffn_gc_b2"], 2),
        gcb3=A("L0_ffn_gc_b3").reshape(1, 1),
        ar=A("L0_ffn_adaptive_range").reshape(1, 1),
        lnw1=_colmajor(inputs["L1_attn_ln_w"], 8), lnb1=_colmajor(inputs["L1_attn_ln_b"], 8),
        wq1=A("L1_attn_w_q"), wkv1=A("L1_attn_w_kv"), wo1=A("L1_attn_w_o"),
        theta=A("L1_attn_theta").reshape(1, 1),
        centT=np.ascontiguousarray(A("L1_ffn_centroids").T),
        lng=_colmajor(inputs["L1_ffn_ln_g"], 8), lnb2=_colmajor(inputs["L1_ffn_ln_b"], 8),
    )
    w1 = A("L1_ffn_w1")
    b1 = np.asarray(inputs["L1_ffn_b1"], f32)
    w2 = A("L1_ffn_w2")
    b2 = np.asarray(inputs["L1_ffn_b2"], f32)
    in_maps = []
    for c in range(NC8):
        m = dict(shared)
        m["x0"] = np.ascontiguousarray(x[2 * c].T)
        m["x1"] = np.ascontiguousarray(x[2 * c + 1].T)
        m["mw1"] = np.ascontiguousarray(w1[c])
        m["mb1"] = _colmajor(b1[c], 32)
        m["mw2"] = np.ascontiguousarray(w2[c])
        m["mb2"] = _colmajor(b2[c], 8)
        es = np.zeros((8, 1), f32)
        es[c, 0] = 1.0
        m["esel"] = es
        e128 = np.zeros((128, 16), f32)
        for pr in range(16):
            e128[16 * c + pr, pr] = 1.0
        m["esel128"] = e128
        in_maps.append(m)
    return in_maps


LAST_RES = [None]


def kernel(**inputs):
    in_maps = _prep(inputs)
    nc = build(_STAGE)
    res = run_bass_kernel_spmd(nc, in_maps, core_ids=list(range(NC8)))
    LAST_RES[0] = res
    outs = [np.asarray(res.results[c]["out"]) for c in range(NC8)]
    return np.concatenate(outs, axis=0).reshape(16, T, D)


# version 06
# speedup vs baseline: 1.1107x; 1.0036x over previous
import math
import numpy as np
import concourse.bass as bass
import concourse.mybir as mybir
from concourse.bass_utils import run_bass_kernel_spmd

F32 = mybir.dt.float32
U32 = mybir.dt.uint32
AF = mybir.ActivationFunctionType
Alu = mybir.AluOpType
AX = mybir.AxisListType.X

NC8 = 8
T = 2048
D = 1024
HD = 64
NB = 2
NTOK = NB * T
CEXP = 4096
EPS = 1e-5

CW1 = 6.25
CW2 = 0.033203125
CW3 = float(np.float32(2 * math.pi - CW1 - CW2))
INV2PI = float(np.float32(1.0 / (2 * math.pi)))
PIO2 = float(np.float32(math.pi / 2))

_STAGE = "full"
_RSTOP = 9
_TSTOP = 9
_MSTOP = 9

FBJ_BITS = [0x0, 0x3ca9130c, 0x3d31cc5a, 0x3d8c577a, 0x3dc517c7, 0x3e01d9a8,
            0x3e246269, 0x3e4a7b7b, 0x3e748306, 0x3e917073, 0x3eab03d5,
            0x3ec73ae1, 0x3ee65b4f, 0x3f045905, 0x3f1749f5, 0x3f2c2f4b,
            0x3f433ca5, 0x3f5caafa, 0x3f78b92b, 0x3f8bd641, 0x3f9ce8bc,
            0x3fafbe41, 0x3fc48550, 0x3fdb7148, 0x3ff4bac7, 0x40085028,
            0x4017b369, 0x4028ad2a, 0x403b6764, 0x40501060, 0x4066db22,
            0x40800002]


class KB:
    ENGS = ("sync", "scalar", "vector", "tensor", "gpsimd")

    def __init__(self, nc):
        self.nc = nc
        self.q = {e: [] for e in self.ENGS}
        self.sems = {}
        self.cnt = {}
        for s in ("LD", "SD", "TT", "TA", "VV", "SS", "GG", "CC"):
            self.sems[s] = nc.semaphore(s).__enter__()
            self.cnt[s] = 0

    def _rec(self, eng, fn, inc, n):
        if inc is not None:
            self.cnt[inc] += n
        self.q[eng].append((fn, inc, n))

    def wait(self, eng, sem, val=None):
        v = self.cnt[sem] if val is None else val
        h = self.sems[sem]
        self.q[eng].append((lambda e, h=h, v=v: e.wait_ge(h, v), None, 0))

    def wait_all(self, eng):
        for s in ("LD", "SD", "TT", "TA", "VV", "SS", "GG", "CC"):
            if self.cnt[s] > 0:
                self.wait(eng, s)

    def barrier(self):
        for e in self.ENGS:
            self.wait_all(e)

    def load(self, out, in_, slow=False):
        if slow:
            self._rec("sync", lambda e, o=out, i=in_: e.dma_start(out=o, in_=i, allow_slow_non_contiguous=True), "LD", 16)
        else:
            self._rec("sync", lambda e, o=out, i=in_: e.dma_start(out=o, in_=i), "LD", 16)

    def store(self, out, in_):
        self._rec("gpsimd", lambda e, o=out, i=in_: e.dma_start(out=o, in_=i), "SD", 16)

    def ind_gather(self, out, dram_ap, idx_ap):
        self._rec("gpsimd", lambda e, o=out, d=dram_ap, ix=idx_ap: e.indirect_dma_start(
            out=o, out_offset=None, in_=d,
            in_offset=bass.IndirectOffsetOnAxis(ap=ix, axis=0)), "SD", 16)

    def ind_scatter(self, dram_ap, in_, idx_ap):
        self._rec("gpsimd", lambda e, d=dram_ap, i=in_, ix=idx_ap: e.indirect_dma_start(
            out=d, out_offset=bass.IndirectOffsetOnAxis(ap=ix, axis=0),
            in_=i, in_offset=None), "SD", 16)

    def cc(self, kind, in_ap, out_ap):
        self.wait("gpsimd", "SD")
        self._rec("gpsimd", lambda e, k=kind, i=in_ap, o=out_ap: e.collective_compute(
            k, Alu.bypass, replica_groups=[list(range(NC8))],
            ins=[i.opt()], outs=[o.opt()]), "CC", 1)

    def mm(self, out, lhsT, rhs, start=True, stop=True, tp=None, mark=False, sem=None):
        s_ = sem if sem is not None else ("TT" if mark else None)
        if tp is None:
            self._rec("tensor", lambda e, o=out, l=lhsT, r=rhs, s1=start, s2=stop:
                      e.matmul(o, l, r, start=s1, stop=s2), s_, 1)
        else:
            self._rec("tensor", lambda e, o=out, l=lhsT, r=rhs, s1=start, s2=stop, t=tp:
                      e.matmul(o, l, r, start=s1, stop=s2, tile_position=t), s_, 1)

    def transp(self, out, in_, ident, mark=False):
        self._rec("tensor", lambda e, o=out, i=in_, d=ident: e.transpose(o, i, d),
                  "TT" if mark else None, 1)

    def tmark(self):
        for j in range(len(self.q["tensor"]) - 1, -1, -1):
            fn, inc, n = self.q["tensor"][j]
            if n == 0:
                continue
            assert inc is None
            self.cnt["TT"] += 1
            self.q["tensor"][j] = (fn, "TT", 1)
            return

    def act(self, out, in_, func, bias=0.0, scale=1.0, accum_out=None):
        self._rec("scalar", lambda e, o=out, i=in_, f=func, b=bias, s=scale, a=accum_out:
                  e.activation(out=o, in_=i, func=f, bias=b, scale=s, accum_out=a), "SS", 1)

    def scopy(self, out, in_):
        self._rec("scalar", lambda e, o=out, i=in_: e.copy(out=o, in_=i), "SS", 1)

    def _v(self, fn):
        self._rec("vector", fn, "VV", 1)

    def _md(self, out):
        try:
            n = 1
            for d in out.shape[1:]:
                n *= d
        except Exception:
            n = 999
        if n <= 64:
            self._v(lambda e: e.drain())

    def tt(self, out, a, b, op):
        self._v(lambda e, o=out, x=a, y=b, p=op: e.tensor_tensor(o, x, y, p))
        self._md(out)

    def ts(self, out, in0, s1, s2, op0, op1=None, accum_out=None, safe=False):
        if not safe and (not isinstance(s1, (int, float)) or
                         (s2 is not None and not isinstance(s2, (int, float)))):
            self._v(lambda e: e.drain())
        if op1 is None:
            self._v(lambda e, o=out, i=in0, a=s1, p0=op0: e.tensor_scalar(o, i, a, None, op0=p0))
        else:
            self._v(lambda e, o=out, i=in0, a=s1, b=s2, p0=op0, p1=op1, ac=accum_out:
                    e.tensor_scalar(o, i, a, b, op0=p0, op1=p1, accum_out=ac))
        if accum_out is not None:
            self._v(lambda e: e.drain())
        else:
            self._md(out)

    def stt(self, out, in0, scalar, in1, op0, op1, accum_out=None, safe=False):
        if not safe and not isinstance(scalar, (int, float)):
            self._v(lambda e: e.drain())
        self._v(lambda e, o=out, i=in0, s=scalar, j=in1, p0=op0, p1=op1, ac=accum_out:
                e.scalar_tensor_tensor(o, i, s, j, op0=p0, op1=p1, accum_out=ac))
        if accum_out is not None:
            self._v(lambda e: e.drain())
        else:
            self._md(out)

    def red(self, out, in_, op):
        self._v(lambda e, o=out, i=in_, p=op: e.tensor_reduce(o, i, axis=AX, op=p))
        self._v(lambda e: e.drain())

    def vcopy(self, out, in_):
        self._v(lambda e, o=out, i=in_: e.tensor_copy(out=o, in_=i))
        self._md(out)

    def memset(self, ap, c):
        self._v(lambda e, a=ap, v=c: e.memset(a, v))
        self._md(ap)

    def recip(self, out, in_):
        self._v(lambda e, o=out, i=in_: e.reciprocal(out=o, in_=i))
        self._v(lambda e: e.drain())

    def scan(self, out, d0, d1, init, op0, op1):
        self._v(lambda e, o=out, a=d0, b=d1, i=init, p0=op0, p1=op1:
                e.tensor_tensor_scan(o, a, b, i, op0=p0, op1=p1))
        self._v(lambda e: e.drain())

    def shuf(self, out, in_, mask):
        self._v(lambda e, o=out, i=in_, m=mask: e.stream_shuffle(o, i, m))

    def _g(self, fn):
        self._rec("gpsimd", fn, "GG", 1)

    def iota(self, out, pattern, base=0, chm=0):
        self._g(lambda e, o=out, p=pattern, b=base, c=chm: e.iota(
            o, pattern=p, base=b, channel_multiplier=c, allow_small_or_imprecise_dtypes=True))

    def affine(self, out, in_, pattern, cmp, fill, base, chm):
        self._g(lambda e, o=out, i=in_, p=pattern, c=cmp, f=fill, b=base, m=chm:
                e.affine_select(o, i, pattern=p, compare_op=c, fill=f, base=b, channel_multiplier=m))

    def gtt(self, out, a, b, op):
        self._g(lambda e, o=out, x=a, y=b, p=op: e.tensor_tensor(o, x, y, p))

    def emit(self):
        blk = self.nc.Block().__enter__()
        for eng in self.ENGS:
            items = self.q[eng]
            sems = self.sems

            def body(e, items=items, sems=sems):
                for fn, inc, n in items:
                    ins = fn(e)
                    if inc is not None:
                        ins.then_inc(sems[inc], n)

            getattr(blk, eng)(body)
        blk.__exit__(None, None, None)


def stage_idx(s):
    return {"x1": 1, "x2": 2, "x3": 3, "route": 4, "m1": 5, "full": 5}[s]


def build(stage):
    SI = stage_idx(stage)
    nc = bass.Bass("TRN2", target_bir_lowering=False)
    di = {}

    def IN(name, shape, dt=F32):
        di[name] = nc.dram_tensor(name, list(shape), dt, kind="ExternalInput").ap()
        return di[name]

    IN("x0", [D, T]); IN("x1", [D, T])
    IN("wqkv", [D, 3 * D]); IN("wo0", [D, D])
    IN("cnw1", [D, D]); IN("cnw2", [D, 512]); IN("cnw3", [512, 256]); IN("cnw4", [256, 1])
    IN("cnb1", [128, 8]); IN("cnb2", [128, 4]); IN("cnb3", [128, 2]); IN("cnb4", [1, 1])
    IN("gcw1", [D, 512]); IN("gcw2", [512, 256]); IN("gcw3", [256, 1])
    IN("gcb1", [128, 4]); IN("gcb2", [128, 2]); IN("gcb3", [1, 1]); IN("ar", [1, 1])
    IN("lnw1", [128, 8]); IN("lnb1", [128, 8])
    IN("wq1", [D, D]); IN("wkv1", [D, 2 * D]); IN("wo1", [D, D]); IN("theta", [1, 1])
    IN("centT", [D, 8]); IN("lng", [128, 8]); IN("lnb2", [128, 8])
    IN("mw1", [D, 4 * D]); IN("mb1", [128, 32]); IN("mw2", [4 * D, D]); IN("mb2", [128, 8])
    IN("esel", [8, 1])
    IN("ones", [128, 128]); IN("lmask", [128, 128]); IN("ident", [128, 128])
    IN("fbhi", [128, 1]); IN("fblo", [128, 1]); IN("signrow", [128, 1]); IN("irow32", [1, 32])
    IN("c4096col", [8, 1]); IN("lb", [128, 128]); IN("bs", [128, 8])
    IN("exp16", [8, 128]); IN("esel128", [128, 16])

    out = nc.dram_tensor("out", [NTOK, D], F32, kind="ExternalOutput").ap()
    dbg = None
    if SI < 5:
        dbg = nc.dram_tensor("dbg", [NB * 128, 8 * T], F32, kind="ExternalOutput").ap()
    dbg3 = None
    if stage == "x2":
        dbg3 = nc.dram_tensor("dbg3", [2 * 128, 1024], F32, kind="ExternalOutput").ap()
    dbg4 = None
    dbg5 = None
    if stage == "x3":
        dbg4 = nc.dram_tensor("dbg4", [128, 4096], F32, kind="ExternalOutput").ap()
        dbg5 = nc.dram_tensor("dbg5", [128, 2048], F32, kind="ExternalOutput").ap()
    dbg6 = None
    dbg7 = None
    if stage == "m1":
        dbg6 = nc.dram_tensor("dbg6", [512, 1024], F32, kind="ExternalOutput").ap()
        dbg7 = nc.dram_tensor("dbg7", [128, 96], U32, kind="ExternalOutput").ap()
    dbg2 = None
    if stage == "route":
        dbg2 = nc.dram_tensor("dbg2", [6 * 128, 256], F32, kind="ExternalOutput").ap()

    wq1p = nc.dram_tensor("wq1p", [D, D], F32).ap()
    wkv1p = nc.dram_tensor("wkv1p", [D, 2 * D], F32).ap()
    park1 = nc.dram_tensor("park1", [128, 8, T], F32).ap()
    feat_tm = nc.dram_tensor("feat_tm", [NTOK, D], F32).ap()
    lnf_tm = nc.dram_tensor("lnf_tm", [NTOK, D], F32).ap()
    agin = nc.dram_tensor("agin", [CEXP, D], F32)
    agout = nc.dram_tensor("agout", [NC8 * CEXP, D], F32)
    scr_blk = nc.dram_tensor("scr_blk", [8, NTOK], F32)
    scr_g = nc.dram_tensor("scr_g", [NC8, 8, NTOK], F32)
    h_loc = nc.dram_tensor("h_loc", [CEXP, D], F32)
    h_g = nc.dram_tensor("h_g", [NC8 * CEXP, D], F32)

    xbuf = nc.sbuf_tensor("xbuf", [128, 16384], F32).__enter__()
    accb = nc.sbuf_tensor("accb", [128, 16384], F32).__enter__()
    qkb = nc.sbuf_tensor("qkb", [128, 4096], F32).__enter__()
    vtb = nc.sbuf_tensor("vtb", [128, 2080], F32).__enter__()
    fscr = nc.sbuf_tensor("fscr", [128, 4096], F32).__enter__()
    gscr = nc.sbuf_tensor("gscr", [128, 4096], F32).__enter__()
    wbuf = nc.sbuf_tensor("wbuf", [128, 4096], F32).__enter__()
    small = nc.sbuf_tensor("small", [128, 1024], F32).__enter__()
    u32b = nc.sbuf_tensor("u32b", [128, 96], U32).__enter__()
    cones = nc.sbuf_tensor("cones", [128, 128], F32).__enter__()
    clmask = nc.sbuf_tensor("clmask", [128, 128], F32).__enter__()
    cident = nc.sbuf_tensor("cident", [128, 128], F32).__enter__()
    cmisc = nc.sbuf_tensor("cmisc", [128, 48], F32).__enter__()
    clb = nc.sbuf_tensor("clb", [128, 128], F32).__enter__()
    cexp16 = nc.sbuf_tensor("cexp16", [8, 128], F32).__enter__()
    cesel128 = nc.sbuf_tensor("cesel128", [128, 16], F32).__enter__()
    ccent = nc.sbuf_tensor("ccent", [128, 64], F32).__enter__()
    cesel = nc.sbuf_tensor("cesel", [8, 1], F32).__enter__()

    ps = [nc.psum_tensor("psb%d" % i, [128, 512], F32).__enter__() for i in range(8)]

    kb = KB(nc)

    S_CNB1, S_CNB2, S_CNB3, S_GCB1, S_GCB2 = 0, 8, 12, 14, 18
    S_CELL = 20
    S_LNW1, S_LNB1, S_LNG, S_LNB2 = 40, 48, 56, 64
    S_BCQ, S_BCK, S_BCV = 72, 80, 88
    S_BVROW = 96
    S_MB1, S_MB2 = 224, 256
    S_P1 = 264
    S_ROWS = 296

    def cell(i):
        return small[0:1, S_CELL + i:S_CELL + i + 1]

    def p1(i):
        return small[:, S_P1 + i:S_P1 + i + 1]

    def srow(i, w=512):
        return small[0:1, S_ROWS + i * 512:S_ROWS + i * 512 + w]

    def xs(kc, lo, hi):
        return xbuf[:, kc * T + lo: kc * T + hi]

    def accs(kc, lo, hi):
        return accb[:, kc * T + lo: kc * T + hi]

    kb.load(cones[:, :], di["ones"])
    kb.load(clmask[:, :], di["lmask"])
    kb.load(cident[:, :], di["ident"])
    kb.load(cmisc[:, 44:45], di["fbhi"])
    kb.load(cmisc[:, 45:46], di["fblo"])
    kb.load(cmisc[:, 0:1], di["signrow"])
    kb.load(clb[:, :], di["lb"])
    kb.load(cmisc[0:8, 46:47], di["c4096col"])
    kb.load(cmisc[:, 1:9], di["bs"])
    kb.load(cexp16[:, :], di["exp16"])
    kb.load(cesel128[:, :], di["esel128"])
    kb.load(cmisc[0:1, 9:41], di["irow32"])
    kb.load(small[:, S_CNB1:S_CNB1 + 8], di["cnb1"])
    kb.load(small[:, S_CNB2:S_CNB2 + 4], di["cnb2"])
    kb.load(small[:, S_CNB3:S_CNB3 + 2], di["cnb3"])
    kb.load(small[:, S_GCB1:S_GCB1 + 4], di["gcb1"])
    kb.load(small[:, S_GCB2:S_GCB2 + 2], di["gcb2"])
    kb.load(cell(0), di["cnb4"])
    kb.load(cell(1), di["gcb3"])
    kb.load(cell(2), di["ar"])
    kb.load(cell(3), di["theta"])
    kb.load(small[:, S_LNW1:S_LNW1 + 8], di["lnw1"])
    kb.load(small[:, S_LNB1:S_LNB1 + 8], di["lnb1"])
    kb.load(small[:, S_LNG:S_LNG + 8], di["lng"])
    kb.load(small[:, S_LNB2:S_LNB2 + 8], di["lnb2"])
    kb.load(small[:, S_MB1:S_MB1 + 32], di["mb1"])
    kb.load(small[:, S_MB2:S_MB2 + 8], di["mb2"])
    kb.load(cesel[:, :], di["esel"])

    kb.load(ccent[:, 0:64].rearrange("p (k n) -> p k n", k=8), di["centT"].rearrange("(k p) n -> p k n", p=128))
    kb.barrier()

    first = [True]
    oc_list = [("q", oc) for oc in range(8)] + [("kv", oc) for oc in range(16)]
    for idx, (which, oc) in enumerate(oc_list):
        wsrc = di["wq1"] if which == "q" else di["wkv1"]
        wdst = wq1p if which == "q" else wkv1p
        half = idx % 2
        wb = wbuf[:, half * 2048: half * 2048 + 1024]
        src = wsrc.rearrange("(k p) n -> p k n", p=128)[:, :, oc * 128:(oc + 1) * 128]
        kb.load(wb.rearrange("p (k n) -> p k n", k=8), src)
        kb.wait("tensor", "LD")
        col = oc if which == "q" else 8 + oc
        for kc in range(8):
            kb.mm(ps[0][:, col:col + 1], wb[:, kc * 128:(kc + 1) * 128],
                  small[:, S_LNB1 + kc:S_LNB1 + kc + 1],
                  start=first[0], stop=(idx == len(oc_list) - 1 and kc == 7))
            first[0] = False
        kb.tmark()
        kb.wait("vector", "TT")
        for kc in range(8):
            kb.ts(wb[:, kc * 128:(kc + 1) * 128], wb[:, kc * 128:(kc + 1) * 128],
                  small[:, S_LNW1 + kc:S_LNW1 + kc + 1], None, Alu.mult)
        kb.wait("gpsimd", "VV")
        kb.store(wdst.rearrange("(k p) n -> p k n", p=128)[:, :, oc * 128:(oc + 1) * 128],
                 wb.rearrange("p (k n) -> p k n", k=8))
        kb.wait("sync", "SD")
    kb.wait("scalar", "TT")
    kb.scopy(small[:, S_BCQ:S_BCQ + 24], ps[0][:, 0:24])
    kb.barrier()

    kb.memset(cmisc[:, 42:43], 1.0)
    kb.memset(cmisc[:, 43:44], PIO2)
    RA = small[0:1, 808:816]
    RB = small[0:1, 816:824]
    kb.memset(RA[0:1, 0:1], 220.0)
    kb.recip(RB, RA)
    kb.tt(cell(8), cell(3), RB[0:1, 0:1], Alu.mult)
    kb.tt(p1(15), cmisc[:, 44:45], cmisc[:, 45:46], Alu.add)
    kb.wait("tensor", "VV")
    kb.mm(ps[3][:, 0:1], cones[0:1, :], cell(8), mark=True)
    kb.wait("vector", "TT")
    kb.tt(p1(1), p1(15), ps[3][:, 0:1], Alu.mult)
    kb.iota(fscr[:, 0:2048], [[1, 2048]], 0, 0)
    kb.wait("vector", "GG")
    kb.ts(qkb[:, 2048:4096], fscr[:, 0:2048], p1(1), None, Alu.mult)
    kb.ts(fscr[:, 2048:4096], qkb[:, 2048:4096], INV2PI, 0.5, Alu.mult, Alu.add)
    kb.vcopy(qkb[:, 0:2048], fscr[:, 2048:4096])
    for sh in range(16, -1, -1):
        c = float(2 ** sh)
        kb.ts(fscr[:, 0:2048], fscr[:, 2048:4096], c, None, Alu.is_ge)
        kb.stt(fscr[:, 2048:4096], fscr[:, 0:2048], -c, fscr[:, 2048:4096], Alu.mult, Alu.add)
    kb.tt(fscr[:, 0:2048], qkb[:, 0:2048], fscr[:, 2048:4096], Alu.subtract)
    kb.stt(fscr[:, 2048:4096], fscr[:, 0:2048], -CW1, qkb[:, 2048:4096], Alu.mult, Alu.add)
    kb.stt(fscr[:, 2048:4096], fscr[:, 0:2048], -CW2, fscr[:, 2048:4096], Alu.mult, Alu.add)
    kb.stt(fscr[:, 2048:4096], fscr[:, 0:2048], -CW3, fscr[:, 2048:4096], Alu.mult, Alu.add)
    kb.wait("scalar", "VV")
    kb.act(gscr[:, 2048:4096], fscr[:, 2048:4096], AF.Sin)
    kb.act(fscr[:, 0:2048], fscr[:, 2048:4096], AF.Abs)
    kb.act(gscr[:, 0:2048], fscr[:, 0:2048], AF.Sin, bias=cmisc[:, 43:44], scale=-1.0)
    kb.wait("vector", "SS")
    kb.ts(gscr[:, 2048:4096], gscr[:, 2048:4096], cmisc[:, 0:1], None, Alu.mult)
    kb.barrier()
    if dbg4 is not None:
        kb.store(dbg4, gscr[:, :])
        kb.wait("sync", "SD")
    kb.barrier()

    def wrow(i, w=512):
        return wbuf[0:1, i * 512:i * 512 + w]

    def fm_ln(src, dst, gamma=None, beta=None):
        for tc in range(4):
            lo, hi = tc * 512, tc * 512 + 512
            kb.wait("tensor", "VV")
            for kc in range(8):
                sq = fscr[:, (kc % 2) * 512:(kc % 2) * 512 + 512]
                if kc >= 2:
                    kb.wait("scalar", "TT")
                kb.act(sq, src(kc, lo, hi), AF.Square)
                kb.wait("tensor", "SS")
                kb.mm(ps[4][0:1, :], cones[:, 0:1], src(kc, lo, hi),
                      start=(kc == 0), stop=(kc == 7))
                kb.mm(ps[5][0:1, :], cones[:, 0:1], sq,
                      start=(kc == 0), stop=(kc == 7), mark=True)
            kb.wait("vector", "TT")
            kb.ts(wrow(0), ps[4][0:1, :], 1.0 / 1024, None, Alu.mult)
            kb.ts(wrow(1), ps[5][0:1, :], 1.0 / 1024, None, Alu.mult)
            kb.tt(wrow(2), wrow(0), wrow(0), Alu.mult)
            kb.tt(wrow(1), wrow(1), wrow(2), Alu.subtract)
            kb.ts(wrow(1), wrow(1), EPS, None, Alu.add)
            kb.wait("scalar", "VV")
            kb.act(wrow(2), wrow(1), AF.Ln)
            kb.act(wrow(2), wrow(2), AF.Exp, scale=-0.5)
            kb.wait("vector", "SS")
            kb.tt(wrow(3), wrow(2), wrow(2), Alu.mult)
            kb.tt(wrow(3), wrow(3), wrow(1), Alu.mult)
            kb.ts(wrow(3), wrow(3), -0.5, 1.5, Alu.mult, Alu.add)
            kb.tt(wrow(2), wrow(2), wrow(3), Alu.mult)
            kb.tt(wrow(0), wrow(0), wrow(2), Alu.mult)
            kb.wait("tensor", "VV")
            kb.mm(ps[6][:, :], cones[0:1, :], wrow(2), mark=True)
            kb.mm(ps[7][:, :], cones[0:1, :], wrow(0), mark=True)
            kb.wait("vector", "TT")
            for kc in range(8):
                tmp = fscr[:, 1024:1536]
                kb.tt(tmp, src(kc, lo, hi), ps[6][:, :], Alu.mult)
                kb.tt(dst(kc, lo, hi), tmp, ps[7][:, :], Alu.subtract)
                if gamma is not None:
                    kb.ts(dst(kc, lo, hi), dst(kc, lo, hi),
                          small[:, gamma + kc:gamma + kc + 1],
                          small[:, beta + kc:beta + kc + 1], Alu.mult, Alu.add)

    def attn_block(b, layer):
        attn_block.b = b
        wom = [0, 0]
        qkmm, vmm, womtt = [], [], []
        for hp in range(8):
            if hp == 0:
                kb.wait("sync", "TT")
                kb.wait("sync", "VV")
            else:
                kb.wait("sync", "TT", qkmm[hp - 1])
            if layer == 0:
                src4 = di["wqkv"].rearrange("(k p) n -> p k n", p=128)
                def l0w(j, base):
                    dst4 = wbuf[:, j * 1024:(j + 1) * 1024].rearrange("p (k h c) -> p k h c", k=8, h=2)
                    for hh in range(2):
                        kb.load(dst4[:, :, hh, :], src4[:, :, base + hh * 192:base + hh * 192 + 64])
                l0w(0, 2 * hp * 192)
                l0w(1, 2 * hp * 192 + 64)
                if hp > 0:
                    kb.wait("sync", "TT", vmm[hp - 1])
                l0w(2, 2 * hp * 192 + 128)
                ldqkv = kb.cnt["LD"]
                if hp > 0:
                    kb.wait("sync", "TT", womtt[hp - 1])
                kb.load(wbuf[:, 3072:4096], di["wo0"][hp * 128:(hp + 1) * 128, :])
            else:
                kb.load(wbuf[:, 0:1024].rearrange("p (k n) -> p k n", k=8),
                        wq1p.rearrange("(k p) n -> p k n", p=128)[:, :, hp * 128:(hp + 1) * 128])
                kb.load(wbuf[:, 1024:2048].rearrange("p (k n) -> p k n", k=8),
                        wkv1p.rearrange("(k p) n -> p k n", p=128)[:, :, hp * 128:(hp + 1) * 128])
                if hp > 0:
                    kb.wait("sync", "TT", vmm[hp - 1])
                kb.load(wbuf[:, 2048:3072].rearrange("p (k n) -> p k n", k=8),
                        wkv1p.rearrange("(k p) n -> p k n", p=128)[:, :, 1024 + hp * 128:1024 + hp * 128 + 128])
                ldqkv = kb.cnt["LD"]
                if hp > 0:
                    kb.wait("sync", "TT", womtt[hp - 1])
                kb.load(wbuf[:, 3072:4096], di["wo1"][hp * 128:(hp + 1) * 128, :])
            ldwo = kb.cnt["LD"]
            if layer == 1:
                kb.store(small[0:1, S_BVROW:S_BVROW + 128], small[:, S_BCV + hp:S_BCV + hp + 1])
            kb.wait("tensor", "LD", ldqkv)
            for tc in range(4):
                lo, hi = tc * 512, tc * 512 + 512
                if tc > 0:
                    kb.wait("tensor", "SS")
                for kc in range(8):
                    kb.mm(ps[0][:, :], wbuf[:, kc * 128:kc * 128 + 128], xs(kc, lo, hi),
                          start=(kc == 0), stop=(kc == 7))
                for kc in range(8):
                    kb.mm(ps[1][:, :], wbuf[:, 1024 + kc * 128:1024 + kc * 128 + 128], xs(kc, lo, hi),
                          start=(kc == 0), stop=(kc == 7), mark=True)
                kb.wait("scalar", "TT")
                if layer == 0:
                    kb.scopy(qkb[:, lo:hi], ps[0][:, :])
                    kb.scopy(qkb[:, 2048 + lo:2048 + hi], ps[1][:, :])
                else:
                    kb.act(qkb[:, lo:hi], ps[0][:, :], AF.Identity,
                           bias=small[:, S_BCQ + hp:S_BCQ + hp + 1])
                    kb.act(qkb[:, 2048 + lo:2048 + hi], ps[1][:, :], AF.Identity,
                           bias=small[:, S_BCK + hp:S_BCK + hp + 1])
            qkmm.append(kb.cnt["TT"])
            kb.memset(vtb[:, :], 1.0)
            kb.wait("tensor", "VV")
            kb.wait("scalar", "VV")
            if layer == 1:
                kb.wait("tensor", "SD")
            vt3 = vtb.rearrange("p (t c) -> p t c", c=130)
            for tkc in range(16):
                bank = ps[2 + (tkc % 2)]
                if tkc >= 2:
                    kb.wait("tensor", "SS")
                for kc in range(8):
                    kb.mm(bank[:, 0:128], xbuf[:, kc * T + tkc * 128:kc * T + tkc * 128 + 128],
                          wbuf[:, 2048 + kc * 128:2048 + kc * 128 + 128],
                          start=(kc == 0), stop=(kc == 7 and layer == 0))
                if layer == 1:
                    kb.mm(bank[:, 0:128], cones[0:1, 0:128],
                          small[0:1, S_BVROW:S_BVROW + 128], start=False, stop=True)
                kb.tmark()
                kb.wait("scalar", "TT")
                kb.scopy(vt3[:, tkc, 0:64], bank[:, 0:64])
                kb.scopy(vt3[:, tkc, 65:129], bank[:, 64:128])
            vmm.append(kb.cnt["TT"])
            if layer == 1:
                kb.wait("vector", "SS")
                for base in (0, 2048):
                    q = qkb[:, base:base + 2048]
                    kb.shuf(fscr[:, 0:2048], q, [i ^ 1 for i in range(32)])
                    kb.tt(q, q, gscr[:, 0:2048], Alu.mult)
                    kb.tt(fscr[:, 0:2048], fscr[:, 0:2048], gscr[:, 2048:4096], Alu.mult)
                    kb.tt(q, q, fscr[:, 0:2048], Alu.add)
                kb.wait("tensor", "VV")
            if layer == 1:
                kb.wait("tensor", "SS")
                for h, bank, dst in ((0, ps[6], 10), (1, ps[7], 11)):
                    for tkc in range(16):
                        kb.mm(bank[0:65, 0:1], vt3[:, tkc, h * 65:h * 65 + 65], cones[:, 0:1],
                              start=(tkc == 0), stop=(tkc == 15))
                    kb.tmark()
                kb.wait("scalar", "TT")
                kb.scopy(p1(10)[0:65, :], ps[6][0:65, 0:1])
                kb.scopy(p1(11)[0:65, :], ps[7][0:65, 0:1])
            scale = (1.0 / 32.0) if layer == 0 else 0.125
            sevac = []
            avm = []
            for qt in range(4):
                kmax = 4 * qt + 4 if layer == 0 else 16
                for kc in range(kmax):
                    sb0 = ps[0] if kc % 2 == 0 else ps[2]
                    sb1 = ps[1] if kc % 2 == 0 else ps[3]
                    if len(sevac) >= 2:
                        kb.wait("tensor", "SS", sevac[-2])
                    kb.mm(sb0[:, :], qkb[0:64, 2048 + kc * 128:2048 + kc * 128 + 128],
                          qkb[0:64, qt * 512:qt * 512 + 512], tp=(0, 0))
                    kb.tmark()
                    kb.mm(sb1[:, :], qkb[64:128, 2048 + kc * 128:2048 + kc * 128 + 128],
                          qkb[64:128, qt * 512:qt * 512 + 512], tp=(64, 0))
                    kb.tmark()
                    if len(avm) >= 2:
                        kb.wait("scalar", "TA", avm[-2])
                    kb.wait("scalar", "TT")
                    p0 = fscr[:, (kc % 2) * 512:(kc % 2) * 512 + 512]
                    p1_ = fscr[:, 1024 + (kc % 2) * 512:1024 + (kc % 2) * 512 + 512]
                    if layer == 0:
                        kb.act(p0, sb0[:, :], AF.Exp, scale=scale)
                        kb.act(p1_, sb1[:, :], AF.Exp, scale=scale)
                        if kc >= 4 * qt:
                            kb.wait("gpsimd", "SS")
                            kb.affine(p0, p0, [[1, 512]], Alu.is_ge, 0.0,
                                      qt * 512 - kc * 128, -1)
                            kb.affine(p1_, p1_, [[1, 512]], Alu.is_ge, 0.0,
                                      qt * 512 - kc * 128, -1)
                            kb.wait("tensor", "GG")
                    else:
                        kb.act(p0, sb0[:, :], AF.Square, scale=scale, bias=cmisc[:, 42:43])
                        kb.act(p1_, sb1[:, :], AF.Square, scale=scale, bias=cmisc[:, 42:43])
                    kb.wait("tensor", "SS")
                    if kc == 0:
                        kb.wait("tensor", "VV")
                    kb.mm(ps[4][0:65, :], vt3[:, kc, 0:65], p0,
                          start=(kc == 0), stop=(kc == kmax - 1))
                    kb.mm(ps[5][0:65, :], vt3[:, kc, 65:130], p1_,
                          start=(kc == 0), stop=(kc == kmax - 1), sem="TA")
                    avm.append(kb.cnt["TA"])
                    sevac.append(kb.cnt["SS"])
                kb.wait("vector", "TT")
                kb.wait("vector", "TA")
                for h, ob in ((0, ps[4]), (1, ps[5])):
                    rr = small[0:1, S_ROWS:S_ROWS + 512]
                    if h == 1:
                        kb.wait("vector", "TT")
                    if layer == 0:
                        kb.recip(rr, ob[64:65, :])
                    else:
                        kb.ts(rr, ob[64:65, :], 2048.0, None, Alu.add)
                        kb.recip(rr, rr)
                    kb.wait("tensor", "VV")
                    rb = ps[6] if h == 0 else ps[7]
                    kb.mm(rb[0:64, :], cones[0:1, 0:64], rr, mark=True)
                    kb.wait("scalar", "TT")
                    kb.scopy(fscr[0:64, 0:512], rb[0:64, :])
                    kb.wait("vector", "SS")
                    dst = fscr[h * 64:h * 64 + 64, 2048 + qt * 512:2048 + qt * 512 + 512]
                    if layer == 0:
                        kb.tt(dst, ob[0:64, :], fscr[0:64, 0:512], Alu.mult)
                    else:
                        kb.stt(dst, ob[0:64, :], p1(10 + h)[0:64, :],
                               fscr[0:64, 0:512], Alu.add, Alu.mult)
            kb.wait("tensor", "LD", ldwo)
            kb.wait("tensor", "VV")
            if layer == 1 and hp == 0 and attn_block.b == 0 and dbg5 is not None:
                kb.wait("gpsimd", "VV")
                kb.store(dbg5, fscr[:, 2048:4096])
                kb.wait("tensor", "SD")
            for tc in range(4):
                for oc in range(8):
                    bank = ps[2 + (oc % 2)]
                    if tc * 8 + oc >= 2:
                        kb.wait("tensor", "VV", wom[-2])
                    kb.mm(bank[:, :], wbuf[:, 3072 + oc * 128:3072 + oc * 128 + 128],
                          fscr[:, 2048 + tc * 512:2048 + tc * 512 + 512], mark=True)
                    kb.wait("vector", "TT")
                    src1 = xs(oc, tc * 512, tc * 512 + 512) if (layer == 0 and hp == 0) \
                        else accs(oc, tc * 512, tc * 512 + 512)
                    kb.tt(accs(oc, tc * 512, tc * 512 + 512), bank[:, :], src1, Alu.add)
                    wom.append(kb.cnt["VV"])
            womtt.append(kb.cnt["TT"])

    def ffn_block(b):
        ffn_block.b = b
        x3 = xbuf.rearrange("p (k t) -> p k t", k=8)
        kb.wait("sync", "TT")
        kb.load(wbuf[:, 0:4096].rearrange("p (k n) -> p k n", k=8),
                di["gcw1"].rearrange("(k p) n -> p k n", p=128))
        kb.wait("gpsimd", "VV")
        kb.wait("gpsimd", "SS")
        kb.store(park1, x3)
        for kc in range(8):
            kb.red(p1(24 + kc), xs(kc, 0, T), Alu.add)
        kb.ts(small[:, S_P1 + 24:S_P1 + 32], small[:, S_P1 + 24:S_P1 + 32],
              1.0 / T, None, Alu.mult)
        kb.wait("tensor", "LD")
        kb.wait("tensor", "VV")
        kb.vcopy(accb[:, :], xbuf[:, :])
        if dbg3 is not None and ffn_block.b == 0:
            kb.wait("gpsimd", "VV")
            kb.store(dbg3[0:128, :], small[:, 0:1024])
        first = True
        for oc in range(4):
            for kc in range(8):
                kb.mm(ps[0][:, oc:oc + 1], wbuf[:, kc * 512 + oc * 128:kc * 512 + oc * 128 + 128],
                      p1(24 + kc), start=first, stop=(oc == 3 and kc == 7))
                first = False
        kb.tmark()
        kb.wait("scalar", "TT")
        for oc in range(4):
            kb.act(p1(16 + oc), ps[0][:, oc:oc + 1], AF.Gelu,
                   bias=small[:, S_GCB1 + oc:S_GCB1 + oc + 1])
        kb.wait("sync", "TT")
        kb.load(wbuf[:, 0:1024].rearrange("p (k n) -> p k n", k=4),
                di["gcw2"].rearrange("(k p) n -> p k n", p=128))
        kb.wait("tensor", "LD")
        kb.wait("tensor", "SS")
        first = True
        for oc in range(2):
            for kc in range(4):
                kb.mm(ps[1][:, oc:oc + 1], wbuf[:, kc * 256 + oc * 128:kc * 256 + oc * 128 + 128],
                      p1(16 + kc), start=first, stop=(oc == 1 and kc == 3))
                first = False
        kb.tmark()
        kb.wait("scalar", "TT")
        for oc in range(2):
            kb.act(p1(20 + oc), ps[1][:, oc:oc + 1], AF.Gelu,
                   bias=small[:, S_GCB2 + oc:S_GCB2 + oc + 1])
        kb.wait("sync", "TT")
        kb.load(wbuf[:, 0:2].rearrange("p (k n) -> p k n", k=2),
                di["gcw3"].rearrange("(k p) n -> p k n", p=128), slow=True)
        kb.wait("tensor", "LD")
        kb.wait("tensor", "SS")
        for kc in range(2):
            kb.mm(ps[2][0:1, 0:1], wbuf[:, kc:kc + 1], p1(20 + kc),
                  start=(kc == 0), stop=(kc == 1))
        kb.tmark()
        kb.wait("scalar", "TT")
        kb.act(cell(4), ps[2][0:1, 0:1], AF.Sigmoid, bias=cell(1))
        kb.wait("vector", "SS")
        kb.ts(cell(5), cell(4), 0.7, None, Alu.mult)
        kb.wait("sync", "SD")
        kb.wait("sync", "VV")
        kb.load(xbuf[:, 8192:16384].rearrange("p (k n) -> p k n", k=8),
                di["cnw1"].rearrange("(k p) n -> p k n", p=128))
        kb.load(qkb[:, 0:4096].rearrange("p (k n) -> p k n", k=8),
                di["cnw2"].rearrange("(k p) n -> p k n", p=128))
        kb.load(vtb[:, 0:1024].rearrange("p (k n) -> p k n", k=4),
                di["cnw3"].rearrange("(k p) n -> p k n", p=128))
        kb.load(vtb[:, 1024:1026].rearrange("p (k n) -> p k n", k=2),
                di["cnw4"].rearrange("(k p) n -> p k n", p=128), slow=True)
        kb.wait("tensor", "LD")
        for cyc in range(3):
            for tc in range(4):
                lo, hi = tc * 512, tc * 512 + 512
                kb.wait("tensor", "VV")
                kb.wait("tensor", "SS")
                h1e = []
                for oc in range(8):
                    bank = ps[oc % 2]
                    if oc >= 2:
                        kb.wait("tensor", "SS", h1e[oc - 2])
                    for kc in range(8):
                        kb.mm(bank[:, :], xbuf[:, 8192 + kc * 1024 + oc * 128:8192 + kc * 1024 + oc * 128 + 128],
                              accs(kc, lo, hi), start=(kc == 0), stop=(kc == 7))
                    kb.tmark()
                    kb.wait("scalar", "TT")
                    kb.act(xbuf[:, oc * 512:oc * 512 + 512], bank[:, :], AF.Gelu,
                           bias=small[:, S_CNB1 + oc:S_CNB1 + oc + 1])
                    h1e.append(kb.cnt["SS"])
                for oc in range(4):
                    bank = ps[2 + oc % 2]
                    if oc >= 2:
                        kb.wait("tensor", "SS")
                    for kc in range(8):
                        kb.mm(bank[:, :], qkb[:, kc * 512 + oc * 128:kc * 512 + oc * 128 + 128],
                              xbuf[:, kc * 512:kc * 512 + 512], start=(kc == 0), stop=(kc == 7))
                    kb.tmark()
                    kb.wait("scalar", "TT")
                    kb.act(xbuf[:, 4096 + oc * 512:4096 + oc * 512 + 512], bank[:, :], AF.Gelu,
                           bias=small[:, S_CNB2 + oc:S_CNB2 + oc + 1])
                for oc in range(2):
                    bank = ps[4 + oc % 2]
                    for kc in range(4):
                        kb.mm(bank[:, :], vtb[:, kc * 256 + oc * 128:kc * 256 + oc * 128 + 128],
                              xbuf[:, 4096 + kc * 512:4096 + kc * 512 + 512],
                              start=(kc == 0), stop=(kc == 3))
                    kb.tmark()
                    kb.wait("scalar", "TT")
                    kb.act(xbuf[:, 6144 + oc * 512:6144 + oc * 512 + 512], bank[:, :], AF.Gelu,
                           bias=small[:, S_CNB3 + oc:S_CNB3 + oc + 1])
                for kc in range(2):
                    kb.mm(ps[6][0:1, :], vtb[:, 1024 + kc:1024 + kc + 1],
                          xbuf[:, 6144 + kc * 512:6144 + kc * 512 + 512],
                          start=(kc == 0), stop=(kc == 1))
                kb.tmark()
                kb.wait("scalar", "TT")
                kb.act(wrow(0), ps[6][0:1, :], AF.Sigmoid, bias=cell(0))
                kb.wait("vector", "SS")
                kb.ts(wrow(1), wrow(0), 0.3, cell(5), Alu.mult, Alu.add)
                kb.ts(wrow(1), wrow(1), 0.5, cell(2), Alu.subtract, Alu.mult)
                kb.ts(wrow(1), wrow(1), 2.0, 0.1, Alu.mult, Alu.add)
                kb.wait("tensor", "VV")
                kb.mm(ps[7][:, :], cones[0:1, :], wrow(1), mark=True)
                kb.wait("vector", "TT")
                for j in range(4):
                    sgn = 1.0 if j < 2 else -1.0
                    a = accs(j, lo, hi)
                    bb = accs(j + 4, lo, hi)
                    kb.tt(fscr[:, 0:512], bb, ps[7][:, :], Alu.mult)
                    kb.tt(fscr[:, 512:1024], a, ps[7][:, :], Alu.mult)
                    kb.stt(a, fscr[:, 0:512], sgn, a, Alu.mult, Alu.add)
                    kb.stt(bb, fscr[:, 512:1024], -sgn, bb, Alu.mult, Alu.add)
        for tc in range(4):
            lo, hi = tc * 512, tc * 512 + 512
            for kc in range(8):
                tile = fscr[:, 1024 + (kc % 2) * 512:1024 + (kc % 2) * 512 + 512]
                if tc * 8 + kc >= 2:
                    kb.wait("sync", "VV")
                kb.load(tile, park1[:, kc, lo:hi])
                kb.wait("vector", "LD")
                kb.tt(xs(kc, lo, hi), tile, accs(kc, lo, hi), Alu.add)
        kb.barrier()
        fm_ln(xs, xs)

    def l1_tail(b):
        fm_ln(xs, accs, gamma=S_LNG, beta=S_LNB2)
        kb.barrier()
        if _TSTOP <= 1:
            return
        for tc in range(4):
            lo, hi = tc * 512, tc * 512 + 512
            for kc in range(8):
                kb.mm(ps[0][0:8, :], ccent[:, kc * 8:kc * 8 + 8], accs(kc, lo, hi),
                      start=(kc == 0), stop=(kc == 7))
            kb.tmark()
            kb.wait("scalar", "TT")
            kb.scopy(fscr[0:8, 0:512], ps[0][0:8, :])
            kb.wait("gpsimd", "SS")
            j0 = b * T + tc * 512
            kb.store(scr_blk.ap()[:, j0:j0 + 512], fscr[0:8, 0:512])
            kb.wait("tensor", "SD")
        if _TSTOP <= 2:
            return
        NTG = 1 if _TSTOP == 4 else 16
        DOST = _TSTOP != 5
        for tg in range(NTG):
            for (srcb, dst_dram, off) in ((xbuf, feat_tm, 1024), (accb, lnf_tm, 2048)):
                for kc in range(8):
                    bank = ps[1 + kc % 2]
                    kb.wait("tensor", "SS")
                    kb.transp(bank[:, 0:128], srcb[:, kc * T + tg * 128:kc * T + tg * 128 + 128],
                              cident[:, :], mark=True)
                    kb.wait("scalar", "TT")
                    kb.scopy(fscr[:, off + kc * 128:off + kc * 128 + 128], bank[:, 0:128])
                if DOST:
                    kb.wait("gpsimd", "SS")
                    r0 = (b * 16 + tg) * 128
                    kb.store(dst_dram[r0:r0 + 128, :], fscr[:, off:off + 1024])
            if DOST:
                kb.wait("scalar", "SD")

    def cross_reduce(dst, src, op, bA, bB):
        kb.wait("gpsimd", "VV")
        kb.store(small[0:1, 680:808], src)
        kb.wait("vector", "SD")
        kb.red(cell(14), small[0:1, 680:808], op)
        kb.wait("tensor", "VV")
        kb.mm(bB[:, 0:1], cones[0:1, :], cell(14), mark=True)
        kb.wait("vector", "TT")
        kb.vcopy(dst, bB[:, 0:1])

    def routing():
        G = lambda c0, w: gscr[:, c0:c0 + w]
        s_all = lambda e: gscr[:, e * 256:e * 256 + 256]
        mem = lambda e: fscr[:, e * 256:e * 256 + 256]
        s_cur = G(2048, 256)
        dump = G(2304, 256)
        eq_t = G(2560, 256)
        gt_t = G(2816, 256)
        u_t = G(3072, 256)
        taken = G(3328, 256)
        scn = G(3584, 256)
        zrow = qkb[0:1, 2048:2304]

        if _RSTOP <= -1:
            return
        kb.cc("AllGather", scr_blk.ap(), scr_g.ap())
        kb.wait("sync", "CC")
        if _RSTOP <= -2:
            return
        for c in range(8):
            kb.load(gscr[c * 16:(c + 1) * 16, 0:2048].rearrange("p (e f) -> p e f", e=8),
                    scr_g.ap().rearrange("c e (pr f) -> c pr e f", pr=16)[c, :, :, :])
        kb.memset(zrow, 0.0)
        kb.memset(taken, 0.0)
        kb.wait("vector", "LD")
        if _RSTOP <= 0:
            return
        LO, HI, TH, PRED, CNT, M_, VREP, NOTP, D_ = (p1(0), p1(1), p1(2), p1(3),
                                                     p1(4), p1(5), p1(6), p1(7), p1(8))
        for e in range(8):
            kb.wait("tensor", "VV")
            if e == 0:
                kb.vcopy(s_cur, s_all(0))
            elif e < 7:
                kb.stt(s_cur, taken, -2e30, s_all(e), Alu.mult, Alu.add)
            if e == 7:
                kb.ts(mem(7), taken, -1.0, 1.0, Alu.mult, Alu.add)
            else:
                kb.stt(u_t, taken, 4e30, s_cur, Alu.mult, Alu.add)
                kb.red(D_, u_t, Alu.min)
                cross_reduce(LO, D_, Alu.min, ps[0], ps[1])
                kb.red(D_, s_cur, Alu.max)
                cross_reduce(HI, D_, Alu.max, ps[0], ps[1])
                for _ in range(30):
                    kb.ts(TH, LO, HI, 0.5, Alu.add, Alu.mult)
                    kb.ts(dump, s_cur, TH, 0.0, Alu.is_ge, Alu.add, accum_out=CNT)
                    kb.wait("tensor", "VV")
                    kb.mm(ps[0][:, 0:1], cones[:, :], CNT, mark=True)
                    kb.wait("vector", "TT")
                    kb.ts(PRED, ps[0][:, 0:1], float(CEXP), None, Alu.is_ge)
                    kb.tt(D_, TH, LO, Alu.subtract)
                    kb.tt(D_, PRED, D_, Alu.mult)
                    kb.tt(LO, LO, D_, Alu.add)
                    kb.ts(NOTP, PRED, -1.0, 1.0, Alu.mult, Alu.add)
                    kb.tt(D_, TH, HI, Alu.subtract)
                    kb.tt(D_, NOTP, D_, Alu.mult)
                    kb.tt(HI, HI, D_, Alu.add)
                kb.ts(dump, s_cur, LO, None, Alu.is_lt)
                kb.stt(u_t, dump, 4e30, s_cur, Alu.mult, Alu.add)
                kb.red(D_, u_t, Alu.min)
                cross_reduce(VREP, D_, Alu.min, ps[0], ps[1])
                kb.ts(dump, s_cur, VREP, 0.0, Alu.is_gt, Alu.add, accum_out=CNT)
                kb.wait("tensor", "VV")
                kb.mm(ps[0][:, 0:1], cones[:, :], CNT, mark=True)
                kb.wait("vector", "TT")
                kb.ts(M_, ps[0][:, 0:1], -1.0, float(CEXP), Alu.mult, Alu.add)
                kb.ts(eq_t, s_cur, VREP, None, Alu.is_equal)
                kb.scan(scn, eq_t, eq_t, 0.0, Alu.add, Alu.bypass)
                kb.vcopy(p1(9), scn[:, 255:256])
                kb.tt(scn, scn, eq_t, Alu.subtract)
                kb.wait("tensor", "VV")
                kb.mm(ps[2][:, 0:1], clmask[:, :], p1(9), mark=True)
                kb.wait("vector", "TT")
                kb.ts(scn, scn, ps[2][:, 0:1], None, Alu.add)
                kb.ts(dump, scn, M_, None, Alu.is_lt)
                kb.ts(gt_t, s_cur, VREP, None, Alu.is_gt)
                kb.tt(eq_t, eq_t, dump, Alu.mult)
                kb.tt(mem(e), gt_t, eq_t, Alu.add)
            if e < 7:
                kb.tt(taken, taken, mem(e), Alu.add)
        if _RSTOP <= 1:
            return
        hrow_full = G(2048, 256)
        lp_full = G(2304, 256)
        kb.wait("vector", "TT")
        kb.memset(hrow_full, 0.0)
        kb.memset(lp_full, 0.0)
        bp8m = qkb[0:8, 2304:2312]
        st8m = qkb[0:8, 2312:2320]
        cnt8 = qkb[0:8, 2320:2328]
        run8 = qkb[0:8, 2328:2336]
        c1 = p1(10)
        c3 = p1(11)
        kb.memset(run8[:, 0:1], 0.0)
        for e in range(8):
            kb.wait("tensor", "VV")
            kb.scan(scn, mem(e), mem(e), 0.0, Alu.add, Alu.bypass)
            kb.vcopy(p1(9), scn[:, 255:256])
            kb.tt(scn, scn, mem(e), Alu.subtract)
            kb.wait("tensor", "VV")
            kb.mm(ps[2][:, 0:1], clmask[:, :], p1(9), start=True, stop=True)
            kb.mm(ps[3][:, 0:1], clb[:, :], p1(9), start=True, stop=True)
            kb.mm(ps[4][0:8, 0:1], cmisc[:, 1:9], p1(9), mark=True)
            kb.wait("vector", "TT")
            kb.vcopy(p1(12)[0:8, :], ps[4][0:8, 0:1])
            kb.vcopy(qkb[0:8, 2320 + e:2321 + e], p1(12)[0:8, :])
            kb.vcopy(st8m[:, e:e + 1], run8[:, 0:1])
            kb.tt(run8[:, 0:1], run8[:, 0:1], p1(12)[0:8, :], Alu.add)
            kb.wait("tensor", "VV")
            kb.mm(ps[5][0:8, 0:1], clmask[0:8, 0:8], p1(12)[0:8, :], mark=True)
            kb.wait("vector", "TT")
            kb.vcopy(bp8m[:, e:e + 1], ps[5][0:8, 0:1])
            kb.ts(u_t, scn, ps[2][:, 0:1], float(CEXP * e), Alu.add, Alu.add)
            kb.tt(taken, mem(e), u_t, Alu.mult)
            kb.tt(hrow_full, hrow_full, taken, Alu.add)
            kb.vcopy(c1, ps[2][:, 0:1])
            kb.tt(c1, c1, ps[3][:, 0:1], Alu.subtract)
            kb.wait("tensor", "VV")
            kb.mm(ps[6][:, 0:1], cexp16[:, :], st8m[:, e:e + 1], mark=True)
            kb.wait("vector", "TT")
            kb.tt(c3, c1, ps[6][:, 0:1], Alu.add)
            kb.ts(u_t, scn, c3, None, Alu.add)
            kb.tt(taken, mem(e), u_t, Alu.mult)
            kb.tt(lp_full, lp_full, taken, Alu.add)
        if _RSTOP <= 2:
            return
        kb.wait("tensor", "VV")
        kb.mm(ps[0][0:16, 0:256], cesel128[:, :], lp_full, mark=True)
        kb.mm(ps[1][0:16, 0:256], cesel128[:, :], hrow_full, mark=True)
        kb.wait("scalar", "TT")
        kb.scopy(gscr[0:16, 2048:2304], ps[0][0:16, 0:256])
        kb.scopy(gscr[0:16, 2304:2560], ps[1][0:16, 0:256])
        kb.wait("gpsimd", "SS")
        for k in range(32):
            pr, half = k // 2, (k % 2) * 128
            kb.store(fscr[:, 2048 + k:2048 + k + 1], gscr[pr:pr + 1, 2048 + half:2048 + half + 128])
            kb.store(fscr[:, 2080 + k:2080 + k + 1], gscr[pr:pr + 1, 2304 + half:2304 + half + 128])
        kb.wait("vector", "SD")
        kb.ts(u32b[:, 0:32], fscr[:, 2048:2080], 0.0, None, Alu.add)
        kb.ts(u32b[:, 32:64], fscr[:, 2080:2112], 0.0, None, Alu.add)
        if _RSTOP <= 3:
            return
        esr = qkb[0:1, 2336:2344]
        kb.wait("gpsimd", "VV")
        kb.store(esr, cesel[:, :])
        kb.wait("tensor", "SD")
        kb.mm(ps[7][0:8, 0:8], cones[0:1, 0:8], esr, mark=True)
        kb.wait("vector", "TT")
        esrep8 = qkb[0:8, 2344:2352]
        kb.vcopy(esrep8, ps[7][0:8, 0:8])
        cumown = p1(13)
        stown = p1(14)
        kb.memset(cumown[0:8, :], 0.0)
        kb.memset(stown[0:8, :], 0.0)
        for e in range(8):
            kb.stt(cumown[0:8, :], bp8m[:, e:e + 1], esrep8[:, e:e + 1], cumown[0:8, :],
                   Alu.mult, Alu.add)
            kb.stt(stown[0:8, :], st8m[:, e:e + 1], esrep8[:, e:e + 1], stown[0:8, :],
                   Alu.mult, Alu.add)
        lam = p1(15)
        kb.tt(lam[0:8, :], stown[0:8, :], cumown[0:8, :], Alu.subtract)
        kb.tt(lam[0:8, :], lam[0:8, :], cmisc[0:8, 46:47], Alu.add)
        kb.wait("gpsimd", "VV")
        kb.store(qkb[0:1, 2352:2360], lam[0:8, :])
        kb.store(qkb[0:1, 2360:2368], cumown[0:8, :])
        kb.wait("tensor", "SD")
        kb.mm(ps[6][:, 0:8], cones[0:1, :], qkb[0:1, 2352:2360])
        kb.mm(ps[7][:, 0:8], cones[0:1, :], qkb[0:1, 2360:2368], mark=True)
        kb.wait("vector", "TT")
        lamrep = G(3584, 8)
        cumrep = G(3592, 8)
        kb.vcopy(lamrep, ps[6][:, 0:8])
        kb.vcopy(cumrep, ps[7][:, 0:8])
        sgrid = G(3600, 32)
        kb.iota(sgrid, [[128, 32]], 0, 1)
        kb.wait("vector", "GG")
        acc32 = G(3632, 32)
        ge_t = lambda c: gscr[:, 3664 + c * 32:3664 + c * 32 + 32]
        kb.memset(acc32, 0.0)
        for c in range(8):
            kb.ts(ge_t(c), sgrid, cumrep[:, c:c + 1], None, Alu.is_ge)
        for c in range(8):
            if c < 7:
                kb.tt(s_cur[:, 0:32], ge_t(c), ge_t(c + 1), Alu.subtract)
                ind = s_cur[:, 0:32]
            else:
                ind = ge_t(7)
            kb.stt(acc32, ind, lamrep[:, c:c + 1], acc32, Alu.mult, Alu.add)
        kb.tt(acc32, sgrid, acc32, Alu.add)
        kb.ts(u32b[:, 64:96], acc32, 0.0, None, Alu.add)
        kb.barrier()
        if stage == "route" and _RSTOP >= 5:
            kb.store(dbg2[0 * 128:1 * 128, :], mem(0))
            kb.memset(gt_t, 0.0)
            kb.vcopy(gt_t[:, 0:64], fscr[:, 2048:2112])
            kb.wait("gpsimd", "VV")
            kb.store(dbg2[1 * 128:2 * 128, :], gt_t)
            kb.store(dbg2[2 * 128:3 * 128, :], mem(7))
            kb.store(dbg2[3 * 128:4 * 128, :], hrow_full)
            kb.store(dbg2[4 * 128:5 * 128, :], lp_full)
            kb.memset(dump, 0.0)
            kb.vcopy(dump[:, 0:32], acc32)
            kb.wait("gpsimd", "VV")
            kb.store(dbg2[5 * 128:6 * 128, :], dump)
            kb.wait("gpsimd", "SD")
        if _RSTOP <= 4:
            return
        for k in range(32):
            tile = fscr[:, (k % 2) * 1024:(k % 2) * 1024 + 1024]
            kb.wait("sync", "SD")
            kb.load(tile, lnf_tm[k * 128:k * 128 + 128, :])
            kb.wait("gpsimd", "LD")
            kb.ind_scatter(agin.ap(), tile, u32b[:, k:k + 1])
        if dbg7 is not None:
            kb.wait("gpsimd", "VV")
            kb.store(dbg7, u32b[:, :])
        kb.cc("AllGather", agin.ap(), agout.ap())
        if dbg6 is not None and _MSTOP == 1:
            kb.wait("gpsimd", "CC")
            kb.store(dbg6[0:256, :], agout.ap()[0:256, :])
            kb.store(dbg6[256:512, :], agout.ap()[CEXP * 3:CEXP * 3 + 256, :])

    def moe():
        kb.wait("gpsimd", "CC")
        kb.wait("tensor", "CC")
        mw1r = di["mw1"].rearrange("(k p) n -> p k n", p=128)
        for st in range(8):
            for g in range(4):
                tile = fscr[:, (g % 2) * 1024:(g % 2) * 1024 + 1024]
                kb.wait("gpsimd", "SS")
                kb.wait("gpsimd", "SD")
                kb.ind_gather(tile, agout.ap(), u32b[:, 64 + st * 4 + g:64 + st * 4 + g + 1])
                if dbg6 is not None and _MSTOP == 5 and st == 4 and g == 0:
                    kb.wait("gpsimd", "SD")
                    kb.store(dbg6[0:128, :], tile)
                    kb.wait("tensor", "SD")
                kb.wait("tensor", "SD")
                for kc in range(8):
                    bank = ps[kc % 2]
                    kb.wait("tensor", "SS")
                    kb.transp(bank[:, 0:128], tile[:, kc * 128:kc * 128 + 128],
                              cident[:, :], mark=True)
                    kb.wait("scalar", "TT")
                    kb.scopy(qkb[:, kc * 512 + g * 128:kc * 512 + g * 128 + 128],
                             bank[:, 0:128])
            if dbg6 is not None and _MSTOP == 3 and st == 4:
                kb.wait("gpsimd", "SS")
                kb.store(dbg6[0:128, :], qkb[:, 0:1024])
                kb.store(dbg6[128:256, :], qkb[:, 1024:2048])
                kb.store(dbg6[256:384, :], qkb[:, 2048:3072])
                kb.store(dbg6[384:512, :], qkb[:, 3072:4096])
                kb.wait("sync", "SD")
            w1evac = []
            w1mm = []
            for hc in range(32):
                wsl = accb[:, (hc % 2) * 1024:(hc % 2) * 1024 + 1024]
                if hc >= 2:
                    kb.wait("sync", "TT", w1mm[hc - 2])
                else:
                    kb.wait("sync", "TT")
                kb.load(wsl.rearrange("p (k n) -> p k n", k=8), mw1r[:, :, hc * 128:hc * 128 + 128])
                kb.wait("tensor", "LD")
                bank = ps[2 + hc % 2]
                if hc < 2:
                    kb.wait("tensor", "SS")
                else:
                    kb.wait("tensor", "SS", w1evac[hc - 2])
                for kc in range(8):
                    kb.mm(bank[:, :], wsl[:, kc * 128:kc * 128 + 128],
                          qkb[:, kc * 512:kc * 512 + 512], start=(kc == 0), stop=(kc == 7))
                kb.tmark()
                w1mm.append(kb.cnt["TT"])
                kb.wait("scalar", "TT")
                kb.act(xbuf[:, hc * 512:hc * 512 + 512], bank[:, :], AF.Gelu_apprx_tanh,
                       bias=small[:, S_MB1 + hc:S_MB1 + hc + 1])
                w1evac.append(kb.cnt["SS"])
            if dbg6 is not None and _MSTOP == 4 and st == 4:
                kb.wait("gpsimd", "SS")
                for q_ in range(4):
                    kb.store(dbg6[q_ * 128:(q_ + 1) * 128, :], xbuf[:, q_ * 1024:(q_ + 1) * 1024])
                kb.wait("sync", "SD")
            kb.wait("tensor", "SS")
            w2mm = []
            for hc in range(32):
                w2sl = accb[:, 2048 + (hc % 2) * 1024:2048 + (hc % 2) * 1024 + 1024]
                if hc >= 2:
                    kb.wait("sync", "TT", w2mm[hc - 2])
                else:
                    kb.wait("sync", "TT")
                kb.load(w2sl, di["mw2"][hc * 128:hc * 128 + 128, :])
                kb.wait("tensor", "LD")
                for oc in range(8):
                    kb.mm(ps[oc][:, :], w2sl[:, oc * 128:oc * 128 + 128],
                          xbuf[:, hc * 512:hc * 512 + 512],
                          start=(hc == 0), stop=(hc == 31))
                kb.tmark()
                w2mm.append(kb.cnt["TT"])
            kb.wait("scalar", "TT")
            for oc in range(8):
                kb.act(qkb[:, oc * 512:oc * 512 + 512], ps[oc][:, :], AF.Identity,
                       bias=small[:, S_MB2 + oc:S_MB2 + oc + 1])
            for g in range(4):
                tile = fscr[:, (g % 2) * 1024:(g % 2) * 1024 + 1024]
                if g >= 2:
                    kb.wait("scalar", "SD")
                for kc in range(8):
                    bank = ps[kc % 2]
                    kb.wait("tensor", "SS")
                    kb.transp(bank[:, 0:128], qkb[:, kc * 512 + g * 128:kc * 512 + g * 128 + 128],
                              cident[:, :], mark=True)
                    kb.wait("scalar", "TT")
                    kb.scopy(tile[:, kc * 128:kc * 128 + 128], bank[:, 0:128])
                kb.wait("gpsimd", "SS")
                kb.store(h_loc.ap()[st * 512 + g * 128:st * 512 + g * 128 + 128, :], tile)
        if dbg6 is not None and _MSTOP == 2:
            kb.wait("gpsimd", "SD")
            kb.store(dbg6[0:256, :], h_loc.ap()[0:256, :])
            kb.store(dbg6[256:512, :], h_loc.ap()[2048:2048 + 256, :])
        kb.cc("AllGather", h_loc.ap(), h_g.ap())

    def final():
        def xslot(g, j):
            return xbuf[:, ((g % 2) * 4 + j) * 1024:((g % 2) * 4 + j) * 1024 + 1024]

        def hslot(g, j):
            return accb[:, ((g % 2) * 4 + j) * 1024:((g % 2) * 4 + j) * 1024 + 1024]

        def uslot(j):
            return qkb[:, j * 1024:j * 1024 + 1024]

        du = vtb[:, 0:1024]
        kb.wait("sync", "TT")
        kb.wait("sync", "SS")
        kb.wait("sync", "VV")
        for g in range(2):
            for j in range(4):
                kb.load(xslot(g, j), feat_tm[(g * 4 + j) * 128:(g * 4 + j) * 128 + 128, :])
        kb.wait("gpsimd", "CC")
        kb.wait("gpsimd", "TT")
        kb.wait("gpsimd", "VV")
        ucons = []
        for g in range(8):
            if g >= 2:
                kb.wait("gpsimd", "VV", ucons[g - 2])
            for j in range(4):
                kb.ind_gather(hslot(g, j), h_g.ap(), u32b[:, 32 + g * 4 + j:32 + g * 4 + j + 1])
            gLD = kb.cnt["LD"]
            gSD = kb.cnt["SD"]
            kb.wait("vector", "LD", gLD)
            kb.wait("vector", "SD", gSD)
            if g == 0:
                kb.wait("vector", "TT")
                kb.wait("vector", "SS")
            for j in range(4):
                kb.stt(uslot(j), xslot(g, j), 2.0, hslot(g, j), Alu.mult, Alu.add)
            ucons.append(kb.cnt["VV"])
            for j in range(4):
                kb.ts(du, uslot(j), 0.0, 0.0, Alu.add, Alu.add, accum_out=p1(0 + j))
                kb.stt(du, uslot(j), 1.0, uslot(j), Alu.mult, Alu.mult, accum_out=p1(8 + j))
            mu4 = small[:, S_P1 + 0:S_P1 + 4]
            m2 = small[:, S_P1 + 8:S_P1 + 12]
            v4 = small[:, S_P1 + 16:S_P1 + 20]
            r4 = small[:, S_P1 + 20:S_P1 + 24]
            t4_ = small[:, S_P1 + 24:S_P1 + 28]
            kb.ts(mu4, mu4, 1.0 / 1024, None, Alu.mult)
            kb.ts(m2, m2, 1.0 / 1024, None, Alu.mult)
            kb.tt(v4, mu4, mu4, Alu.mult)
            kb.tt(v4, m2, v4, Alu.subtract)
            kb.ts(v4, v4, EPS, None, Alu.add)
            kb.wait("scalar", "VV")
            kb.act(r4, v4, AF.Ln)
            kb.act(r4, r4, AF.Exp, scale=-0.5)
            kb.wait("vector", "SS")
            kb.tt(t4_, r4, r4, Alu.mult)
            kb.tt(t4_, t4_, v4, Alu.mult)
            kb.ts(t4_, t4_, -0.5, 1.5, Alu.mult, Alu.add)
            kb.tt(r4, r4, t4_, Alu.mult)
            for j in range(4):
                kb.ts(uslot(j), uslot(j), p1(0 + j), p1(20 + j), Alu.subtract, Alu.mult)
                kb.wait("gpsimd", "VV")
                kb.store(out[(g * 4 + j) * 128:(g * 4 + j) * 128 + 128, :], uslot(j))
            if g + 2 < 8:
                kb.wait("sync", "VV", ucons[g])
                for j in range(4):
                    kb.load(xslot(g + 2, j), feat_tm[((g + 2) * 4 + j) * 128:((g + 2) * 4 + j) * 128 + 128, :])

    x3v = xbuf.rearrange("p (k t) -> p k t", k=8)
    for b in range(NB):
        kb.wait("sync", "VV")
        kb.wait("sync", "TT")
        kb.wait("sync", "SS")
        kb.wait("sync", "SD")
        kb.load(x3v, di["x0" if b == 0 else "x1"].rearrange("(k p) t -> p k t", p=128))
        kb.barrier()
        attn_block(b, 0)
        kb.barrier()
        fm_ln(accs, xs)
        kb.barrier()
        if SI == 1:
            kb.store(dbg[b * 128:(b + 1) * 128, :], xbuf[:, :])
            continue
        ffn_block(b)
        kb.barrier()
        if SI == 2:
            kb.store(dbg[b * 128:(b + 1) * 128, :], xbuf[:, :])
            if b == 0:
                kb.store(dbg3[128:256, :], small[:, 0:1024])
            continue
        kb.vcopy(accb[:, :], xbuf[:, :])
        kb.barrier()
        fm_ln(xs, xs)
        kb.barrier()
        attn_block(b, 1)
        kb.barrier()
        fm_ln(accs, xs)
        kb.barrier()
        if SI == 3:
            kb.store(dbg[b * 128:(b + 1) * 128, :], xbuf[:, :])
            continue
        l1_tail(b)
        kb.barrier()
    if SI >= 4:
        routing()
        kb.barrier()
    if SI >= 5:
        moe()
        final()
    kb.barrier()
    kb.emit()
    return nc


def _consts():
    f32 = np.float32
    c = {}
    c["ones"] = np.ones((128, 128), f32)
    l = np.zeros((128, 128), f32)
    for k in range(128):
        l[k, k + 1:] = 1.0
    c["lmask"] = l
    c["ident"] = np.eye(128, dtype=f32)
    top = 2595.0 * np.log10(21.0)
    mel = np.power(10.0, np.linspace(0.0, top, 32) / 2595.0) - 1.0
    fb64 = 200.0 * mel / 1000.0
    fbhi32 = np.array(FBJ_BITS, dtype=np.uint32).view(np.float32)
    lo = fb64 - fbhi32.astype(np.float64)
    lim = 0.4999 * np.spacing(np.abs(fbhi32)).astype(np.float64)
    lo = np.clip(lo, -lim, lim)
    idx = (np.arange(128) % 64) // 2
    c["fbhi"] = fbhi32[idx].astype(np.float32).reshape(128, 1)
    c["fblo"] = lo[idx].astype(np.float32).reshape(128, 1)
    c["signrow"] = np.array([[-1.0 if d % 2 == 0 else 1.0] for d in range(128)], f32)
    c["irow32"] = np.arange(32, dtype=f32).reshape(1, 32)
    c["c4096col"] = (np.arange(8, dtype=f32) * 4096.0).reshape(8, 1)
    lb = np.zeros((128, 128), f32)
    for m in range(128):
        lb[: 16 * (m // 16), m] = 1.0
    c["lb"] = lb
    bs = np.zeros((128, 8), f32)
    for p in range(128):
        bs[p, p // 16] = 1.0
    c["bs"] = bs
    e16 = np.zeros((8, 128), f32)
    for p in range(128):
        e16[p // 16, p] = 1.0
    c["exp16"] = e16
    return c


def _colmajor(v, k):
    return np.ascontiguousarray(np.asarray(v, np.float32).reshape(k, 128).T)


def _prep(inputs):
    f32 = np.float32
    A = lambda n: np.asarray(inputs[n], f32)
    x = A("L0_attn_x")
    consts = _consts()
    shared = dict(consts)
    shared.update(
        wqkv=A("L0_attn_w_qkv"), wo0=A("L0_attn_w_o"),
        cnw1=A("L0_ffn_cn_w1"), cnw2=A("L0_ffn_cn_w2"), cnw3=A("L0_ffn_cn_w3"),
        cnw4=A("L0_ffn_cn_w4"),
        cnb1=_colmajor(inputs["L0_ffn_cn_b1"], 8), cnb2=_colmajor(inputs["L0_ffn_cn_b2"], 4),
        cnb3=_colmajor(inputs["L0_ffn_cn_b3"], 2),
        cnb4=A("L0_ffn_cn_b4").reshape(1, 1),
        gcw1=A("L0_ffn_gc_w1"), gcw2=A("L0_ffn_gc_w2"), gcw3=A("L0_ffn_gc_w3"),
        gcb1=_colmajor(inputs["L0_ffn_gc_b1"], 4), gcb2=_colmajor(inputs["L0_ffn_gc_b2"], 2),
        gcb3=A("L0_ffn_gc_b3").reshape(1, 1),
        ar=A("L0_ffn_adaptive_range").reshape(1, 1),
        lnw1=_colmajor(inputs["L1_attn_ln_w"], 8), lnb1=_colmajor(inputs["L1_attn_ln_b"], 8),
        wq1=A("L1_attn_w_q"), wkv1=A("L1_attn_w_kv"), wo1=A("L1_attn_w_o"),
        theta=A("L1_attn_theta").reshape(1, 1),
        centT=np.ascontiguousarray(A("L1_ffn_centroids").T),
        lng=_colmajor(inputs["L1_ffn_ln_g"], 8), lnb2=_colmajor(inputs["L1_ffn_ln_b"], 8),
    )
    w1 = A("L1_ffn_w1")
    b1 = np.asarray(inputs["L1_ffn_b1"], f32)
    w2 = A("L1_ffn_w2")
    b2 = np.asarray(inputs["L1_ffn_b2"], f32)
    in_maps = []
    for c in range(NC8):
        m = dict(shared)
        m["x0"] = np.ascontiguousarray(x[2 * c].T)
        m["x1"] = np.ascontiguousarray(x[2 * c + 1].T)
        m["mw1"] = np.ascontiguousarray(w1[c])
        m["mb1"] = _colmajor(b1[c], 32)
        m["mw2"] = np.ascontiguousarray(w2[c])
        m["mb2"] = _colmajor(b2[c], 8)
        es = np.zeros((8, 1), f32)
        es[c, 0] = 1.0
        m["esel"] = es
        e128 = np.zeros((128, 16), f32)
        for pr in range(16):
            e128[16 * c + pr, pr] = 1.0
        m["esel128"] = e128
        in_maps.append(m)
    return in_maps


LAST_RES = [None]


def kernel(**inputs):
    in_maps = _prep(inputs)
    nc = build(_STAGE)
    res = run_bass_kernel_spmd(nc, in_maps, core_ids=list(range(NC8)))
    LAST_RES[0] = res
    outs = [np.asarray(res.results[c]["out"]) for c in range(NC8)]
    return np.concatenate(outs, axis=0).reshape(16, T, D)
